# version 05
# baseline (speedup 1.0000x reference)
import contextlib
import numpy as np
import concourse.bass as bass
import concourse.mybir as mybir

F32 = mybir.dt.float32
BF16 = mybir.dt.bfloat16
AF = mybir.ActivationFunctionType
ALU = mybir.AluOpType




DMA_POOL = 10


class _Eng:
    def __init__(self, name, sem, dsems):
        self.name = name
        self.sem = sem
        self.dsems = dsems
        self.n_ops = 0
        self.n_dma = 0
        self.marks = {}


class Emitter:
    def __init__(self, nc, ctx, engines=("sync", "tensor", "vector", "scalar", "gpsimd")):
        self.nc = nc
        self.engs = {}
        for name in engines:
            sem = ctx.enter_context(nc.semaphore(f"op_{name}"))
            dsems = ([ctx.enter_context(nc.semaphore(f"dma_{name}{i}")) for i in range(DMA_POOL)]
                     if name in ("sync", "scalar", "gpsimd") else [])
            self.engs[name] = _Eng(name, sem, dsems)
        self.last_w = {}
        self.readers = {}
        self._bodies = {n: [] for n in engines}
        self._final = {n: [] for n in engines}

    def op(self, eng, emit, reads=(), writes=(), dma=False):
        e = self.engs[eng]
        deps = []
        for k in reads:
            lw = self.last_w.get(k)
            if lw is not None:
                deps.append(lw)
        for k in writes:
            lw = self.last_w.get(k)
            if lw is not None:
                deps.append(lw)
            deps.extend(self.readers.get(k, ()))
        if dma:
            slot = e.n_dma % DMA_POOL
            rank = e.n_dma // DMA_POOL
            sem = e.dsems[slot]
            rec = (eng, sem, 16 * (rank + 1))
            throttle = (sem, 16 * rank) if rank >= 1 else None
            e.n_dma += 1
        else:
            sem = e.sem
            rec = (eng, sem, e.n_ops + 1)
            throttle = None
            e.n_ops += 1
        for k in reads:
            self.readers.setdefault(k, []).append(rec)
        for k in writes:
            self.last_w[k] = rec
            self.readers[k] = []
        self._bodies[eng].append((emit, deps, dma, sem, throttle))
        return rec

    def dma(self, eng, out, in_, reads=(), writes=(), **kw):
        def emit(h):
            return h.dma_start(out=out, in_=in_, **kw)
        return self.op(eng, emit, reads=reads, writes=writes, dma=True)

    def build(self, block):
        for name in self.engs:
            body = self._bodies[name]
            fins = self._final[name]
            if not body and not fins:
                continue

            def make(body=body, fins=fins, name=name):
                e = self.engs[name]

                def run(h):
                    for emit, deps, dma, sem, throttle in body:
                        for (peng, psem, pval) in deps:
                            mk = e.marks.get(id(psem), -1)
                            if pval > mk:
                                h.wait_ge(psem, pval)
                                e.marks[id(psem)] = pval
                        if throttle is not None:
                            tsem, tval = throttle
                            mk = e.marks.get(id(tsem), -1)
                            if tval > mk:
                                h.wait_ge(tsem, tval)
                                e.marks[id(tsem)] = tval
                        inst = emit(h)
                        inst.then_inc(sem, 16 if dma else 1)
                    for fin in fins:
                        fin(h)
                return run
            getattr(block, name)(make())

    def final(self, eng, fn):
        self._final[eng].append(fn)

    def wait_all(self, eng):
        targets = []
        for name, e in self.engs.items():
            if e.n_ops:
                targets.append((e.sem, e.n_ops))
            for i, ds in enumerate(e.dsems):
                cnt = (e.n_dma - i + DMA_POOL - 1) // DMA_POOL
                if cnt > 0:
                    targets.append((ds, 16 * cnt))

        def fin(h):
            for sem, val in targets:
                h.wait_ge(sem, val)
        self._final[eng].append(fin)

    def barrier(self):
        targets = []
        for name, e in self.engs.items():
            if e.n_ops:
                targets.append((e.sem, e.n_ops))
            for i, ds in enumerate(e.dsems):
                cnt = (e.n_dma - i + DMA_POOL - 1) // DMA_POOL
                if cnt > 0:
                    targets.append((ds, 16 * cnt))
        for name in self.engs:
            def emit(h, targets=targets):
                for sem, val in targets:
                    h.wait_ge(sem, val)
                return h.nop()
            self.op(name, emit)


class CfgA:
    def __init__(self, T=2048, D=1024, E=2048, R=64, NST=16):
        self.T, self.D, self.E, self.R, self.NST = T, D, E, R, NST
        self.TP = T + 128
        self.NT = self.TP // 128
        self.KD = D // 128
        self.ME = 2 * E // 128
        self.MC = E // 128
        self.NT5 = [(i * 512, min(512, self.TP - i * 512)) for i in range((self.TP + 511) // 512)]
        self.NT4 = [(i * 512, min(512, T - i * 512)) for i in range((T + 511) // 512)]


def emit_phase_a(nc, em, EN, cfg, io=None, fused=False, bc_sb=None, pspool=None):
    T, D, E, R, NST = cfg.T, cfg.D, cfg.E, cfg.R, cfg.NST
    TP, NT, KD, ME, MC = cfg.TP, cfg.NT, cfg.KD, cfg.ME, cfg.MC

    if io is None:
        io = {}
        io["x_in"] = nc.dram_tensor("x_in", [TP, D], F32, kind="ExternalInput").ap()
        io["W_in"] = nc.dram_tensor("W_in", [2 * E, D], F32, kind="ExternalInput").ap()
        io["W_xproj"] = nc.dram_tensor("W_xproj", [96, E], F32, kind="ExternalInput").ap()
        io["W_dt"] = nc.dram_tensor("W_dt", [E, R], F32, kind="ExternalInput").ap()
        io["b_dt"] = nc.dram_tensor("b_dt", [E], F32, kind="ExternalInput").ap()
        io["conv_w"] = nc.dram_tensor("conv_w", [E, 4], F32, kind="ExternalInput").ap()
        io["conv_b"] = nc.dram_tensor("conv_b", [E], F32, kind="ExternalInput").ap()
        io["norm_w"] = nc.dram_tensor("norm_w", [D], F32, kind="ExternalInput").ap()
        io["delta_out"] = nc.dram_tensor("delta_out", [E, T], F32, kind="ExternalOutput").ap()
        io["w_out"] = nc.dram_tensor("w_out", [E, T], BF16, kind="ExternalOutput").ap()
        io["xs_out"] = nc.dram_tensor("xs_out", [E, T], BF16, kind="ExternalOutput").ap()
        io["g_out"] = nc.dram_tensor("g_out", [E, T], BF16, kind="ExternalOutput").ap()
        io["B_out"] = nc.dram_tensor("B_out", [NST, T], F32, kind="ExternalOutput").ap()
        io["C_out"] = nc.dram_tensor("C_out", [NST, T], F32, kind="ExternalOutput").ap()
        io["Brep_out"] = nc.dram_tensor("Brep_out", [128, NST, T], BF16, kind="ExternalOutput").ap()
        io["Crep_out"] = nc.dram_tensor("Crep_out", [128, NST, T], BF16, kind="ExternalOutput").ap()
    ident = nc.inline_tensor(np.eye(128, dtype=np.float32), "ident128").ap()

    ident_sb = EN(nc.sbuf_tensor("ident_sb", [128, 128], F32))
    ones_sb = EN(nc.sbuf_tensor("ones_sb", [1, 128], F32))
    Wnat = [EN(nc.sbuf_tensor(f"Wnat{i}", [128, D], F32)) for i in range(2)]
    WinT = [EN(nc.sbuf_tensor(f"WinT{i}", [128, D], BF16)) for i in range(2)]
    WxN = EN(nc.sbuf_tensor("WxN", [96, E], F32))
    WxT = EN(nc.sbuf_tensor("WxT", [128, 96 * MC], BF16))
    WdtN = EN(nc.sbuf_tensor("WdtN", [128, R * MC], F32))
    WdtT = EN(nc.sbuf_tensor("WdtT", [R, E], BF16))
    nw_col = EN(nc.sbuf_tensor("nw_col", [128, KD], F32))
    bdt_col = EN(nc.sbuf_tensor("bdt_col", [128, MC], F32))
    cw_col = EN(nc.sbuf_tensor("cw_col", [128, 4 * MC], F32))
    cb_col = EN(nc.sbuf_tensor("cb_col", [128, MC], F32))

    x_tm = [EN(nc.sbuf_tensor(f"x_tm{i}", [128, D], F32)) for i in range(2)]
    ssum = EN(nc.sbuf_tensor("ssum", [128, NT], F32))
    rstd_cols = EN(nc.sbuf_tensor("rstd_cols", [128, NT], F32))
    rstd_T = EN(nc.sbuf_tensor("rstd_T", [NT, 128], F32))
    rstd_row = EN(nc.sbuf_tensor("rstd_row", [1, TP], F32))
    rstd_rep = EN(nc.sbuf_tensor("rstd_rep", [128, TP], F32))
    xn_bf = [EN(nc.sbuf_tensor(f"xn_bf{k}", [128, TP], BF16)) for k in range(KD)]

    xc_strip = EN(nc.sbuf_tensor("xc_strip", [128, TP], F32))
    acc_str = EN(nc.sbuf_tensor("acc_str", [128, T], F32))
    xs_m = [EN(nc.sbuf_tensor(f"xs_m{i}", [128, T], BF16)) for i in range(2)]
    xs_rl = [EN(nc.sbuf_tensor(f"xs_rl{i}", [128, T], BF16)) for i in range(2)]
    g_stage = [EN(nc.sbuf_tensor(f"g_stage{i}", [128, 512], BF16)) for i in range(2)]

    dbc_sb = EN(nc.sbuf_tensor("dbc_sb", [96, T], F32))
    dt_bf = EN(nc.sbuf_tensor("dt_bf", [R, T], BF16))
    bcr = EN(nc.sbuf_tensor("bcr", [1, NST * 512], F32))
    brep_stage = [EN(nc.sbuf_tensor(f"brep_stage{i}", [128, 512], BF16)) for i in range(4)]
    e1_sb = acc_str
    delta_sb = [EN(nc.sbuf_tensor(f"delta_sb{i}", [128, T], F32)) for i in range(2)]
    wm_sb = [EN(nc.sbuf_tensor(f"wm_sb{i}", [128, T], BF16)) for i in range(2)]

    if pspool is None:
        ps = [EN(nc.psum_tensor(f"ps{i}", [128, 512], F32)) for i in range(4)]
        ps_big = EN(nc.psum_tensor("ps_big", [128, 4 * 512], F32))
    else:
        ps = pspool["ps"]
        ps_big = pspool["ps_big"]
    NPS = len(ps)
    psi = [0]

    def next_ps():
        k = psi[0] % NPS
        psi[0] += 1
        return ps[k], f"ps{k}"

    em.dma("sync", ident_sb[:], ident, writes=["ident_sb"])
    em.dma("sync", nw_col[:], io["norm_w"].rearrange("(k p) -> p k", p=128), writes=["nw_col"], allow_slow_non_contiguous=True)
    em.dma("sync", bdt_col[:], io["b_dt"].rearrange("(m p) -> p m", p=128), writes=["bdt_col"], allow_slow_non_contiguous=True)
    em.dma("sync", cb_col[:], io["conv_b"].rearrange("(m p) -> p m", p=128), writes=["cb_col"], allow_slow_non_contiguous=True)
    em.dma("sync", cw_col[:].rearrange("p (m k) -> p m k", k=4),
           io["conv_w"].rearrange("(m p) k -> p m k", p=128), writes=["cw_col"], allow_slow_non_contiguous=True)
    em.dma("sync", WxN[:], io["W_xproj"], writes=["WxN"])
    em.dma("sync", WdtN[:].rearrange("p (m r) -> p m r", r=R),
           io["W_dt"].rearrange("(m p) r -> p m r", p=128), writes=["WdtN"], allow_slow_non_contiguous=True)
    em.op("vector", lambda h: h.memset(ones_sb[:], 1.0), writes=["ones_sb"])

    for g0 in range(0, MC, 4):
        gn = min(4, MC - g0)
        pst, psk = next_ps()

        def emit_tp(h, g0=g0, gn=gn, pst=pst):
            last = None
            for j in range(gn):
                k = g0 + j
                last = h.transpose(pst[:, j * 128:j * 128 + 96],
                                   WxN[0:96, k * 128:(k + 1) * 128], ident_sb[0:96, 0:96])
            return last
        em.op("tensor", emit_tp, reads=["WxN", "ident_sb"], writes=[psk])

        def emit_ev(h, g0=g0, gn=gn, pst=pst):
            last = None
            for j in range(gn):
                k = g0 + j
                last = h.tensor_copy(out=WxT[:, k * 96:(k + 1) * 96],
                                     in_=pst[:, j * 128:j * 128 + 96])
            return last
        em.op("vector", emit_ev, reads=[psk], writes=[("WxT", g0)])
    wxk = [("WxT", g0) for g0 in range(0, MC, 4)]
    for g0 in range(0, MC, 4):
        gn = min(4, MC - g0)
        pst, psk = next_ps()

        def emit_tp(h, g0=g0, gn=gn, pst=pst):
            last = None
            for j in range(gn):
                mm_ = g0 + j
                last = h.transpose(pst[0:R, j * 128:(j + 1) * 128],
                                   WdtN[:, mm_ * R:(mm_ + 1) * R], ident_sb[:])
            return last
        em.op("tensor", emit_tp, reads=["WdtN", "ident_sb"], writes=[psk])

        def emit_ev(h, g0=g0, gn=gn, pst=pst):
            last = None
            for j in range(gn):
                mm_ = g0 + j
                last = h.tensor_copy(out=WdtT[0:R, mm_ * 128:(mm_ + 1) * 128],
                                     in_=pst[0:R, j * 128:(j + 1) * 128])
            return last
        em.op("vector", emit_ev, reads=[psk], writes=["WdtT"])

    for j in range(NT):
        xb = j % 2
        em.dma("sync", x_tm[xb][:], io["x_in"][j * 128:(j + 1) * 128, :], writes=[("x_tm", xb)])
        em.op("scalar", (lambda h, xb=xb, j=j: h.activation(
            out=x_tm[xb][:], in_=x_tm[xb][:], func=AF.Square, accum_out=ssum[:, j:j + 1])),
            reads=[("x_tm", xb)], writes=[("x_tm", xb), ("ssum", j)])
    em.op("vector", lambda h: h.tensor_scalar(
        out=rstd_cols[:, 0:NT], in0=ssum[:, 0:NT], scalar1=1.0 / D, scalar2=1e-5,
        op0=ALU.mult, op1=ALU.add),
        reads=[("ssum", j) for j in range(NT)], writes=["rstd_cols"])
    em.op("scalar", lambda h: h.activation(out=rstd_cols[:, 0:NT], in_=rstd_cols[:, 0:NT],
                                           func=AF.Ln), reads=["rstd_cols"], writes=["rstd_cols"])
    em.op("scalar", lambda h: h.activation(out=rstd_cols[:, 0:NT], in_=rstd_cols[:, 0:NT],
                                           func=AF.Exp, scale=-0.5),
          reads=["rstd_cols"], writes=["rstd_cols"])
    pst, psk = next_ps()
    em.op("tensor", (lambda h, pst=pst: h.transpose(pst[0:NT, 0:128], rstd_cols[:, 0:NT], ident_sb[:])),
          reads=["rstd_cols", "ident_sb"], writes=[psk])
    em.op("vector", (lambda h, pst=pst: h.tensor_copy(out=rstd_T[0:NT, :], in_=pst[0:NT, 0:128])),
          reads=[psk], writes=["rstd_T"])
    em.dma("sync", rstd_row[0:1, :], rstd_T[0:NT, :],
           reads=["rstd_T"], writes=["rstd_row"], allow_slow_non_contiguous=True)
    for i, (c0, cw) in enumerate(cfg.NT5):
        pst, psk = next_ps()
        em.op("tensor", (lambda h, c0=c0, cw=cw, pst=pst: h.matmul(
            pst[:, 0:cw], ones_sb[0:1, :], rstd_row[0:1, c0:c0 + cw], start=True, stop=True)),
            reads=["rstd_row", "ones_sb"], writes=[psk])
        em.op("vector", (lambda h, c0=c0, cw=cw, pst=pst: h.tensor_copy(
            out=rstd_rep[:, c0:c0 + cw], in_=pst[:, 0:cw])),
            reads=[psk], writes=[("rstd_rep", i)])
    rstd_all = [("rstd_rep", i) for i in range(len(cfg.NT5))]

    for j in range(NT):
        xb = j % 2
        em.dma("sync", x_tm[xb][:], io["x_in"][j * 128:(j + 1) * 128, :], writes=[("x_tm", xb)])
        for g0 in range(0, KD, 4):
            gn = min(4, KD - g0)
            pst, psk = next_ps()

            def emit_tp(h, xb=xb, g0=g0, gn=gn, pst=pst):
                last = None
                for jj in range(gn):
                    k = g0 + jj
                    last = h.transpose(pst[:, jj * 128:(jj + 1) * 128],
                                       x_tm[xb][:, k * 128:(k + 1) * 128], ident_sb[:])
                return last
            em.op("tensor", emit_tp, reads=[("x_tm", xb), "ident_sb"], writes=[psk])

            def emit_ev(h, j=j, g0=g0, gn=gn, pst=pst):
                last = None
                for jj in range(gn):
                    k = g0 + jj
                    last = h.tensor_tensor(out=xn_bf[k][:, j * 128:(j + 1) * 128],
                                           in0=pst[:, jj * 128:(jj + 1) * 128],
                                           in1=rstd_rep[:, j * 128:(j + 1) * 128],
                                           op=ALU.mult)
                return last
            em.op("vector", emit_ev, reads=[psk] + rstd_all, writes=[("xn_bf", j, g0)])
    xn_all = [("xn_bf", j, g0) for j in range(NT) for g0 in range(0, KD, 4)]

    for m in range(ME):
        wb = m % 2
        isconv = m < MC
        em.dma("sync", Wnat[wb][:], io["W_in"][m * 128:(m + 1) * 128, :], writes=[("Wnat", wb)])
        for g0 in range(0, KD, 4):
            gn = min(4, KD - g0)
            pst, psk = next_ps()

            def emit_tp(h, wb=wb, g0=g0, gn=gn, pst=pst):
                last = None
                for jj in range(gn):
                    k = g0 + jj
                    last = h.transpose(pst[:, jj * 128:(jj + 1) * 128],
                                       Wnat[wb][:, k * 128:(k + 1) * 128], ident_sb[:])
                return last
            em.op("tensor", emit_tp, reads=[("Wnat", wb), "ident_sb"], writes=[psk])

            def emit_ev(h, wb=wb, g0=g0, gn=gn, pst=pst):
                last = None
                for jj in range(gn):
                    k = g0 + jj
                    last = h.tensor_scalar(out=WinT[wb][:, k * 128:(k + 1) * 128],
                                           in0=pst[:, jj * 128:(jj + 1) * 128],
                                           scalar1=nw_col[:, k:k + 1], scalar2=None,
                                           op0=ALU.mult)
                return last
            em.op("vector", emit_ev, reads=[psk, "nw_col"], writes=[("WinT", wb, g0)])
        wkeys = [("WinT", wb, g0) for g0 in range(0, KD, 4)]

        for it, (c0, cw) in enumerate(cfg.NT5):
            pst, psk = next_ps()

            def emit_mm(h, wb=wb, c0=c0, cw=cw, pst=pst):
                last = None
                for k in range(KD):
                    last = h.matmul(pst[:, 0:cw], WinT[wb][:, k * 128:(k + 1) * 128],
                                    xn_bf[k][:, c0:c0 + cw],
                                    start=(k == 0), stop=(k == KD - 1))
                return last
            em.op("tensor", emit_mm, reads=wkeys + xn_all, writes=[psk])
            if isconv:
                em.op("vector", (lambda h, c0=c0, cw=cw, pst=pst: h.tensor_copy(
                    out=xc_strip[:, c0:c0 + cw], in_=pst[:, 0:cw])),
                    reads=[psk], writes=[("xc", it)])
            else:
                slot = it % 2
                o0 = max(0, c0 - 128)
                s0 = max(0, 128 - c0)
                em.op("scalar", (lambda h, cw=cw, pst=pst, slot=slot: h.activation(
                    out=g_stage[slot][:, 0:cw], in_=pst[:, 0:cw], func=AF.Silu)),
                    reads=[psk], writes=[("g_stage", slot)])
                if cw - s0 > 0:
                    em.dma("scalar",
                           io["g_out"][(m - MC) * 128:(m - MC + 1) * 128, o0:o0 + cw - s0],
                           g_stage[slot][:, s0:cw],
                           reads=[("g_stage", slot)], writes=[("ghbm", m - MC, it)])
        if isconv:
            xck = [("xc", it) for it in range(len(cfg.NT5))]
            em.op("vector", (lambda h, m=m: h.tensor_scalar(
                out=acc_str[:], in0=xc_strip[:, 125:125 + T], scalar1=cw_col[:, 4 * m:4 * m + 1],
                scalar2=None, op0=ALU.mult)),
                reads=xck + ["cw_col"], writes=["acc"])
            for kk in range(1, 4):
                em.op("vector", (lambda h, m=m, kk=kk: h.scalar_tensor_tensor(
                    out=acc_str[:], in0=xc_strip[:, 125 + kk:125 + kk + T],
                    scalar=cw_col[:, 4 * m + kk:4 * m + kk + 1], in1=acc_str[:],
                    op0=ALU.mult, op1=ALU.add)),
                    reads=xck + ["cw_col", "acc"], writes=["acc"])
            xb2 = m % 2
            em.op("scalar", (lambda h, m=m, xb2=xb2: h.activation(
                out=xs_m[xb2][:], in_=acc_str[:], func=AF.Silu, bias=cb_col[:, m:m + 1])),
                reads=["acc", "cb_col"], writes=[("xs_m", xb2)])
            em.dma("scalar", io["xs_out"][m * 128:(m + 1) * 128, :], xs_m[xb2][:],
                   reads=[("xs_m", xb2)], writes=[("xshbm", m)])
            def emit_dbc(h, m=m, xb2=xb2):
                last = None
                for it, (c0, cw) in enumerate(cfg.NT4):
                    last = h.matmul(ps_big[0:96, it * 512:it * 512 + cw],
                                    WxT[:, m * 96:(m + 1) * 96],
                                    xs_m[xb2][:, c0:c0 + cw],
                                    start=(m == 0), stop=(m == MC - 1),
                                    skip_group_check=True)
                return last
            em.op("tensor", emit_dbc, reads=[("xs_m", xb2)] + wxk,
                  writes=["ps_big"])

    em.op("vector", (lambda h: h.tensor_copy(out=dbc_sb[0:96, :],
                                             in_=ps_big[0:96, 0:T])),
          reads=["ps_big"], writes=["dbc"])
    em.op("vector", lambda h: h.tensor_copy(out=dt_bf[0:R, :], in_=dbc_sb[0:R, :]),
          reads=["dbc"], writes=["dt_bf"])
    if fused:
        bc_bf = EN(nc.sbuf_tensor("bc_bf", [2 * NST, T], BF16))
        em.op("vector", (lambda h: h.tensor_copy(out=bc_bf[:, :],
                                                 in_=dbc_sb[R:R + 2 * NST, :])),
              reads=["dbc"], writes=["bc_bf"])
        em.dma("sync", io["bc_hbm"], bc_bf[:, :], reads=["bc_bf"], writes=["bc_hbm"],
               allow_slow_non_contiguous=True)
    else:
        em.dma("scalar", io["B_out"], dbc_sb[R:R + NST, :], reads=["dbc"])
        em.dma("scalar", io["C_out"], dbc_sb[R + NST:R + 2 * NST, :], reads=["dbc"])

    for i, (c0, cw) in enumerate([] if fused else cfg.NT4):
        for half in range(2):
            dst = io["Brep_out"] if half == 0 else io["Crep_out"]
            em.dma("sync", bcr[0:1, 0:NST * cw],
                   dbc_sb[R + half * NST:R + (half + 1) * NST, c0:c0 + cw],
                   reads=["dbc"], writes=["bcr"], allow_slow_non_contiguous=True)
            for n in range(NST):
                pst, psk = next_ps()
                sb = brep_stage[n % 4]
                em.op("tensor", (lambda h, n=n, cw=cw, pst=pst: h.matmul(
                    pst[:, 0:cw], ones_sb[0:1, :], bcr[0:1, n * cw:(n + 1) * cw],
                    start=True, stop=True)),
                    reads=["bcr", "ones_sb"], writes=[psk])
                em.op("vector", (lambda h, cw=cw, pst=pst, sb=sb: h.tensor_copy(
                    out=sb[:, 0:cw], in_=pst[:, 0:cw])),
                    reads=[psk], writes=[("brs", n % 4)])
                em.dma("scalar", dst[:, n, c0:c0 + cw], sb[:, 0:cw], reads=[("brs", n % 4)])

    for m in range(MC):
        xb2 = m % 2
        em.dma("sync", xs_rl[xb2][:], io["xs_out"][m * 128:(m + 1) * 128, :],
               reads=[("xshbm", m)], writes=[("xs_rl", xb2)])

        def emit_mm(h, m=m):
            last = None
            for it, (c0, cw) in enumerate(cfg.NT4):
                last = h.matmul(ps_big[:, c0:c0 + cw], WdtT[0:R, m * 128:(m + 1) * 128],
                                dt_bf[0:R, c0:c0 + cw], start=True, stop=True)
            return last
        em.op("tensor", emit_mm, reads=["dt_bf", "WdtT"], writes=["ps_big"])
        em.op("scalar", (lambda h, m=m: h.activation(
            out=e1_sb[:], in_=ps_big[:, 0:T], func=AF.Exp, bias=bdt_col[:, m:m + 1])),
            reads=["ps_big", "bdt_col", "acc"], writes=["e1", "acc"])
        db = m % 2
        em.op("scalar", (lambda h, db=db: h.activation(
            out=delta_sb[db][:], in_=e1_sb[:], func=AF.Ln, bias=1.0)),
            reads=["e1"], writes=[("delta", db)])
        em.op("vector", (lambda h, db=db, xb2=xb2: h.tensor_tensor(
            out=wm_sb[db][:], in0=delta_sb[db][:], in1=xs_rl[xb2][:], op=ALU.mult)),
            reads=[("delta", db), ("xs_rl", xb2)], writes=[("wm", db)])
        em.dma("scalar", io["delta_out"][m * 128:(m + 1) * 128, :], delta_sb[db][:],
               reads=[("delta", db)], writes=[("dhbm", m)])
        em.dma("scalar", io["w_out"][m * 128:(m + 1) * 128, :], wm_sb[db][:],
               reads=[("wm", db)], writes=[("whbm", m)])

    return io


def np_phase_a(x_in, W_in, W_xproj, W_dt, b_dt, conv_w, conv_b, norm_w, cfg):
    bf = ml_dtypes.bfloat16

    def b16(v):
        return v.astype(bf).astype(np.float32)
    T, D, E, R, NST = cfg.T, cfg.D, cfg.E, cfg.R, cfg.NST
    ssum = (x_in * x_in).sum(axis=1)
    rstd = np.exp(-0.5 * np.log(ssum / D + 1e-5)).astype(np.float32)
    xn = b16(x_in * rstd[:, None])
    WinT = b16(W_in.T * norm_w[:, None])
    xz_fm = (WinT.T.astype(np.float32) @ xn.T.astype(np.float32)).astype(np.float32)
    xc = xz_fm[:E]
    res = xz_fm[E:][:, 128:]
    acc = np.zeros((E, T), np.float32)
    for k in range(4):
        acc += xc[:, 125 + k:125 + k + T] * conv_w[:, k:k + 1]
    acc_b = acc + conv_b[:, None]
    xs = (acc_b / (1 + np.exp(-acc_b))).astype(np.float32)
    xs_bf = b16(xs)
    g = b16(res / (1 + np.exp(-res)))
    WxT = b16(W_xproj.T)
    dbc = (WxT.T.astype(np.float32) @ xs_bf).astype(np.float32)
    dt_bf = b16(dbc[:R])
    B = dbc[R:R + NST].astype(np.float32)
    C = dbc[R + NST:R + 2 * NST].astype(np.float32)
    WdtT = b16(W_dt.T)
    dl_raw = (WdtT.T.astype(np.float32) @ dt_bf).astype(np.float32)
    z = dl_raw + b_dt[:, None]
    delta = np.log1p(np.exp(z)).astype(np.float32)
    w = b16(delta * xs_bf)
    return dict(delta=delta, w=w, xs=xs_bf, g=g, B=B, C=C,
                Brep=np.broadcast_to(b16(B)[None], (128, NST, T)).copy(),
                Crep=np.broadcast_to(b16(C)[None], (128, NST, T)).copy())


class CfgB:
    def __init__(self, L=16384, Dsl=256, NST=16, Fc=2048, G=2):
        self.L, self.Dsl, self.NST, self.Fc, self.G = L, Dsl, NST, Fc, G
        self.DT = Dsl // 128
        self.NC = L // Fc
        self.NG = NST // G


def emit_phase_b(nc, em, EN, cfg, io=None):
    L, Dsl, NST, Fc, G = cfg.L, cfg.Dsl, cfg.NST, cfg.Fc, cfg.G
    DT, NC, NG = cfg.DT, cfg.NC, cfg.NG

    if io is None:
        io = {}
        io["delta_in"] = nc.dram_tensor("delta_in", [Dsl, L], F32, kind="ExternalInput").ap()
        io["wv_in"] = nc.dram_tensor("wv_in", [Dsl, L], BF16, kind="ExternalInput").ap()
        io["Brep_in"] = nc.dram_tensor("Brep_in", [128, NST, L], BF16, kind="ExternalInput").ap()
        io["Crep_in"] = nc.dram_tensor("Crep_in", [128, NST, L], BF16, kind="ExternalInput").ap()
        io["A_log_in"] = nc.dram_tensor("A_log_in", [Dsl, NST], F32, kind="ExternalInput").ap()
        io["y_out"] = nc.dram_tensor("y_out", [Dsl, L], BF16, kind="ExternalOutput").ap()

    A_sb = [EN(nc.sbuf_tensor(f"A_sb{d}", [128, NST], F32)) for d in range(DT)]
    carry = [EN(nc.sbuf_tensor(f"carry{d}", [128, NST], F32)) for d in range(DT)]
    dc = [EN(nc.sbuf_tensor(f"dc{i}", [128, Fc], F32)) for i in range(2)]
    wc = [EN(nc.sbuf_tensor(f"wc{i}", [128, Fc], BF16)) for i in range(2)]
    Bg = [EN(nc.sbuf_tensor(f"Bg{i}", [128, G * Fc], BF16)) for i in range(2)]
    Cg = [EN(nc.sbuf_tensor(f"Cg{i}", [128, G * Fc], BF16)) for i in range(2)]
    a_pl = [EN(nc.sbuf_tensor(f"a_pl{i}", [128, Fc], F32)) for i in range(3)]
    u_pl = [EN(nc.sbuf_tensor(f"u_pl{i}", [128, Fc], BF16)) for i in range(3)]
    h_pl = [EN(nc.sbuf_tensor(f"h_pl{i}", [128, Fc], BF16)) for i in range(3)]
    z_pl = [EN(nc.sbuf_tensor(f"z_pl{i}", [128, Fc], BF16)) for i in range(2)]
    yac = [EN(nc.sbuf_tensor(f"yac{i}", [128, Fc], BF16)) for i in range(2)]

    for d in range(DT):
        em.dma("sync", A_sb[d][:], io["A_log_in"][d * 128:(d + 1) * 128, :], writes=[("Asb", d)])
        em.op("scalar", (lambda h, d=d: h.activation(out=A_sb[d][:], in_=A_sb[d][:], func=AF.Exp)),
              reads=[("Asb", d)], writes=[("Asb", d)])
        em.op("vector", (lambda h, d=d: h.tensor_scalar_mul(A_sb[d][:], A_sb[d][:], -1.0)),
              reads=[("Asb", d)], writes=[("Asb", d)])
        em.op("vector", (lambda h, d=d: h.memset(carry[d][:], 0.0)), writes=[("carry", d, n) for n in range(NST)])

    for d in range(DT):
        for c in range(NC):
            ci = c % 2
            c0 = c * Fc
            em.dma("sync", dc[ci][:], io["delta_in"][d * 128:(d + 1) * 128, c0:c0 + Fc],
                   writes=[("dc", ci)])
            em.dma("sync", wc[ci][:], io["wv_in"][d * 128:(d + 1) * 128, c0:c0 + Fc],
                   writes=[("wc", ci)])
            for g in range(NG):
                gi = (c * NG + g) % 2
                n0 = g * G
                em.dma("sync", Bg[gi][:], io["Brep_in"][:, n0:n0 + G, c0:c0 + Fc],
                       writes=[("Bg", gi)])
                em.dma("sync", Cg[gi][:], io["Crep_in"][:, n0:n0 + G, c0:c0 + Fc],
                       writes=[("Cg", gi)])
                for j in range(G):
                    n = n0 + j
                    ai = n % 3
                    em.op("scalar", (lambda h, d=d, ci=ci, n=n, ai=ai: h.activation(
                        out=a_pl[ai][:], in_=dc[ci][:], func=AF.Exp,
                        scale=A_sb[d][:, n:n + 1])),
                        reads=[("dc", ci), ("Asb", d)], writes=[("a", ai)])
                    em.op("vector", (lambda h, ci=ci, gi=gi, j=j, ai=ai: h.tensor_tensor(
                        out=u_pl[ai][:], in0=Bg[gi][:, j * Fc:(j + 1) * Fc], in1=wc[ci][:],
                        op=ALU.mult)),
                        reads=[("Bg", gi), ("wc", ci)], writes=[("u", ai)])
                    em.op("vector", (lambda h, d=d, n=n, ai=ai: h.tensor_tensor_scan(
                        out=h_pl[ai][:], data0=a_pl[ai][:], data1=u_pl[ai][:],
                        initial=carry[d][:, n:n + 1], op0=ALU.mult, op1=ALU.add)),
                        reads=[("a", ai), ("u", ai), ("carry", d, n)], writes=[("h", ai)])
                    em.op("vector", (lambda h, d=d, n=n, ai=ai: h.tensor_copy(
                        out=carry[d][:, n:n + 1], in_=h_pl[ai][:, Fc - 1:Fc])),
                        reads=[("h", ai)], writes=[("carry", d, n)])
                    if n == 0:
                        em.op("vector", (lambda h, gi=gi, j=j, ai=ai, ci=ci: h.tensor_tensor(
                            out=yac[ci][:], in0=h_pl[ai][:], in1=Cg[gi][:, j * Fc:(j + 1) * Fc],
                            op=ALU.mult)),
                            reads=[("h", ai), ("Cg", gi)], writes=[("yac", ci)])
                    else:
                        zi = n % 2
                        em.op("vector", (lambda h, gi=gi, j=j, ai=ai, zi=zi: h.tensor_tensor(
                            out=z_pl[zi][:], in0=h_pl[ai][:], in1=Cg[gi][:, j * Fc:(j + 1) * Fc],
                            op=ALU.mult)),
                            reads=[("h", ai), ("Cg", gi)], writes=[("z", zi)])
                        em.op("vector", (lambda h, zi=zi, ci=ci: h.tensor_tensor(
                            out=yac[ci][:], in0=yac[ci][:], in1=z_pl[zi][:], op=ALU.add)),
                            reads=[("z", zi), ("yac", ci)], writes=[("yac", ci)])
            em.dma("scalar", io["y_out"][d * 128:(d + 1) * 128, c0:c0 + Fc], yac[ci][:],
                   reads=[("yac", ci)])
    return io


def np_phase_b(delta_in, wv_in, Brep_in, Crep_in, A_log_in, cfg):
    bf = ml_dtypes.bfloat16

    def b16(v):
        return v.astype(bf).astype(np.float32)
    L, Dsl, NST, Fc = cfg.L, cfg.Dsl, cfg.NST, cfg.Fc
    A = -np.exp(A_log_in.astype(np.float32))
    B = Brep_in[0].astype(np.float32)
    C = Crep_in[0].astype(np.float32)
    w = wv_in.astype(np.float32)
    y = np.zeros((Dsl, L), np.float32)
    h = np.zeros((Dsl, NST), np.float32)
    for t in range(L):
        a = np.exp(A * delta_in[:, t:t + 1])
        u = b16(w[:, t:t + 1] * B[:, t][None, :])
        h = a * h + u
        hb = b16(h)
        z = b16(hb * C[:, t][None, :])
        acc = z[:, 0]
        for n in range(1, NST):
            acc = b16(acc + z[:, n])
        y[:, t] = acc
        if (t + 1) % Fc == 0:
            h = b16(h)
    return b16(y)


class CfgC:
    def __init__(self, T=2048, D=1024, E=2048):
        self.T, self.D, self.E = T, D, E
        self.MC = E // 128
        self.KD = D // 128
        self.NTT = T // 128
        self.MAGIC = 12582912.0


def emit_phase_c(nc, em, EN, cfg, io=None, fused=False, pspool=None):
    T, D, E = cfg.T, cfg.D, cfg.E
    MC, KD, NTT = cfg.MC, cfg.KD, cfg.NTT
    MAGIC = cfg.MAGIC

    if io is None:
        io = {}
        io["y_in"] = nc.dram_tensor("y_in", [E, T], BF16, kind="ExternalInput").ap()
        io["xs_in"] = nc.dram_tensor("xs_in", [E, T], BF16, kind="ExternalInput").ap()
        io["g_in"] = nc.dram_tensor("g_in", [E, T], BF16, kind="ExternalInput").ap()
        io["x2_in"] = nc.dram_tensor("x2_in", [T, D], F32, kind="ExternalInput").ap()
        io["D_in"] = nc.dram_tensor("D_in", [E], F32, kind="ExternalInput").ap()
        io["W_out"] = nc.dram_tensor("W_out", [D, E], F32, kind="ExternalInput").ap()
        io["rhs"] = nc.dram_tensor("rhs", [D, D], F32, kind="ExternalInput").ap()
        io["out"] = nc.dram_tensor("out", [T, D], F32, kind="ExternalOutput").ap()
    x2_src = io["x2_in"]
    identc = nc.inline_tensor(np.eye(128, dtype=np.float32), "ident128c").ap()

    ident_sb = EN(nc.sbuf_tensor("identc_sb", [128, 128], F32))
    onesc_sb = EN(nc.sbuf_tensor("onesc_sb", [1, 128], F32))
    if not fused:
        D_col = EN(nc.sbuf_tensor("D_col", [128, MC], F32))
        ym = [EN(nc.sbuf_tensor(f"ym{i}", [128, T], BF16)) for i in range(2)]
        xsm = [EN(nc.sbuf_tensor(f"xsm{i}", [128, T], BF16)) for i in range(2)]
        gm = [EN(nc.sbuf_tensor(f"gm{i}", [128, T], BF16)) for i in range(2)]
        t1 = EN(nc.sbuf_tensor("t1", [128, T], F32))
    y2 = [EN(nc.sbuf_tensor(f"y2_{k}", [128, T], BF16)) for k in range(MC)]
    Wo_nat = [EN(nc.sbuf_tensor(f"Wo_nat{i}", [128, E], F32)) for i in range(2)]
    WoT = [EN(nc.sbuf_tensor(f"WoT{k}", [128, D], BF16)) for k in range(MC)]
    rh_nat = [EN(nc.sbuf_tensor(f"rh_nat{i}", [128, 128], F32)) for i in range(2)]
    rt_buf = EN(nc.sbuf_tensor("rt_buf", [128, D], F32))
    qr_e = EN(nc.sbuf_tensor("qr_e", [128, D], F32))
    q_rhs = [EN(nc.sbuf_tensor(f"q_rhs{k}", [128, D], BF16)) for k in range(KD)]
    rsc = EN(nc.sbuf_tensor("rsc", [128, KD], F32))
    rsc_T = EN(nc.sbuf_tensor("rsc_T", [KD, 128], F32))
    rs_row = EN(nc.sbuf_tensor("rs_row", [1, D], F32))
    rs_rep = EN(nc.sbuf_tensor("rs_rep", [128, D], F32))
    x2t = [EN(nc.sbuf_tensor(f"x2t{i}", [128, D], F32)) for i in range(2)]
    res1b = [EN(nc.sbuf_tensor(f"res1_{i}", [128, D], F32)) for i in range(2)]
    out1b = [EN(nc.sbuf_tensor(f"out1t{i}", [128, D], F32)) for i in range(2)]
    qvb = [EN(nc.sbuf_tensor(f"qv{i}", [128, D], F32)) for i in range(2)]
    qbfb = [EN(nc.sbuf_tensor(f"qbf{i}", [128, D], F32)) for i in range(2)]
    qTtb = [EN(nc.sbuf_tensor(f"qTt{i}", [128, KD * 128], BF16)) for i in range(2)]
    colsb = [EN(nc.sbuf_tensor(f"cols{i}", [128, 16], F32)) for i in range(2)]
    outt = [EN(nc.sbuf_tensor(f"outt{i}", [128, D], F32)) for i in range(2)]

    if pspool is None:
        psc = [EN(nc.psum_tensor(f"psc{i}", [128, 512], F32)) for i in range(4)]
        psck = "psc"
    else:
        psc = pspool["ps"]
        psck = "ps"
    NPS = len(psc)
    psi = [0]

    def next_ps():
        k = psi[0] % NPS
        psi[0] += 1
        return psc[k], f"{psck}{k}"

    em.dma("sync", ident_sb[:], identc, writes=["ident_sb"])
    if not fused:
        em.dma("sync", D_col[:], io["D_in"].rearrange("(m p) -> p m", p=128),
               writes=["D_col"], allow_slow_non_contiguous=True)
    em.op("vector", lambda h: h.memset(onesc_sb[:], 1.0), writes=["ones"])

    if fused:
        for m in range(MC):
            em.dma("sync", y2[m][:], io["y2_st"][m * 128:(m + 1) * 128, :],
                   reads=[("y2st", m)], writes=[("y2", m)])
    else:
        for m in range(MC):
            b = m % 2
            em.dma("sync", ym[b][:], io["y_in"][m * 128:(m + 1) * 128, :], writes=[("ym", b)])
            em.dma("sync", xsm[b][:], io["xs_in"][m * 128:(m + 1) * 128, :], writes=[("xsm", b)])
            em.dma("sync", gm[b][:], io["g_in"][m * 128:(m + 1) * 128, :], writes=[("gm", b)])
            em.op("vector", (lambda h, b=b, m=m: h.scalar_tensor_tensor(
                out=t1[:], in0=xsm[b][:], scalar=D_col[:, m:m + 1], in1=ym[b][:],
                op0=ALU.mult, op1=ALU.add)),
                reads=[("xsm", b), ("ym", b), "D_col"], writes=["t1"])
            em.op("vector", (lambda h, b=b, m=m: h.tensor_tensor(
                out=y2[m][:], in0=t1[:], in1=gm[b][:], op=ALU.mult)),
                reads=["t1", ("gm", b)], writes=[("y2", m)])

    for j in range(KD):
        b = j % 2
        em.dma("sync", Wo_nat[b][:], io["W_out"][j * 128:(j + 1) * 128, :], writes=[("Wo", b)])
        for g0 in range(0, MC, 4):
            gn = min(4, MC - g0)
            pst, psk = next_ps()

            def emit_tp(h, b=b, g0=g0, gn=gn, pst=pst):
                last = None
                for jj in range(gn):
                    k = g0 + jj
                    last = h.transpose(pst[:, jj * 128:(jj + 1) * 128],
                                       Wo_nat[b][:, k * 128:(k + 1) * 128], ident_sb[:])
                return last
            em.op("tensor", emit_tp, reads=[("Wo", b), "ident_sb"], writes=[psk])

            def emit_ev(h, j=j, g0=g0, gn=gn, pst=pst):
                last = None
                for jj in range(gn):
                    k = g0 + jj
                    last = h.tensor_copy(out=WoT[k][:, j * 128:(j + 1) * 128],
                                         in_=pst[:, jj * 128:(jj + 1) * 128])
                return last
            em.op("vector", emit_ev, reads=[psk], writes=[("WoT", j, g0)])
    wot_all = [("WoT", j, g0) for j in range(KD) for g0 in range(0, MC, 4)]

    for e in range(KD):
        for k in range(KD):
            rb = k % 2
            em.dma("sync", rh_nat[rb][:], io["rhs"][k * 128:(k + 1) * 128, e * 128:(e + 1) * 128],
                   writes=[("rh_nat", rb)], allow_slow_non_contiguous=True)
            pst, psk = next_ps()
            em.op("tensor", (lambda h, rb=rb, pst=pst: h.transpose(
                pst[:, 0:128], rh_nat[rb][:], ident_sb[:])),
                reads=[("rh_nat", rb), "ident_sb"], writes=[psk])
            em.op("vector", (lambda h, k=k, pst=pst: h.tensor_copy(
                out=rt_buf[:, k * 128:(k + 1) * 128], in_=pst[:, 0:128])),
                reads=[psk], writes=[("rt", k)])
        rtk = [("rt", k) for k in range(KD)]
        em.op("vector", (lambda h: h.tensor_reduce(
            out=cols[:, 0:1], in_=rt_buf[:], axis=mybir.AxisListType.X, op=ALU.max,
            apply_absolute_value=True)),
            reads=rtk, writes=[("cols", 0)])
        em.op("vector", (lambda h: h.tensor_scalar(
            out=cols[:, 1:2], in0=cols[:, 0:1], scalar1=1e-12, scalar2=1.0 / 127.0,
            op0=ALU.max, op1=ALU.mult)),
            reads=[("cols", 0)], writes=[("cols", 1)])
        em.op("vector", (lambda h: h.reciprocal(cols[:, 2:3], cols[:, 1:2])),
              reads=[("cols", 1)], writes=[("cols", 2)])
        em.op("vector", (lambda h, e=e: h.tensor_copy(out=rsc[:, e:e + 1], in_=cols[:, 1:2])),
              reads=[("cols", 1)], writes=[("rsc", e)])
        em.op("vector", (lambda h: h.tensor_scalar(
            out=rt_buf[:], in0=rt_buf[:], scalar1=cols[:, 2:3], scalar2=MAGIC,
            op0=ALU.mult, op1=ALU.add)),
            reads=rtk + [("cols", 2)], writes=rtk)
        em.op("vector", (lambda h: h.tensor_scalar(
            out=qr_e[:], in0=rt_buf[:], scalar1=MAGIC, scalar2=None, op0=ALU.subtract)),
            reads=rtk, writes=["qr_e"])
        for k in range(KD):
            pst, psk = next_ps()
            em.op("tensor", (lambda h, k=k, pst=pst: h.transpose(
                pst[:, 0:128], qr_e[:, k * 128:(k + 1) * 128], ident_sb[:])),
                reads=["qr_e", "ident_sb"], writes=[psk])
            em.op("vector", (lambda h, e=e, k=k, pst=pst: h.tensor_copy(
                out=q_rhs[k][:, e * 128:(e + 1) * 128], in_=pst[:, 0:128])),
                reads=[psk], writes=[("q_rhs", k, e)])
    qrhs_all = [("q_rhs", k, e) for k in range(KD) for e in range(KD)]
    rsc_all = [("rsc", e) for e in range(KD)]

    pst, psk = next_ps()
    em.op("tensor", (lambda h, pst=pst: h.transpose(pst[0:KD, 0:128], rsc[:, 0:KD], ident_sb[:])),
          reads=rsc_all + ["ident_sb"], writes=[psk])
    em.op("vector", (lambda h, pst=pst: h.tensor_copy(out=rsc_T[0:KD, :], in_=pst[0:KD, 0:128])),
          reads=[psk], writes=["rsc_T"])
    em.dma("sync", rs_row[0:1, :], rsc_T[0:KD, :], reads=["rsc_T"], writes=["rs_row"],
           allow_slow_non_contiguous=True)
    for i in range(0, D, 512):
        cw = min(512, D - i)
        pst, psk = next_ps()
        em.op("tensor", (lambda h, i=i, cw=cw, pst=pst: h.matmul(
            pst[:, 0:cw], onesc_sb[0:1, :], rs_row[0:1, i:i + cw], start=True, stop=True)),
            reads=["rs_row", "ones"], writes=[psk])
        em.op("vector", (lambda h, i=i, cw=cw, pst=pst: h.tensor_copy(
            out=rs_rep[:, i:i + cw], in_=pst[:, 0:cw])),
            reads=[psk], writes=[("rs_rep", i)])
    rsrep_all = [("rs_rep", i) for i in range(0, D, 512)]

    y2_all = [("y2", m) for m in range(MC)]
    for mt in range(NTT):
        xb = mt % 2
        ob = mt % 2
        res1 = res1b[xb]
        out1t = out1b[xb]
        qv = qvb[xb]
        qbf = qbfb[xb]
        qTt = qTtb[xb]
        cols = colsb[xb]
        f2 = res1
        em.dma("sync", x2t[xb][:], x2_src[mt * 128:(mt + 1) * 128, :], writes=[("x2t", xb)])
        NTD = (D + 511) // 512
        for nt in range(NTD):
            c0 = nt * 512
            cw = min(512, D - c0)
            pst, psk = next_ps()

            def emit_mm(h, mt=mt, c0=c0, cw=cw, pst=pst):
                last = None
                for k in range(MC):
                    last = h.matmul(pst[:, 0:cw], y2[k][:, mt * 128:(mt + 1) * 128],
                                    WoT[k][:, c0:c0 + cw],
                                    start=(k == 0), stop=(k == MC - 1))
                return last
            em.op("tensor", emit_mm, reads=y2_all + wot_all, writes=[psk])
            em.op("vector", (lambda h, c0=c0, cw=cw, xb=xb, pst=pst: h.scalar_tensor_tensor(
                out=res1[:, c0:c0 + cw], in0=x2t[xb][:, c0:c0 + cw],
                scalar=2.0, in1=pst[:, 0:cw], op0=ALU.mult, op1=ALU.add)),
                reads=[psk, ("x2t", xb)], writes=[("res1", xb, nt)])
        r1k = [("res1", xb, nt) for nt in range(NTD)]
        em.op("vector", (lambda h: h.tensor_reduce(
            out=cols[:, 0:1], in_=res1[:], axis=mybir.AxisListType.X, op=ALU.add)),
            reads=r1k, writes=[("cols", xb, 0)])
        em.op("scalar", (lambda h: h.activation(
            out=qv[:], in_=res1[:], func=AF.Square, accum_out=cols[:, 1:2])),
            reads=r1k, writes=[("qv", xb), ("cols", xb, 1)])
        em.op("vector", (lambda h: h.tensor_scalar(
            out=cols[:, 2:3], in0=cols[:, 0:1], scalar1=1.0 / D, scalar2=None, op0=ALU.mult)),
            reads=[("cols", xb, 0)], writes=[("cols", xb, 2)])
        em.op("vector", (lambda h: h.tensor_tensor(
            out=cols[:, 3:4], in0=cols[:, 2:3], in1=cols[:, 2:3], op=ALU.mult)),
            reads=[("cols", xb, 2)], writes=[("cols", xb, 3)])
        em.op("vector", (lambda h: h.tensor_scalar(
            out=cols[:, 4:5], in0=cols[:, 1:2], scalar1=1.0 / D, scalar2=1e-5,
            op0=ALU.mult, op1=ALU.add)),
            reads=[("cols", xb, 1)], writes=[("cols", xb, 4)])
        em.op("vector", (lambda h: h.tensor_tensor(
            out=cols[:, 5:6], in0=cols[:, 4:5], in1=cols[:, 3:4], op=ALU.subtract)),
            reads=[("cols", xb, 3), ("cols", xb, 4)], writes=[("cols", xb, 5)])
        em.op("scalar", (lambda h: h.activation(
            out=cols[:, 6:7], in_=cols[:, 5:6], func=AF.Ln)),
            reads=[("cols", xb, 5)], writes=[("cols", xb, 6)])
        em.op("scalar", (lambda h: h.activation(
            out=cols[:, 7:8], in_=cols[:, 6:7], func=AF.Exp, scale=-0.5)),
            reads=[("cols", xb, 6)], writes=[("cols", xb, 7)])
        em.op("vector", (lambda h: h.tensor_scalar(
            out=cols[:, 8:9], in0=cols[:, 2:3], scalar1=-1.0, scalar2=None, op0=ALU.mult)),
            reads=[("cols", xb, 2)], writes=[("cols", xb, 8)])
        em.op("vector", (lambda h: h.tensor_scalar(
            out=out1t[:], in0=res1[:], scalar1=cols[:, 8:9], scalar2=cols[:, 7:8],
            op0=ALU.add, op1=ALU.mult)),
            reads=r1k + [("cols", xb, 7), ("cols", xb, 8)], writes=[("out1t", xb)])
        em.op("vector", (lambda h: h.tensor_reduce(
            out=cols[:, 9:10], in_=out1t[:], axis=mybir.AxisListType.X, op=ALU.max,
            apply_absolute_value=True)),
            reads=[("out1t", xb)], writes=[("cols", xb, 9)])
        em.op("vector", (lambda h: h.tensor_scalar(
            out=cols[:, 10:11], in0=cols[:, 9:10], scalar1=1e-12, scalar2=1.0 / 127.0,
            op0=ALU.max, op1=ALU.mult)),
            reads=[("cols", xb, 9)], writes=[("cols", xb, 10)])
        em.op("vector", (lambda h: h.reciprocal(cols[:, 11:12], cols[:, 10:11])),
              reads=[("cols", xb, 10)], writes=[("cols", xb, 11)])
        em.op("vector", (lambda h: h.tensor_scalar(
            out=qv[:], in0=out1t[:], scalar1=cols[:, 11:12], scalar2=MAGIC,
            op0=ALU.mult, op1=ALU.add)),
            reads=[("out1t", xb), ("cols", xb, 11)], writes=[("qv", xb)])
        em.op("vector", (lambda h: h.tensor_scalar(
            out=qbf[:], in0=qv[:], scalar1=MAGIC, scalar2=None, op0=ALU.subtract)),
            reads=[("qv", xb)], writes=[("qbf", xb)])
        for g0 in range(0, KD, 4):
            gn = min(4, KD - g0)
            pst, psk = next_ps()

            def emit_tp(h, g0=g0, gn=gn, pst=pst):
                last = None
                for jj in range(gn):
                    k = g0 + jj
                    last = h.transpose(pst[:, jj * 128:(jj + 1) * 128],
                                       qbf[:, k * 128:(k + 1) * 128], ident_sb[:])
                return last
            em.op("tensor", emit_tp, reads=[("qbf", xb), "ident_sb"], writes=[psk])

            def emit_ev(h, g0=g0, gn=gn, pst=pst):
                last = None
                for jj in range(gn):
                    k = g0 + jj
                    last = h.tensor_copy(out=qTt[:, k * 128:(k + 1) * 128],
                                         in_=pst[:, jj * 128:(jj + 1) * 128])
                return last
            em.op("vector", emit_ev, reads=[psk], writes=[("qTt", xb, g0)])
        qt_all = [("qTt", xb, g0) for g0 in range(0, KD, 4)]
        for nt in range(NTD):
            c0 = nt * 512
            cw = min(512, D - c0)
            pst, psk = next_ps()

            def emit_mm(h, c0=c0, cw=cw, pst=pst):
                last = None
                for k in range(KD):
                    last = h.matmul(pst[:, 0:cw], qTt[:, k * 128:(k + 1) * 128],
                                    q_rhs[k][:, c0:c0 + cw],
                                    start=(k == 0), stop=(k == KD - 1))
                return last
            em.op("tensor", emit_mm, reads=qt_all + qrhs_all, writes=[psk])
            em.op("vector", (lambda h, c0=c0, cw=cw, pst=pst: h.scalar_tensor_tensor(
                out=f2[:, c0:c0 + cw], in0=pst[:, 0:cw],
                scalar=cols[:, 10:11], in1=rs_rep[:, c0:c0 + cw],
                op0=ALU.mult, op1=ALU.mult)),
                reads=[psk, ("cols", xb, 10)] + rsrep_all, writes=[("f2", xb, nt), ("res1", xb, nt)])
        f2k = [("f2", xb, nt) for nt in range(NTD)]
        em.op("vector", (lambda h: h.tensor_tensor(
            out=f2[:], in0=f2[:], in1=out1t[:], op=ALU.add)),
            reads=f2k + [("out1t", xb)], writes=[("f2s", xb)] + f2k)
        em.op("vector", (lambda h: h.tensor_reduce(
            out=cols[:, 0:1], in_=f2[:], axis=mybir.AxisListType.X, op=ALU.add)),
            reads=[("f2s", xb)], writes=[("cols", xb, 0)])
        em.op("scalar", (lambda h: h.activation(
            out=qv[:], in_=f2[:], func=AF.Square, accum_out=cols[:, 1:2])),
            reads=[("f2s", xb)], writes=[("qv", xb), ("cols", xb, 1)])
        em.op("vector", (lambda h: h.tensor_scalar(
            out=cols[:, 2:3], in0=cols[:, 0:1], scalar1=1.0 / D, scalar2=None, op0=ALU.mult)),
            reads=[("cols", xb, 0)], writes=[("cols", xb, 2)])
        em.op("vector", (lambda h: h.tensor_tensor(
            out=cols[:, 3:4], in0=cols[:, 2:3], in1=cols[:, 2:3], op=ALU.mult)),
            reads=[("cols", xb, 2)], writes=[("cols", xb, 3)])
        em.op("vector", (lambda h: h.tensor_scalar(
            out=cols[:, 4:5], in0=cols[:, 1:2], scalar1=1.0 / D, scalar2=1e-5,
            op0=ALU.mult, op1=ALU.add)),
            reads=[("cols", xb, 1)], writes=[("cols", xb, 4)])
        em.op("vector", (lambda h: h.tensor_tensor(
            out=cols[:, 5:6], in0=cols[:, 4:5], in1=cols[:, 3:4], op=ALU.subtract)),
            reads=[("cols", xb, 3), ("cols", xb, 4)], writes=[("cols", xb, 5)])
        em.op("scalar", (lambda h: h.activation(
            out=cols[:, 6:7], in_=cols[:, 5:6], func=AF.Ln)),
            reads=[("cols", xb, 5)], writes=[("cols", xb, 6)])
        em.op("scalar", (lambda h: h.activation(
            out=cols[:, 7:8], in_=cols[:, 6:7], func=AF.Exp, scale=-0.5)),
            reads=[("cols", xb, 6)], writes=[("cols", xb, 7)])
        em.op("vector", (lambda h: h.tensor_scalar(
            out=cols[:, 8:9], in0=cols[:, 2:3], scalar1=-1.0, scalar2=None, op0=ALU.mult)),
            reads=[("cols", xb, 2)], writes=[("cols", xb, 8)])
        em.op("vector", (lambda h, ob=ob: h.tensor_scalar(
            out=outt[ob][:], in0=f2[:], scalar1=cols[:, 8:9], scalar2=cols[:, 7:8],
            op0=ALU.add, op1=ALU.mult)),
            reads=[("f2s", xb), ("cols", xb, 7), ("cols", xb, 8)], writes=[("outt", ob)])
        em.dma("scalar", io["out"][mt * 128:(mt + 1) * 128, :], outt[ob][:],
               reads=[("outt", ob)])
    return io


def np_phase_c(y_in, xs_in, g_in, x2_in, D_in, W_out, rhs, cfg):
    bf = ml_dtypes.bfloat16

    def b16(v):
        return v.astype(bf).astype(np.float32)
    T, D, E = cfg.T, cfg.D, cfg.E
    y = y_in.astype(np.float32)
    xs = xs_in.astype(np.float32)
    g = g_in.astype(np.float32)
    t1 = xs * D_in[:, None] + y
    y2 = b16(t1 * g)
    WoT = b16(W_out.T)
    mam = (y2.T @ WoT.astype(np.float32)).astype(np.float32)
    res1 = 2.0 * x2_in + mam
    m = res1.mean(axis=1, keepdims=True)
    v = (res1 * res1).mean(axis=1, keepdims=True) - m * m
    rstd = np.exp(-0.5 * np.log(v + 1e-5)).astype(np.float32)
    out1 = ((res1 - m) * rstd).astype(np.float32)
    ls = np.maximum(np.abs(out1).max(axis=1, keepdims=True), 1e-12) / 127.0
    ql = np.round(out1 * (1.0 / ls)).astype(np.float32)
    rs = np.maximum(np.abs(rhs).max(axis=0, keepdims=True), 1e-12) / 127.0
    qr = np.round(rhs * (1.0 / rs)).astype(np.float32)
    acc = ql @ qr
    f2 = acc * ls * rs + out1
    m2 = f2.mean(axis=1, keepdims=True)
    v2 = (f2 * f2).mean(axis=1, keepdims=True) - m2 * m2
    rstd2 = np.exp(-0.5 * np.log(v2 + 1e-5)).astype(np.float32)
    return ((f2 - m2) * rstd2).astype(np.float32)


def cfg2_nt4(T):
    return [(i * 512, min(512, T - i * 512)) for i in range((T + 511) // 512)]


class CfgS:
    def __init__(self, T=2048, E=2048, R=64, NST=16, W=160, NH=4):
        self.T, self.E, self.R, self.NST, self.W = T, E, R, NST, W
        self.MC = E // 128
        self.NH = NH
        self.NG = NST // NH


def emit_scan_fused(nc, em, EN, cfg, io, bc_sb, pspool, ncore=8):
    T, E, NST, W = cfg.T, cfg.E, cfg.NST, cfg.W
    MC, NH, NG = cfg.MC, cfg.NH, cfg.NG

    ones1 = EN(nc.sbuf_tensor("ones1_s", [1, 128], F32))
    ones1b = EN(nc.sbuf_tensor("ones1b_s", [1, 128], BF16))
    bcB = EN(nc.sbuf_tensor("bcB_row", [1, max(cfg.NG * cfg.T, cfg.NST * cfg.W)], BF16))
    bcC = EN(nc.sbuf_tensor("bcC_row", [1, cfg.NG * cfg.T], BF16))
    A_sb = EN(nc.sbuf_tensor("A_sb_s", [128, NST * MC], F32))
    D_col = EN(nc.sbuf_tensor("D_col_s", [128, MC], F32))
    selv_sb = EN(nc.sbuf_tensor("selv_sb", [128, 8], F32))
    Hout = EN(nc.sbuf_tensor("Hout", [128, NST * MC], F32))
    Hin = EN(nc.sbuf_tensor("Hin", [128, NST * MC], F32))
    Hall_sb = EN(nc.sbuf_tensor("Hall_sb", [128, ncore * MC * NST], F32))
    Brep = EN(nc.sbuf_tensor("Brep", [128, max(NG * T, NST * W)], BF16))
    Crep = EN(nc.sbuf_tensor("Crep", [128, NG * T], BF16))
    BrepW = Brep
    dcm = [EN(nc.sbuf_tensor(f"dcm{i}", [128, T], F32)) for i in range(2)]
    wcm = [EN(nc.sbuf_tensor(f"wcm{i}", [128, T], BF16)) for i in range(2)]
    xscm = [EN(nc.sbuf_tensor(f"xscm{i}", [128, T], BF16)) for i in range(2)]
    gcm = [EN(nc.sbuf_tensor(f"gcm{i}", [128, T], BF16)) for i in range(2)]
    a_big = EN(nc.sbuf_tensor("a_big", [128, 3 * T], F32))
    u_big = EN(nc.sbuf_tensor("u_big", [128, 3 * T], BF16))
    h_big = EN(nc.sbuf_tensor("h_big", [128, 3 * T], BF16))
    a_pl = [a_big[:, i * T:(i + 1) * T] for i in range(3)]
    u_pl = [u_big[:, i * T:(i + 1) * T] for i in range(3)]
    h_pl = [h_big[:, i * T:(i + 1) * T] for i in range(3)]
    z_pl = [EN(nc.sbuf_tensor(f"z_pl_s{i}", [128, T], BF16)) for i in range(2)]
    yac = EN(nc.sbuf_tensor("yac_s", [128, T], BF16))
    yh0 = [EN(nc.sbuf_tensor(f"yh0_{i}", [128, T], BF16)) for i in range(2)]
    y2m = [EN(nc.sbuf_tensor(f"y2m_{i}", [128, T], BF16)) for i in range(2)]
    t1s = EN(nc.sbuf_tensor("t1s", [128, T], F32))

    pss = pspool["ps"]
    NPS = len(pss)
    psi = [0]

    def next_ps():
        k = psi[0] % NPS
        psi[0] += 1
        return pss[k], f"ps{k}"

    em.op("vector", lambda h: h.memset(ones1[:], 1.0), writes=["ones1s_f"])
    em.op("vector", lambda h: h.memset(ones1b[:], 1.0), writes=["ones1s"])
    em.dma("sync", A_sb[:].rearrange("p (m n) -> p m n", n=NST),
           io["A_log_in"].rearrange("(m p) n -> p m n", p=128),
           writes=["Asb"], allow_slow_non_contiguous=True)
    em.dma("sync", D_col[:], io["D_in"].rearrange("(m p) -> p m", p=128),
           writes=["Dcol"], allow_slow_non_contiguous=True)
    em.dma("sync", selv_sb[:], io["selv"], writes=["selv"])
    em.op("scalar", lambda h: h.activation(out=A_sb[:], in_=A_sb[:], func=AF.Exp),
          reads=["Asb"], writes=["Asb"])
    em.op("vector", lambda h: h.tensor_scalar_mul(A_sb[:], A_sb[:], -1.0),
          reads=["Asb"], writes=["Asb"])

    def load_rows(buf, bufkey, n0, nplanes, col0, width):
        em.dma("sync", buf[0:1, 0:nplanes * width],
               io["bc_hbm"][n0:n0 + nplanes, col0:col0 + width],
               reads=["bc_hbm"], writes=[bufkey], allow_slow_non_contiguous=True)

    def bcast_build(dst, rows_key, srcbuf, srckey, nplanes, width, dst_w):
        for j in range(nplanes):
            for cc in range(0, width, 512):
                cw = min(512, width - cc)
                pst, psk = next_ps()
                em.op("tensor", (lambda h, j=j, cc=cc, cw=cw, pst=pst, srcbuf=srcbuf, width=width:
                                 h.matmul(pst[:, 0:cw], ones1b[0:1, :],
                                          srcbuf[0:1, j * width + cc:j * width + cc + cw],
                                          start=True, stop=True)),
                      reads=[srckey, "ones1s"], writes=[psk])
                em.op("vector", (lambda h, j=j, cc=cc, cw=cw, pst=pst:
                                 h.tensor_copy(out=dst[:, j * dst_w + cc:j * dst_w + cc + cw],
                                               in_=pst[:, 0:cw])),
                      reads=[psk], writes=[(rows_key, j)])

    load_rows(bcB, "bcB", 0, NST, T - W, W)
    bcast_build(BrepW, "BrepW", bcB, "bcB", NST, W, W)

    FW = NST * W
    if 2 * FW <= 3 * T:
        apre = [a_big[:, j * FW:(j + 1) * FW] for j in range(2)]
        upre = [u_big[:, j * FW:(j + 1) * FW] for j in range(2)]
        hpre = [h_big[:, j * FW:(j + 1) * FW] for j in range(2)]

        def alias(fam, j):
            lo, hi = j * FW // T, ((j + 1) * FW - 1) // T
            return [(fam, i) for i in range(lo, hi + 1)]
    else:
        apre = [EN(nc.sbuf_tensor(f"apre{i}", [128, FW], F32))[:] for i in range(2)]
        upre = [EN(nc.sbuf_tensor(f"upre{i}", [128, FW], BF16))[:] for i in range(2)]
        hpre = [EN(nc.sbuf_tensor(f"hpre{i}", [128, FW], BF16))[:] for i in range(2)]

        def alias(fam, j):
            return []
    for m in range(MC):
        mb = m % 2
        em.dma("sync", dcm[mb][:, 0:W], io["delta_st"][m * 128:(m + 1) * 128, T - W:T],
               reads=[("dhbm", m)], writes=[("dcm", mb)])
        em.dma("sync", wcm[mb][:, 0:W], io["w_st"][m * 128:(m + 1) * 128, T - W:T],
               reads=[("whbm", m)], writes=[("wcm", mb)])
        for n in range(NST):
            em.op("scalar", (lambda h, m=m, mb=mb, n=n: h.activation(
                out=apre[mb][:, n * W:n * W + W], in_=dcm[mb][:, 0:W], func=AF.Exp,
                scale=A_sb[:, m * NST + n:m * NST + n + 1])),
                reads=[("dcm", mb)] + (["Asb"] if m + n == 0 else []),
                writes=[("apre", mb, n)] + alias("a", mb))
        em.op("vector", (lambda h, mb=mb: h.memset(
            apre[mb].rearrange("p (n w) -> p n w", n=NST)[:, :, 0:1], 0.0)),
            reads=[("apre", mb, n) for n in range(NST)],
            writes=[("apre0", mb)] + alias("a", mb))
        em.op("gpsimd", (lambda h, mb=mb: h.tensor_tensor(
            out=upre[mb].rearrange("p (n w) -> p n w", n=NST),
            in0=BrepW[:, 0:NST * W].rearrange("p (n w) -> p n w", n=NST),
            in1=wcm[mb][:, 0:W].unsqueeze(1).broadcast_to([128, NST, W]),
            op=ALU.mult)),
            reads=[("BrepW", n) for n in range(NST)] + [("wcm", mb)],
            writes=[("upre", mb)] + alias("u", mb))
        em.op("vector", (lambda h, mb=mb: h.tensor_tensor_scan(
            out=hpre[mb], data0=apre[mb], data1=upre[mb],
            initial=0.0, op0=ALU.mult, op1=ALU.add)),
            reads=[("apre", mb, n) for n in range(NST)] + [("apre0", mb), ("upre", mb)]
                  + alias("a", mb) + alias("u", mb),
            writes=[("hpre", mb)] + alias("h", mb))
        em.op("vector", (lambda h, m=m, mb=mb: h.tensor_copy(
            out=Hout[:, m * NST:(m + 1) * NST],
            in_=hpre[mb].rearrange("p (n w) -> p n w", n=NST)[:, :, W - 1])),
            reads=[("hpre", mb)] + alias("h", mb), writes=[("Hout", m)])
    hout_all = [("Hout", m) for m in range(MC)]

    em.dma("scalar", io["hout_st"], Hout[:].rearrange("p (m n) -> p m n", n=NST),
           reads=hout_all, writes=["hout_st"], allow_slow_non_contiguous=True)
    em.op("gpsimd", (lambda h: h.collective_compute(
        "AllGather", ALU.bypass, replica_groups=[list(range(ncore))],
        ins=[io["hout_st"].rearrange("p m n -> (p m n)")],
        outs=[io["hall_st"].rearrange("s p m n -> (s p m n)")])),
        reads=["hout_st"], writes=["hall_st"])

    em.dma("sync", Hall_sb[:].rearrange("p (s m n) -> p s m n", s=ncore, n=NST),
           io["hall_st"].rearrange("s p m n -> p s m n"),
           reads=["hall_st"], writes=["Hall_sb"], allow_slow_non_contiguous=True)
    for si in range(ncore):
        if si == 0:
            em.op("vector", (lambda h, si=si: h.tensor_scalar(
                out=Hin[:], in0=Hall_sb[:, si * MC * NST:(si + 1) * MC * NST],
                scalar1=selv_sb[:, si:si + 1], scalar2=None, op0=ALU.mult)),
                reads=["Hall_sb", "selv"], writes=["Hin"])
        else:
            em.op("vector", (lambda h, si=si: h.scalar_tensor_tensor(
                out=Hin[:], in0=Hall_sb[:, si * MC * NST:(si + 1) * MC * NST],
                scalar=selv_sb[:, si:si + 1], in1=Hin[:],
                op0=ALU.mult, op1=ALU.add)),
                reads=["Hall_sb", "selv", "Hin"], writes=["Hin"])

    em.op("vector", lambda h: h.nop(), reads=[("BrepW", j) for j in range(NST)] +
          [("u", i) for i in range(3)], writes=["BrepW_done"] + [("BrhJ", j) for j in range(NST // NH)])
    for half in range(NH):
        n0 = half * NG
        load_rows(bcB, "bcB", n0, NG, 0, T)
        load_rows(bcC, "bcC", NST + n0, NG, 0, T)
        bcast_build(Brep, "BrhJ", bcB, "bcB", NG, T, T)
        bcast_build(Crep, "CrhJ", bcC, "bcC", NG, T, T)
        for m in range(MC):
            mb = m % 2
            em.dma("sync", dcm[mb][:], io["delta_st"][m * 128:(m + 1) * 128, :],
                   reads=[("dhbm", m)], writes=[("dcm", mb)])
            em.dma("sync", wcm[mb][:], io["w_st"][m * 128:(m + 1) * 128, :],
                   reads=[("whbm", m)], writes=[("wcm", mb)])
            if half == NH - 1:
                em.dma("sync", xscm[mb][:], io["xs_st"][m * 128:(m + 1) * 128, :],
                       reads=[("xshbm", m)], writes=[("xscm", mb)])
                em.dma("sync", gcm[mb][:], io["g_st"][m * 128:(m + 1) * 128, :],
                       reads=[("ghbm", m, it) for it in range(len(cfg2_nt4(T)))],
                       writes=[("gcm", mb)])
                em.dma("sync", yh0[mb][:], io["yh_st"][m * 128:(m + 1) * 128, :],
                       reads=[("yhst", m)], writes=[("yh0", mb)])
            for j in range(NG):
                n = n0 + j
                ai = n % 3
                em.op("scalar", (lambda h, m=m, mb=mb, n=n, ai=ai: h.activation(
                    out=a_pl[ai][:], in_=dcm[mb][:], func=AF.Exp,
                    scale=A_sb[:, m * NST + n:m * NST + n + 1])),
                    reads=[("dcm", mb), "Asb"], writes=[("a", ai)])
                em.op("vector", (lambda h, mb=mb, j=j, ai=ai: h.tensor_tensor(
                    out=u_pl[ai][:], in0=Brep[:, j * T:(j + 1) * T], in1=wcm[mb][:],
                    op=ALU.mult)),
                    reads=[("BrhJ", j), ("wcm", mb)],
                    writes=[("u", ai)])
                em.op("vector", (lambda h, m=m, n=n, ai=ai: h.tensor_tensor_scan(
                    out=h_pl[ai][:], data0=a_pl[ai][:], data1=u_pl[ai][:],
                    initial=Hin[:, m * NST + n:m * NST + n + 1],
                    op0=ALU.mult, op1=ALU.add)),
                    reads=[("a", ai), ("u", ai), "Hin"], writes=[("h", ai)])
                zi = n % 2
                if j == 0:
                    em.op("vector", (lambda h, j=j, ai=ai: h.tensor_tensor(
                        out=yac[:], in0=h_pl[ai][:], in1=Crep[:, j * T:(j + 1) * T],
                        op=ALU.mult)),
                        reads=[("h", ai), ("CrhJ", j)], writes=["yac"])
                else:
                    em.op("vector", (lambda h, j=j, ai=ai, zi=zi: h.tensor_tensor(
                        out=z_pl[zi][:], in0=h_pl[ai][:], in1=Crep[:, j * T:(j + 1) * T],
                        op=ALU.mult)),
                        reads=[("h", ai), ("CrhJ", j)], writes=[("z", zi)])
                    em.op("vector", (lambda h, zi=zi: h.tensor_tensor(
                        out=yac[:], in0=yac[:], in1=z_pl[zi][:], op=ALU.add)),
                        reads=[("z", zi), "yac"], writes=["yac"])
            if half == 0:
                em.dma("scalar", io["yh_st"][m * 128:(m + 1) * 128, :], yac[:],
                       reads=["yac"], writes=[("yhst", m)])
            elif half < NH - 1:
                em.dma("sync", yh0[mb][:], io["yh_st"][m * 128:(m + 1) * 128, :],
                       reads=[("yhst", m)], writes=[("yh0", mb)])
                em.dma("gpsimd", yh0[mb][:], yac[:], accum_op=ALU.add,
                       reads=["yac"], writes=[("yh0", mb)])
                em.dma("scalar", io["yh_st"][m * 128:(m + 1) * 128, :], yh0[mb][:],
                       reads=[("yh0", mb)], writes=[("yhst", m)])
            else:
                em.dma("gpsimd", yac[:], yh0[mb][:], accum_op=ALU.add,
                       reads=[("yh0", mb)], writes=["yac"])
                em.op("vector", (lambda h, m=m, mb=mb: h.scalar_tensor_tensor(
                    out=t1s[:], in0=xscm[mb][:], scalar=D_col[:, m:m + 1], in1=yac[:],
                    op0=ALU.mult, op1=ALU.add)),
                    reads=[("xscm", mb), "yac", "Dcol"], writes=["t1s"])
                y2b = m % 2
                em.op("vector", (lambda h, mb=mb, y2b=y2b: h.tensor_tensor(
                    out=y2m[y2b][:], in0=t1s[:], in1=gcm[mb][:], op=ALU.mult)),
                    reads=["t1s", ("gcm", mb)], writes=[("y2m", y2b)])
                em.dma("scalar", io["y2_st"][m * 128:(m + 1) * 128, :], y2m[y2b][:],
                       reads=[("y2m", y2b)], writes=[("y2st", m)])
    return io


def build_fused(T=2048, D=1024, E=2048, R=64, NST=16, W=160, ncore=8):
    cfa = CfgA(T=T, D=D, E=E, R=R, NST=NST)
    cfs = CfgS(T=T, E=E, R=R, NST=NST, W=W)
    cfc = CfgC(T=T, D=D, E=E)
    nc = bass.Bass("TRN2", target_bir_lowering=False)
    MC = E // 128

    io = {}
    io["x_in"] = nc.dram_tensor("x_in", [T + 128, D], F32, kind="ExternalInput").ap()
    io["W_in"] = nc.dram_tensor("W_in", [2 * E, D], F32, kind="ExternalInput").ap()
    io["W_xproj"] = nc.dram_tensor("W_xproj", [96, E], F32, kind="ExternalInput").ap()
    io["W_dt"] = nc.dram_tensor("W_dt", [E, R], F32, kind="ExternalInput").ap()
    io["b_dt"] = nc.dram_tensor("b_dt", [E], F32, kind="ExternalInput").ap()
    io["conv_w"] = nc.dram_tensor("conv_w", [E, 4], F32, kind="ExternalInput").ap()
    io["conv_b"] = nc.dram_tensor("conv_b", [E], F32, kind="ExternalInput").ap()
    io["norm_w"] = nc.dram_tensor("norm_w", [D], F32, kind="ExternalInput").ap()
    io["A_log_in"] = nc.dram_tensor("A_log_in", [E, NST], F32, kind="ExternalInput").ap()
    io["D_in"] = nc.dram_tensor("D_in", [E], F32, kind="ExternalInput").ap()
    io["W_out"] = nc.dram_tensor("W_out", [D, E], F32, kind="ExternalInput").ap()
    io["rhs"] = nc.dram_tensor("rhs", [D, D], F32, kind="ExternalInput").ap()
    io["selv"] = nc.dram_tensor("selv", [128, 8], F32, kind="ExternalInput").ap()
    io["out"] = nc.dram_tensor("out", [T, D], F32, kind="ExternalOutput").ap()
    io["delta_out"] = nc.dram_tensor("delta_st", [E, T], F32, kind="Internal").ap()
    io["w_out"] = nc.dram_tensor("w_st", [E, T], BF16, kind="Internal").ap()
    io["xs_out"] = nc.dram_tensor("xs_st", [E, T], BF16, kind="Internal").ap()
    io["g_out"] = nc.dram_tensor("g_st", [E, T], BF16, kind="Internal").ap()
    io["delta_st"] = io["delta_out"]
    io["w_st"] = io["w_out"]
    io["xs_st"] = io["xs_out"]
    io["g_st"] = io["g_out"]
    io["y2_st"] = nc.dram_tensor("y2_st", [E, T], BF16, kind="Internal").ap()
    io["yh_st"] = nc.dram_tensor("yh_st", [E, T], BF16, kind="Internal").ap()
    io["bc_hbm"] = nc.dram_tensor("bc_hbm", [2 * NST, T], BF16, kind="Internal").ap()
    io["hout_st"] = nc.dram_tensor("hout_st", [128, MC, NST], F32, kind="Internal").ap()
    io["hall_st"] = nc.dram_tensor("hall_st", [ncore, 128, MC, NST], F32, kind="Internal").ap()

    with contextlib.ExitStack() as octx:
        em = Emitter(nc, octx)
        EN0 = octx.enter_context
        pspool = {
            "ps": [EN0(nc.psum_tensor(f"ps{i}", [128, 512], F32)) for i in range(4)],
            "ps_big": EN0(nc.psum_tensor("ps_big", [128, 4 * 512], F32)),
        }
        bc_sb = None
        with contextlib.ExitStack() as c1:
            emit_phase_a(nc, em, c1.enter_context, cfa, io=io, fused=True,
                         bc_sb=bc_sb, pspool=pspool)
            em.barrier()
        with contextlib.ExitStack() as c2:
            emit_scan_fused(nc, em, c2.enter_context, cfs, io, bc_sb, pspool, ncore=ncore)
            em.barrier()
        with contextlib.ExitStack() as c3:
            io["x2_in"] = io["x_in"][128:, :]
            io["y_in"] = io["y2_st"]; io["xs_in"] = io["xs_st"]; io["g_in"] = io["g_st"]
            emit_phase_c(nc, em, c3.enter_context, cfc, io=io, fused=True, pspool=pspool)
        for e in ("sync", "tensor", "vector", "scalar", "gpsimd"):
            em.wait_all(e)
        block = octx.enter_context(nc.Block())
        em.build(block)
    return nc


def make_in_maps(ins0, T, ncore=8):
    x = np.ascontiguousarray(np.asarray(ins0["L0_attn_x"])[0], dtype=np.float32)
    L, D = x.shape
    xpad = np.concatenate([np.zeros((128, D), np.float32), x], axis=0)
    common = dict(W_in=np.asarray(ins0["L0_attn_W_in"]), W_xproj=np.asarray(ins0["L0_attn_W_xproj"]),
                  W_dt=np.asarray(ins0["L0_attn_W_dt"]), b_dt=np.asarray(ins0["L0_attn_b_dt"]),
                  conv_w=np.asarray(ins0["L0_attn_conv_w"]), conv_b=np.asarray(ins0["L0_attn_conv_b"]),
                  norm_w=np.asarray(ins0["L0_attn_norm_w"]),
                  A_log_in=np.asarray(ins0["L0_attn_A_log"]), D_in=np.asarray(ins0["L0_attn_D"]),
                  W_out=np.asarray(ins0["L0_attn_W_out"]), rhs=np.asarray(ins0["L0_ffn_rhs"]))
    maps = []
    for i in range(ncore):
        sel = np.zeros((128, 8), np.float32)
        if i > 0:
            sel[:, i - 1] = 1.0
        maps.append(dict(
            x_in=np.ascontiguousarray(xpad[i * T:(i + 1) * T + 128]),
            selv=sel,
            **common))
    return maps


def kernel(L0_attn_x, L0_attn_W_in, L0_attn_conv_w, L0_attn_conv_b, L0_attn_W_xproj,
           L0_attn_W_dt, L0_attn_b_dt, L0_attn_A_log, L0_attn_D, L0_attn_W_out,
           L0_attn_norm_w, L0_ffn_rhs):
    from concourse.bass_utils import run_bass_kernel_spmd
    NCORE = 8
    x = np.asarray(L0_attn_x)
    L = x.shape[1]
    T = L // NCORE
    ins0 = dict(L0_attn_x=x, L0_attn_W_in=L0_attn_W_in, L0_attn_conv_w=L0_attn_conv_w,
                L0_attn_conv_b=L0_attn_conv_b, L0_attn_W_xproj=L0_attn_W_xproj,
                L0_attn_W_dt=L0_attn_W_dt, L0_attn_b_dt=L0_attn_b_dt,
                L0_attn_A_log=L0_attn_A_log, L0_attn_D=L0_attn_D,
                L0_attn_W_out=L0_attn_W_out, L0_attn_norm_w=L0_attn_norm_w,
                L0_ffn_rhs=L0_ffn_rhs)
    nc = build_fused(T=T, ncore=NCORE)
    maps = make_in_maps(ins0, T, NCORE)
    res = run_bass_kernel_spmd(nc, maps, core_ids=list(range(NCORE)))
    out = np.concatenate([res.results[i]["out"] for i in range(NCORE)], axis=0)
    return out[None]



# version 06
# speedup vs baseline: 1.0154x; 1.0154x over previous
import contextlib
import numpy as np
import concourse.bass as bass
import concourse.mybir as mybir

F32 = mybir.dt.float32
BF16 = mybir.dt.bfloat16
AF = mybir.ActivationFunctionType
ALU = mybir.AluOpType




DMA_POOL = 10


class _Eng:
    def __init__(self, name, sem, dsems):
        self.name = name
        self.sem = sem
        self.dsems = dsems
        self.n_ops = 0
        self.n_dma = 0
        self.marks = {}


class Emitter:
    def __init__(self, nc, ctx, engines=("sync", "tensor", "vector", "scalar", "gpsimd")):
        self.nc = nc
        self.engs = {}
        for name in engines:
            sem = ctx.enter_context(nc.semaphore(f"op_{name}"))
            dsems = ([ctx.enter_context(nc.semaphore(f"dma_{name}{i}")) for i in range(DMA_POOL)]
                     if name in ("sync", "scalar", "gpsimd") else [])
            self.engs[name] = _Eng(name, sem, dsems)
        self.last_w = {}
        self.readers = {}
        self._bodies = {n: [] for n in engines}
        self._final = {n: [] for n in engines}

    def op(self, eng, emit, reads=(), writes=(), dma=False):
        e = self.engs[eng]
        deps = []
        for k in reads:
            lw = self.last_w.get(k)
            if lw is not None:
                deps.append(lw)
        for k in writes:
            lw = self.last_w.get(k)
            if lw is not None:
                deps.append(lw)
            deps.extend(self.readers.get(k, ()))
        if dma:
            slot = e.n_dma % DMA_POOL
            rank = e.n_dma // DMA_POOL
            sem = e.dsems[slot]
            rec = (eng, sem, 16 * (rank + 1))
            throttle = (sem, 16 * rank) if rank >= 1 else None
            e.n_dma += 1
        else:
            sem = e.sem
            rec = (eng, sem, e.n_ops + 1)
            throttle = None
            e.n_ops += 1
        for k in reads:
            self.readers.setdefault(k, []).append(rec)
        for k in writes:
            self.last_w[k] = rec
            self.readers[k] = []
        self._bodies[eng].append((emit, deps, dma, sem, throttle))
        return rec

    def dma(self, eng, out, in_, reads=(), writes=(), **kw):
        def emit(h):
            return h.dma_start(out=out, in_=in_, **kw)
        return self.op(eng, emit, reads=reads, writes=writes, dma=True)

    def build(self, block):
        for name in self.engs:
            body = self._bodies[name]
            fins = self._final[name]
            if not body and not fins:
                continue

            def make(body=body, fins=fins, name=name):
                e = self.engs[name]

                def run(h):
                    for emit, deps, dma, sem, throttle in body:
                        for (peng, psem, pval) in deps:
                            mk = e.marks.get(id(psem), -1)
                            if pval > mk:
                                h.wait_ge(psem, pval)
                                e.marks[id(psem)] = pval
                        if throttle is not None:
                            tsem, tval = throttle
                            mk = e.marks.get(id(tsem), -1)
                            if tval > mk:
                                h.wait_ge(tsem, tval)
                                e.marks[id(tsem)] = tval
                        inst = emit(h)
                        inst.then_inc(sem, 16 if dma else 1)
                    for fin in fins:
                        fin(h)
                return run
            getattr(block, name)(make())

    def final(self, eng, fn):
        self._final[eng].append(fn)

    def wait_all(self, eng):
        targets = []
        for name, e in self.engs.items():
            if e.n_ops:
                targets.append((e.sem, e.n_ops))
            for i, ds in enumerate(e.dsems):
                cnt = (e.n_dma - i + DMA_POOL - 1) // DMA_POOL
                if cnt > 0:
                    targets.append((ds, 16 * cnt))

        def fin(h):
            for sem, val in targets:
                h.wait_ge(sem, val)
        self._final[eng].append(fin)

    def barrier(self):
        targets = []
        for name, e in self.engs.items():
            if e.n_ops:
                targets.append((e.sem, e.n_ops))
            for i, ds in enumerate(e.dsems):
                cnt = (e.n_dma - i + DMA_POOL - 1) // DMA_POOL
                if cnt > 0:
                    targets.append((ds, 16 * cnt))
        for name in self.engs:
            def emit(h, targets=targets):
                for sem, val in targets:
                    h.wait_ge(sem, val)
                return h.nop()
            self.op(name, emit)


class CfgA:
    def __init__(self, T=2048, D=1024, E=2048, R=64, NST=16):
        self.T, self.D, self.E, self.R, self.NST = T, D, E, R, NST
        self.TP = T + 128
        self.NT = self.TP // 128
        self.KD = D // 128
        self.ME = 2 * E // 128
        self.MC = E // 128
        self.NT5 = [(i * 512, min(512, self.TP - i * 512)) for i in range((self.TP + 511) // 512)]
        self.NT4 = [(i * 512, min(512, T - i * 512)) for i in range((T + 511) // 512)]


def emit_phase_a(nc, em, EN, cfg, io=None, fused=False, bc_sb=None, pspool=None):
    T, D, E, R, NST = cfg.T, cfg.D, cfg.E, cfg.R, cfg.NST
    TP, NT, KD, ME, MC = cfg.TP, cfg.NT, cfg.KD, cfg.ME, cfg.MC

    if io is None:
        io = {}
        io["x_in"] = nc.dram_tensor("x_in", [TP, D], F32, kind="ExternalInput").ap()
        io["W_in"] = nc.dram_tensor("W_in", [2 * E, D], F32, kind="ExternalInput").ap()
        io["W_xproj"] = nc.dram_tensor("W_xproj", [96, E], F32, kind="ExternalInput").ap()
        io["W_dt"] = nc.dram_tensor("W_dt", [E, R], F32, kind="ExternalInput").ap()
        io["b_dt"] = nc.dram_tensor("b_dt", [E], F32, kind="ExternalInput").ap()
        io["conv_w"] = nc.dram_tensor("conv_w", [E, 4], F32, kind="ExternalInput").ap()
        io["conv_b"] = nc.dram_tensor("conv_b", [E], F32, kind="ExternalInput").ap()
        io["norm_w"] = nc.dram_tensor("norm_w", [D], F32, kind="ExternalInput").ap()
        io["delta_out"] = nc.dram_tensor("delta_out", [E, T], F32, kind="ExternalOutput").ap()
        io["w_out"] = nc.dram_tensor("w_out", [E, T], BF16, kind="ExternalOutput").ap()
        io["xs_out"] = nc.dram_tensor("xs_out", [E, T], BF16, kind="ExternalOutput").ap()
        io["g_out"] = nc.dram_tensor("g_out", [E, T], BF16, kind="ExternalOutput").ap()
        io["B_out"] = nc.dram_tensor("B_out", [NST, T], F32, kind="ExternalOutput").ap()
        io["C_out"] = nc.dram_tensor("C_out", [NST, T], F32, kind="ExternalOutput").ap()
        io["Brep_out"] = nc.dram_tensor("Brep_out", [128, NST, T], BF16, kind="ExternalOutput").ap()
        io["Crep_out"] = nc.dram_tensor("Crep_out", [128, NST, T], BF16, kind="ExternalOutput").ap()
    ident = nc.inline_tensor(np.eye(128, dtype=np.float32), "ident128").ap()

    ident_sb = EN(nc.sbuf_tensor("ident_sb", [128, 128], F32))
    ones_sb = EN(nc.sbuf_tensor("ones_sb", [1, 128], F32))
    Wnat = [EN(nc.sbuf_tensor(f"Wnat{i}", [128, D], F32)) for i in range(2)]
    WinT = [EN(nc.sbuf_tensor(f"WinT{i}", [128, D], BF16)) for i in range(2)]
    WxN = EN(nc.sbuf_tensor("WxN", [96, E], F32))
    WxT = EN(nc.sbuf_tensor("WxT", [128, 96 * MC], BF16))
    WdtN = EN(nc.sbuf_tensor("WdtN", [128, R * MC], F32))
    WdtT = EN(nc.sbuf_tensor("WdtT", [R, E], BF16))
    nw_col = EN(nc.sbuf_tensor("nw_col", [128, KD], F32))
    bdt_col = EN(nc.sbuf_tensor("bdt_col", [128, MC], F32))
    cw_col = EN(nc.sbuf_tensor("cw_col", [128, 4 * MC], F32))
    cb_col = EN(nc.sbuf_tensor("cb_col", [128, MC], F32))

    x_tm = [EN(nc.sbuf_tensor(f"x_tm{i}", [128, D], F32)) for i in range(2)]
    ssum = EN(nc.sbuf_tensor("ssum", [128, NT], F32))
    rstd_cols = EN(nc.sbuf_tensor("rstd_cols", [128, NT], F32))
    rstd_T = EN(nc.sbuf_tensor("rstd_T", [NT, 128], F32))
    rstd_row = EN(nc.sbuf_tensor("rstd_row", [1, TP], F32))
    rstd_rep = EN(nc.sbuf_tensor("rstd_rep", [128, TP], F32))
    xn_big = EN(nc.sbuf_tensor("xn_big", [128, KD * TP], BF16))
    xn_bf = [xn_big[:, k * TP:(k + 1) * TP] for k in range(KD)]

    xc_strip = EN(nc.sbuf_tensor("xc_strip", [128, TP], F32))
    acc_str = EN(nc.sbuf_tensor("acc_str", [128, T], F32))
    xs_m = [EN(nc.sbuf_tensor(f"xs_m{i}", [128, T], BF16)) for i in range(2)]
    xs_rl = [EN(nc.sbuf_tensor(f"xs_rl{i}", [128, T], BF16)) for i in range(2)]
    g_stage = [EN(nc.sbuf_tensor(f"g_stage{i}", [128, 512], BF16)) for i in range(2)]

    dbc_sb = EN(nc.sbuf_tensor("dbc_sb", [96, T], F32))
    dt_bf = EN(nc.sbuf_tensor("dt_bf", [R, T], BF16))
    bcr = EN(nc.sbuf_tensor("bcr", [1, NST * 512], F32))
    brep_stage = [EN(nc.sbuf_tensor(f"brep_stage{i}", [128, 512], BF16)) for i in range(4)]
    e1_sb = acc_str
    delta_sb = [EN(nc.sbuf_tensor(f"delta_sb{i}", [128, T], F32)) for i in range(2)]
    wm_sb = [EN(nc.sbuf_tensor(f"wm_sb{i}", [128, T], BF16)) for i in range(2)]

    if pspool is None:
        ps = [EN(nc.psum_tensor(f"ps{i}", [128, 512], F32)) for i in range(4)]
        ps_big = EN(nc.psum_tensor("ps_big", [128, 4 * 512], F32))
    else:
        ps = pspool["ps"]
        ps_big = pspool["ps_big"]
    NPS = len(ps)
    psi = [0]

    def next_ps():
        k = psi[0] % NPS
        psi[0] += 1
        return ps[k], f"ps{k}"

    em.dma("sync", ident_sb[:], ident, writes=["ident_sb"])
    em.dma("sync", nw_col[:], io["norm_w"].rearrange("(k p) -> p k", p=128), writes=["nw_col"], allow_slow_non_contiguous=True)
    em.dma("sync", bdt_col[:], io["b_dt"].rearrange("(m p) -> p m", p=128), writes=["bdt_col"], allow_slow_non_contiguous=True)
    em.dma("sync", cb_col[:], io["conv_b"].rearrange("(m p) -> p m", p=128), writes=["cb_col"], allow_slow_non_contiguous=True)
    em.dma("sync", cw_col[:].rearrange("p (m k) -> p m k", k=4),
           io["conv_w"].rearrange("(m p) k -> p m k", p=128), writes=["cw_col"], allow_slow_non_contiguous=True)
    em.dma("sync", WxN[:], io["W_xproj"], writes=["WxN"])
    em.dma("sync", WdtN[:].rearrange("p (m r) -> p m r", r=R),
           io["W_dt"].rearrange("(m p) r -> p m r", p=128), writes=["WdtN"], allow_slow_non_contiguous=True)
    em.op("vector", lambda h: h.memset(ones_sb[:], 1.0), writes=["ones_sb"])
    if fused:
        A_sb = pspool["A_sb"]
        em.dma("sync", A_sb[:].rearrange("p (m n) -> p m n", n=NST),
               io["A_log_in"].rearrange("(m p) n -> p m n", p=128),
               writes=["Asb"], allow_slow_non_contiguous=True)
        em.op("scalar", lambda h: h.activation(out=A_sb[:], in_=A_sb[:], func=AF.Exp),
              reads=["Asb"], writes=["Asb"])
        em.op("vector", lambda h: h.tensor_scalar_mul(A_sb[:], A_sb[:], -1.0),
              reads=["Asb"], writes=["Asb"])

    for g0 in range(0, MC, 4):
        gn = min(4, MC - g0)
        pst, psk = next_ps()

        def emit_tp(h, g0=g0, gn=gn, pst=pst):
            last = None
            for j in range(gn):
                k = g0 + j
                last = h.transpose(pst[:, j * 128:j * 128 + 96],
                                   WxN[0:96, k * 128:(k + 1) * 128], ident_sb[0:96, 0:96])
            return last
        em.op("tensor", emit_tp, reads=["WxN", "ident_sb"], writes=[psk])

        def emit_ev(h, g0=g0, gn=gn, pst=pst):
            last = None
            for j in range(gn):
                k = g0 + j
                last = h.tensor_copy(out=WxT[:, k * 96:(k + 1) * 96],
                                     in_=pst[:, j * 128:j * 128 + 96])
            return last
        em.op("vector", emit_ev, reads=[psk], writes=[("WxT", g0)])
    wxk = [("WxT", g0) for g0 in range(0, MC, 4)]
    for g0 in range(0, MC, 4):
        gn = min(4, MC - g0)
        pst, psk = next_ps()

        def emit_tp(h, g0=g0, gn=gn, pst=pst):
            last = None
            for j in range(gn):
                mm_ = g0 + j
                last = h.transpose(pst[0:R, j * 128:(j + 1) * 128],
                                   WdtN[:, mm_ * R:(mm_ + 1) * R], ident_sb[:])
            return last
        em.op("tensor", emit_tp, reads=["WdtN", "ident_sb"], writes=[psk])

        def emit_ev(h, g0=g0, gn=gn, pst=pst):
            last = None
            for j in range(gn):
                mm_ = g0 + j
                last = h.tensor_copy(out=WdtT[0:R, mm_ * 128:(mm_ + 1) * 128],
                                     in_=pst[0:R, j * 128:(j + 1) * 128])
            return last
        em.op("vector", emit_ev, reads=[psk], writes=["WdtT"])

    for j in range(NT):
        xb = j % 2
        em.dma("sync", x_tm[xb][:], io["x_in"][j * 128:(j + 1) * 128, :], writes=[("x_tm", xb)])
        em.op("scalar", (lambda h, xb=xb, j=j: h.activation(
            out=x_tm[xb][:], in_=x_tm[xb][:], func=AF.Square, accum_out=ssum[:, j:j + 1])),
            reads=[("x_tm", xb)], writes=[("x_tm", xb), ("ssum", j)])
    em.op("vector", lambda h: h.tensor_scalar(
        out=rstd_cols[:, 0:NT], in0=ssum[:, 0:NT], scalar1=1.0 / D, scalar2=1e-5,
        op0=ALU.mult, op1=ALU.add),
        reads=[("ssum", j) for j in range(NT)], writes=["rstd_cols"])
    em.op("scalar", lambda h: h.activation(out=rstd_cols[:, 0:NT], in_=rstd_cols[:, 0:NT],
                                           func=AF.Ln), reads=["rstd_cols"], writes=["rstd_cols"])
    em.op("scalar", lambda h: h.activation(out=rstd_cols[:, 0:NT], in_=rstd_cols[:, 0:NT],
                                           func=AF.Exp, scale=-0.5),
          reads=["rstd_cols"], writes=["rstd_cols"])
    pst, psk = next_ps()
    em.op("tensor", (lambda h, pst=pst: h.transpose(pst[0:NT, 0:128], rstd_cols[:, 0:NT], ident_sb[:])),
          reads=["rstd_cols", "ident_sb"], writes=[psk])
    em.op("vector", (lambda h, pst=pst: h.tensor_copy(out=rstd_T[0:NT, :], in_=pst[0:NT, 0:128])),
          reads=[psk], writes=["rstd_T"])
    em.dma("sync", rstd_row[0:1, :], rstd_T[0:NT, :],
           reads=["rstd_T"], writes=["rstd_row"], allow_slow_non_contiguous=True)
    for i, (c0, cw) in enumerate(cfg.NT5):
        pst, psk = next_ps()
        em.op("tensor", (lambda h, c0=c0, cw=cw, pst=pst: h.matmul(
            pst[:, 0:cw], ones_sb[0:1, :], rstd_row[0:1, c0:c0 + cw], start=True, stop=True)),
            reads=["rstd_row", "ones_sb"], writes=[psk])
        em.op("vector", (lambda h, c0=c0, cw=cw, pst=pst: h.tensor_copy(
            out=rstd_rep[:, c0:c0 + cw], in_=pst[:, 0:cw])),
            reads=[psk], writes=[("rstd_rep", i)])
    rstd_all = [("rstd_rep", i) for i in range(len(cfg.NT5))]

    for j in range(NT):
        xb = j % 2
        em.dma("sync", x_tm[xb][:], io["x_in"][j * 128:(j + 1) * 128, :], writes=[("x_tm", xb)])
        for g0 in range(0, KD, 4):
            gn = min(4, KD - g0)
            pst, psk = next_ps()

            def emit_tp(h, xb=xb, g0=g0, gn=gn, pst=pst):
                last = None
                for jj in range(gn):
                    k = g0 + jj
                    last = h.transpose(pst[:, jj * 128:(jj + 1) * 128],
                                       x_tm[xb][:, k * 128:(k + 1) * 128], ident_sb[:])
                return last
            em.op("tensor", emit_tp, reads=[("x_tm", xb), "ident_sb"], writes=[psk])

            def emit_ev(h, j=j, g0=g0, gn=gn, pst=pst):
                last = None
                for jj in range(gn):
                    k = g0 + jj
                    last = h.tensor_tensor(out=xn_bf[k][:, j * 128:(j + 1) * 128],
                                           in0=pst[:, jj * 128:(jj + 1) * 128],
                                           in1=rstd_rep[:, j * 128:(j + 1) * 128],
                                           op=ALU.mult)
                return last
            em.op("vector", emit_ev, reads=[psk] + rstd_all, writes=[("xn_bf", j, g0)])
    xn_all = [("xn_bf", j, g0) for j in range(NT) for g0 in range(0, KD, 4)]

    for m in range(ME):
        wb = m % 2
        isconv = m < MC
        em.dma("sync", Wnat[wb][:], io["W_in"][m * 128:(m + 1) * 128, :], writes=[("Wnat", wb)])
        for g0 in range(0, KD, 4):
            gn = min(4, KD - g0)
            pst, psk = next_ps()

            def emit_tp(h, wb=wb, g0=g0, gn=gn, pst=pst):
                last = None
                for jj in range(gn):
                    k = g0 + jj
                    last = h.transpose(pst[:, jj * 128:(jj + 1) * 128],
                                       Wnat[wb][:, k * 128:(k + 1) * 128], ident_sb[:])
                return last
            em.op("tensor", emit_tp, reads=[("Wnat", wb), "ident_sb"], writes=[psk])

            def emit_ev(h, wb=wb, g0=g0, gn=gn, pst=pst):
                last = None
                for jj in range(gn):
                    k = g0 + jj
                    last = h.tensor_scalar(out=WinT[wb][:, k * 128:(k + 1) * 128],
                                           in0=pst[:, jj * 128:(jj + 1) * 128],
                                           scalar1=nw_col[:, k:k + 1], scalar2=None,
                                           op0=ALU.mult)
                return last
            em.op("vector", emit_ev, reads=[psk, "nw_col"], writes=[("WinT", wb, g0)])
        wkeys = [("WinT", wb, g0) for g0 in range(0, KD, 4)]

        for it, (c0, cw) in enumerate(cfg.NT5):
            pst, psk = next_ps()

            def emit_mm(h, wb=wb, c0=c0, cw=cw, pst=pst):
                last = None
                for k in range(KD):
                    last = h.matmul(pst[:, 0:cw], WinT[wb][:, k * 128:(k + 1) * 128],
                                    xn_bf[k][:, c0:c0 + cw],
                                    start=(k == 0), stop=(k == KD - 1))
                return last
            em.op("tensor", emit_mm, reads=wkeys + xn_all, writes=[psk])
            if isconv:
                em.op("vector", (lambda h, c0=c0, cw=cw, pst=pst: h.tensor_copy(
                    out=xc_strip[:, c0:c0 + cw], in_=pst[:, 0:cw])),
                    reads=[psk], writes=[("xc", it)])
            else:
                slot = it % 2
                o0 = max(0, c0 - 128)
                s0 = max(0, 128 - c0)
                em.op("scalar", (lambda h, cw=cw, pst=pst, slot=slot: h.activation(
                    out=g_stage[slot][:, 0:cw], in_=pst[:, 0:cw], func=AF.Silu)),
                    reads=[psk], writes=[("g_stage", slot)])
                if cw - s0 > 0:
                    em.dma("scalar",
                           io["g_out"][(m - MC) * 128:(m - MC + 1) * 128, o0:o0 + cw - s0],
                           g_stage[slot][:, s0:cw],
                           reads=[("g_stage", slot)], writes=[("ghbm", m - MC, it)])
        if isconv:
            xck = [("xc", it) for it in range(len(cfg.NT5))]
            em.op("vector", (lambda h, m=m: h.tensor_scalar(
                out=acc_str[:], in0=xc_strip[:, 125:125 + T], scalar1=cw_col[:, 4 * m:4 * m + 1],
                scalar2=None, op0=ALU.mult)),
                reads=xck + ["cw_col"], writes=["acc"])
            for kk in range(1, 4):
                em.op("vector", (lambda h, m=m, kk=kk: h.scalar_tensor_tensor(
                    out=acc_str[:], in0=xc_strip[:, 125 + kk:125 + kk + T],
                    scalar=cw_col[:, 4 * m + kk:4 * m + kk + 1], in1=acc_str[:],
                    op0=ALU.mult, op1=ALU.add)),
                    reads=xck + ["cw_col", "acc"], writes=["acc"])
            xb2 = m % 2
            em.op("scalar", (lambda h, m=m, xb2=xb2: h.activation(
                out=xs_m[xb2][:], in_=acc_str[:], func=AF.Silu, bias=cb_col[:, m:m + 1])),
                reads=["acc", "cb_col"], writes=[("xs_m", xb2)])
            em.dma("scalar", io["xs_out"][m * 128:(m + 1) * 128, :], xs_m[xb2][:],
                   reads=[("xs_m", xb2)], writes=[("xshbm", m)])
            def emit_dbc(h, m=m, xb2=xb2):
                last = None
                for it, (c0, cw) in enumerate(cfg.NT4):
                    last = h.matmul(ps_big[0:96, it * 512:it * 512 + cw],
                                    WxT[:, m * 96:(m + 1) * 96],
                                    xs_m[xb2][:, c0:c0 + cw],
                                    start=(m == 0), stop=(m == MC - 1),
                                    skip_group_check=True)
                return last
            em.op("tensor", emit_dbc, reads=[("xs_m", xb2)] + wxk,
                  writes=["ps_big"])

    em.op("vector", (lambda h: h.tensor_copy(out=dbc_sb[0:96, :],
                                             in_=ps_big[0:96, 0:T])),
          reads=["ps_big"], writes=["dbc"])
    em.op("vector", lambda h: h.tensor_copy(out=dt_bf[0:R, :], in_=dbc_sb[0:R, :]),
          reads=["dbc"], writes=["dt_bf"])
    if fused:
        bc_bf = EN(nc.sbuf_tensor("bc_bf", [2 * NST, T], BF16))
        em.op("vector", (lambda h: h.tensor_copy(out=bc_bf[:, :],
                                                 in_=dbc_sb[R:R + 2 * NST, :])),
              reads=["dbc"], writes=["bc_bf"])
        em.dma("sync", io["bc_hbm"], bc_bf[:, :], reads=["bc_bf"], writes=["bc_hbm"],
               allow_slow_non_contiguous=True)
        Wn = pspool["W"]
        FW = NST * Wn
        ones1b_a = EN(nc.sbuf_tensor("ones1b_a", [1, 128], BF16))
        if 6 * FW <= KD * TP:
            apre = xn_big[:, 0:2 * FW].bitcast(F32)
            upre = xn_big[:, 2 * FW:3 * FW]
            hpre = xn_big[:, 3 * FW:4 * FW]
            BrepW = xn_big[:, 4 * FW:5 * FW]
            bcWrow = xn_big[0:1, 5 * FW:6 * FW]
        else:
            apre = EN(nc.sbuf_tensor("apre_a", [128, FW], F32))[:]
            upre = EN(nc.sbuf_tensor("upre_a", [128, FW], BF16))[:]
            hpre = EN(nc.sbuf_tensor("hpre_a", [128, FW], BF16))[:]
            BrepW = EN(nc.sbuf_tensor("BrepW_a", [128, FW], BF16))[:]
            bcWrow = EN(nc.sbuf_tensor("bcWrow_a", [1, FW], BF16))[:]
        Hout = pspool["Hout"]
        em.op("vector", lambda h: h.memset(ones1b_a[:], 1.0),
              writes=["ones1b_a"]
              + [("xn_bf", j, g0) for j in range(NT) for g0 in range(0, KD, 4)]
              + ["xn_dead"])
        em.dma("sync", bcWrow[0:1, :], bc_bf[0:NST, T - Wn:T],
               reads=["bc_bf", "xn_dead"], writes=["bcWrow"], allow_slow_non_contiguous=True)
        for n in range(NST):
            pst, psk = next_ps()
            em.op("tensor", (lambda h, n=n, pst=pst: h.matmul(
                pst[:, 0:Wn], ones1b_a[0:1, :], bcWrow[0:1, n * Wn:(n + 1) * Wn],
                start=True, stop=True)),
                reads=["bcWrow", "ones1b_a"], writes=[psk])
            em.op("vector", (lambda h, n=n, pst=pst: h.tensor_copy(
                out=BrepW[:, n * Wn:(n + 1) * Wn], in_=pst[:, 0:Wn])),
                reads=[psk, "xn_dead"], writes=[("BrepW", n)])
    else:
        em.dma("scalar", io["B_out"], dbc_sb[R:R + NST, :], reads=["dbc"])
        em.dma("scalar", io["C_out"], dbc_sb[R + NST:R + 2 * NST, :], reads=["dbc"])

    for i, (c0, cw) in enumerate([] if fused else cfg.NT4):
        for half in range(2):
            dst = io["Brep_out"] if half == 0 else io["Crep_out"]
            em.dma("sync", bcr[0:1, 0:NST * cw],
                   dbc_sb[R + half * NST:R + (half + 1) * NST, c0:c0 + cw],
                   reads=["dbc"], writes=["bcr"], allow_slow_non_contiguous=True)
            for n in range(NST):
                pst, psk = next_ps()
                sb = brep_stage[n % 4]
                em.op("tensor", (lambda h, n=n, cw=cw, pst=pst: h.matmul(
                    pst[:, 0:cw], ones_sb[0:1, :], bcr[0:1, n * cw:(n + 1) * cw],
                    start=True, stop=True)),
                    reads=["bcr", "ones_sb"], writes=[psk])
                em.op("vector", (lambda h, cw=cw, pst=pst, sb=sb: h.tensor_copy(
                    out=sb[:, 0:cw], in_=pst[:, 0:cw])),
                    reads=[psk], writes=[("brs", n % 4)])
                em.dma("scalar", dst[:, n, c0:c0 + cw], sb[:, 0:cw], reads=[("brs", n % 4)])

    for m in range(MC):
        xb2 = m % 2
        em.dma("sync", xs_rl[xb2][:], io["xs_out"][m * 128:(m + 1) * 128, :],
               reads=[("xshbm", m)], writes=[("xs_rl", xb2)])

        def emit_mm(h, m=m):
            last = None
            for it, (c0, cw) in enumerate(cfg.NT4):
                last = h.matmul(ps_big[:, c0:c0 + cw], WdtT[0:R, m * 128:(m + 1) * 128],
                                dt_bf[0:R, c0:c0 + cw], start=True, stop=True)
            return last
        em.op("tensor", emit_mm, reads=["dt_bf", "WdtT"], writes=["ps_big"])
        em.op("scalar", (lambda h, m=m: h.activation(
            out=e1_sb[:], in_=ps_big[:, 0:T], func=AF.Exp, bias=bdt_col[:, m:m + 1])),
            reads=["ps_big", "bdt_col", "acc"], writes=["e1", "acc"])
        db = m % 2
        em.op("scalar", (lambda h, db=db: h.activation(
            out=delta_sb[db][:], in_=e1_sb[:], func=AF.Ln, bias=1.0)),
            reads=["e1"], writes=[("delta", db)])
        em.op("vector", (lambda h, db=db, xb2=xb2: h.tensor_tensor(
            out=wm_sb[db][:], in0=delta_sb[db][:], in1=xs_rl[xb2][:], op=ALU.mult)),
            reads=[("delta", db), ("xs_rl", xb2)], writes=[("wm", db)])
        em.dma("scalar", io["delta_out"][m * 128:(m + 1) * 128, :], delta_sb[db][:],
               reads=[("delta", db)], writes=[("dhbm", m)])
        em.dma("scalar", io["w_out"][m * 128:(m + 1) * 128, :], wm_sb[db][:],
               reads=[("wm", db)], writes=[("whbm", m)])
        if fused:
            em.op("vector", (lambda h, db=db, m=m: h.tensor_tensor(
                out=apre[:].rearrange("p (n w) -> p n w", n=NST),
                in0=delta_sb[db][:, T - Wn:T].unsqueeze(1).broadcast_to([128, NST, Wn]),
                in1=A_sb[:, m * NST:(m + 1) * NST].unsqueeze(2).broadcast_to([128, NST, Wn]),
                op=ALU.mult)),
                reads=[("delta", db), "Asb", "xn_dead"], writes=["apre"])
            em.op("scalar", (lambda h: h.activation(
                out=apre[:], in_=apre[:], func=AF.Exp)),
                reads=["apre"], writes=["apre"])
            em.op("vector", (lambda h: h.memset(
                apre[:].rearrange("p (n w) -> p n w", n=NST)[:, :, 0:1], 0.0)),
                reads=["apre"], writes=["apre0"])
            em.op("vector", (lambda h, db=db: h.tensor_tensor(
                out=upre[:].rearrange("p (n w) -> p n w", n=NST),
                in0=BrepW[:].rearrange("p (n w) -> p n w", n=NST),
                in1=wm_sb[db][:, T - Wn:T].unsqueeze(1).broadcast_to([128, NST, Wn]),
                op=ALU.mult)),
                reads=[("BrepW", n) for n in range(NST)] + [("wm", db), "xn_dead"],
                writes=["upre"])
            em.op("vector", (lambda h: h.tensor_tensor_scan(
                out=hpre[:], data0=apre[:], data1=upre[:],
                initial=0.0, op0=ALU.mult, op1=ALU.add)),
                reads=["apre", "apre0", "upre", "xn_dead"], writes=["hpre"])
            em.op("vector", (lambda h, m=m: h.tensor_copy(
                out=Hout[:, m * NST:(m + 1) * NST],
                in_=hpre[:].rearrange("p (n w) -> p n w", n=NST)[:, :, Wn - 1])),
                reads=["hpre"], writes=[("Hout", m)])

    if fused:
        em.dma("scalar", io["hout_st"], Hout[:].rearrange("p (m n) -> p m n", n=NST),
               reads=[("Hout", m) for m in range(MC)], writes=["hout_st"],
               allow_slow_non_contiguous=True)
        em.op("gpsimd", (lambda h: h.collective_compute(
            "AllGather", ALU.bypass,
            replica_groups=[list(range(pspool["ncore"]))],
            ins=[io["hout_st"].rearrange("p m n -> (p m n)")],
            outs=[io["hall_st"].rearrange("s p m n -> (s p m n)")])),
            reads=["hout_st"], writes=["hall_st"])

    return io


def np_phase_a(x_in, W_in, W_xproj, W_dt, b_dt, conv_w, conv_b, norm_w, cfg):
    bf = ml_dtypes.bfloat16

    def b16(v):
        return v.astype(bf).astype(np.float32)
    T, D, E, R, NST = cfg.T, cfg.D, cfg.E, cfg.R, cfg.NST
    ssum = (x_in * x_in).sum(axis=1)
    rstd = np.exp(-0.5 * np.log(ssum / D + 1e-5)).astype(np.float32)
    xn = b16(x_in * rstd[:, None])
    WinT = b16(W_in.T * norm_w[:, None])
    xz_fm = (WinT.T.astype(np.float32) @ xn.T.astype(np.float32)).astype(np.float32)
    xc = xz_fm[:E]
    res = xz_fm[E:][:, 128:]
    acc = np.zeros((E, T), np.float32)
    for k in range(4):
        acc += xc[:, 125 + k:125 + k + T] * conv_w[:, k:k + 1]
    acc_b = acc + conv_b[:, None]
    xs = (acc_b / (1 + np.exp(-acc_b))).astype(np.float32)
    xs_bf = b16(xs)
    g = b16(res / (1 + np.exp(-res)))
    WxT = b16(W_xproj.T)
    dbc = (WxT.T.astype(np.float32) @ xs_bf).astype(np.float32)
    dt_bf = b16(dbc[:R])
    B = dbc[R:R + NST].astype(np.float32)
    C = dbc[R + NST:R + 2 * NST].astype(np.float32)
    WdtT = b16(W_dt.T)
    dl_raw = (WdtT.T.astype(np.float32) @ dt_bf).astype(np.float32)
    z = dl_raw + b_dt[:, None]
    delta = np.log1p(np.exp(z)).astype(np.float32)
    w = b16(delta * xs_bf)
    return dict(delta=delta, w=w, xs=xs_bf, g=g, B=B, C=C,
                Brep=np.broadcast_to(b16(B)[None], (128, NST, T)).copy(),
                Crep=np.broadcast_to(b16(C)[None], (128, NST, T)).copy())


class CfgB:
    def __init__(self, L=16384, Dsl=256, NST=16, Fc=2048, G=2):
        self.L, self.Dsl, self.NST, self.Fc, self.G = L, Dsl, NST, Fc, G
        self.DT = Dsl // 128
        self.NC = L // Fc
        self.NG = NST // G


def emit_phase_b(nc, em, EN, cfg, io=None):
    L, Dsl, NST, Fc, G = cfg.L, cfg.Dsl, cfg.NST, cfg.Fc, cfg.G
    DT, NC, NG = cfg.DT, cfg.NC, cfg.NG

    if io is None:
        io = {}
        io["delta_in"] = nc.dram_tensor("delta_in", [Dsl, L], F32, kind="ExternalInput").ap()
        io["wv_in"] = nc.dram_tensor("wv_in", [Dsl, L], BF16, kind="ExternalInput").ap()
        io["Brep_in"] = nc.dram_tensor("Brep_in", [128, NST, L], BF16, kind="ExternalInput").ap()
        io["Crep_in"] = nc.dram_tensor("Crep_in", [128, NST, L], BF16, kind="ExternalInput").ap()
        io["A_log_in"] = nc.dram_tensor("A_log_in", [Dsl, NST], F32, kind="ExternalInput").ap()
        io["y_out"] = nc.dram_tensor("y_out", [Dsl, L], BF16, kind="ExternalOutput").ap()

    A_sb = [EN(nc.sbuf_tensor(f"A_sb{d}", [128, NST], F32)) for d in range(DT)]
    carry = [EN(nc.sbuf_tensor(f"carry{d}", [128, NST], F32)) for d in range(DT)]
    dc = [EN(nc.sbuf_tensor(f"dc{i}", [128, Fc], F32)) for i in range(2)]
    wc = [EN(nc.sbuf_tensor(f"wc{i}", [128, Fc], BF16)) for i in range(2)]
    Bg = [EN(nc.sbuf_tensor(f"Bg{i}", [128, G * Fc], BF16)) for i in range(2)]
    Cg = [EN(nc.sbuf_tensor(f"Cg{i}", [128, G * Fc], BF16)) for i in range(2)]
    a_pl = [EN(nc.sbuf_tensor(f"a_pl{i}", [128, Fc], F32)) for i in range(3)]
    u_pl = [EN(nc.sbuf_tensor(f"u_pl{i}", [128, Fc], BF16)) for i in range(3)]
    h_pl = [EN(nc.sbuf_tensor(f"h_pl{i}", [128, Fc], BF16)) for i in range(3)]
    z_pl = [EN(nc.sbuf_tensor(f"z_pl{i}", [128, Fc], BF16)) for i in range(2)]
    yac = [EN(nc.sbuf_tensor(f"yac{i}", [128, Fc], BF16)) for i in range(2)]

    for d in range(DT):
        em.dma("sync", A_sb[d][:], io["A_log_in"][d * 128:(d + 1) * 128, :], writes=[("Asb", d)])
        em.op("scalar", (lambda h, d=d: h.activation(out=A_sb[d][:], in_=A_sb[d][:], func=AF.Exp)),
              reads=[("Asb", d)], writes=[("Asb", d)])
        em.op("vector", (lambda h, d=d: h.tensor_scalar_mul(A_sb[d][:], A_sb[d][:], -1.0)),
              reads=[("Asb", d)], writes=[("Asb", d)])
        em.op("vector", (lambda h, d=d: h.memset(carry[d][:], 0.0)), writes=[("carry", d, n) for n in range(NST)])

    for d in range(DT):
        for c in range(NC):
            ci = c % 2
            c0 = c * Fc
            em.dma("sync", dc[ci][:], io["delta_in"][d * 128:(d + 1) * 128, c0:c0 + Fc],
                   writes=[("dc", ci)])
            em.dma("sync", wc[ci][:], io["wv_in"][d * 128:(d + 1) * 128, c0:c0 + Fc],
                   writes=[("wc", ci)])
            for g in range(NG):
                gi = (c * NG + g) % 2
                n0 = g * G
                em.dma("sync", Bg[gi][:], io["Brep_in"][:, n0:n0 + G, c0:c0 + Fc],
                       writes=[("Bg", gi)])
                em.dma("sync", Cg[gi][:], io["Crep_in"][:, n0:n0 + G, c0:c0 + Fc],
                       writes=[("Cg", gi)])
                for j in range(G):
                    n = n0 + j
                    ai = n % 3
                    em.op("scalar", (lambda h, d=d, ci=ci, n=n, ai=ai: h.activation(
                        out=a_pl[ai][:], in_=dc[ci][:], func=AF.Exp,
                        scale=A_sb[d][:, n:n + 1])),
                        reads=[("dc", ci), ("Asb", d)], writes=[("a", ai)])
                    em.op("vector", (lambda h, ci=ci, gi=gi, j=j, ai=ai: h.tensor_tensor(
                        out=u_pl[ai][:], in0=Bg[gi][:, j * Fc:(j + 1) * Fc], in1=wc[ci][:],
                        op=ALU.mult)),
                        reads=[("Bg", gi), ("wc", ci)], writes=[("u", ai)])
                    em.op("vector", (lambda h, d=d, n=n, ai=ai: h.tensor_tensor_scan(
                        out=h_pl[ai][:], data0=a_pl[ai][:], data1=u_pl[ai][:],
                        initial=carry[d][:, n:n + 1], op0=ALU.mult, op1=ALU.add)),
                        reads=[("a", ai), ("u", ai), ("carry", d, n)], writes=[("h", ai)])
                    em.op("vector", (lambda h, d=d, n=n, ai=ai: h.tensor_copy(
                        out=carry[d][:, n:n + 1], in_=h_pl[ai][:, Fc - 1:Fc])),
                        reads=[("h", ai)], writes=[("carry", d, n)])
                    if n == 0:
                        em.op("vector", (lambda h, gi=gi, j=j, ai=ai, ci=ci: h.tensor_tensor(
                            out=yac[ci][:], in0=h_pl[ai][:], in1=Cg[gi][:, j * Fc:(j + 1) * Fc],
                            op=ALU.mult)),
                            reads=[("h", ai), ("Cg", gi)], writes=[("yac", ci)])
                    else:
                        zi = n % 2
                        em.op("vector", (lambda h, gi=gi, j=j, ai=ai, zi=zi: h.tensor_tensor(
                            out=z_pl[zi][:], in0=h_pl[ai][:], in1=Cg[gi][:, j * Fc:(j + 1) * Fc],
                            op=ALU.mult)),
                            reads=[("h", ai), ("Cg", gi)], writes=[("z", zi)])
                        em.op("vector", (lambda h, zi=zi, ci=ci: h.tensor_tensor(
                            out=yac[ci][:], in0=yac[ci][:], in1=z_pl[zi][:], op=ALU.add)),
                            reads=[("z", zi), ("yac", ci)], writes=[("yac", ci)])
            em.dma("scalar", io["y_out"][d * 128:(d + 1) * 128, c0:c0 + Fc], yac[ci][:],
                   reads=[("yac", ci)])
    return io


def np_phase_b(delta_in, wv_in, Brep_in, Crep_in, A_log_in, cfg):
    bf = ml_dtypes.bfloat16

    def b16(v):
        return v.astype(bf).astype(np.float32)
    L, Dsl, NST, Fc = cfg.L, cfg.Dsl, cfg.NST, cfg.Fc
    A = -np.exp(A_log_in.astype(np.float32))
    B = Brep_in[0].astype(np.float32)
    C = Crep_in[0].astype(np.float32)
    w = wv_in.astype(np.float32)
    y = np.zeros((Dsl, L), np.float32)
    h = np.zeros((Dsl, NST), np.float32)
    for t in range(L):
        a = np.exp(A * delta_in[:, t:t + 1])
        u = b16(w[:, t:t + 1] * B[:, t][None, :])
        h = a * h + u
        hb = b16(h)
        z = b16(hb * C[:, t][None, :])
        acc = z[:, 0]
        for n in range(1, NST):
            acc = b16(acc + z[:, n])
        y[:, t] = acc
        if (t + 1) % Fc == 0:
            h = b16(h)
    return b16(y)


class CfgC:
    def __init__(self, T=2048, D=1024, E=2048):
        self.T, self.D, self.E = T, D, E
        self.MC = E // 128
        self.KD = D // 128
        self.NTT = T // 128
        self.MAGIC = 12582912.0


def emit_phase_c(nc, em, EN, cfg, io=None, fused=False, pspool=None):
    T, D, E = cfg.T, cfg.D, cfg.E
    MC, KD, NTT = cfg.MC, cfg.KD, cfg.NTT
    MAGIC = cfg.MAGIC

    if io is None:
        io = {}
        io["y_in"] = nc.dram_tensor("y_in", [E, T], BF16, kind="ExternalInput").ap()
        io["xs_in"] = nc.dram_tensor("xs_in", [E, T], BF16, kind="ExternalInput").ap()
        io["g_in"] = nc.dram_tensor("g_in", [E, T], BF16, kind="ExternalInput").ap()
        io["x2_in"] = nc.dram_tensor("x2_in", [T, D], F32, kind="ExternalInput").ap()
        io["D_in"] = nc.dram_tensor("D_in", [E], F32, kind="ExternalInput").ap()
        io["W_out"] = nc.dram_tensor("W_out", [D, E], F32, kind="ExternalInput").ap()
        io["rhs"] = nc.dram_tensor("rhs", [D, D], F32, kind="ExternalInput").ap()
        io["out"] = nc.dram_tensor("out", [T, D], F32, kind="ExternalOutput").ap()
    x2_src = io["x2_in"]
    identc = nc.inline_tensor(np.eye(128, dtype=np.float32), "ident128c").ap()

    ident_sb = EN(nc.sbuf_tensor("identc_sb", [128, 128], F32))
    onesc_sb = EN(nc.sbuf_tensor("onesc_sb", [1, 128], F32))
    if not fused:
        D_col = EN(nc.sbuf_tensor("D_col", [128, MC], F32))
        ym = [EN(nc.sbuf_tensor(f"ym{i}", [128, T], BF16)) for i in range(2)]
        xsm = [EN(nc.sbuf_tensor(f"xsm{i}", [128, T], BF16)) for i in range(2)]
        gm = [EN(nc.sbuf_tensor(f"gm{i}", [128, T], BF16)) for i in range(2)]
        t1 = EN(nc.sbuf_tensor("t1", [128, T], F32))
    y2 = [EN(nc.sbuf_tensor(f"y2_{k}", [128, T], BF16)) for k in range(MC)]
    Wo_nat = [EN(nc.sbuf_tensor(f"Wo_nat{i}", [128, E], F32)) for i in range(2)]
    WoT = [EN(nc.sbuf_tensor(f"WoT{k}", [128, D], BF16)) for k in range(MC)]
    rh_nat = [EN(nc.sbuf_tensor(f"rh_nat{i}", [128, 128], F32)) for i in range(2)]
    rt_buf = EN(nc.sbuf_tensor("rt_buf", [128, D], F32))
    qr_e = EN(nc.sbuf_tensor("qr_e", [128, D], F32))
    q_rhs = [EN(nc.sbuf_tensor(f"q_rhs{k}", [128, D], BF16)) for k in range(KD)]
    rsc = EN(nc.sbuf_tensor("rsc", [128, KD], F32))
    rsc_T = EN(nc.sbuf_tensor("rsc_T", [KD, 128], F32))
    rs_row = EN(nc.sbuf_tensor("rs_row", [1, D], F32))
    rs_rep = EN(nc.sbuf_tensor("rs_rep", [128, D], F32))
    x2t = [EN(nc.sbuf_tensor(f"x2t{i}", [128, D], F32)) for i in range(2)]
    res1b = [EN(nc.sbuf_tensor(f"res1_{i}", [128, D], F32)) for i in range(2)]
    out1b = [EN(nc.sbuf_tensor(f"out1t{i}", [128, D], F32)) for i in range(2)]
    qvb = [EN(nc.sbuf_tensor(f"qv{i}", [128, D], F32)) for i in range(2)]
    qbfb = [EN(nc.sbuf_tensor(f"qbf{i}", [128, D], F32)) for i in range(2)]
    qTtb = [EN(nc.sbuf_tensor(f"qTt{i}", [128, KD * 128], BF16)) for i in range(2)]
    colsb = [EN(nc.sbuf_tensor(f"cols{i}", [128, 16], F32)) for i in range(2)]
    outt = [EN(nc.sbuf_tensor(f"outt{i}", [128, D], F32)) for i in range(2)]

    if pspool is None:
        psc = [EN(nc.psum_tensor(f"psc{i}", [128, 512], F32)) for i in range(4)]
        psck = "psc"
    else:
        psc = pspool["ps"]
        psck = "ps"
    NPS = len(psc)
    psi = [0]

    def next_ps():
        k = psi[0] % NPS
        psi[0] += 1
        return psc[k], f"{psck}{k}"

    em.dma("sync", ident_sb[:], identc, writes=["ident_sb"])
    if not fused:
        em.dma("sync", D_col[:], io["D_in"].rearrange("(m p) -> p m", p=128),
               writes=["D_col"], allow_slow_non_contiguous=True)
    em.op("vector", lambda h: h.memset(onesc_sb[:], 1.0), writes=["ones"])

    if fused:
        for m in range(MC):
            em.dma("sync", y2[m][:], io["y2_st"][m * 128:(m + 1) * 128, :],
                   reads=[("y2st", m)], writes=[("y2", m)])
    else:
        for m in range(MC):
            b = m % 2
            em.dma("sync", ym[b][:], io["y_in"][m * 128:(m + 1) * 128, :], writes=[("ym", b)])
            em.dma("sync", xsm[b][:], io["xs_in"][m * 128:(m + 1) * 128, :], writes=[("xsm", b)])
            em.dma("sync", gm[b][:], io["g_in"][m * 128:(m + 1) * 128, :], writes=[("gm", b)])
            em.op("vector", (lambda h, b=b, m=m: h.scalar_tensor_tensor(
                out=t1[:], in0=xsm[b][:], scalar=D_col[:, m:m + 1], in1=ym[b][:],
                op0=ALU.mult, op1=ALU.add)),
                reads=[("xsm", b), ("ym", b), "D_col"], writes=["t1"])
            em.op("vector", (lambda h, b=b, m=m: h.tensor_tensor(
                out=y2[m][:], in0=t1[:], in1=gm[b][:], op=ALU.mult)),
                reads=["t1", ("gm", b)], writes=[("y2", m)])

    for j in range(KD):
        b = j % 2
        em.dma("sync", Wo_nat[b][:], io["W_out"][j * 128:(j + 1) * 128, :], writes=[("Wo", b)])
        for g0 in range(0, MC, 4):
            gn = min(4, MC - g0)
            pst, psk = next_ps()

            def emit_tp(h, b=b, g0=g0, gn=gn, pst=pst):
                last = None
                for jj in range(gn):
                    k = g0 + jj
                    last = h.transpose(pst[:, jj * 128:(jj + 1) * 128],
                                       Wo_nat[b][:, k * 128:(k + 1) * 128], ident_sb[:])
                return last
            em.op("tensor", emit_tp, reads=[("Wo", b), "ident_sb"], writes=[psk])

            def emit_ev(h, j=j, g0=g0, gn=gn, pst=pst):
                last = None
                for jj in range(gn):
                    k = g0 + jj
                    last = h.tensor_copy(out=WoT[k][:, j * 128:(j + 1) * 128],
                                         in_=pst[:, jj * 128:(jj + 1) * 128])
                return last
            em.op("vector", emit_ev, reads=[psk], writes=[("WoT", j, g0)])
    wot_all = [("WoT", j, g0) for j in range(KD) for g0 in range(0, MC, 4)]

    for e in range(KD):
        for k in range(KD):
            rb = k % 2
            em.dma("sync", rh_nat[rb][:], io["rhs"][k * 128:(k + 1) * 128, e * 128:(e + 1) * 128],
                   writes=[("rh_nat", rb)], allow_slow_non_contiguous=True)
            pst, psk = next_ps()
            em.op("tensor", (lambda h, rb=rb, pst=pst: h.transpose(
                pst[:, 0:128], rh_nat[rb][:], ident_sb[:])),
                reads=[("rh_nat", rb), "ident_sb"], writes=[psk])
            em.op("vector", (lambda h, k=k, pst=pst: h.tensor_copy(
                out=rt_buf[:, k * 128:(k + 1) * 128], in_=pst[:, 0:128])),
                reads=[psk], writes=[("rt", k)])
        rtk = [("rt", k) for k in range(KD)]
        em.op("vector", (lambda h: h.tensor_reduce(
            out=cols[:, 0:1], in_=rt_buf[:], axis=mybir.AxisListType.X, op=ALU.max,
            apply_absolute_value=True)),
            reads=rtk, writes=[("cols", 0)])
        em.op("vector", (lambda h: h.tensor_scalar(
            out=cols[:, 1:2], in0=cols[:, 0:1], scalar1=1e-12, scalar2=1.0 / 127.0,
            op0=ALU.max, op1=ALU.mult)),
            reads=[("cols", 0)], writes=[("cols", 1)])
        em.op("vector", (lambda h: h.reciprocal(cols[:, 2:3], cols[:, 1:2])),
              reads=[("cols", 1)], writes=[("cols", 2)])
        em.op("vector", (lambda h, e=e: h.tensor_copy(out=rsc[:, e:e + 1], in_=cols[:, 1:2])),
              reads=[("cols", 1)], writes=[("rsc", e)])
        em.op("vector", (lambda h: h.tensor_scalar(
            out=rt_buf[:], in0=rt_buf[:], scalar1=cols[:, 2:3], scalar2=MAGIC,
            op0=ALU.mult, op1=ALU.add)),
            reads=rtk + [("cols", 2)], writes=rtk)
        em.op("vector", (lambda h: h.tensor_scalar(
            out=qr_e[:], in0=rt_buf[:], scalar1=MAGIC, scalar2=None, op0=ALU.subtract)),
            reads=rtk, writes=["qr_e"])
        for k in range(KD):
            pst, psk = next_ps()
            em.op("tensor", (lambda h, k=k, pst=pst: h.transpose(
                pst[:, 0:128], qr_e[:, k * 128:(k + 1) * 128], ident_sb[:])),
                reads=["qr_e", "ident_sb"], writes=[psk])
            em.op("vector", (lambda h, e=e, k=k, pst=pst: h.tensor_copy(
                out=q_rhs[k][:, e * 128:(e + 1) * 128], in_=pst[:, 0:128])),
                reads=[psk], writes=[("q_rhs", k, e)])
    qrhs_all = [("q_rhs", k, e) for k in range(KD) for e in range(KD)]
    rsc_all = [("rsc", e) for e in range(KD)]

    pst, psk = next_ps()
    em.op("tensor", (lambda h, pst=pst: h.transpose(pst[0:KD, 0:128], rsc[:, 0:KD], ident_sb[:])),
          reads=rsc_all + ["ident_sb"], writes=[psk])
    em.op("vector", (lambda h, pst=pst: h.tensor_copy(out=rsc_T[0:KD, :], in_=pst[0:KD, 0:128])),
          reads=[psk], writes=["rsc_T"])
    em.dma("sync", rs_row[0:1, :], rsc_T[0:KD, :], reads=["rsc_T"], writes=["rs_row"],
           allow_slow_non_contiguous=True)
    for i in range(0, D, 512):
        cw = min(512, D - i)
        pst, psk = next_ps()
        em.op("tensor", (lambda h, i=i, cw=cw, pst=pst: h.matmul(
            pst[:, 0:cw], onesc_sb[0:1, :], rs_row[0:1, i:i + cw], start=True, stop=True)),
            reads=["rs_row", "ones"], writes=[psk])
        em.op("vector", (lambda h, i=i, cw=cw, pst=pst: h.tensor_copy(
            out=rs_rep[:, i:i + cw], in_=pst[:, 0:cw])),
            reads=[psk], writes=[("rs_rep", i)])
    rsrep_all = [("rs_rep", i) for i in range(0, D, 512)]

    y2_all = [("y2", m) for m in range(MC)]
    for mt in range(NTT):
        xb = mt % 2
        ob = mt % 2
        res1 = res1b[xb]
        out1t = out1b[xb]
        qv = qvb[xb]
        qbf = qbfb[xb]
        qTt = qTtb[xb]
        cols = colsb[xb]
        f2 = res1
        em.dma("sync", x2t[xb][:], x2_src[mt * 128:(mt + 1) * 128, :], writes=[("x2t", xb)])
        NTD = (D + 511) // 512
        for nt in range(NTD):
            c0 = nt * 512
            cw = min(512, D - c0)
            pst, psk = next_ps()

            def emit_mm(h, mt=mt, c0=c0, cw=cw, pst=pst):
                last = None
                for k in range(MC):
                    last = h.matmul(pst[:, 0:cw], y2[k][:, mt * 128:(mt + 1) * 128],
                                    WoT[k][:, c0:c0 + cw],
                                    start=(k == 0), stop=(k == MC - 1))
                return last
            em.op("tensor", emit_mm, reads=y2_all + wot_all, writes=[psk])
            em.op("vector", (lambda h, c0=c0, cw=cw, xb=xb, pst=pst: h.scalar_tensor_tensor(
                out=res1[:, c0:c0 + cw], in0=x2t[xb][:, c0:c0 + cw],
                scalar=2.0, in1=pst[:, 0:cw], op0=ALU.mult, op1=ALU.add)),
                reads=[psk, ("x2t", xb)], writes=[("res1", xb, nt)])
        r1k = [("res1", xb, nt) for nt in range(NTD)]
        em.op("vector", (lambda h: h.tensor_reduce(
            out=cols[:, 0:1], in_=res1[:], axis=mybir.AxisListType.X, op=ALU.add)),
            reads=r1k, writes=[("cols", xb, 0)])
        em.op("scalar", (lambda h: h.activation(
            out=qv[:], in_=res1[:], func=AF.Square, accum_out=cols[:, 1:2])),
            reads=r1k, writes=[("qv", xb), ("cols", xb, 1)])
        em.op("vector", (lambda h: h.tensor_scalar(
            out=cols[:, 2:3], in0=cols[:, 0:1], scalar1=1.0 / D, scalar2=None, op0=ALU.mult)),
            reads=[("cols", xb, 0)], writes=[("cols", xb, 2)])
        em.op("vector", (lambda h: h.tensor_tensor(
            out=cols[:, 3:4], in0=cols[:, 2:3], in1=cols[:, 2:3], op=ALU.mult)),
            reads=[("cols", xb, 2)], writes=[("cols", xb, 3)])
        em.op("vector", (lambda h: h.tensor_scalar(
            out=cols[:, 4:5], in0=cols[:, 1:2], scalar1=1.0 / D, scalar2=1e-5,
            op0=ALU.mult, op1=ALU.add)),
            reads=[("cols", xb, 1)], writes=[("cols", xb, 4)])
        em.op("vector", (lambda h: h.tensor_tensor(
            out=cols[:, 5:6], in0=cols[:, 4:5], in1=cols[:, 3:4], op=ALU.subtract)),
            reads=[("cols", xb, 3), ("cols", xb, 4)], writes=[("cols", xb, 5)])
        em.op("scalar", (lambda h: h.activation(
            out=cols[:, 6:7], in_=cols[:, 5:6], func=AF.Ln)),
            reads=[("cols", xb, 5)], writes=[("cols", xb, 6)])
        em.op("scalar", (lambda h: h.activation(
            out=cols[:, 7:8], in_=cols[:, 6:7], func=AF.Exp, scale=-0.5)),
            reads=[("cols", xb, 6)], writes=[("cols", xb, 7)])
        em.op("vector", (lambda h: h.tensor_scalar(
            out=cols[:, 8:9], in0=cols[:, 2:3], scalar1=-1.0, scalar2=None, op0=ALU.mult)),
            reads=[("cols", xb, 2)], writes=[("cols", xb, 8)])
        em.op("vector", (lambda h: h.tensor_scalar(
            out=out1t[:], in0=res1[:], scalar1=cols[:, 8:9], scalar2=cols[:, 7:8],
            op0=ALU.add, op1=ALU.mult)),
            reads=r1k + [("cols", xb, 7), ("cols", xb, 8)], writes=[("out1t", xb)])
        em.op("vector", (lambda h: h.tensor_reduce(
            out=cols[:, 9:10], in_=out1t[:], axis=mybir.AxisListType.X, op=ALU.max,
            apply_absolute_value=True)),
            reads=[("out1t", xb)], writes=[("cols", xb, 9)])
        em.op("vector", (lambda h: h.tensor_scalar(
            out=cols[:, 10:11], in0=cols[:, 9:10], scalar1=1e-12, scalar2=1.0 / 127.0,
            op0=ALU.max, op1=ALU.mult)),
            reads=[("cols", xb, 9)], writes=[("cols", xb, 10)])
        em.op("vector", (lambda h: h.reciprocal(cols[:, 11:12], cols[:, 10:11])),
              reads=[("cols", xb, 10)], writes=[("cols", xb, 11)])
        em.op("vector", (lambda h: h.tensor_scalar(
            out=qv[:], in0=out1t[:], scalar1=cols[:, 11:12], scalar2=MAGIC,
            op0=ALU.mult, op1=ALU.add)),
            reads=[("out1t", xb), ("cols", xb, 11)], writes=[("qv", xb)])
        em.op("vector", (lambda h: h.tensor_scalar(
            out=qbf[:], in0=qv[:], scalar1=MAGIC, scalar2=None, op0=ALU.subtract)),
            reads=[("qv", xb)], writes=[("qbf", xb)])
        for g0 in range(0, KD, 4):
            gn = min(4, KD - g0)
            pst, psk = next_ps()

            def emit_tp(h, g0=g0, gn=gn, pst=pst):
                last = None
                for jj in range(gn):
                    k = g0 + jj
                    last = h.transpose(pst[:, jj * 128:(jj + 1) * 128],
                                       qbf[:, k * 128:(k + 1) * 128], ident_sb[:])
                return last
            em.op("tensor", emit_tp, reads=[("qbf", xb), "ident_sb"], writes=[psk])

            def emit_ev(h, g0=g0, gn=gn, pst=pst):
                last = None
                for jj in range(gn):
                    k = g0 + jj
                    last = h.tensor_copy(out=qTt[:, k * 128:(k + 1) * 128],
                                         in_=pst[:, jj * 128:(jj + 1) * 128])
                return last
            em.op("vector", emit_ev, reads=[psk], writes=[("qTt", xb, g0)])
        qt_all = [("qTt", xb, g0) for g0 in range(0, KD, 4)]
        for nt in range(NTD):
            c0 = nt * 512
            cw = min(512, D - c0)
            pst, psk = next_ps()

            def emit_mm(h, c0=c0, cw=cw, pst=pst):
                last = None
                for k in range(KD):
                    last = h.matmul(pst[:, 0:cw], qTt[:, k * 128:(k + 1) * 128],
                                    q_rhs[k][:, c0:c0 + cw],
                                    start=(k == 0), stop=(k == KD - 1))
                return last
            em.op("tensor", emit_mm, reads=qt_all + qrhs_all, writes=[psk])
            em.op("vector", (lambda h, c0=c0, cw=cw, pst=pst: h.scalar_tensor_tensor(
                out=f2[:, c0:c0 + cw], in0=pst[:, 0:cw],
                scalar=cols[:, 10:11], in1=rs_rep[:, c0:c0 + cw],
                op0=ALU.mult, op1=ALU.mult)),
                reads=[psk, ("cols", xb, 10)] + rsrep_all, writes=[("f2", xb, nt), ("res1", xb, nt)])
        f2k = [("f2", xb, nt) for nt in range(NTD)]
        em.op("vector", (lambda h: h.tensor_tensor(
            out=f2[:], in0=f2[:], in1=out1t[:], op=ALU.add)),
            reads=f2k + [("out1t", xb)], writes=[("f2s", xb)] + f2k)
        em.op("vector", (lambda h: h.tensor_reduce(
            out=cols[:, 0:1], in_=f2[:], axis=mybir.AxisListType.X, op=ALU.add)),
            reads=[("f2s", xb)], writes=[("cols", xb, 0)])
        em.op("scalar", (lambda h: h.activation(
            out=qv[:], in_=f2[:], func=AF.Square, accum_out=cols[:, 1:2])),
            reads=[("f2s", xb)], writes=[("qv", xb), ("cols", xb, 1)])
        em.op("vector", (lambda h: h.tensor_scalar(
            out=cols[:, 2:3], in0=cols[:, 0:1], scalar1=1.0 / D, scalar2=None, op0=ALU.mult)),
            reads=[("cols", xb, 0)], writes=[("cols", xb, 2)])
        em.op("vector", (lambda h: h.tensor_tensor(
            out=cols[:, 3:4], in0=cols[:, 2:3], in1=cols[:, 2:3], op=ALU.mult)),
            reads=[("cols", xb, 2)], writes=[("cols", xb, 3)])
        em.op("vector", (lambda h: h.tensor_scalar(
            out=cols[:, 4:5], in0=cols[:, 1:2], scalar1=1.0 / D, scalar2=1e-5,
            op0=ALU.mult, op1=ALU.add)),
            reads=[("cols", xb, 1)], writes=[("cols", xb, 4)])
        em.op("vector", (lambda h: h.tensor_tensor(
            out=cols[:, 5:6], in0=cols[:, 4:5], in1=cols[:, 3:4], op=ALU.subtract)),
            reads=[("cols", xb, 3), ("cols", xb, 4)], writes=[("cols", xb, 5)])
        em.op("scalar", (lambda h: h.activation(
            out=cols[:, 6:7], in_=cols[:, 5:6], func=AF.Ln)),
            reads=[("cols", xb, 5)], writes=[("cols", xb, 6)])
        em.op("scalar", (lambda h: h.activation(
            out=cols[:, 7:8], in_=cols[:, 6:7], func=AF.Exp, scale=-0.5)),
            reads=[("cols", xb, 6)], writes=[("cols", xb, 7)])
        em.op("vector", (lambda h: h.tensor_scalar(
            out=cols[:, 8:9], in0=cols[:, 2:3], scalar1=-1.0, scalar2=None, op0=ALU.mult)),
            reads=[("cols", xb, 2)], writes=[("cols", xb, 8)])
        em.op("vector", (lambda h, ob=ob: h.tensor_scalar(
            out=outt[ob][:], in0=f2[:], scalar1=cols[:, 8:9], scalar2=cols[:, 7:8],
            op0=ALU.add, op1=ALU.mult)),
            reads=[("f2s", xb), ("cols", xb, 7), ("cols", xb, 8)], writes=[("outt", ob)])
        em.dma("scalar", io["out"][mt * 128:(mt + 1) * 128, :], outt[ob][:],
               reads=[("outt", ob)])
    return io


def np_phase_c(y_in, xs_in, g_in, x2_in, D_in, W_out, rhs, cfg):
    bf = ml_dtypes.bfloat16

    def b16(v):
        return v.astype(bf).astype(np.float32)
    T, D, E = cfg.T, cfg.D, cfg.E
    y = y_in.astype(np.float32)
    xs = xs_in.astype(np.float32)
    g = g_in.astype(np.float32)
    t1 = xs * D_in[:, None] + y
    y2 = b16(t1 * g)
    WoT = b16(W_out.T)
    mam = (y2.T @ WoT.astype(np.float32)).astype(np.float32)
    res1 = 2.0 * x2_in + mam
    m = res1.mean(axis=1, keepdims=True)
    v = (res1 * res1).mean(axis=1, keepdims=True) - m * m
    rstd = np.exp(-0.5 * np.log(v + 1e-5)).astype(np.float32)
    out1 = ((res1 - m) * rstd).astype(np.float32)
    ls = np.maximum(np.abs(out1).max(axis=1, keepdims=True), 1e-12) / 127.0
    ql = np.round(out1 * (1.0 / ls)).astype(np.float32)
    rs = np.maximum(np.abs(rhs).max(axis=0, keepdims=True), 1e-12) / 127.0
    qr = np.round(rhs * (1.0 / rs)).astype(np.float32)
    acc = ql @ qr
    f2 = acc * ls * rs + out1
    m2 = f2.mean(axis=1, keepdims=True)
    v2 = (f2 * f2).mean(axis=1, keepdims=True) - m2 * m2
    rstd2 = np.exp(-0.5 * np.log(v2 + 1e-5)).astype(np.float32)
    return ((f2 - m2) * rstd2).astype(np.float32)


def cfg2_nt4(T):
    return [(i * 512, min(512, T - i * 512)) for i in range((T + 511) // 512)]


class CfgS:
    def __init__(self, T=2048, E=2048, R=64, NST=16, W=160, NH=4):
        self.T, self.E, self.R, self.NST, self.W = T, E, R, NST, W
        self.MC = E // 128
        self.NH = NH
        self.NG = NST // NH


def emit_scan_fused(nc, em, EN, cfg, io, bc_sb, pspool, ncore=8):
    T, E, NST, W = cfg.T, cfg.E, cfg.NST, cfg.W
    MC, NH, NG = cfg.MC, cfg.NH, cfg.NG

    ones1 = EN(nc.sbuf_tensor("ones1_s", [1, 128], F32))
    ones1b = EN(nc.sbuf_tensor("ones1b_s", [1, 128], BF16))
    bcB = EN(nc.sbuf_tensor("bcB_row", [1, max(cfg.NG * cfg.T, cfg.NST * cfg.W)], BF16))
    bcC = EN(nc.sbuf_tensor("bcC_row", [1, cfg.NG * cfg.T], BF16))
    A_sb = pspool["A_sb"]
    D_col = EN(nc.sbuf_tensor("D_col_s", [128, MC], F32))
    selv_sb = EN(nc.sbuf_tensor("selv_sb", [128, 8], F32))
    Hin = EN(nc.sbuf_tensor("Hin", [128, NST * MC], F32))
    Hall_sb = EN(nc.sbuf_tensor("Hall_sb", [128, ncore * MC * NST], F32))
    Brep = EN(nc.sbuf_tensor("Brep", [128, NG * T], BF16))
    Crep = EN(nc.sbuf_tensor("Crep", [128, NG * T], BF16))
    dcm = [EN(nc.sbuf_tensor(f"dcm{i}", [128, T], F32)) for i in range(2)]
    wcm = [EN(nc.sbuf_tensor(f"wcm{i}", [128, T], BF16)) for i in range(2)]
    xscm = [EN(nc.sbuf_tensor(f"xscm{i}", [128, T], BF16)) for i in range(2)]
    gcm = [EN(nc.sbuf_tensor(f"gcm{i}", [128, T], BF16)) for i in range(2)]
    a_big = EN(nc.sbuf_tensor("a_big", [128, 3 * T], F32))
    u_big = EN(nc.sbuf_tensor("u_big", [128, 3 * T], BF16))
    h_big = EN(nc.sbuf_tensor("h_big", [128, 3 * T], BF16))
    a_pl = [a_big[:, i * T:(i + 1) * T] for i in range(3)]
    u_pl = [u_big[:, i * T:(i + 1) * T] for i in range(3)]
    h_pl = [h_big[:, i * T:(i + 1) * T] for i in range(3)]
    z_pl = [EN(nc.sbuf_tensor(f"z_pl_s{i}", [128, T], BF16)) for i in range(2)]
    yac = EN(nc.sbuf_tensor("yac_s", [128, T], BF16))
    yh0 = [EN(nc.sbuf_tensor(f"yh0_{i}", [128, T], BF16)) for i in range(2)]
    y2m = [EN(nc.sbuf_tensor(f"y2m_{i}", [128, T], BF16)) for i in range(2)]
    t1s = EN(nc.sbuf_tensor("t1s", [128, T], F32))

    pss = pspool["ps"]
    NPS = len(pss)
    psi = [0]

    def next_ps():
        k = psi[0] % NPS
        psi[0] += 1
        return pss[k], f"ps{k}"

    em.op("vector", lambda h: h.memset(ones1[:], 1.0), writes=["ones1s_f"])
    em.op("vector", lambda h: h.memset(ones1b[:], 1.0), writes=["ones1s"])
    em.dma("sync", D_col[:], io["D_in"].rearrange("(m p) -> p m", p=128),
           writes=["Dcol"], allow_slow_non_contiguous=True)
    em.dma("sync", selv_sb[:], io["selv"], writes=["selv"])

    def load_rows(buf, bufkey, n0, nplanes, col0, width):
        em.dma("sync", buf[0:1, 0:nplanes * width],
               io["bc_hbm"][n0:n0 + nplanes, col0:col0 + width],
               reads=["bc_hbm"], writes=[bufkey], allow_slow_non_contiguous=True)

    def bcast_build(dst, rows_key, srcbuf, srckey, nplanes, width, dst_w):
        for j in range(nplanes):
            for cc in range(0, width, 512):
                cw = min(512, width - cc)
                pst, psk = next_ps()
                em.op("tensor", (lambda h, j=j, cc=cc, cw=cw, pst=pst, srcbuf=srcbuf, width=width:
                                 h.matmul(pst[:, 0:cw], ones1b[0:1, :],
                                          srcbuf[0:1, j * width + cc:j * width + cc + cw],
                                          start=True, stop=True)),
                      reads=[srckey, "ones1s"], writes=[psk])
                em.op("vector", (lambda h, j=j, cc=cc, cw=cw, pst=pst:
                                 h.tensor_copy(out=dst[:, j * dst_w + cc:j * dst_w + cc + cw],
                                               in_=pst[:, 0:cw])),
                      reads=[psk], writes=[(rows_key, j)])

    em.dma("sync", Hall_sb[:].rearrange("p (s m n) -> p s m n", s=ncore, n=NST),
           io["hall_st"].rearrange("s p m n -> p s m n"),
           reads=["hall_st"], writes=["Hall_sb"], allow_slow_non_contiguous=True)
    for si in range(ncore):
        if si == 0:
            em.op("vector", (lambda h, si=si: h.tensor_scalar(
                out=Hin[:], in0=Hall_sb[:, si * MC * NST:(si + 1) * MC * NST],
                scalar1=selv_sb[:, si:si + 1], scalar2=None, op0=ALU.mult)),
                reads=["Hall_sb", "selv"], writes=["Hin"])
        else:
            em.op("vector", (lambda h, si=si: h.scalar_tensor_tensor(
                out=Hin[:], in0=Hall_sb[:, si * MC * NST:(si + 1) * MC * NST],
                scalar=selv_sb[:, si:si + 1], in1=Hin[:],
                op0=ALU.mult, op1=ALU.add)),
                reads=["Hall_sb", "selv", "Hin"], writes=["Hin"])

    for half in range(NH):
        n0 = half * NG
        load_rows(bcB, "bcB", n0, NG, 0, T)
        load_rows(bcC, "bcC", NST + n0, NG, 0, T)
        bcast_build(Brep, "BrhJ", bcB, "bcB", NG, T, T)
        bcast_build(Crep, "CrhJ", bcC, "bcC", NG, T, T)
        for m in range(MC):
            mb = m % 2
            em.dma("sync", dcm[mb][:], io["delta_st"][m * 128:(m + 1) * 128, :],
                   reads=[("dhbm", m)], writes=[("dcm", mb)])
            em.dma("sync", wcm[mb][:], io["w_st"][m * 128:(m + 1) * 128, :],
                   reads=[("whbm", m)], writes=[("wcm", mb)])
            if half == NH - 1:
                em.dma("sync", xscm[mb][:], io["xs_st"][m * 128:(m + 1) * 128, :],
                       reads=[("xshbm", m)], writes=[("xscm", mb)])
                em.dma("sync", gcm[mb][:], io["g_st"][m * 128:(m + 1) * 128, :],
                       reads=[("ghbm", m, it) for it in range(len(cfg2_nt4(T)))],
                       writes=[("gcm", mb)])
                em.dma("sync", yh0[mb][:], io["yh_st"][m * 128:(m + 1) * 128, :],
                       reads=[("yhst", m)], writes=[("yh0", mb)])
            for j in range(NG):
                n = n0 + j
                ai = n % 3
                em.op("scalar", (lambda h, m=m, mb=mb, n=n, ai=ai: h.activation(
                    out=a_pl[ai][:], in_=dcm[mb][:], func=AF.Exp,
                    scale=A_sb[:, m * NST + n:m * NST + n + 1])),
                    reads=[("dcm", mb), "Asb"], writes=[("a", ai)])
                em.op("vector", (lambda h, mb=mb, j=j, ai=ai: h.tensor_tensor(
                    out=u_pl[ai][:], in0=Brep[:, j * T:(j + 1) * T], in1=wcm[mb][:],
                    op=ALU.mult)),
                    reads=[("BrhJ", j), ("wcm", mb)],
                    writes=[("u", ai)])
                em.op("vector", (lambda h, m=m, n=n, ai=ai: h.tensor_tensor_scan(
                    out=h_pl[ai][:], data0=a_pl[ai][:], data1=u_pl[ai][:],
                    initial=Hin[:, m * NST + n:m * NST + n + 1],
                    op0=ALU.mult, op1=ALU.add)),
                    reads=[("a", ai), ("u", ai), "Hin"], writes=[("h", ai)])
                zi = n % 2
                if j == 0:
                    em.op("vector", (lambda h, j=j, ai=ai: h.tensor_tensor(
                        out=yac[:], in0=h_pl[ai][:], in1=Crep[:, j * T:(j + 1) * T],
                        op=ALU.mult)),
                        reads=[("h", ai), ("CrhJ", j)], writes=["yac"])
                else:
                    em.op("vector", (lambda h, j=j, ai=ai, zi=zi: h.tensor_tensor(
                        out=z_pl[zi][:], in0=h_pl[ai][:], in1=Crep[:, j * T:(j + 1) * T],
                        op=ALU.mult)),
                        reads=[("h", ai), ("CrhJ", j)], writes=[("z", zi)])
                    em.op("vector", (lambda h, zi=zi: h.tensor_tensor(
                        out=yac[:], in0=yac[:], in1=z_pl[zi][:], op=ALU.add)),
                        reads=[("z", zi), "yac"], writes=["yac"])
            if half == 0:
                em.dma("scalar", io["yh_st"][m * 128:(m + 1) * 128, :], yac[:],
                       reads=["yac"], writes=[("yhst", m)])
            elif half < NH - 1:
                em.dma("sync", yh0[mb][:], io["yh_st"][m * 128:(m + 1) * 128, :],
                       reads=[("yhst", m)], writes=[("yh0", mb)])
                em.dma("gpsimd", yh0[mb][:], yac[:], accum_op=ALU.add,
                       reads=["yac"], writes=[("yh0", mb)])
                em.dma("scalar", io["yh_st"][m * 128:(m + 1) * 128, :], yh0[mb][:],
                       reads=[("yh0", mb)], writes=[("yhst", m)])
            else:
                em.dma("gpsimd", yac[:], yh0[mb][:], accum_op=ALU.add,
                       reads=[("yh0", mb)], writes=["yac"])
                em.op("vector", (lambda h, m=m, mb=mb: h.scalar_tensor_tensor(
                    out=t1s[:], in0=xscm[mb][:], scalar=D_col[:, m:m + 1], in1=yac[:],
                    op0=ALU.mult, op1=ALU.add)),
                    reads=[("xscm", mb), "yac", "Dcol"], writes=["t1s"])
                y2b = m % 2
                em.op("vector", (lambda h, mb=mb, y2b=y2b: h.tensor_tensor(
                    out=y2m[y2b][:], in0=t1s[:], in1=gcm[mb][:], op=ALU.mult)),
                    reads=["t1s", ("gcm", mb)], writes=[("y2m", y2b)])
                em.dma("scalar", io["y2_st"][m * 128:(m + 1) * 128, :], y2m[y2b][:],
                       reads=[("y2m", y2b)], writes=[("y2st", m)])
    return io


def build_fused(T=2048, D=1024, E=2048, R=64, NST=16, W=160, ncore=8):
    cfa = CfgA(T=T, D=D, E=E, R=R, NST=NST)
    cfs = CfgS(T=T, E=E, R=R, NST=NST, W=W)
    cfc = CfgC(T=T, D=D, E=E)
    nc = bass.Bass("TRN2", target_bir_lowering=False)
    MC = E // 128

    io = {}
    io["x_in"] = nc.dram_tensor("x_in", [T + 128, D], F32, kind="ExternalInput").ap()
    io["W_in"] = nc.dram_tensor("W_in", [2 * E, D], F32, kind="ExternalInput").ap()
    io["W_xproj"] = nc.dram_tensor("W_xproj", [96, E], F32, kind="ExternalInput").ap()
    io["W_dt"] = nc.dram_tensor("W_dt", [E, R], F32, kind="ExternalInput").ap()
    io["b_dt"] = nc.dram_tensor("b_dt", [E], F32, kind="ExternalInput").ap()
    io["conv_w"] = nc.dram_tensor("conv_w", [E, 4], F32, kind="ExternalInput").ap()
    io["conv_b"] = nc.dram_tensor("conv_b", [E], F32, kind="ExternalInput").ap()
    io["norm_w"] = nc.dram_tensor("norm_w", [D], F32, kind="ExternalInput").ap()
    io["A_log_in"] = nc.dram_tensor("A_log_in", [E, NST], F32, kind="ExternalInput").ap()
    io["D_in"] = nc.dram_tensor("D_in", [E], F32, kind="ExternalInput").ap()
    io["W_out"] = nc.dram_tensor("W_out", [D, E], F32, kind="ExternalInput").ap()
    io["rhs"] = nc.dram_tensor("rhs", [D, D], F32, kind="ExternalInput").ap()
    io["selv"] = nc.dram_tensor("selv", [128, 8], F32, kind="ExternalInput").ap()
    io["out"] = nc.dram_tensor("out", [T, D], F32, kind="ExternalOutput").ap()
    io["delta_out"] = nc.dram_tensor("delta_st", [E, T], F32, kind="Internal").ap()
    io["w_out"] = nc.dram_tensor("w_st", [E, T], BF16, kind="Internal").ap()
    io["xs_out"] = nc.dram_tensor("xs_st", [E, T], BF16, kind="Internal").ap()
    io["g_out"] = nc.dram_tensor("g_st", [E, T], BF16, kind="Internal").ap()
    io["delta_st"] = io["delta_out"]
    io["w_st"] = io["w_out"]
    io["xs_st"] = io["xs_out"]
    io["g_st"] = io["g_out"]
    io["y2_st"] = nc.dram_tensor("y2_st", [E, T], BF16, kind="Internal").ap()
    io["yh_st"] = nc.dram_tensor("yh_st", [E, T], BF16, kind="Internal").ap()
    io["bc_hbm"] = nc.dram_tensor("bc_hbm", [2 * NST, T], BF16, kind="Internal").ap()
    io["hout_st"] = nc.dram_tensor("hout_st", [128, MC, NST], F32, kind="Internal").ap()
    io["hall_st"] = nc.dram_tensor("hall_st", [ncore, 128, MC, NST], F32, kind="Internal").ap()

    with contextlib.ExitStack() as octx:
        em = Emitter(nc, octx)
        EN0 = octx.enter_context
        pspool = {
            "ps": [EN0(nc.psum_tensor(f"ps{i}", [128, 512], F32)) for i in range(4)],
            "ps_big": EN0(nc.psum_tensor("ps_big", [128, 4 * 512], F32)),
            "A_sb": EN0(nc.sbuf_tensor("A_sb", [128, NST * MC], F32)),
            "Hout": EN0(nc.sbuf_tensor("Hout", [128, MC * NST], F32)),
        }
        pspool["W"] = W
        pspool["ncore"] = ncore
        bc_sb = None
        with contextlib.ExitStack() as c1:
            emit_phase_a(nc, em, c1.enter_context, cfa, io=io, fused=True,
                         bc_sb=bc_sb, pspool=pspool)
            em.barrier()
        with contextlib.ExitStack() as c2:
            emit_scan_fused(nc, em, c2.enter_context, cfs, io, bc_sb, pspool, ncore=ncore)
            em.barrier()
        with contextlib.ExitStack() as c3:
            io["x2_in"] = io["x_in"][128:, :]
            io["y_in"] = io["y2_st"]; io["xs_in"] = io["xs_st"]; io["g_in"] = io["g_st"]
            emit_phase_c(nc, em, c3.enter_context, cfc, io=io, fused=True, pspool=pspool)
        for e in ("sync", "tensor", "vector", "scalar", "gpsimd"):
            em.wait_all(e)
        block = octx.enter_context(nc.Block())
        em.build(block)
    return nc


def make_in_maps(ins0, T, ncore=8):
    x = np.ascontiguousarray(np.asarray(ins0["L0_attn_x"])[0], dtype=np.float32)
    L, D = x.shape
    xpad = np.concatenate([np.zeros((128, D), np.float32), x], axis=0)
    common = dict(W_in=np.asarray(ins0["L0_attn_W_in"]), W_xproj=np.asarray(ins0["L0_attn_W_xproj"]),
                  W_dt=np.asarray(ins0["L0_attn_W_dt"]), b_dt=np.asarray(ins0["L0_attn_b_dt"]),
                  conv_w=np.asarray(ins0["L0_attn_conv_w"]), conv_b=np.asarray(ins0["L0_attn_conv_b"]),
                  norm_w=np.asarray(ins0["L0_attn_norm_w"]),
                  A_log_in=np.asarray(ins0["L0_attn_A_log"]), D_in=np.asarray(ins0["L0_attn_D"]),
                  W_out=np.asarray(ins0["L0_attn_W_out"]), rhs=np.asarray(ins0["L0_ffn_rhs"]))
    maps = []
    for i in range(ncore):
        sel = np.zeros((128, 8), np.float32)
        if i > 0:
            sel[:, i - 1] = 1.0
        maps.append(dict(
            x_in=np.ascontiguousarray(xpad[i * T:(i + 1) * T + 128]),
            selv=sel,
            **common))
    return maps


def kernel(L0_attn_x, L0_attn_W_in, L0_attn_conv_w, L0_attn_conv_b, L0_attn_W_xproj,
           L0_attn_W_dt, L0_attn_b_dt, L0_attn_A_log, L0_attn_D, L0_attn_W_out,
           L0_attn_norm_w, L0_ffn_rhs):
    from concourse.bass_utils import run_bass_kernel_spmd
    NCORE = 8
    x = np.asarray(L0_attn_x)
    L = x.shape[1]
    T = L // NCORE
    ins0 = dict(L0_attn_x=x, L0_attn_W_in=L0_attn_W_in, L0_attn_conv_w=L0_attn_conv_w,
                L0_attn_conv_b=L0_attn_conv_b, L0_attn_W_xproj=L0_attn_W_xproj,
                L0_attn_W_dt=L0_attn_W_dt, L0_attn_b_dt=L0_attn_b_dt,
                L0_attn_A_log=L0_attn_A_log, L0_attn_D=L0_attn_D,
                L0_attn_W_out=L0_attn_W_out, L0_attn_norm_w=L0_attn_norm_w,
                L0_ffn_rhs=L0_ffn_rhs)
    nc = build_fused(T=T, ncore=NCORE)
    maps = make_in_maps(ins0, T, NCORE)
    res = run_bass_kernel_spmd(nc, maps, core_ids=list(range(NCORE)))
    out = np.concatenate([res.results[i]["out"] for i in range(NCORE)], axis=0)
    return out[None]



# version 07
# speedup vs baseline: 1.0730x; 1.0566x over previous
import contextlib
import numpy as np
import concourse.bass as bass
import concourse.mybir as mybir

F32 = mybir.dt.float32
BF16 = mybir.dt.bfloat16
AF = mybir.ActivationFunctionType
ALU = mybir.AluOpType




DMA_POOL = 10


class _Eng:
    def __init__(self, name, sem, dsems):
        self.name = name
        self.sem = sem
        self.dsems = dsems
        self.n_ops = 0
        self.n_dma = 0
        self.marks = {}


class Emitter:
    def __init__(self, nc, ctx, engines=("sync", "tensor", "vector", "scalar", "gpsimd")):
        self.nc = nc
        self.engs = {}
        for name in engines:
            sem = ctx.enter_context(nc.semaphore(f"op_{name}"))
            dsems = ([ctx.enter_context(nc.semaphore(f"dma_{name}{i}")) for i in range(DMA_POOL)]
                     if name in ("sync", "scalar", "gpsimd") else [])
            self.engs[name] = _Eng(name, sem, dsems)
        self.last_w = {}
        self.readers = {}
        self._bodies = {n: [] for n in engines}
        self._final = {n: [] for n in engines}

    def op(self, eng, emit, reads=(), writes=(), dma=False):
        e = self.engs[eng]
        deps = []
        for k in reads:
            lw = self.last_w.get(k)
            if lw is not None:
                deps.append(lw)
        for k in writes:
            lw = self.last_w.get(k)
            if lw is not None:
                deps.append(lw)
            deps.extend(self.readers.get(k, ()))
        if dma:
            slot = e.n_dma % DMA_POOL
            rank = e.n_dma // DMA_POOL
            sem = e.dsems[slot]
            rec = (eng, sem, 16 * (rank + 1))
            throttle = (sem, 16 * rank) if rank >= 1 else None
            e.n_dma += 1
        else:
            sem = e.sem
            rec = (eng, sem, e.n_ops + 1)
            throttle = None
            e.n_ops += 1
        for k in reads:
            self.readers.setdefault(k, []).append(rec)
        for k in writes:
            self.last_w[k] = rec
            self.readers[k] = []
        self._bodies[eng].append((emit, deps, dma, sem, throttle))
        return rec

    def dma(self, eng, out, in_, reads=(), writes=(), **kw):
        def emit(h):
            return h.dma_start(out=out, in_=in_, **kw)
        return self.op(eng, emit, reads=reads, writes=writes, dma=True)

    def build(self, block):
        for name in self.engs:
            body = self._bodies[name]
            fins = self._final[name]
            if not body and not fins:
                continue

            def make(body=body, fins=fins, name=name):
                e = self.engs[name]

                def run(h):
                    for emit, deps, dma, sem, throttle in body:
                        for (peng, psem, pval) in deps:
                            mk = e.marks.get(id(psem), -1)
                            if pval > mk:
                                h.wait_ge(psem, pval)
                                e.marks[id(psem)] = pval
                        if throttle is not None:
                            tsem, tval = throttle
                            mk = e.marks.get(id(tsem), -1)
                            if tval > mk:
                                h.wait_ge(tsem, tval)
                                e.marks[id(tsem)] = tval
                        inst = emit(h)
                        inst.then_inc(sem, 16 if dma else 1)
                    for fin in fins:
                        fin(h)
                return run
            getattr(block, name)(make())

    def final(self, eng, fn):
        self._final[eng].append(fn)

    def wait_all(self, eng):
        targets = []
        for name, e in self.engs.items():
            if e.n_ops:
                targets.append((e.sem, e.n_ops))
            for i, ds in enumerate(e.dsems):
                cnt = (e.n_dma - i + DMA_POOL - 1) // DMA_POOL
                if cnt > 0:
                    targets.append((ds, 16 * cnt))

        def fin(h):
            for sem, val in targets:
                h.wait_ge(sem, val)
        self._final[eng].append(fin)

    def barrier(self):
        targets = []
        for name, e in self.engs.items():
            if e.n_ops:
                targets.append((e.sem, e.n_ops))
            for i, ds in enumerate(e.dsems):
                cnt = (e.n_dma - i + DMA_POOL - 1) // DMA_POOL
                if cnt > 0:
                    targets.append((ds, 16 * cnt))
        for name in self.engs:
            def emit(h, targets=targets):
                for sem, val in targets:
                    h.wait_ge(sem, val)
                return h.nop()
            self.op(name, emit)


class CfgA:
    def __init__(self, T=2048, D=1024, E=2048, R=64, NST=16):
        self.T, self.D, self.E, self.R, self.NST = T, D, E, R, NST
        self.TP = T + 128
        self.NT = self.TP // 128
        self.KD = D // 128
        self.ME = 2 * E // 128
        self.MC = E // 128
        self.NT5 = [(i * 512, min(512, self.TP - i * 512)) for i in range((self.TP + 511) // 512)]
        self.NT4 = [(i * 512, min(512, T - i * 512)) for i in range((T + 511) // 512)]


def emit_phase_a(nc, em, EN, cfg, io=None, fused=False, bc_sb=None, pspool=None):
    T, D, E, R, NST = cfg.T, cfg.D, cfg.E, cfg.R, cfg.NST
    TP, NT, KD, ME, MC = cfg.TP, cfg.NT, cfg.KD, cfg.ME, cfg.MC

    if io is None:
        io = {}
        io["x_in"] = nc.dram_tensor("x_in", [TP, D], F32, kind="ExternalInput").ap()
        io["W_in"] = nc.dram_tensor("W_in", [2 * E, D], F32, kind="ExternalInput").ap()
        io["W_xproj"] = nc.dram_tensor("W_xproj", [96, E], F32, kind="ExternalInput").ap()
        io["W_dt"] = nc.dram_tensor("W_dt", [E, R], F32, kind="ExternalInput").ap()
        io["b_dt"] = nc.dram_tensor("b_dt", [E], F32, kind="ExternalInput").ap()
        io["conv_w"] = nc.dram_tensor("conv_w", [E, 4], F32, kind="ExternalInput").ap()
        io["conv_b"] = nc.dram_tensor("conv_b", [E], F32, kind="ExternalInput").ap()
        io["norm_w"] = nc.dram_tensor("norm_w", [D], F32, kind="ExternalInput").ap()
        io["delta_out"] = nc.dram_tensor("delta_out", [E, T], F32, kind="ExternalOutput").ap()
        io["w_out"] = nc.dram_tensor("w_out", [E, T], BF16, kind="ExternalOutput").ap()
        io["xs_out"] = nc.dram_tensor("xs_out", [E, T], BF16, kind="ExternalOutput").ap()
        io["g_out"] = nc.dram_tensor("g_out", [E, T], BF16, kind="ExternalOutput").ap()
        io["B_out"] = nc.dram_tensor("B_out", [NST, T], F32, kind="ExternalOutput").ap()
        io["C_out"] = nc.dram_tensor("C_out", [NST, T], F32, kind="ExternalOutput").ap()
        io["Brep_out"] = nc.dram_tensor("Brep_out", [128, NST, T], BF16, kind="ExternalOutput").ap()
        io["Crep_out"] = nc.dram_tensor("Crep_out", [128, NST, T], BF16, kind="ExternalOutput").ap()
    ident = nc.inline_tensor(np.eye(128, dtype=np.float32), "ident128").ap()

    ident_sb = EN(nc.sbuf_tensor("ident_sb", [128, 128], F32))
    ones_sb = EN(nc.sbuf_tensor("ones_sb", [1, 128], F32))
    Wnat = [EN(nc.sbuf_tensor(f"Wnat{i}", [128, D], F32)) for i in range(2)]
    WinT = [EN(nc.sbuf_tensor(f"WinT{i}", [128, D], BF16)) for i in range(2)]
    WxN = EN(nc.sbuf_tensor("WxN", [96, E], F32))
    WxT = EN(nc.sbuf_tensor("WxT", [128, 96 * MC], BF16))
    WdtN = EN(nc.sbuf_tensor("WdtN", [128, R * MC], F32))
    WdtT = EN(nc.sbuf_tensor("WdtT", [R, E], BF16))
    nw_col = EN(nc.sbuf_tensor("nw_col", [128, KD], F32))
    bdt_col = EN(nc.sbuf_tensor("bdt_col", [128, MC], F32))
    cw_col = EN(nc.sbuf_tensor("cw_col", [128, 4 * MC], F32))
    cb_col = EN(nc.sbuf_tensor("cb_col", [128, MC], F32))

    x_tm = [EN(nc.sbuf_tensor(f"x_tm{i}", [128, D], F32)) for i in range(2)]
    ssum = EN(nc.sbuf_tensor("ssum", [128, NT], F32))
    rstd_cols = EN(nc.sbuf_tensor("rstd_cols", [128, NT], F32))
    rstd_T = EN(nc.sbuf_tensor("rstd_T", [NT, 128], F32))
    rstd_row = EN(nc.sbuf_tensor("rstd_row", [1, TP], F32))
    rstd_rep = EN(nc.sbuf_tensor("rstd_rep", [128, TP], F32))
    xn_big = EN(nc.sbuf_tensor("xn_big", [128, KD * TP], BF16))
    xn_bf = [xn_big[:, k * TP:(k + 1) * TP] for k in range(KD)]

    xc_strip = EN(nc.sbuf_tensor("xc_strip", [128, TP], F32))
    acc_str = EN(nc.sbuf_tensor("acc_str", [128, T], F32))
    xs_m = [EN(nc.sbuf_tensor(f"xs_m{i}", [128, T], BF16)) for i in range(2)]
    xs_rl = [EN(nc.sbuf_tensor(f"xs_rl{i}", [128, T], BF16)) for i in range(2)]
    g_stage = [EN(nc.sbuf_tensor(f"g_stage{i}", [128, 512], BF16)) for i in range(2)]

    dbc_sb = EN(nc.sbuf_tensor("dbc_sb", [96, T], F32))
    dt_bf = EN(nc.sbuf_tensor("dt_bf", [R, T], BF16))
    bcr = EN(nc.sbuf_tensor("bcr", [1, NST * 512], F32))
    brep_stage = [EN(nc.sbuf_tensor(f"brep_stage{i}", [128, 512], BF16)) for i in range(4)]
    e1_sb = acc_str
    delta_sb = [EN(nc.sbuf_tensor(f"delta_sb{i}", [128, T], F32)) for i in range(2)]
    wm_sb = [EN(nc.sbuf_tensor(f"wm_sb{i}", [128, T], BF16)) for i in range(2)]

    if pspool is None:
        ps = [EN(nc.psum_tensor(f"ps{i}", [128, 512], F32)) for i in range(4)]
        ps_big = EN(nc.psum_tensor("ps_big", [128, 4 * 512], F32))
    else:
        ps = pspool["ps"]
        ps_big = pspool["ps_big"]
    NPS = len(ps)
    psi = [0]

    def next_ps():
        k = psi[0] % NPS
        psi[0] += 1
        return ps[k], f"ps{k}"

    em.dma("sync", ident_sb[:], ident, writes=["ident_sb"])
    em.dma("sync", nw_col[:], io["norm_w"].rearrange("(k p) -> p k", p=128), writes=["nw_col"], allow_slow_non_contiguous=True)
    em.dma("sync", bdt_col[:], io["b_dt"].rearrange("(m p) -> p m", p=128), writes=["bdt_col"], allow_slow_non_contiguous=True)
    em.dma("sync", cb_col[:], io["conv_b"].rearrange("(m p) -> p m", p=128), writes=["cb_col"], allow_slow_non_contiguous=True)
    em.dma("sync", cw_col[:].rearrange("p (m k) -> p m k", k=4),
           io["conv_w"].rearrange("(m p) k -> p m k", p=128), writes=["cw_col"], allow_slow_non_contiguous=True)
    em.dma("sync", WxN[:], io["W_xproj"], writes=["WxN"])
    em.dma("sync", WdtN[:].rearrange("p (m r) -> p m r", r=R),
           io["W_dt"].rearrange("(m p) r -> p m r", p=128), writes=["WdtN"], allow_slow_non_contiguous=True)
    em.op("vector", lambda h: h.memset(ones_sb[:], 1.0), writes=["ones_sb"])
    if fused:
        A_sb = pspool["A_sb"]
        em.dma("sync", A_sb[:].rearrange("p (m n) -> p m n", n=NST),
               io["A_log_in"].rearrange("(m p) n -> p m n", p=128),
               writes=["Asb"], allow_slow_non_contiguous=True)
        em.op("scalar", lambda h: h.activation(out=A_sb[:], in_=A_sb[:], func=AF.Exp),
              reads=["Asb"], writes=["Asb"])
        em.op("vector", lambda h: h.tensor_scalar_mul(A_sb[:], A_sb[:], -1.0),
              reads=["Asb"], writes=["Asb"])

    for g0 in range(0, MC, 4):
        gn = min(4, MC - g0)
        pst, psk = next_ps()

        def emit_tp(h, g0=g0, gn=gn, pst=pst):
            last = None
            for j in range(gn):
                k = g0 + j
                last = h.transpose(pst[:, j * 128:j * 128 + 96],
                                   WxN[0:96, k * 128:(k + 1) * 128], ident_sb[0:96, 0:96])
            return last
        em.op("tensor", emit_tp, reads=["WxN", "ident_sb"], writes=[psk])

        def emit_ev(h, g0=g0, gn=gn, pst=pst):
            last = None
            for j in range(gn):
                k = g0 + j
                last = h.tensor_copy(out=WxT[:, k * 96:(k + 1) * 96],
                                     in_=pst[:, j * 128:j * 128 + 96])
            return last
        em.op("vector", emit_ev, reads=[psk], writes=[("WxT", g0)])
    wxk = [("WxT", g0) for g0 in range(0, MC, 4)]
    for g0 in range(0, MC, 4):
        gn = min(4, MC - g0)
        pst, psk = next_ps()

        def emit_tp(h, g0=g0, gn=gn, pst=pst):
            last = None
            for j in range(gn):
                mm_ = g0 + j
                last = h.transpose(pst[0:R, j * 128:(j + 1) * 128],
                                   WdtN[:, mm_ * R:(mm_ + 1) * R], ident_sb[:])
            return last
        em.op("tensor", emit_tp, reads=["WdtN", "ident_sb"], writes=[psk])

        def emit_ev(h, g0=g0, gn=gn, pst=pst):
            last = None
            for j in range(gn):
                mm_ = g0 + j
                last = h.tensor_copy(out=WdtT[0:R, mm_ * 128:(mm_ + 1) * 128],
                                     in_=pst[0:R, j * 128:(j + 1) * 128])
            return last
        em.op("vector", emit_ev, reads=[psk], writes=["WdtT"])

    for j in range(NT):
        xb = j % 2
        em.dma("sync", x_tm[xb][:], io["x_in"][j * 128:(j + 1) * 128, :], writes=[("x_tm", xb)])
        em.op("scalar", (lambda h, xb=xb, j=j: h.activation(
            out=x_tm[xb][:], in_=x_tm[xb][:], func=AF.Square, accum_out=ssum[:, j:j + 1])),
            reads=[("x_tm", xb)], writes=[("x_tm", xb), ("ssum", j)])
    em.op("vector", lambda h: h.tensor_scalar(
        out=rstd_cols[:, 0:NT], in0=ssum[:, 0:NT], scalar1=1.0 / D, scalar2=1e-5,
        op0=ALU.mult, op1=ALU.add),
        reads=[("ssum", j) for j in range(NT)], writes=["rstd_cols"])
    em.op("scalar", lambda h: h.activation(out=rstd_cols[:, 0:NT], in_=rstd_cols[:, 0:NT],
                                           func=AF.Ln), reads=["rstd_cols"], writes=["rstd_cols"])
    em.op("scalar", lambda h: h.activation(out=rstd_cols[:, 0:NT], in_=rstd_cols[:, 0:NT],
                                           func=AF.Exp, scale=-0.5),
          reads=["rstd_cols"], writes=["rstd_cols"])
    pst, psk = next_ps()
    em.op("tensor", (lambda h, pst=pst: h.transpose(pst[0:NT, 0:128], rstd_cols[:, 0:NT], ident_sb[:])),
          reads=["rstd_cols", "ident_sb"], writes=[psk])
    em.op("vector", (lambda h, pst=pst: h.tensor_copy(out=rstd_T[0:NT, :], in_=pst[0:NT, 0:128])),
          reads=[psk], writes=["rstd_T"])
    em.dma("sync", rstd_row[0:1, :], rstd_T[0:NT, :],
           reads=["rstd_T"], writes=["rstd_row"], allow_slow_non_contiguous=True)
    for i, (c0, cw) in enumerate(cfg.NT5):
        pst, psk = next_ps()
        em.op("tensor", (lambda h, c0=c0, cw=cw, pst=pst: h.matmul(
            pst[:, 0:cw], ones_sb[0:1, :], rstd_row[0:1, c0:c0 + cw], start=True, stop=True)),
            reads=["rstd_row", "ones_sb"], writes=[psk])
        em.op("vector", (lambda h, c0=c0, cw=cw, pst=pst: h.tensor_copy(
            out=rstd_rep[:, c0:c0 + cw], in_=pst[:, 0:cw])),
            reads=[psk], writes=[("rstd_rep", i)])
    rstd_all = [("rstd_rep", i) for i in range(len(cfg.NT5))]

    for j in range(NT):
        xb = j % 2
        em.dma("sync", x_tm[xb][:], io["x_in"][j * 128:(j + 1) * 128, :], writes=[("x_tm", xb)])
        for g0 in range(0, KD, 4):
            gn = min(4, KD - g0)
            pst, psk = next_ps()

            def emit_tp(h, xb=xb, g0=g0, gn=gn, pst=pst):
                last = None
                for jj in range(gn):
                    k = g0 + jj
                    last = h.transpose(pst[:, jj * 128:(jj + 1) * 128],
                                       x_tm[xb][:, k * 128:(k + 1) * 128], ident_sb[:])
                return last
            em.op("tensor", emit_tp, reads=[("x_tm", xb), "ident_sb"], writes=[psk])

            def emit_ev(h, j=j, g0=g0, gn=gn, pst=pst):
                last = None
                for jj in range(gn):
                    k = g0 + jj
                    last = h.tensor_tensor(out=xn_bf[k][:, j * 128:(j + 1) * 128],
                                           in0=pst[:, jj * 128:(jj + 1) * 128],
                                           in1=rstd_rep[:, j * 128:(j + 1) * 128],
                                           op=ALU.mult)
                return last
            em.op("vector", emit_ev, reads=[psk] + rstd_all, writes=[("xn_bf", j, g0)])
    xn_all = [("xn_bf", j, g0) for j in range(NT) for g0 in range(0, KD, 4)]

    for m in range(ME):
        wb = m % 2
        isconv = m < MC
        em.dma("sync", Wnat[wb][:], io["W_in"][m * 128:(m + 1) * 128, :], writes=[("Wnat", wb)])
        for g0 in range(0, KD, 4):
            gn = min(4, KD - g0)
            pst, psk = next_ps()

            def emit_tp(h, wb=wb, g0=g0, gn=gn, pst=pst):
                last = None
                for jj in range(gn):
                    k = g0 + jj
                    last = h.transpose(pst[:, jj * 128:(jj + 1) * 128],
                                       Wnat[wb][:, k * 128:(k + 1) * 128], ident_sb[:])
                return last
            em.op("tensor", emit_tp, reads=[("Wnat", wb), "ident_sb"], writes=[psk])

            def emit_ev(h, wb=wb, g0=g0, gn=gn, pst=pst):
                last = None
                for jj in range(gn):
                    k = g0 + jj
                    last = h.tensor_scalar(out=WinT[wb][:, k * 128:(k + 1) * 128],
                                           in0=pst[:, jj * 128:(jj + 1) * 128],
                                           scalar1=nw_col[:, k:k + 1], scalar2=None,
                                           op0=ALU.mult)
                return last
            em.op("vector", emit_ev, reads=[psk, "nw_col"], writes=[("WinT", wb, g0)])
        wkeys = [("WinT", wb, g0) for g0 in range(0, KD, 4)]

        for it, (c0, cw) in enumerate(cfg.NT5):
            pst, psk = next_ps()

            def emit_mm(h, wb=wb, c0=c0, cw=cw, pst=pst):
                last = None
                for k in range(KD):
                    last = h.matmul(pst[:, 0:cw], WinT[wb][:, k * 128:(k + 1) * 128],
                                    xn_bf[k][:, c0:c0 + cw],
                                    start=(k == 0), stop=(k == KD - 1))
                return last
            em.op("tensor", emit_mm, reads=wkeys + xn_all, writes=[psk])
            if isconv:
                em.op("vector", (lambda h, c0=c0, cw=cw, pst=pst: h.tensor_copy(
                    out=xc_strip[:, c0:c0 + cw], in_=pst[:, 0:cw])),
                    reads=[psk], writes=[("xc", it)])
            else:
                slot = it % 2
                o0 = max(0, c0 - 128)
                s0 = max(0, 128 - c0)
                em.op("scalar", (lambda h, cw=cw, pst=pst, slot=slot: h.activation(
                    out=g_stage[slot][:, 0:cw], in_=pst[:, 0:cw], func=AF.Silu)),
                    reads=[psk], writes=[("g_stage", slot)])
                if cw - s0 > 0:
                    em.dma("scalar",
                           io["g_out"][(m - MC) * 128:(m - MC + 1) * 128, o0:o0 + cw - s0],
                           g_stage[slot][:, s0:cw],
                           reads=[("g_stage", slot)], writes=[("ghbm", m - MC, it)])
        if isconv:
            xck = [("xc", it) for it in range(len(cfg.NT5))]
            em.op("vector", (lambda h, m=m: h.tensor_scalar(
                out=acc_str[:], in0=xc_strip[:, 125:125 + T], scalar1=cw_col[:, 4 * m:4 * m + 1],
                scalar2=None, op0=ALU.mult)),
                reads=xck + ["cw_col"], writes=["acc"])
            for kk in range(1, 4):
                em.op("vector", (lambda h, m=m, kk=kk: h.scalar_tensor_tensor(
                    out=acc_str[:], in0=xc_strip[:, 125 + kk:125 + kk + T],
                    scalar=cw_col[:, 4 * m + kk:4 * m + kk + 1], in1=acc_str[:],
                    op0=ALU.mult, op1=ALU.add)),
                    reads=xck + ["cw_col", "acc"], writes=["acc"])
            xb2 = m % 2
            em.op("scalar", (lambda h, m=m, xb2=xb2: h.activation(
                out=xs_m[xb2][:], in_=acc_str[:], func=AF.Silu, bias=cb_col[:, m:m + 1])),
                reads=["acc", "cb_col"], writes=[("xs_m", xb2)])
            em.dma("scalar", io["xs_out"][m * 128:(m + 1) * 128, :], xs_m[xb2][:],
                   reads=[("xs_m", xb2)], writes=[("xshbm", m)])
            def emit_dbc(h, m=m, xb2=xb2):
                last = None
                for it, (c0, cw) in enumerate(cfg.NT4):
                    last = h.matmul(ps_big[0:96, it * 512:it * 512 + cw],
                                    WxT[:, m * 96:(m + 1) * 96],
                                    xs_m[xb2][:, c0:c0 + cw],
                                    start=(m == 0), stop=(m == MC - 1),
                                    skip_group_check=True)
                return last
            em.op("tensor", emit_dbc, reads=[("xs_m", xb2)] + wxk,
                  writes=["ps_big"])

    em.op("vector", (lambda h: h.tensor_copy(out=dbc_sb[0:96, :],
                                             in_=ps_big[0:96, 0:T])),
          reads=["ps_big"], writes=["dbc"])
    em.op("vector", lambda h: h.tensor_copy(out=dt_bf[0:R, :], in_=dbc_sb[0:R, :]),
          reads=["dbc"], writes=["dt_bf"])
    if fused:
        bc_bf = EN(nc.sbuf_tensor("bc_bf", [2 * NST, T], BF16))
        em.op("vector", (lambda h: h.tensor_copy(out=bc_bf[:, :],
                                                 in_=dbc_sb[R:R + 2 * NST, :])),
              reads=["dbc"], writes=["bc_bf"])
        em.dma("sync", io["bc_hbm"], bc_bf[:, :], reads=["bc_bf"], writes=["bc_hbm"],
               allow_slow_non_contiguous=True)
        Wn = pspool["W"]
        FW = NST * Wn
        ones1b_a = EN(nc.sbuf_tensor("ones1b_a", [1, 128], BF16))
        if 6 * FW <= KD * TP:
            apre = xn_big[:, 0:2 * FW].bitcast(F32)
            upre = xn_big[:, 2 * FW:3 * FW]
            hpre = xn_big[:, 3 * FW:4 * FW]
            BrepW = xn_big[:, 4 * FW:5 * FW]
            bcWrow = xn_big[0:1, 5 * FW:6 * FW]
        else:
            apre = EN(nc.sbuf_tensor("apre_a", [128, FW], F32))[:]
            upre = EN(nc.sbuf_tensor("upre_a", [128, FW], BF16))[:]
            hpre = EN(nc.sbuf_tensor("hpre_a", [128, FW], BF16))[:]
            BrepW = EN(nc.sbuf_tensor("BrepW_a", [128, FW], BF16))[:]
            bcWrow = EN(nc.sbuf_tensor("bcWrow_a", [1, FW], BF16))[:]
        Hout = pspool["Hout"]
        em.op("vector", lambda h: h.memset(ones1b_a[:], 1.0),
              writes=["ones1b_a"]
              + [("xn_bf", j, g0) for j in range(NT) for g0 in range(0, KD, 4)]
              + ["xn_dead"])
        em.dma("sync", bcWrow[0:1, :], bc_bf[0:NST, T - Wn:T],
               reads=["bc_bf", "xn_dead"], writes=["bcWrow"], allow_slow_non_contiguous=True)
        for n in range(NST):
            pst, psk = next_ps()
            em.op("tensor", (lambda h, n=n, pst=pst: h.matmul(
                pst[:, 0:Wn], ones1b_a[0:1, :], bcWrow[0:1, n * Wn:(n + 1) * Wn],
                start=True, stop=True)),
                reads=["bcWrow", "ones1b_a"], writes=[psk])
            em.op("vector", (lambda h, n=n, pst=pst: h.tensor_copy(
                out=BrepW[:, n * Wn:(n + 1) * Wn], in_=pst[:, 0:Wn])),
                reads=[psk, "xn_dead"], writes=[("BrepW", n)])
    else:
        em.dma("scalar", io["B_out"], dbc_sb[R:R + NST, :], reads=["dbc"])
        em.dma("scalar", io["C_out"], dbc_sb[R + NST:R + 2 * NST, :], reads=["dbc"])

    for i, (c0, cw) in enumerate([] if fused else cfg.NT4):
        for half in range(2):
            dst = io["Brep_out"] if half == 0 else io["Crep_out"]
            em.dma("sync", bcr[0:1, 0:NST * cw],
                   dbc_sb[R + half * NST:R + (half + 1) * NST, c0:c0 + cw],
                   reads=["dbc"], writes=["bcr"], allow_slow_non_contiguous=True)
            for n in range(NST):
                pst, psk = next_ps()
                sb = brep_stage[n % 4]
                em.op("tensor", (lambda h, n=n, cw=cw, pst=pst: h.matmul(
                    pst[:, 0:cw], ones_sb[0:1, :], bcr[0:1, n * cw:(n + 1) * cw],
                    start=True, stop=True)),
                    reads=["bcr", "ones_sb"], writes=[psk])
                em.op("vector", (lambda h, cw=cw, pst=pst, sb=sb: h.tensor_copy(
                    out=sb[:, 0:cw], in_=pst[:, 0:cw])),
                    reads=[psk], writes=[("brs", n % 4)])
                em.dma("scalar", dst[:, n, c0:c0 + cw], sb[:, 0:cw], reads=[("brs", n % 4)])

    for m in range(MC):
        xb2 = m % 2
        em.dma("sync", xs_rl[xb2][:], io["xs_out"][m * 128:(m + 1) * 128, :],
               reads=[("xshbm", m)], writes=[("xs_rl", xb2)])

        def emit_mm(h, m=m):
            last = None
            for it, (c0, cw) in enumerate(cfg.NT4):
                last = h.matmul(ps_big[:, c0:c0 + cw], WdtT[0:R, m * 128:(m + 1) * 128],
                                dt_bf[0:R, c0:c0 + cw], start=True, stop=True)
            return last
        em.op("tensor", emit_mm, reads=["dt_bf", "WdtT"], writes=["ps_big"])
        em.op("scalar", (lambda h, m=m: h.activation(
            out=e1_sb[:], in_=ps_big[:, 0:T], func=AF.Exp, bias=bdt_col[:, m:m + 1])),
            reads=["ps_big", "bdt_col", "acc"], writes=["e1", "acc"])
        db = m % 2
        em.op("scalar", (lambda h, db=db: h.activation(
            out=delta_sb[db][:], in_=e1_sb[:], func=AF.Ln, bias=1.0)),
            reads=["e1"], writes=[("delta", db)])
        em.op("vector", (lambda h, db=db, xb2=xb2: h.tensor_tensor(
            out=wm_sb[db][:], in0=delta_sb[db][:], in1=xs_rl[xb2][:], op=ALU.mult)),
            reads=[("delta", db), ("xs_rl", xb2)], writes=[("wm", db)])
        em.dma("scalar", io["delta_out"][m * 128:(m + 1) * 128, :], delta_sb[db][:],
               reads=[("delta", db)], writes=[("dhbm", m)])
        em.dma("scalar", io["w_out"][m * 128:(m + 1) * 128, :], wm_sb[db][:],
               reads=[("wm", db)], writes=[("whbm", m)])
        if fused:
            em.op("vector", (lambda h, db=db, m=m: h.tensor_tensor(
                out=apre[:].rearrange("p (n w) -> p n w", n=NST),
                in0=delta_sb[db][:, T - Wn:T].unsqueeze(1).broadcast_to([128, NST, Wn]),
                in1=A_sb[:, m * NST:(m + 1) * NST].unsqueeze(2).broadcast_to([128, NST, Wn]),
                op=ALU.mult)),
                reads=[("delta", db), "Asb", "xn_dead"], writes=["apre"])
            em.op("scalar", (lambda h: h.activation(
                out=apre[:], in_=apre[:], func=AF.Exp)),
                reads=["apre"], writes=["apre"])
            em.op("vector", (lambda h: h.memset(
                apre[:].rearrange("p (n w) -> p n w", n=NST)[:, :, 0:1], 0.0)),
                reads=["apre"], writes=["apre0"])
            em.op("vector", (lambda h, db=db: h.tensor_tensor(
                out=upre[:].rearrange("p (n w) -> p n w", n=NST),
                in0=BrepW[:].rearrange("p (n w) -> p n w", n=NST),
                in1=wm_sb[db][:, T - Wn:T].unsqueeze(1).broadcast_to([128, NST, Wn]),
                op=ALU.mult)),
                reads=[("BrepW", n) for n in range(NST)] + [("wm", db), "xn_dead"],
                writes=["upre"])
            em.op("vector", (lambda h: h.tensor_tensor_scan(
                out=hpre[:], data0=apre[:], data1=upre[:],
                initial=0.0, op0=ALU.mult, op1=ALU.add)),
                reads=["apre", "apre0", "upre", "xn_dead"], writes=["hpre"])
            em.op("vector", (lambda h, m=m: h.tensor_copy(
                out=Hout[:, m * NST:(m + 1) * NST],
                in_=hpre[:].rearrange("p (n w) -> p n w", n=NST)[:, :, Wn - 1])),
                reads=["hpre"], writes=[("Hout", m)])

    if fused:
        em.dma("scalar", io["hout_st"], Hout[:].rearrange("p (m n) -> p m n", n=NST),
               reads=[("Hout", m) for m in range(MC)], writes=["hout_st"],
               allow_slow_non_contiguous=True)

    return io


def np_phase_a(x_in, W_in, W_xproj, W_dt, b_dt, conv_w, conv_b, norm_w, cfg):
    bf = ml_dtypes.bfloat16

    def b16(v):
        return v.astype(bf).astype(np.float32)
    T, D, E, R, NST = cfg.T, cfg.D, cfg.E, cfg.R, cfg.NST
    ssum = (x_in * x_in).sum(axis=1)
    rstd = np.exp(-0.5 * np.log(ssum / D + 1e-5)).astype(np.float32)
    xn = b16(x_in * rstd[:, None])
    WinT = b16(W_in.T * norm_w[:, None])
    xz_fm = (WinT.T.astype(np.float32) @ xn.T.astype(np.float32)).astype(np.float32)
    xc = xz_fm[:E]
    res = xz_fm[E:][:, 128:]
    acc = np.zeros((E, T), np.float32)
    for k in range(4):
        acc += xc[:, 125 + k:125 + k + T] * conv_w[:, k:k + 1]
    acc_b = acc + conv_b[:, None]
    xs = (acc_b / (1 + np.exp(-acc_b))).astype(np.float32)
    xs_bf = b16(xs)
    g = b16(res / (1 + np.exp(-res)))
    WxT = b16(W_xproj.T)
    dbc = (WxT.T.astype(np.float32) @ xs_bf).astype(np.float32)
    dt_bf = b16(dbc[:R])
    B = dbc[R:R + NST].astype(np.float32)
    C = dbc[R + NST:R + 2 * NST].astype(np.float32)
    WdtT = b16(W_dt.T)
    dl_raw = (WdtT.T.astype(np.float32) @ dt_bf).astype(np.float32)
    z = dl_raw + b_dt[:, None]
    delta = np.log1p(np.exp(z)).astype(np.float32)
    w = b16(delta * xs_bf)
    return dict(delta=delta, w=w, xs=xs_bf, g=g, B=B, C=C,
                Brep=np.broadcast_to(b16(B)[None], (128, NST, T)).copy(),
                Crep=np.broadcast_to(b16(C)[None], (128, NST, T)).copy())


class CfgB:
    def __init__(self, L=16384, Dsl=256, NST=16, Fc=2048, G=2):
        self.L, self.Dsl, self.NST, self.Fc, self.G = L, Dsl, NST, Fc, G
        self.DT = Dsl // 128
        self.NC = L // Fc
        self.NG = NST // G


def emit_phase_b(nc, em, EN, cfg, io=None):
    L, Dsl, NST, Fc, G = cfg.L, cfg.Dsl, cfg.NST, cfg.Fc, cfg.G
    DT, NC, NG = cfg.DT, cfg.NC, cfg.NG

    if io is None:
        io = {}
        io["delta_in"] = nc.dram_tensor("delta_in", [Dsl, L], F32, kind="ExternalInput").ap()
        io["wv_in"] = nc.dram_tensor("wv_in", [Dsl, L], BF16, kind="ExternalInput").ap()
        io["Brep_in"] = nc.dram_tensor("Brep_in", [128, NST, L], BF16, kind="ExternalInput").ap()
        io["Crep_in"] = nc.dram_tensor("Crep_in", [128, NST, L], BF16, kind="ExternalInput").ap()
        io["A_log_in"] = nc.dram_tensor("A_log_in", [Dsl, NST], F32, kind="ExternalInput").ap()
        io["y_out"] = nc.dram_tensor("y_out", [Dsl, L], BF16, kind="ExternalOutput").ap()

    A_sb = [EN(nc.sbuf_tensor(f"A_sb{d}", [128, NST], F32)) for d in range(DT)]
    carry = [EN(nc.sbuf_tensor(f"carry{d}", [128, NST], F32)) for d in range(DT)]
    dc = [EN(nc.sbuf_tensor(f"dc{i}", [128, Fc], F32)) for i in range(2)]
    wc = [EN(nc.sbuf_tensor(f"wc{i}", [128, Fc], BF16)) for i in range(2)]
    Bg = [EN(nc.sbuf_tensor(f"Bg{i}", [128, G * Fc], BF16)) for i in range(2)]
    Cg = [EN(nc.sbuf_tensor(f"Cg{i}", [128, G * Fc], BF16)) for i in range(2)]
    a_pl = [EN(nc.sbuf_tensor(f"a_pl{i}", [128, Fc], F32)) for i in range(3)]
    u_pl = [EN(nc.sbuf_tensor(f"u_pl{i}", [128, Fc], BF16)) for i in range(3)]
    h_pl = [EN(nc.sbuf_tensor(f"h_pl{i}", [128, Fc], BF16)) for i in range(3)]
    z_pl = [EN(nc.sbuf_tensor(f"z_pl{i}", [128, Fc], BF16)) for i in range(2)]
    yac = [EN(nc.sbuf_tensor(f"yac{i}", [128, Fc], BF16)) for i in range(2)]

    for d in range(DT):
        em.dma("sync", A_sb[d][:], io["A_log_in"][d * 128:(d + 1) * 128, :], writes=[("Asb", d)])
        em.op("scalar", (lambda h, d=d: h.activation(out=A_sb[d][:], in_=A_sb[d][:], func=AF.Exp)),
              reads=[("Asb", d)], writes=[("Asb", d)])
        em.op("vector", (lambda h, d=d: h.tensor_scalar_mul(A_sb[d][:], A_sb[d][:], -1.0)),
              reads=[("Asb", d)], writes=[("Asb", d)])
        em.op("vector", (lambda h, d=d: h.memset(carry[d][:], 0.0)), writes=[("carry", d, n) for n in range(NST)])

    for d in range(DT):
        for c in range(NC):
            ci = c % 2
            c0 = c * Fc
            em.dma("sync", dc[ci][:], io["delta_in"][d * 128:(d + 1) * 128, c0:c0 + Fc],
                   writes=[("dc", ci)])
            em.dma("sync", wc[ci][:], io["wv_in"][d * 128:(d + 1) * 128, c0:c0 + Fc],
                   writes=[("wc", ci)])
            for g in range(NG):
                gi = (c * NG + g) % 2
                n0 = g * G
                em.dma("sync", Bg[gi][:], io["Brep_in"][:, n0:n0 + G, c0:c0 + Fc],
                       writes=[("Bg", gi)])
                em.dma("sync", Cg[gi][:], io["Crep_in"][:, n0:n0 + G, c0:c0 + Fc],
                       writes=[("Cg", gi)])
                for j in range(G):
                    n = n0 + j
                    ai = n % 3
                    em.op("scalar", (lambda h, d=d, ci=ci, n=n, ai=ai: h.activation(
                        out=a_pl[ai][:], in_=dc[ci][:], func=AF.Exp,
                        scale=A_sb[d][:, n:n + 1])),
                        reads=[("dc", ci), ("Asb", d)], writes=[("a", ai)])
                    em.op("vector", (lambda h, ci=ci, gi=gi, j=j, ai=ai: h.tensor_tensor(
                        out=u_pl[ai][:], in0=Bg[gi][:, j * Fc:(j + 1) * Fc], in1=wc[ci][:],
                        op=ALU.mult)),
                        reads=[("Bg", gi), ("wc", ci)], writes=[("u", ai)])
                    em.op("vector", (lambda h, d=d, n=n, ai=ai: h.tensor_tensor_scan(
                        out=h_pl[ai][:], data0=a_pl[ai][:], data1=u_pl[ai][:],
                        initial=carry[d][:, n:n + 1], op0=ALU.mult, op1=ALU.add)),
                        reads=[("a", ai), ("u", ai), ("carry", d, n)], writes=[("h", ai)])
                    em.op("vector", (lambda h, d=d, n=n, ai=ai: h.tensor_copy(
                        out=carry[d][:, n:n + 1], in_=h_pl[ai][:, Fc - 1:Fc])),
                        reads=[("h", ai)], writes=[("carry", d, n)])
                    if n == 0:
                        em.op("vector", (lambda h, gi=gi, j=j, ai=ai, ci=ci: h.tensor_tensor(
                            out=yac[ci][:], in0=h_pl[ai][:], in1=Cg[gi][:, j * Fc:(j + 1) * Fc],
                            op=ALU.mult)),
                            reads=[("h", ai), ("Cg", gi)], writes=[("yac", ci)])
                    else:
                        zi = n % 2
                        em.op("vector", (lambda h, gi=gi, j=j, ai=ai, zi=zi: h.tensor_tensor(
                            out=z_pl[zi][:], in0=h_pl[ai][:], in1=Cg[gi][:, j * Fc:(j + 1) * Fc],
                            op=ALU.mult)),
                            reads=[("h", ai), ("Cg", gi)], writes=[("z", zi)])
                        em.op("vector", (lambda h, zi=zi, ci=ci: h.tensor_tensor(
                            out=yac[ci][:], in0=yac[ci][:], in1=z_pl[zi][:], op=ALU.add)),
                            reads=[("z", zi), ("yac", ci)], writes=[("yac", ci)])
            em.dma("scalar", io["y_out"][d * 128:(d + 1) * 128, c0:c0 + Fc], yac[ci][:],
                   reads=[("yac", ci)])
    return io


def np_phase_b(delta_in, wv_in, Brep_in, Crep_in, A_log_in, cfg):
    bf = ml_dtypes.bfloat16

    def b16(v):
        return v.astype(bf).astype(np.float32)
    L, Dsl, NST, Fc = cfg.L, cfg.Dsl, cfg.NST, cfg.Fc
    A = -np.exp(A_log_in.astype(np.float32))
    B = Brep_in[0].astype(np.float32)
    C = Crep_in[0].astype(np.float32)
    w = wv_in.astype(np.float32)
    y = np.zeros((Dsl, L), np.float32)
    h = np.zeros((Dsl, NST), np.float32)
    for t in range(L):
        a = np.exp(A * delta_in[:, t:t + 1])
        u = b16(w[:, t:t + 1] * B[:, t][None, :])
        h = a * h + u
        hb = b16(h)
        z = b16(hb * C[:, t][None, :])
        acc = z[:, 0]
        for n in range(1, NST):
            acc = b16(acc + z[:, n])
        y[:, t] = acc
        if (t + 1) % Fc == 0:
            h = b16(h)
    return b16(y)


class CfgC:
    def __init__(self, T=2048, D=1024, E=2048):
        self.T, self.D, self.E = T, D, E
        self.MC = E // 128
        self.KD = D // 128
        self.NTT = T // 128
        self.MAGIC = 12582912.0


def emit_phase_c(nc, em, EN, cfg, io=None, fused=False, pspool=None):
    T, D, E = cfg.T, cfg.D, cfg.E
    MC, KD, NTT = cfg.MC, cfg.KD, cfg.NTT
    MAGIC = cfg.MAGIC

    if io is None:
        io = {}
        io["y_in"] = nc.dram_tensor("y_in", [E, T], BF16, kind="ExternalInput").ap()
        io["xs_in"] = nc.dram_tensor("xs_in", [E, T], BF16, kind="ExternalInput").ap()
        io["g_in"] = nc.dram_tensor("g_in", [E, T], BF16, kind="ExternalInput").ap()
        io["x2_in"] = nc.dram_tensor("x2_in", [T, D], F32, kind="ExternalInput").ap()
        io["D_in"] = nc.dram_tensor("D_in", [E], F32, kind="ExternalInput").ap()
        io["W_out"] = nc.dram_tensor("W_out", [D, E], F32, kind="ExternalInput").ap()
        io["rhs"] = nc.dram_tensor("rhs", [D, D], F32, kind="ExternalInput").ap()
        io["out"] = nc.dram_tensor("out", [T, D], F32, kind="ExternalOutput").ap()
    x2_src = io["x2_in"]
    identc = nc.inline_tensor(np.eye(128, dtype=np.float32), "ident128c").ap()

    ident_sb = EN(nc.sbuf_tensor("identc_sb", [128, 128], F32))
    onesc_sb = EN(nc.sbuf_tensor("onesc_sb", [1, 128], F32))
    if not fused:
        D_col = EN(nc.sbuf_tensor("D_col", [128, MC], F32))
        ym = [EN(nc.sbuf_tensor(f"ym{i}", [128, T], BF16)) for i in range(2)]
        xsm = [EN(nc.sbuf_tensor(f"xsm{i}", [128, T], BF16)) for i in range(2)]
        gm = [EN(nc.sbuf_tensor(f"gm{i}", [128, T], BF16)) for i in range(2)]
        t1 = EN(nc.sbuf_tensor("t1", [128, T], F32))
    y2 = [EN(nc.sbuf_tensor(f"y2_{k}", [128, T], BF16)) for k in range(MC)]
    Wo_nat = [EN(nc.sbuf_tensor(f"Wo_nat{i}", [128, E], F32)) for i in range(2)]
    WoT = [EN(nc.sbuf_tensor(f"WoT{k}", [128, D], BF16)) for k in range(MC)]
    rh_nat = [EN(nc.sbuf_tensor(f"rh_nat{i}", [128, 128], F32)) for i in range(2)]
    rt_buf = EN(nc.sbuf_tensor("rt_buf", [128, D], F32))
    qr_e = EN(nc.sbuf_tensor("qr_e", [128, D], F32))
    q_rhs = [EN(nc.sbuf_tensor(f"q_rhs{k}", [128, D], BF16)) for k in range(KD)]
    rsc = EN(nc.sbuf_tensor("rsc", [128, KD], F32))
    rsc_T = EN(nc.sbuf_tensor("rsc_T", [KD, 128], F32))
    rs_row = EN(nc.sbuf_tensor("rs_row", [1, D], F32))
    rs_rep = EN(nc.sbuf_tensor("rs_rep", [128, D], F32))
    x2t = [EN(nc.sbuf_tensor(f"x2t{i}", [128, D], F32)) for i in range(2)]
    res1b = [EN(nc.sbuf_tensor(f"res1_{i}", [128, D], F32)) for i in range(2)]
    out1b = [EN(nc.sbuf_tensor(f"out1t{i}", [128, D], F32)) for i in range(2)]
    qvb = [EN(nc.sbuf_tensor(f"qv{i}", [128, D], F32)) for i in range(2)]
    qbfb = [EN(nc.sbuf_tensor(f"qbf{i}", [128, D], F32)) for i in range(2)]
    qTtb = [EN(nc.sbuf_tensor(f"qTt{i}", [128, KD * 128], BF16)) for i in range(2)]
    colsb = [EN(nc.sbuf_tensor(f"cols{i}", [128, 16], F32)) for i in range(2)]
    outt = [EN(nc.sbuf_tensor(f"outt{i}", [128, D], F32)) for i in range(2)]

    if pspool is None:
        psc = [EN(nc.psum_tensor(f"psc{i}", [128, 512], F32)) for i in range(4)]
        psck = "psc"
    else:
        psc = pspool["ps"]
        psck = "ps"
    NPS = len(psc)
    psi = [0]

    def next_ps():
        k = psi[0] % NPS
        psi[0] += 1
        return psc[k], f"{psck}{k}"

    em.dma("sync", ident_sb[:], identc, writes=["ident_sb"])
    if not fused:
        em.dma("sync", D_col[:], io["D_in"].rearrange("(m p) -> p m", p=128),
               writes=["D_col"], allow_slow_non_contiguous=True)
    em.op("vector", lambda h: h.memset(onesc_sb[:], 1.0), writes=["ones"])

    if fused:
        for m in range(MC):
            em.dma("sync", y2[m][:], io["y2_st"][m * 128:(m + 1) * 128, :],
                   reads=[("y2st", m)], writes=[("y2", m)])
    else:
        for m in range(MC):
            b = m % 2
            em.dma("sync", ym[b][:], io["y_in"][m * 128:(m + 1) * 128, :], writes=[("ym", b)])
            em.dma("sync", xsm[b][:], io["xs_in"][m * 128:(m + 1) * 128, :], writes=[("xsm", b)])
            em.dma("sync", gm[b][:], io["g_in"][m * 128:(m + 1) * 128, :], writes=[("gm", b)])
            em.op("vector", (lambda h, b=b, m=m: h.scalar_tensor_tensor(
                out=t1[:], in0=xsm[b][:], scalar=D_col[:, m:m + 1], in1=ym[b][:],
                op0=ALU.mult, op1=ALU.add)),
                reads=[("xsm", b), ("ym", b), "D_col"], writes=["t1"])
            em.op("vector", (lambda h, b=b, m=m: h.tensor_tensor(
                out=y2[m][:], in0=t1[:], in1=gm[b][:], op=ALU.mult)),
                reads=["t1", ("gm", b)], writes=[("y2", m)])

    for j in range(KD):
        b = j % 2
        em.dma("sync", Wo_nat[b][:], io["W_out"][j * 128:(j + 1) * 128, :], writes=[("Wo", b)])
        for g0 in range(0, MC, 4):
            gn = min(4, MC - g0)
            pst, psk = next_ps()

            def emit_tp(h, b=b, g0=g0, gn=gn, pst=pst):
                last = None
                for jj in range(gn):
                    k = g0 + jj
                    last = h.transpose(pst[:, jj * 128:(jj + 1) * 128],
                                       Wo_nat[b][:, k * 128:(k + 1) * 128], ident_sb[:])
                return last
            em.op("tensor", emit_tp, reads=[("Wo", b), "ident_sb"], writes=[psk])

            def emit_ev(h, j=j, g0=g0, gn=gn, pst=pst):
                last = None
                for jj in range(gn):
                    k = g0 + jj
                    last = h.tensor_copy(out=WoT[k][:, j * 128:(j + 1) * 128],
                                         in_=pst[:, jj * 128:(jj + 1) * 128])
                return last
            em.op("vector", emit_ev, reads=[psk], writes=[("WoT", j, g0)])
    wot_all = [("WoT", j, g0) for j in range(KD) for g0 in range(0, MC, 4)]

    for e in range(KD):
        for k in range(KD):
            rb = k % 2
            em.dma("sync", rh_nat[rb][:], io["rhs"][k * 128:(k + 1) * 128, e * 128:(e + 1) * 128],
                   writes=[("rh_nat", rb)], allow_slow_non_contiguous=True)
            pst, psk = next_ps()
            em.op("tensor", (lambda h, rb=rb, pst=pst: h.transpose(
                pst[:, 0:128], rh_nat[rb][:], ident_sb[:])),
                reads=[("rh_nat", rb), "ident_sb"], writes=[psk])
            em.op("vector", (lambda h, k=k, pst=pst: h.tensor_copy(
                out=rt_buf[:, k * 128:(k + 1) * 128], in_=pst[:, 0:128])),
                reads=[psk], writes=[("rt", k)])
        rtk = [("rt", k) for k in range(KD)]
        em.op("vector", (lambda h: h.tensor_reduce(
            out=cols[:, 0:1], in_=rt_buf[:], axis=mybir.AxisListType.X, op=ALU.max,
            apply_absolute_value=True)),
            reads=rtk, writes=[("cols", 0)])
        em.op("vector", (lambda h: h.tensor_scalar(
            out=cols[:, 1:2], in0=cols[:, 0:1], scalar1=1e-12, scalar2=1.0 / 127.0,
            op0=ALU.max, op1=ALU.mult)),
            reads=[("cols", 0)], writes=[("cols", 1)])
        em.op("vector", (lambda h: h.reciprocal(cols[:, 2:3], cols[:, 1:2])),
              reads=[("cols", 1)], writes=[("cols", 2)])
        em.op("vector", (lambda h, e=e: h.tensor_copy(out=rsc[:, e:e + 1], in_=cols[:, 1:2])),
              reads=[("cols", 1)], writes=[("rsc", e)])
        em.op("vector", (lambda h: h.tensor_scalar(
            out=rt_buf[:], in0=rt_buf[:], scalar1=cols[:, 2:3], scalar2=MAGIC,
            op0=ALU.mult, op1=ALU.add)),
            reads=rtk + [("cols", 2)], writes=rtk)
        em.op("vector", (lambda h: h.tensor_scalar(
            out=qr_e[:], in0=rt_buf[:], scalar1=MAGIC, scalar2=None, op0=ALU.subtract)),
            reads=rtk, writes=["qr_e"])
        for k in range(KD):
            pst, psk = next_ps()
            em.op("tensor", (lambda h, k=k, pst=pst: h.transpose(
                pst[:, 0:128], qr_e[:, k * 128:(k + 1) * 128], ident_sb[:])),
                reads=["qr_e", "ident_sb"], writes=[psk])
            em.op("vector", (lambda h, e=e, k=k, pst=pst: h.tensor_copy(
                out=q_rhs[k][:, e * 128:(e + 1) * 128], in_=pst[:, 0:128])),
                reads=[psk], writes=[("q_rhs", k, e)])
    qrhs_all = [("q_rhs", k, e) for k in range(KD) for e in range(KD)]
    rsc_all = [("rsc", e) for e in range(KD)]

    pst, psk = next_ps()
    em.op("tensor", (lambda h, pst=pst: h.transpose(pst[0:KD, 0:128], rsc[:, 0:KD], ident_sb[:])),
          reads=rsc_all + ["ident_sb"], writes=[psk])
    em.op("vector", (lambda h, pst=pst: h.tensor_copy(out=rsc_T[0:KD, :], in_=pst[0:KD, 0:128])),
          reads=[psk], writes=["rsc_T"])
    em.dma("sync", rs_row[0:1, :], rsc_T[0:KD, :], reads=["rsc_T"], writes=["rs_row"],
           allow_slow_non_contiguous=True)
    for i in range(0, D, 512):
        cw = min(512, D - i)
        pst, psk = next_ps()
        em.op("tensor", (lambda h, i=i, cw=cw, pst=pst: h.matmul(
            pst[:, 0:cw], onesc_sb[0:1, :], rs_row[0:1, i:i + cw], start=True, stop=True)),
            reads=["rs_row", "ones"], writes=[psk])
        em.op("vector", (lambda h, i=i, cw=cw, pst=pst: h.tensor_copy(
            out=rs_rep[:, i:i + cw], in_=pst[:, 0:cw])),
            reads=[psk], writes=[("rs_rep", i)])
    rsrep_all = [("rs_rep", i) for i in range(0, D, 512)]

    y2_all = [("y2", m) for m in range(MC)]
    for mt in range(NTT):
        xb = mt % 2
        ob = mt % 2
        res1 = res1b[xb]
        out1t = out1b[xb]
        qv = qvb[xb]
        qbf = qbfb[xb]
        qTt = qTtb[xb]
        cols = colsb[xb]
        f2 = res1
        em.dma("sync", x2t[xb][:], x2_src[mt * 128:(mt + 1) * 128, :], writes=[("x2t", xb)])
        NTD = (D + 511) // 512
        for nt in range(NTD):
            c0 = nt * 512
            cw = min(512, D - c0)
            pst, psk = next_ps()

            def emit_mm(h, mt=mt, c0=c0, cw=cw, pst=pst):
                last = None
                for k in range(MC):
                    last = h.matmul(pst[:, 0:cw], y2[k][:, mt * 128:(mt + 1) * 128],
                                    WoT[k][:, c0:c0 + cw],
                                    start=(k == 0), stop=(k == MC - 1))
                return last
            em.op("tensor", emit_mm, reads=y2_all + wot_all, writes=[psk])
            em.op("vector", (lambda h, c0=c0, cw=cw, xb=xb, pst=pst: h.scalar_tensor_tensor(
                out=res1[:, c0:c0 + cw], in0=x2t[xb][:, c0:c0 + cw],
                scalar=2.0, in1=pst[:, 0:cw], op0=ALU.mult, op1=ALU.add)),
                reads=[psk, ("x2t", xb)], writes=[("res1", xb, nt)])
        r1k = [("res1", xb, nt) for nt in range(NTD)]
        em.op("vector", (lambda h: h.tensor_reduce(
            out=cols[:, 0:1], in_=res1[:], axis=mybir.AxisListType.X, op=ALU.add)),
            reads=r1k, writes=[("cols", xb, 0)])
        em.op("scalar", (lambda h: h.activation(
            out=qv[:], in_=res1[:], func=AF.Square, accum_out=cols[:, 1:2])),
            reads=r1k, writes=[("qv", xb), ("cols", xb, 1)])
        em.op("vector", (lambda h: h.tensor_scalar(
            out=cols[:, 2:3], in0=cols[:, 0:1], scalar1=1.0 / D, scalar2=None, op0=ALU.mult)),
            reads=[("cols", xb, 0)], writes=[("cols", xb, 2)])
        em.op("vector", (lambda h: h.tensor_tensor(
            out=cols[:, 3:4], in0=cols[:, 2:3], in1=cols[:, 2:3], op=ALU.mult)),
            reads=[("cols", xb, 2)], writes=[("cols", xb, 3)])
        em.op("vector", (lambda h: h.tensor_scalar(
            out=cols[:, 4:5], in0=cols[:, 1:2], scalar1=1.0 / D, scalar2=1e-5,
            op0=ALU.mult, op1=ALU.add)),
            reads=[("cols", xb, 1)], writes=[("cols", xb, 4)])
        em.op("vector", (lambda h: h.tensor_tensor(
            out=cols[:, 5:6], in0=cols[:, 4:5], in1=cols[:, 3:4], op=ALU.subtract)),
            reads=[("cols", xb, 3), ("cols", xb, 4)], writes=[("cols", xb, 5)])
        em.op("scalar", (lambda h: h.activation(
            out=cols[:, 6:7], in_=cols[:, 5:6], func=AF.Ln)),
            reads=[("cols", xb, 5)], writes=[("cols", xb, 6)])
        em.op("scalar", (lambda h: h.activation(
            out=cols[:, 7:8], in_=cols[:, 6:7], func=AF.Exp, scale=-0.5)),
            reads=[("cols", xb, 6)], writes=[("cols", xb, 7)])
        em.op("vector", (lambda h: h.tensor_scalar(
            out=cols[:, 8:9], in0=cols[:, 2:3], scalar1=-1.0, scalar2=None, op0=ALU.mult)),
            reads=[("cols", xb, 2)], writes=[("cols", xb, 8)])
        em.op("vector", (lambda h: h.tensor_scalar(
            out=out1t[:], in0=res1[:], scalar1=cols[:, 8:9], scalar2=cols[:, 7:8],
            op0=ALU.add, op1=ALU.mult)),
            reads=r1k + [("cols", xb, 7), ("cols", xb, 8)], writes=[("out1t", xb)])
        em.op("vector", (lambda h: h.tensor_reduce(
            out=cols[:, 9:10], in_=out1t[:], axis=mybir.AxisListType.X, op=ALU.max,
            apply_absolute_value=True)),
            reads=[("out1t", xb)], writes=[("cols", xb, 9)])
        em.op("vector", (lambda h: h.tensor_scalar(
            out=cols[:, 10:11], in0=cols[:, 9:10], scalar1=1e-12, scalar2=1.0 / 127.0,
            op0=ALU.max, op1=ALU.mult)),
            reads=[("cols", xb, 9)], writes=[("cols", xb, 10)])
        em.op("vector", (lambda h: h.reciprocal(cols[:, 11:12], cols[:, 10:11])),
              reads=[("cols", xb, 10)], writes=[("cols", xb, 11)])
        em.op("vector", (lambda h: h.tensor_scalar(
            out=qv[:], in0=out1t[:], scalar1=cols[:, 11:12], scalar2=MAGIC,
            op0=ALU.mult, op1=ALU.add)),
            reads=[("out1t", xb), ("cols", xb, 11)], writes=[("qv", xb)])
        em.op("vector", (lambda h: h.tensor_scalar(
            out=qbf[:], in0=qv[:], scalar1=MAGIC, scalar2=None, op0=ALU.subtract)),
            reads=[("qv", xb)], writes=[("qbf", xb)])
        for g0 in range(0, KD, 4):
            gn = min(4, KD - g0)
            pst, psk = next_ps()

            def emit_tp(h, g0=g0, gn=gn, pst=pst):
                last = None
                for jj in range(gn):
                    k = g0 + jj
                    last = h.transpose(pst[:, jj * 128:(jj + 1) * 128],
                                       qbf[:, k * 128:(k + 1) * 128], ident_sb[:])
                return last
            em.op("tensor", emit_tp, reads=[("qbf", xb), "ident_sb"], writes=[psk])

            def emit_ev(h, g0=g0, gn=gn, pst=pst):
                last = None
                for jj in range(gn):
                    k = g0 + jj
                    last = h.tensor_copy(out=qTt[:, k * 128:(k + 1) * 128],
                                         in_=pst[:, jj * 128:(jj + 1) * 128])
                return last
            em.op("vector", emit_ev, reads=[psk], writes=[("qTt", xb, g0)])
        qt_all = [("qTt", xb, g0) for g0 in range(0, KD, 4)]
        for nt in range(NTD):
            c0 = nt * 512
            cw = min(512, D - c0)
            pst, psk = next_ps()

            def emit_mm(h, c0=c0, cw=cw, pst=pst):
                last = None
                for k in range(KD):
                    last = h.matmul(pst[:, 0:cw], qTt[:, k * 128:(k + 1) * 128],
                                    q_rhs[k][:, c0:c0 + cw],
                                    start=(k == 0), stop=(k == KD - 1))
                return last
            em.op("tensor", emit_mm, reads=qt_all + qrhs_all, writes=[psk])
            em.op("vector", (lambda h, c0=c0, cw=cw, pst=pst: h.scalar_tensor_tensor(
                out=f2[:, c0:c0 + cw], in0=pst[:, 0:cw],
                scalar=cols[:, 10:11], in1=rs_rep[:, c0:c0 + cw],
                op0=ALU.mult, op1=ALU.mult)),
                reads=[psk, ("cols", xb, 10)] + rsrep_all, writes=[("f2", xb, nt), ("res1", xb, nt)])
        f2k = [("f2", xb, nt) for nt in range(NTD)]
        em.op("vector", (lambda h: h.tensor_tensor(
            out=f2[:], in0=f2[:], in1=out1t[:], op=ALU.add)),
            reads=f2k + [("out1t", xb)], writes=[("f2s", xb)] + f2k)
        em.op("vector", (lambda h: h.tensor_reduce(
            out=cols[:, 0:1], in_=f2[:], axis=mybir.AxisListType.X, op=ALU.add)),
            reads=[("f2s", xb)], writes=[("cols", xb, 0)])
        em.op("scalar", (lambda h: h.activation(
            out=qv[:], in_=f2[:], func=AF.Square, accum_out=cols[:, 1:2])),
            reads=[("f2s", xb)], writes=[("qv", xb), ("cols", xb, 1)])
        em.op("vector", (lambda h: h.tensor_scalar(
            out=cols[:, 2:3], in0=cols[:, 0:1], scalar1=1.0 / D, scalar2=None, op0=ALU.mult)),
            reads=[("cols", xb, 0)], writes=[("cols", xb, 2)])
        em.op("vector", (lambda h: h.tensor_tensor(
            out=cols[:, 3:4], in0=cols[:, 2:3], in1=cols[:, 2:3], op=ALU.mult)),
            reads=[("cols", xb, 2)], writes=[("cols", xb, 3)])
        em.op("vector", (lambda h: h.tensor_scalar(
            out=cols[:, 4:5], in0=cols[:, 1:2], scalar1=1.0 / D, scalar2=1e-5,
            op0=ALU.mult, op1=ALU.add)),
            reads=[("cols", xb, 1)], writes=[("cols", xb, 4)])
        em.op("vector", (lambda h: h.tensor_tensor(
            out=cols[:, 5:6], in0=cols[:, 4:5], in1=cols[:, 3:4], op=ALU.subtract)),
            reads=[("cols", xb, 3), ("cols", xb, 4)], writes=[("cols", xb, 5)])
        em.op("scalar", (lambda h: h.activation(
            out=cols[:, 6:7], in_=cols[:, 5:6], func=AF.Ln)),
            reads=[("cols", xb, 5)], writes=[("cols", xb, 6)])
        em.op("scalar", (lambda h: h.activation(
            out=cols[:, 7:8], in_=cols[:, 6:7], func=AF.Exp, scale=-0.5)),
            reads=[("cols", xb, 6)], writes=[("cols", xb, 7)])
        em.op("vector", (lambda h: h.tensor_scalar(
            out=cols[:, 8:9], in0=cols[:, 2:3], scalar1=-1.0, scalar2=None, op0=ALU.mult)),
            reads=[("cols", xb, 2)], writes=[("cols", xb, 8)])
        em.op("vector", (lambda h, ob=ob: h.tensor_scalar(
            out=outt[ob][:], in0=f2[:], scalar1=cols[:, 8:9], scalar2=cols[:, 7:8],
            op0=ALU.add, op1=ALU.mult)),
            reads=[("f2s", xb), ("cols", xb, 7), ("cols", xb, 8)], writes=[("outt", ob)])
        em.dma("scalar", io["out"][mt * 128:(mt + 1) * 128, :], outt[ob][:],
               reads=[("outt", ob)])
    return io


def np_phase_c(y_in, xs_in, g_in, x2_in, D_in, W_out, rhs, cfg):
    bf = ml_dtypes.bfloat16

    def b16(v):
        return v.astype(bf).astype(np.float32)
    T, D, E = cfg.T, cfg.D, cfg.E
    y = y_in.astype(np.float32)
    xs = xs_in.astype(np.float32)
    g = g_in.astype(np.float32)
    t1 = xs * D_in[:, None] + y
    y2 = b16(t1 * g)
    WoT = b16(W_out.T)
    mam = (y2.T @ WoT.astype(np.float32)).astype(np.float32)
    res1 = 2.0 * x2_in + mam
    m = res1.mean(axis=1, keepdims=True)
    v = (res1 * res1).mean(axis=1, keepdims=True) - m * m
    rstd = np.exp(-0.5 * np.log(v + 1e-5)).astype(np.float32)
    out1 = ((res1 - m) * rstd).astype(np.float32)
    ls = np.maximum(np.abs(out1).max(axis=1, keepdims=True), 1e-12) / 127.0
    ql = np.round(out1 * (1.0 / ls)).astype(np.float32)
    rs = np.maximum(np.abs(rhs).max(axis=0, keepdims=True), 1e-12) / 127.0
    qr = np.round(rhs * (1.0 / rs)).astype(np.float32)
    acc = ql @ qr
    f2 = acc * ls * rs + out1
    m2 = f2.mean(axis=1, keepdims=True)
    v2 = (f2 * f2).mean(axis=1, keepdims=True) - m2 * m2
    rstd2 = np.exp(-0.5 * np.log(v2 + 1e-5)).astype(np.float32)
    return ((f2 - m2) * rstd2).astype(np.float32)


def cfg2_nt4(T):
    return [(i * 512, min(512, T - i * 512)) for i in range((T + 511) // 512)]


class CfgS:
    def __init__(self, T=2048, E=2048, R=64, NST=16, W=160, NH=4):
        self.T, self.E, self.R, self.NST, self.W = T, E, R, NST, W
        self.MC = E // 128
        self.NH = NH
        self.NG = NST // NH


def emit_scan_fused(nc, em, EN, cfg, io, bc_sb, pspool, ncore=8):
    T, E, NST, W = cfg.T, cfg.E, cfg.NST, cfg.W
    MC, NH, NG = cfg.MC, cfg.NH, cfg.NG

    ones1 = EN(nc.sbuf_tensor("ones1_s", [1, 128], F32))
    ones1b = EN(nc.sbuf_tensor("ones1b_s", [1, 128], BF16))
    bcB = EN(nc.sbuf_tensor("bcB_row", [1, max(cfg.NG * cfg.T, cfg.NST * cfg.W)], BF16))
    bcC = EN(nc.sbuf_tensor("bcC_row", [1, cfg.NG * cfg.T], BF16))
    A_sb = pspool["A_sb"]
    D_col = EN(nc.sbuf_tensor("D_col_s", [128, MC], F32))
    selv_sb = EN(nc.sbuf_tensor("selv_sb", [128, 8], F32))
    Hin = EN(nc.sbuf_tensor("Hin", [128, NST * MC], F32))
    Hall_sb = EN(nc.sbuf_tensor("Hall_sb", [128, ncore * MC * NST], F32))
    Brep = EN(nc.sbuf_tensor("Brep", [128, NG * T], BF16))
    Crep = EN(nc.sbuf_tensor("Crep", [128, NG * T], BF16))
    dcm = [EN(nc.sbuf_tensor(f"dcm{i}", [128, T], F32)) for i in range(2)]
    wcm = [EN(nc.sbuf_tensor(f"wcm{i}", [128, T], BF16)) for i in range(2)]
    xscm = [EN(nc.sbuf_tensor(f"xscm{i}", [128, T], BF16)) for i in range(2)]
    gcm = [EN(nc.sbuf_tensor(f"gcm{i}", [128, T], BF16)) for i in range(2)]
    a_big = EN(nc.sbuf_tensor("a_big", [128, 3 * T], F32))
    u_big = EN(nc.sbuf_tensor("u_big", [128, 3 * T], BF16))
    h_big = EN(nc.sbuf_tensor("h_big", [128, 3 * T], BF16))
    a_pl = [a_big[:, i * T:(i + 1) * T] for i in range(3)]
    u_pl = [u_big[:, i * T:(i + 1) * T] for i in range(3)]
    h_pl = [h_big[:, i * T:(i + 1) * T] for i in range(3)]
    z_pl = [EN(nc.sbuf_tensor(f"z_pl_s{i}", [128, T], BF16)) for i in range(2)]
    yacb = [EN(nc.sbuf_tensor(f"yac_s{i}", [128, T], BF16)) for i in range(2)]
    yh0 = [EN(nc.sbuf_tensor(f"yh0_{i}", [128, T], BF16)) for i in range(2)]
    y2m = [EN(nc.sbuf_tensor(f"y2m_{i}", [128, T], BF16)) for i in range(2)]
    t1s = EN(nc.sbuf_tensor("t1s", [128, T], F32))

    pss = pspool["ps"]
    NPS = len(pss)
    psi = [0]

    def next_ps():
        k = psi[0] % NPS
        psi[0] += 1
        return pss[k], f"ps{k}"

    em.op("vector", lambda h: h.memset(ones1[:], 1.0), writes=["ones1s_f"])
    em.op("vector", lambda h: h.memset(ones1b[:], 1.0), writes=["ones1s"])
    em.op("gpsimd", (lambda h: h.collective_compute(
        "AllGather", ALU.bypass, replica_groups=[list(range(ncore))],
        ins=[io["hout_st"].rearrange("p m n -> (p m n)")],
        outs=[io["hall_st"].rearrange("s p m n -> (s p m n)")])),
        reads=["hout_st"], writes=["hall_st"])
    em.dma("sync", D_col[:], io["D_in"].rearrange("(m p) -> p m", p=128),
           writes=["Dcol"], allow_slow_non_contiguous=True)
    em.dma("sync", selv_sb[:], io["selv"], writes=["selv"])

    def load_rows(buf, bufkey, n0, nplanes, col0, width):
        em.dma("sync", buf[0:1, 0:nplanes * width],
               io["bc_hbm"][n0:n0 + nplanes, col0:col0 + width],
               reads=["bc_hbm"], writes=[bufkey], allow_slow_non_contiguous=True)

    def bcast_build(dst, rows_key, srcbuf, srckey, nplanes, width, dst_w):
        for j in range(nplanes):
            for cc in range(0, width, 512):
                cw = min(512, width - cc)
                pst, psk = next_ps()
                em.op("tensor", (lambda h, j=j, cc=cc, cw=cw, pst=pst, srcbuf=srcbuf, width=width:
                                 h.matmul(pst[:, 0:cw], ones1b[0:1, :],
                                          srcbuf[0:1, j * width + cc:j * width + cc + cw],
                                          start=True, stop=True)),
                      reads=[srckey, "ones1s"], writes=[psk])
                em.op("vector", (lambda h, j=j, cc=cc, cw=cw, pst=pst:
                                 h.tensor_copy(out=dst[:, j * dst_w + cc:j * dst_w + cc + cw],
                                               in_=pst[:, 0:cw])),
                      reads=[psk], writes=[(rows_key, j)])

    S1 = ["xn_dead", "apre", "apre0", "upre", "hpre"]
    for half in range(NH):
        n0 = half * NG
        load_rows(bcB, "bcB", n0, NG, 0, T)
        load_rows(bcC, "bcC", NST + n0, NG, 0, T)
        bcast_build(Brep, "BrhJ", bcB, "bcB", NG, T, T)
        bcast_build(Crep, "CrhJ", bcC, "bcC", NG, T, T)
        if half == 0:
            em.dma("sync", Hall_sb[:].rearrange("p (s m n) -> p s m n", s=ncore, n=NST),
                   io["hall_st"].rearrange("s p m n -> p s m n"),
                   reads=["hall_st"], writes=["Hall_sb"], allow_slow_non_contiguous=True)
            for si in range(ncore):
                if si == 0:
                    em.op("vector", (lambda h, si=si: h.tensor_scalar(
                        out=Hin[:], in0=Hall_sb[:, si * MC * NST:(si + 1) * MC * NST],
                        scalar1=selv_sb[:, si:si + 1], scalar2=None, op0=ALU.mult)),
                        reads=["Hall_sb", "selv"], writes=["Hin"])
                else:
                    em.op("vector", (lambda h, si=si: h.scalar_tensor_tensor(
                        out=Hin[:], in0=Hall_sb[:, si * MC * NST:(si + 1) * MC * NST],
                        scalar=selv_sb[:, si:si + 1], in1=Hin[:],
                        op0=ALU.mult, op1=ALU.add)),
                        reads=["Hall_sb", "selv", "Hin"], writes=["Hin"])

        for m in range(MC):
            mb = m % 2
            yac = yacb[mb]
            yk = ("yacs", mb)
            em.dma("sync", dcm[mb][:], io["delta_st"][m * 128:(m + 1) * 128, :],
                   reads=[("dhbm", m)], writes=[("dcm", mb)])
            em.dma("sync", wcm[mb][:], io["w_st"][m * 128:(m + 1) * 128, :],
                   reads=[("whbm", m)], writes=[("wcm", mb)])
            if half == NH - 1:
                em.dma("sync", xscm[mb][:], io["xs_st"][m * 128:(m + 1) * 128, :],
                       reads=[("xshbm", m)], writes=[("xscm", mb)])
                em.dma("sync", gcm[mb][:], io["g_st"][m * 128:(m + 1) * 128, :],
                       reads=[("ghbm", m, it) for it in range(len(cfg2_nt4(T)))],
                       writes=[("gcm", mb)])
            for j in range(NG):
                n = n0 + j
                ai = n % 3
                em.op("scalar", (lambda h, m=m, mb=mb, n=n, ai=ai: h.activation(
                    out=a_pl[ai][:], in_=dcm[mb][:], func=AF.Exp,
                    scale=A_sb[:, m * NST + n:m * NST + n + 1])),
                    reads=[("dcm", mb), "Asb"], writes=[("a", ai)])
                em.op("vector", (lambda h, mb=mb, j=j, ai=ai: h.tensor_tensor(
                    out=u_pl[ai][:], in0=Brep[:, j * T:(j + 1) * T], in1=wcm[mb][:],
                    op=ALU.mult)),
                    reads=[("BrhJ", j), ("wcm", mb)],
                    writes=[("u", ai)])
                em.op("vector", (lambda h, m=m, n=n, ai=ai: h.tensor_tensor_scan(
                    out=h_pl[ai][:], data0=a_pl[ai][:], data1=u_pl[ai][:],
                    initial=Hin[:, m * NST + n:m * NST + n + 1],
                    op0=ALU.mult, op1=ALU.add)),
                    reads=[("a", ai), ("u", ai), "Hin"], writes=[("h", ai)])
                zi = n % 2
                if j == 0:
                    em.op("vector", (lambda h, j=j, ai=ai, yac=yac: h.tensor_tensor(
                        out=yac[:], in0=h_pl[ai][:], in1=Crep[:, j * T:(j + 1) * T],
                        op=ALU.mult)),
                        reads=[("h", ai), ("CrhJ", j)], writes=[yk])
                else:
                    em.op("vector", (lambda h, j=j, ai=ai, zi=zi: h.tensor_tensor(
                        out=z_pl[zi][:], in0=h_pl[ai][:], in1=Crep[:, j * T:(j + 1) * T],
                        op=ALU.mult)),
                        reads=[("h", ai), ("CrhJ", j)], writes=[("z", zi)])
                    em.op("vector", (lambda h, zi=zi, yac=yac: h.tensor_tensor(
                        out=yac[:], in0=yac[:], in1=z_pl[zi][:], op=ALU.add)),
                        reads=[("z", zi), yk], writes=[yk])
            if half == 0:
                em.dma("scalar", io["yh_st"][m * 128:(m + 1) * 128, :], yac[:],
                       reads=[yk] + S1, writes=[("yhst", m)])
            elif half < NH - 1:
                em.dma("gpsimd", io["yh_st"][m * 128:(m + 1) * 128, :], yac[:],
                       accum_op=ALU.add,
                       reads=[yk, ("yhst", m)] + S1, writes=[("yhst", m)])
            else:
                em.dma("gpsimd", yac[:], io["yh_st"][m * 128:(m + 1) * 128, :],
                       accum_op=ALU.add,
                       reads=[("yhst", m)] + S1, writes=[yk])
                em.op("vector", (lambda h, m=m, mb=mb, yac=yac: h.scalar_tensor_tensor(
                    out=t1s[:], in0=xscm[mb][:], scalar=D_col[:, m:m + 1], in1=yac[:],
                    op0=ALU.mult, op1=ALU.add)),
                    reads=[("xscm", mb), yk, "Dcol"], writes=["t1s"])
                y2b = m % 2
                em.op("vector", (lambda h, mb=mb, y2b=y2b: h.tensor_tensor(
                    out=y2m[y2b][:], in0=t1s[:], in1=gcm[mb][:], op=ALU.mult)),
                    reads=["t1s", ("gcm", mb)], writes=[("y2m", y2b)])
                em.dma("scalar", io["y2_st"][m * 128:(m + 1) * 128, :], y2m[y2b][:],
                       reads=[("y2m", y2b)] + S1, writes=[("y2st", m)])
    return io


def build_fused(T=2048, D=1024, E=2048, R=64, NST=16, W=160, ncore=8):
    cfa = CfgA(T=T, D=D, E=E, R=R, NST=NST)
    cfs = CfgS(T=T, E=E, R=R, NST=NST, W=W)
    cfc = CfgC(T=T, D=D, E=E)
    nc = bass.Bass("TRN2", target_bir_lowering=False)
    MC = E // 128

    io = {}
    io["x_in"] = nc.dram_tensor("x_in", [T + 128, D], F32, kind="ExternalInput").ap()
    io["W_in"] = nc.dram_tensor("W_in", [2 * E, D], F32, kind="ExternalInput").ap()
    io["W_xproj"] = nc.dram_tensor("W_xproj", [96, E], F32, kind="ExternalInput").ap()
    io["W_dt"] = nc.dram_tensor("W_dt", [E, R], F32, kind="ExternalInput").ap()
    io["b_dt"] = nc.dram_tensor("b_dt", [E], F32, kind="ExternalInput").ap()
    io["conv_w"] = nc.dram_tensor("conv_w", [E, 4], F32, kind="ExternalInput").ap()
    io["conv_b"] = nc.dram_tensor("conv_b", [E], F32, kind="ExternalInput").ap()
    io["norm_w"] = nc.dram_tensor("norm_w", [D], F32, kind="ExternalInput").ap()
    io["A_log_in"] = nc.dram_tensor("A_log_in", [E, NST], F32, kind="ExternalInput").ap()
    io["D_in"] = nc.dram_tensor("D_in", [E], F32, kind="ExternalInput").ap()
    io["W_out"] = nc.dram_tensor("W_out", [D, E], F32, kind="ExternalInput").ap()
    io["rhs"] = nc.dram_tensor("rhs", [D, D], F32, kind="ExternalInput").ap()
    io["selv"] = nc.dram_tensor("selv", [128, 8], F32, kind="ExternalInput").ap()
    io["out"] = nc.dram_tensor("out", [T, D], F32, kind="ExternalOutput").ap()
    io["delta_out"] = nc.dram_tensor("delta_st", [E, T], F32, kind="Internal").ap()
    io["w_out"] = nc.dram_tensor("w_st", [E, T], BF16, kind="Internal").ap()
    io["xs_out"] = nc.dram_tensor("xs_st", [E, T], BF16, kind="Internal").ap()
    io["g_out"] = nc.dram_tensor("g_st", [E, T], BF16, kind="Internal").ap()
    io["delta_st"] = io["delta_out"]
    io["w_st"] = io["w_out"]
    io["xs_st"] = io["xs_out"]
    io["g_st"] = io["g_out"]
    io["y2_st"] = nc.dram_tensor("y2_st", [E, T], BF16, kind="Internal").ap()
    io["yh_st"] = nc.dram_tensor("yh_st", [E, T], BF16, kind="Internal").ap()
    io["bc_hbm"] = nc.dram_tensor("bc_hbm", [2 * NST, T], BF16, kind="Internal").ap()
    io["hout_st"] = nc.dram_tensor("hout_st", [128, MC, NST], F32, kind="Internal").ap()
    io["hall_st"] = nc.dram_tensor("hall_st", [ncore, 128, MC, NST], F32, kind="Internal").ap()

    with contextlib.ExitStack() as octx:
        em = Emitter(nc, octx)
        EN0 = octx.enter_context
        pspool = {
            "ps": [EN0(nc.psum_tensor(f"ps{i}", [128, 512], F32)) for i in range(4)],
            "ps_big": EN0(nc.psum_tensor("ps_big", [128, 4 * 512], F32)),
            "A_sb": EN0(nc.sbuf_tensor("A_sb", [128, NST * MC], F32)),
            "Hout": EN0(nc.sbuf_tensor("Hout", [128, MC * NST], F32)),
        }
        pspool["W"] = W
        pspool["ncore"] = ncore
        bc_sb = None
        with contextlib.ExitStack() as c1:
            emit_phase_a(nc, em, c1.enter_context, cfa, io=io, fused=True,
                         bc_sb=bc_sb, pspool=pspool)
            em.barrier()
        with contextlib.ExitStack() as c2:
            emit_scan_fused(nc, em, c2.enter_context, cfs, io, bc_sb, pspool, ncore=ncore)
            em.barrier()
        with contextlib.ExitStack() as c3:
            io["x2_in"] = io["x_in"][128:, :]
            io["y_in"] = io["y2_st"]; io["xs_in"] = io["xs_st"]; io["g_in"] = io["g_st"]
            emit_phase_c(nc, em, c3.enter_context, cfc, io=io, fused=True, pspool=pspool)
        for e in ("sync", "tensor", "vector", "scalar", "gpsimd"):
            em.wait_all(e)
        block = octx.enter_context(nc.Block())
        em.build(block)
    return nc


def make_in_maps(ins0, T, ncore=8):
    x = np.ascontiguousarray(np.asarray(ins0["L0_attn_x"])[0], dtype=np.float32)
    L, D = x.shape
    xpad = np.concatenate([np.zeros((128, D), np.float32), x], axis=0)
    common = dict(W_in=np.asarray(ins0["L0_attn_W_in"]), W_xproj=np.asarray(ins0["L0_attn_W_xproj"]),
                  W_dt=np.asarray(ins0["L0_attn_W_dt"]), b_dt=np.asarray(ins0["L0_attn_b_dt"]),
                  conv_w=np.asarray(ins0["L0_attn_conv_w"]), conv_b=np.asarray(ins0["L0_attn_conv_b"]),
                  norm_w=np.asarray(ins0["L0_attn_norm_w"]),
                  A_log_in=np.asarray(ins0["L0_attn_A_log"]), D_in=np.asarray(ins0["L0_attn_D"]),
                  W_out=np.asarray(ins0["L0_attn_W_out"]), rhs=np.asarray(ins0["L0_ffn_rhs"]))
    maps = []
    for i in range(ncore):
        sel = np.zeros((128, 8), np.float32)
        if i > 0:
            sel[:, i - 1] = 1.0
        maps.append(dict(
            x_in=np.ascontiguousarray(xpad[i * T:(i + 1) * T + 128]),
            selv=sel,
            **common))
    return maps


def kernel(L0_attn_x, L0_attn_W_in, L0_attn_conv_w, L0_attn_conv_b, L0_attn_W_xproj,
           L0_attn_W_dt, L0_attn_b_dt, L0_attn_A_log, L0_attn_D, L0_attn_W_out,
           L0_attn_norm_w, L0_ffn_rhs):
    from concourse.bass_utils import run_bass_kernel_spmd
    NCORE = 8
    x = np.asarray(L0_attn_x)
    L = x.shape[1]
    T = L // NCORE
    ins0 = dict(L0_attn_x=x, L0_attn_W_in=L0_attn_W_in, L0_attn_conv_w=L0_attn_conv_w,
                L0_attn_conv_b=L0_attn_conv_b, L0_attn_W_xproj=L0_attn_W_xproj,
                L0_attn_W_dt=L0_attn_W_dt, L0_attn_b_dt=L0_attn_b_dt,
                L0_attn_A_log=L0_attn_A_log, L0_attn_D=L0_attn_D,
                L0_attn_W_out=L0_attn_W_out, L0_attn_norm_w=L0_attn_norm_w,
                L0_ffn_rhs=L0_ffn_rhs)
    nc = build_fused(T=T, ncore=NCORE)
    maps = make_in_maps(ins0, T, NCORE)
    res = run_bass_kernel_spmd(nc, maps, core_ids=list(range(NCORE)))
    out = np.concatenate([res.results[i]["out"] for i in range(NCORE)], axis=0)
    return out[None]



# version 09
# speedup vs baseline: 1.1075x; 1.0322x over previous
import contextlib
import numpy as np
import concourse.bass as bass
import concourse.mybir as mybir

F32 = mybir.dt.float32
BF16 = mybir.dt.bfloat16
AF = mybir.ActivationFunctionType
ALU = mybir.AluOpType




DMA_POOL = 10


class _Eng:
    def __init__(self, name, sem, dsems):
        self.name = name
        self.sem = sem
        self.dsems = dsems
        self.n_ops = 0
        self.n_dma = 0
        self.marks = {}


class Emitter:
    def __init__(self, nc, ctx, engines=("sync", "tensor", "vector", "scalar", "gpsimd")):
        self.nc = nc
        self.engs = {}
        for name in engines:
            sem = ctx.enter_context(nc.semaphore(f"op_{name}"))
            dsems = ([ctx.enter_context(nc.semaphore(f"dma_{name}{i}")) for i in range(DMA_POOL)]
                     if name in ("sync", "scalar", "gpsimd") else [])
            self.engs[name] = _Eng(name, sem, dsems)
        self.last_w = {}
        self.readers = {}
        self._bodies = {n: [] for n in engines}
        self._final = {n: [] for n in engines}

    def op(self, eng, emit, reads=(), writes=(), dma=False):
        e = self.engs[eng]
        deps = []
        for k in reads:
            lw = self.last_w.get(k)
            if lw is not None:
                deps.append(lw)
        for k in writes:
            lw = self.last_w.get(k)
            if lw is not None:
                deps.append(lw)
            deps.extend(self.readers.get(k, ()))
        if dma:
            slot = e.n_dma % DMA_POOL
            rank = e.n_dma // DMA_POOL
            sem = e.dsems[slot]
            rec = (eng, sem, 16 * (rank + 1))
            throttle = (sem, 16 * rank) if rank >= 1 else None
            e.n_dma += 1
        else:
            sem = e.sem
            rec = (eng, sem, e.n_ops + 1)
            throttle = None
            e.n_ops += 1
        for k in reads:
            self.readers.setdefault(k, []).append(rec)
        for k in writes:
            self.last_w[k] = rec
            self.readers[k] = []
        self._bodies[eng].append((emit, deps, dma, sem, throttle))
        return rec

    def dma(self, eng, out, in_, reads=(), writes=(), **kw):
        def emit(h):
            return h.dma_start(out=out, in_=in_, **kw)
        return self.op(eng, emit, reads=reads, writes=writes, dma=True)

    def build(self, block):
        for name in self.engs:
            body = self._bodies[name]
            fins = self._final[name]
            if not body and not fins:
                continue

            def make(body=body, fins=fins, name=name):
                e = self.engs[name]

                def run(h):
                    for emit, deps, dma, sem, throttle in body:
                        for (peng, psem, pval) in deps:
                            mk = e.marks.get(id(psem), -1)
                            if pval > mk:
                                h.wait_ge(psem, pval)
                                e.marks[id(psem)] = pval
                        if throttle is not None:
                            tsem, tval = throttle
                            mk = e.marks.get(id(tsem), -1)
                            if tval > mk:
                                h.wait_ge(tsem, tval)
                                e.marks[id(tsem)] = tval
                        inst = emit(h)
                        inst.then_inc(sem, 16 if dma else 1)
                    for fin in fins:
                        fin(h)
                return run
            getattr(block, name)(make())

    def final(self, eng, fn):
        self._final[eng].append(fn)

    def wait_all(self, eng):
        targets = []
        for name, e in self.engs.items():
            if e.n_ops:
                targets.append((e.sem, e.n_ops))
            for i, ds in enumerate(e.dsems):
                cnt = (e.n_dma - i + DMA_POOL - 1) // DMA_POOL
                if cnt > 0:
                    targets.append((ds, 16 * cnt))

        def fin(h):
            for sem, val in targets:
                h.wait_ge(sem, val)
        self._final[eng].append(fin)

    def barrier(self):
        targets = []
        for name, e in self.engs.items():
            if e.n_ops:
                targets.append((e.sem, e.n_ops))
            for i, ds in enumerate(e.dsems):
                cnt = (e.n_dma - i + DMA_POOL - 1) // DMA_POOL
                if cnt > 0:
                    targets.append((ds, 16 * cnt))
        for name in self.engs:
            def emit(h, targets=targets):
                for sem, val in targets:
                    h.wait_ge(sem, val)
                return h.nop()
            self.op(name, emit)


class CfgA:
    def __init__(self, T=2048, D=1024, E=2048, R=64, NST=16):
        self.T, self.D, self.E, self.R, self.NST = T, D, E, R, NST
        self.TP = T + 128
        self.NT = self.TP // 128
        self.KD = D // 128
        self.ME = 2 * E // 128
        self.MC = E // 128
        self.NT5 = [(i * 512, min(512, self.TP - i * 512)) for i in range((self.TP + 511) // 512)]
        self.NT4 = [(i * 512, min(512, T - i * 512)) for i in range((T + 511) // 512)]


def emit_phase_a(nc, em, EN, cfg, io=None, fused=False, bc_sb=None, pspool=None):
    T, D, E, R, NST = cfg.T, cfg.D, cfg.E, cfg.R, cfg.NST
    TP, NT, KD, ME, MC = cfg.TP, cfg.NT, cfg.KD, cfg.ME, cfg.MC

    if io is None:
        io = {}
        io["x_in"] = nc.dram_tensor("x_in", [TP, D], F32, kind="ExternalInput").ap()
        io["W_in"] = nc.dram_tensor("W_in", [2 * E, D], F32, kind="ExternalInput").ap()
        io["W_xproj"] = nc.dram_tensor("W_xproj", [96, E], F32, kind="ExternalInput").ap()
        io["W_dt"] = nc.dram_tensor("W_dt", [E, R], F32, kind="ExternalInput").ap()
        io["b_dt"] = nc.dram_tensor("b_dt", [E], F32, kind="ExternalInput").ap()
        io["conv_w"] = nc.dram_tensor("conv_w", [E, 4], F32, kind="ExternalInput").ap()
        io["conv_b"] = nc.dram_tensor("conv_b", [E], F32, kind="ExternalInput").ap()
        io["norm_w"] = nc.dram_tensor("norm_w", [D], F32, kind="ExternalInput").ap()
        io["delta_out"] = nc.dram_tensor("delta_out", [E, T], F32, kind="ExternalOutput").ap()
        io["w_out"] = nc.dram_tensor("w_out", [E, T], BF16, kind="ExternalOutput").ap()
        io["xs_out"] = nc.dram_tensor("xs_out", [E, T], BF16, kind="ExternalOutput").ap()
        io["g_out"] = nc.dram_tensor("g_out", [E, T], BF16, kind="ExternalOutput").ap()
        io["B_out"] = nc.dram_tensor("B_out", [NST, T], F32, kind="ExternalOutput").ap()
        io["C_out"] = nc.dram_tensor("C_out", [NST, T], F32, kind="ExternalOutput").ap()
        io["Brep_out"] = nc.dram_tensor("Brep_out", [128, NST, T], BF16, kind="ExternalOutput").ap()
        io["Crep_out"] = nc.dram_tensor("Crep_out", [128, NST, T], BF16, kind="ExternalOutput").ap()
    ident = nc.inline_tensor(np.eye(128, dtype=np.float32), "ident128").ap()

    ident_sb = EN(nc.sbuf_tensor("ident_sb", [128, 128], F32))
    ones_sb = EN(nc.sbuf_tensor("ones_sb", [1, 128], F32))
    Wnat = [EN(nc.sbuf_tensor(f"Wnat{i}", [128, D], F32)) for i in range(2)]
    WinT = [EN(nc.sbuf_tensor(f"WinT{i}", [128, D], BF16)) for i in range(2)]
    WxN = EN(nc.sbuf_tensor("WxN", [96, E], F32))
    WxT = EN(nc.sbuf_tensor("WxT", [128, 96 * MC], BF16))
    WdtN = EN(nc.sbuf_tensor("WdtN", [128, R * MC], F32))
    WdtT = EN(nc.sbuf_tensor("WdtT", [R, E], BF16))
    nw_col = EN(nc.sbuf_tensor("nw_col", [128, KD], F32))
    bdt_col = EN(nc.sbuf_tensor("bdt_col", [128, MC], F32))
    cw_col = EN(nc.sbuf_tensor("cw_col", [128, 4 * MC], F32))
    cb_col = EN(nc.sbuf_tensor("cb_col", [128, MC], F32))

    x_tm = [EN(nc.sbuf_tensor(f"x_tm{i}", [128, D], F32)) for i in range(2)]
    ssum = EN(nc.sbuf_tensor("ssum", [128, NT], F32))
    rstd_cols = EN(nc.sbuf_tensor("rstd_cols", [128, NT], F32))
    rstd_T = EN(nc.sbuf_tensor("rstd_T", [NT, 128], F32))
    rstd_row = EN(nc.sbuf_tensor("rstd_row", [1, TP], F32))
    rstd_rep = EN(nc.sbuf_tensor("rstd_rep", [128, TP], F32))
    xn_big = EN(nc.sbuf_tensor("xn_big", [128, KD * TP], BF16))
    xn_bf = [xn_big[:, k * TP:(k + 1) * TP] for k in range(KD)]

    xc_strip = EN(nc.sbuf_tensor("xc_strip", [128, TP], F32))
    acc_str = EN(nc.sbuf_tensor("acc_str", [128, T], F32))
    xs_m = [EN(nc.sbuf_tensor(f"xs_m{i}", [128, T], BF16)) for i in range(2)]
    xs_rl = [EN(nc.sbuf_tensor(f"xs_rl{i}", [128, T], BF16)) for i in range(2)]
    g_stage = [EN(nc.sbuf_tensor(f"g_stage{i}", [128, 512], BF16)) for i in range(2)]

    dbc_sb = EN(nc.sbuf_tensor("dbc_sb", [96, T], F32))
    dt_bf = EN(nc.sbuf_tensor("dt_bf", [R, T], BF16))
    bcr = EN(nc.sbuf_tensor("bcr", [1, NST * 512], F32))
    brep_stage = [EN(nc.sbuf_tensor(f"brep_stage{i}", [128, 512], BF16)) for i in range(4)]
    e1_sb = acc_str
    delta_sb = [EN(nc.sbuf_tensor(f"delta_sb{i}", [128, T], F32)) for i in range(2)]
    wm_sb = [EN(nc.sbuf_tensor(f"wm_sb{i}", [128, T], BF16)) for i in range(2)]

    if pspool is None:
        ps = [EN(nc.psum_tensor(f"ps{i}", [128, 512], F32)) for i in range(4)]
        ps_big = EN(nc.psum_tensor("ps_big", [128, 4 * 512], F32))
    else:
        ps = pspool["ps"]
        ps_big = pspool["ps_big"]
    NPS = len(ps)
    psi = [0]

    def next_ps():
        k = psi[0] % NPS
        psi[0] += 1
        return ps[k], f"ps{k}"

    em.dma("sync", ident_sb[:], ident, writes=["ident_sb"])
    em.dma("sync", nw_col[:], io["norm_w"].rearrange("(k p) -> p k", p=128), writes=["nw_col"], allow_slow_non_contiguous=True)
    em.dma("sync", bdt_col[:], io["b_dt"].rearrange("(m p) -> p m", p=128), writes=["bdt_col"], allow_slow_non_contiguous=True)
    em.dma("sync", cb_col[:], io["conv_b"].rearrange("(m p) -> p m", p=128), writes=["cb_col"], allow_slow_non_contiguous=True)
    em.dma("sync", cw_col[:].rearrange("p (m k) -> p m k", k=4),
           io["conv_w"].rearrange("(m p) k -> p m k", p=128), writes=["cw_col"], allow_slow_non_contiguous=True)
    em.dma("sync", WxN[:], io["W_xproj"], writes=["WxN"])
    em.dma("sync", WdtN[:].rearrange("p (m r) -> p m r", r=R),
           io["W_dt"].rearrange("(m p) r -> p m r", p=128), writes=["WdtN"], allow_slow_non_contiguous=True)
    em.op("vector", lambda h: h.memset(ones_sb[:], 1.0), writes=["ones_sb"])
    if fused:
        A_sb = pspool["A_sb"]
        em.dma("sync", A_sb[:].rearrange("p (m n) -> p m n", n=NST),
               io["A_log_in"].rearrange("(m p) n -> p m n", p=128),
               writes=["Asb"], allow_slow_non_contiguous=True)
        em.op("scalar", lambda h: h.activation(out=A_sb[:], in_=A_sb[:], func=AF.Exp),
              reads=["Asb"], writes=["Asb"])
        em.op("vector", lambda h: h.tensor_scalar_mul(A_sb[:], A_sb[:], -1.0),
              reads=["Asb"], writes=["Asb"])

    for g0 in range(0, MC, 4):
        gn = min(4, MC - g0)
        pst, psk = next_ps()

        def emit_tp(h, g0=g0, gn=gn, pst=pst):
            last = None
            for j in range(gn):
                k = g0 + j
                last = h.transpose(pst[:, j * 128:j * 128 + 96],
                                   WxN[0:96, k * 128:(k + 1) * 128], ident_sb[0:96, 0:96])
            return last
        em.op("tensor", emit_tp, reads=["WxN", "ident_sb"], writes=[psk])

        def emit_ev(h, g0=g0, gn=gn, pst=pst):
            last = None
            for j in range(gn):
                k = g0 + j
                last = h.tensor_copy(out=WxT[:, k * 96:(k + 1) * 96],
                                     in_=pst[:, j * 128:j * 128 + 96])
            return last
        em.op("vector", emit_ev, reads=[psk], writes=[("WxT", g0)])
    wxk = [("WxT", g0) for g0 in range(0, MC, 4)]
    for g0 in range(0, MC, 4):
        gn = min(4, MC - g0)
        pst, psk = next_ps()

        def emit_tp(h, g0=g0, gn=gn, pst=pst):
            last = None
            for j in range(gn):
                mm_ = g0 + j
                last = h.transpose(pst[0:R, j * 128:(j + 1) * 128],
                                   WdtN[:, mm_ * R:(mm_ + 1) * R], ident_sb[:])
            return last
        em.op("tensor", emit_tp, reads=["WdtN", "ident_sb"], writes=[psk])

        def emit_ev(h, g0=g0, gn=gn, pst=pst):
            last = None
            for j in range(gn):
                mm_ = g0 + j
                last = h.tensor_copy(out=WdtT[0:R, mm_ * 128:(mm_ + 1) * 128],
                                     in_=pst[0:R, j * 128:(j + 1) * 128])
            return last
        em.op("vector", emit_ev, reads=[psk], writes=["WdtT"])

    for j in range(NT):
        xb = j % 2
        em.dma("sync", x_tm[xb][:], io["x_in"][j * 128:(j + 1) * 128, :], writes=[("x_tm", xb)])
        em.op("scalar", (lambda h, xb=xb, j=j: h.activation(
            out=x_tm[xb][:], in_=x_tm[xb][:], func=AF.Square, accum_out=ssum[:, j:j + 1])),
            reads=[("x_tm", xb)], writes=[("x_tm", xb), ("ssum", j)])
    em.op("vector", lambda h: h.tensor_scalar(
        out=rstd_cols[:, 0:NT], in0=ssum[:, 0:NT], scalar1=1.0 / D, scalar2=1e-5,
        op0=ALU.mult, op1=ALU.add),
        reads=[("ssum", j) for j in range(NT)], writes=["rstd_cols"])
    em.op("scalar", lambda h: h.activation(out=rstd_cols[:, 0:NT], in_=rstd_cols[:, 0:NT],
                                           func=AF.Ln), reads=["rstd_cols"], writes=["rstd_cols"])
    em.op("scalar", lambda h: h.activation(out=rstd_cols[:, 0:NT], in_=rstd_cols[:, 0:NT],
                                           func=AF.Exp, scale=-0.5),
          reads=["rstd_cols"], writes=["rstd_cols"])
    pst, psk = next_ps()
    em.op("tensor", (lambda h, pst=pst: h.transpose(pst[0:NT, 0:128], rstd_cols[:, 0:NT], ident_sb[:])),
          reads=["rstd_cols", "ident_sb"], writes=[psk])
    em.op("vector", (lambda h, pst=pst: h.tensor_copy(out=rstd_T[0:NT, :], in_=pst[0:NT, 0:128])),
          reads=[psk], writes=["rstd_T"])
    em.dma("sync", rstd_row[0:1, :], rstd_T[0:NT, :],
           reads=["rstd_T"], writes=["rstd_row"], allow_slow_non_contiguous=True)
    for i, (c0, cw) in enumerate(cfg.NT5):
        pst, psk = next_ps()
        em.op("tensor", (lambda h, c0=c0, cw=cw, pst=pst: h.matmul(
            pst[:, 0:cw], ones_sb[0:1, :], rstd_row[0:1, c0:c0 + cw], start=True, stop=True)),
            reads=["rstd_row", "ones_sb"], writes=[psk])
        em.op("vector", (lambda h, c0=c0, cw=cw, pst=pst: h.tensor_copy(
            out=rstd_rep[:, c0:c0 + cw], in_=pst[:, 0:cw])),
            reads=[psk], writes=[("rstd_rep", i)])
    rstd_all = [("rstd_rep", i) for i in range(len(cfg.NT5))]

    for j in range(NT):
        xb = j % 2
        em.dma("sync", x_tm[xb][:], io["x_in"][j * 128:(j + 1) * 128, :], writes=[("x_tm", xb)])
        for g0 in range(0, KD, 4):
            gn = min(4, KD - g0)
            pst, psk = next_ps()

            def emit_tp(h, xb=xb, g0=g0, gn=gn, pst=pst):
                last = None
                for jj in range(gn):
                    k = g0 + jj
                    last = h.transpose(pst[:, jj * 128:(jj + 1) * 128],
                                       x_tm[xb][:, k * 128:(k + 1) * 128], ident_sb[:])
                return last
            em.op("tensor", emit_tp, reads=[("x_tm", xb), "ident_sb"], writes=[psk])

            def emit_ev(h, j=j, g0=g0, gn=gn, pst=pst):
                last = None
                for jj in range(gn):
                    k = g0 + jj
                    last = h.tensor_tensor(out=xn_bf[k][:, j * 128:(j + 1) * 128],
                                           in0=pst[:, jj * 128:(jj + 1) * 128],
                                           in1=rstd_rep[:, j * 128:(j + 1) * 128],
                                           op=ALU.mult)
                return last
            em.op("vector", emit_ev, reads=[psk] + rstd_all, writes=[("xn_bf", j, g0)])
    xn_all = [("xn_bf", j, g0) for j in range(NT) for g0 in range(0, KD, 4)]

    for m in range(ME):
        wb = m % 2
        isconv = m < MC
        em.dma("sync", Wnat[wb][:], io["W_in"][m * 128:(m + 1) * 128, :], writes=[("Wnat", wb)])
        for g0 in range(0, KD, 4):
            gn = min(4, KD - g0)
            pst, psk = next_ps()

            def emit_tp(h, wb=wb, g0=g0, gn=gn, pst=pst):
                last = None
                for jj in range(gn):
                    k = g0 + jj
                    last = h.transpose(pst[:, jj * 128:(jj + 1) * 128],
                                       Wnat[wb][:, k * 128:(k + 1) * 128], ident_sb[:])
                return last
            em.op("tensor", emit_tp, reads=[("Wnat", wb), "ident_sb"], writes=[psk])

            def emit_ev(h, wb=wb, g0=g0, gn=gn, pst=pst):
                last = None
                for jj in range(gn):
                    k = g0 + jj
                    last = h.tensor_scalar(out=WinT[wb][:, k * 128:(k + 1) * 128],
                                           in0=pst[:, jj * 128:(jj + 1) * 128],
                                           scalar1=nw_col[:, k:k + 1], scalar2=None,
                                           op0=ALU.mult)
                return last
            em.op("vector", emit_ev, reads=[psk, "nw_col"], writes=[("WinT", wb, g0)])
        wkeys = [("WinT", wb, g0) for g0 in range(0, KD, 4)]

        for it, (c0, cw) in enumerate(cfg.NT5):
            pst, psk = next_ps()

            def emit_mm(h, wb=wb, c0=c0, cw=cw, pst=pst):
                last = None
                for k in range(KD):
                    last = h.matmul(pst[:, 0:cw], WinT[wb][:, k * 128:(k + 1) * 128],
                                    xn_bf[k][:, c0:c0 + cw],
                                    start=(k == 0), stop=(k == KD - 1))
                return last
            em.op("tensor", emit_mm, reads=wkeys + xn_all, writes=[psk])
            if isconv:
                em.op("vector", (lambda h, c0=c0, cw=cw, pst=pst: h.tensor_copy(
                    out=xc_strip[:, c0:c0 + cw], in_=pst[:, 0:cw])),
                    reads=[psk], writes=[("xc", it)])
            else:
                slot = it % 2
                o0 = max(0, c0 - 128)
                s0 = max(0, 128 - c0)
                em.op("scalar", (lambda h, cw=cw, pst=pst, slot=slot: h.activation(
                    out=g_stage[slot][:, 0:cw], in_=pst[:, 0:cw], func=AF.Silu)),
                    reads=[psk], writes=[("g_stage", slot)])
                if cw - s0 > 0:
                    em.dma("scalar",
                           io["g_out"][(m - MC) * 128:(m - MC + 1) * 128, o0:o0 + cw - s0],
                           g_stage[slot][:, s0:cw],
                           reads=[("g_stage", slot)], writes=[("ghbm", m - MC, it)])
        if isconv:
            xck = [("xc", it) for it in range(len(cfg.NT5))]
            em.op("vector", (lambda h, m=m: h.tensor_scalar(
                out=acc_str[:], in0=xc_strip[:, 125:125 + T], scalar1=cw_col[:, 4 * m:4 * m + 1],
                scalar2=None, op0=ALU.mult)),
                reads=xck + ["cw_col"], writes=["acc"])
            for kk in range(1, 4):
                em.op("vector", (lambda h, m=m, kk=kk: h.scalar_tensor_tensor(
                    out=acc_str[:], in0=xc_strip[:, 125 + kk:125 + kk + T],
                    scalar=cw_col[:, 4 * m + kk:4 * m + kk + 1], in1=acc_str[:],
                    op0=ALU.mult, op1=ALU.add)),
                    reads=xck + ["cw_col", "acc"], writes=["acc"])
            xb2 = m % 2
            em.op("scalar", (lambda h, m=m, xb2=xb2: h.activation(
                out=xs_m[xb2][:], in_=acc_str[:], func=AF.Silu, bias=cb_col[:, m:m + 1])),
                reads=["acc", "cb_col"], writes=[("xs_m", xb2)])
            em.dma("scalar", io["xs_out"][m * 128:(m + 1) * 128, :], xs_m[xb2][:],
                   reads=[("xs_m", xb2)], writes=[("xshbm", m)])
            def emit_dbc(h, m=m, xb2=xb2):
                last = None
                for it, (c0, cw) in enumerate(cfg.NT4):
                    last = h.matmul(ps_big[0:96, it * 512:it * 512 + cw],
                                    WxT[:, m * 96:(m + 1) * 96],
                                    xs_m[xb2][:, c0:c0 + cw],
                                    start=(m == 0), stop=(m == MC - 1),
                                    skip_group_check=True)
                return last
            em.op("tensor", emit_dbc, reads=[("xs_m", xb2)] + wxk,
                  writes=["ps_big"])

    em.op("vector", (lambda h: h.tensor_copy(out=dbc_sb[0:96, :],
                                             in_=ps_big[0:96, 0:T])),
          reads=["ps_big"], writes=["dbc"])
    em.op("vector", lambda h: h.tensor_copy(out=dt_bf[0:R, :], in_=dbc_sb[0:R, :]),
          reads=["dbc"], writes=["dt_bf"])
    if fused:
        bc_bf = EN(nc.sbuf_tensor("bc_bf", [2 * NST, T], BF16))
        em.op("vector", (lambda h: h.tensor_copy(out=bc_bf[:, :],
                                                 in_=dbc_sb[R:R + 2 * NST, :])),
              reads=["dbc"], writes=["bc_bf"])
        em.dma("sync", io["bc_hbm"], bc_bf[:, :], reads=["bc_bf"], writes=["bc_hbm"],
               allow_slow_non_contiguous=True)
        Wn = pspool["W"]
        FW = NST * Wn
        ones1b_a = EN(nc.sbuf_tensor("ones1b_a", [1, 128], BF16))
        if 6 * FW <= KD * TP:
            apre = xn_big[:, 0:2 * FW].bitcast(F32)
            upre = xn_big[:, 2 * FW:3 * FW]
            hpre = xn_big[:, 3 * FW:4 * FW]
            BrepW = xn_big[:, 4 * FW:5 * FW]
            bcWrow = xn_big[0:1, 5 * FW:6 * FW]
        else:
            apre = EN(nc.sbuf_tensor("apre_a", [128, FW], F32))[:]
            upre = EN(nc.sbuf_tensor("upre_a", [128, FW], BF16))[:]
            hpre = EN(nc.sbuf_tensor("hpre_a", [128, FW], BF16))[:]
            BrepW = EN(nc.sbuf_tensor("BrepW_a", [128, FW], BF16))[:]
            bcWrow = EN(nc.sbuf_tensor("bcWrow_a", [1, FW], BF16))[:]
        Hout = pspool["Hout"]
        em.op("vector", lambda h: h.memset(ones1b_a[:], 1.0),
              writes=["ones1b_a"]
              + [("xn_bf", j, g0) for j in range(NT) for g0 in range(0, KD, 4)]
              + ["xn_dead"])
        em.dma("sync", bcWrow[0:1, :], bc_bf[0:NST, T - Wn:T],
               reads=["bc_bf", "xn_dead"], writes=["bcWrow"], allow_slow_non_contiguous=True)
        for n in range(NST):
            pst, psk = next_ps()
            em.op("tensor", (lambda h, n=n, pst=pst: h.matmul(
                pst[:, 0:Wn], ones1b_a[0:1, :], bcWrow[0:1, n * Wn:(n + 1) * Wn],
                start=True, stop=True)),
                reads=["bcWrow", "ones1b_a"], writes=[psk])
            em.op("vector", (lambda h, n=n, pst=pst: h.tensor_copy(
                out=BrepW[:, n * Wn:(n + 1) * Wn], in_=pst[:, 0:Wn])),
                reads=[psk, "xn_dead"], writes=[("BrepW", n)])
    else:
        em.dma("scalar", io["B_out"], dbc_sb[R:R + NST, :], reads=["dbc"])
        em.dma("scalar", io["C_out"], dbc_sb[R + NST:R + 2 * NST, :], reads=["dbc"])

    for i, (c0, cw) in enumerate([] if fused else cfg.NT4):
        for half in range(2):
            dst = io["Brep_out"] if half == 0 else io["Crep_out"]
            em.dma("sync", bcr[0:1, 0:NST * cw],
                   dbc_sb[R + half * NST:R + (half + 1) * NST, c0:c0 + cw],
                   reads=["dbc"], writes=["bcr"], allow_slow_non_contiguous=True)
            for n in range(NST):
                pst, psk = next_ps()
                sb = brep_stage[n % 4]
                em.op("tensor", (lambda h, n=n, cw=cw, pst=pst: h.matmul(
                    pst[:, 0:cw], ones_sb[0:1, :], bcr[0:1, n * cw:(n + 1) * cw],
                    start=True, stop=True)),
                    reads=["bcr", "ones_sb"], writes=[psk])
                em.op("vector", (lambda h, cw=cw, pst=pst, sb=sb: h.tensor_copy(
                    out=sb[:, 0:cw], in_=pst[:, 0:cw])),
                    reads=[psk], writes=[("brs", n % 4)])
                em.dma("scalar", dst[:, n, c0:c0 + cw], sb[:, 0:cw], reads=[("brs", n % 4)])

    for m in range(MC):
        xb2 = m % 2
        em.dma("sync", xs_rl[xb2][:], io["xs_out"][m * 128:(m + 1) * 128, :],
               reads=[("xshbm", m)], writes=[("xs_rl", xb2)])

        def emit_mm(h, m=m):
            last = None
            for it, (c0, cw) in enumerate(cfg.NT4):
                last = h.matmul(ps_big[:, c0:c0 + cw], WdtT[0:R, m * 128:(m + 1) * 128],
                                dt_bf[0:R, c0:c0 + cw], start=True, stop=True)
            return last
        em.op("tensor", emit_mm, reads=["dt_bf", "WdtT"], writes=["ps_big"])
        em.op("scalar", (lambda h, m=m: h.activation(
            out=e1_sb[:], in_=ps_big[:, 0:T], func=AF.Exp, bias=bdt_col[:, m:m + 1])),
            reads=["ps_big", "bdt_col", "acc"], writes=["e1", "acc"])
        db = m % 2
        em.op("scalar", (lambda h, db=db: h.activation(
            out=delta_sb[db][:], in_=e1_sb[:], func=AF.Ln, bias=1.0)),
            reads=["e1"], writes=[("delta", db)])
        em.op("vector", (lambda h, db=db, xb2=xb2: h.tensor_tensor(
            out=wm_sb[db][:], in0=delta_sb[db][:], in1=xs_rl[xb2][:], op=ALU.mult)),
            reads=[("delta", db), ("xs_rl", xb2)], writes=[("wm", db)])
        em.dma("scalar", io["delta_out"][m * 128:(m + 1) * 128, :], delta_sb[db][:],
               reads=[("delta", db)], writes=[("dhbm", m)])
        em.dma("scalar", io["w_out"][m * 128:(m + 1) * 128, :], wm_sb[db][:],
               reads=[("wm", db)], writes=[("whbm", m)])
        if fused:
            em.op("vector", (lambda h, db=db, m=m: h.tensor_tensor(
                out=apre[:].rearrange("p (n w) -> p n w", n=NST),
                in0=delta_sb[db][:, T - Wn:T].unsqueeze(1).broadcast_to([128, NST, Wn]),
                in1=A_sb[:, m * NST:(m + 1) * NST].unsqueeze(2).broadcast_to([128, NST, Wn]),
                op=ALU.mult)),
                reads=[("delta", db), "Asb", "xn_dead"], writes=["apre"])
            em.op("scalar", (lambda h: h.activation(
                out=apre[:], in_=apre[:], func=AF.Exp)),
                reads=["apre"], writes=["apre"])
            em.op("vector", (lambda h: h.memset(
                apre[:].rearrange("p (n w) -> p n w", n=NST)[:, :, 0:1], 0.0)),
                reads=["apre"], writes=["apre0"])
            em.op("vector", (lambda h, db=db: h.tensor_tensor(
                out=upre[:].rearrange("p (n w) -> p n w", n=NST),
                in0=BrepW[:].rearrange("p (n w) -> p n w", n=NST),
                in1=wm_sb[db][:, T - Wn:T].unsqueeze(1).broadcast_to([128, NST, Wn]),
                op=ALU.mult)),
                reads=[("BrepW", n) for n in range(NST)] + [("wm", db), "xn_dead"],
                writes=["upre"])
            em.op("vector", (lambda h: h.tensor_tensor_scan(
                out=hpre[:], data0=apre[:], data1=upre[:],
                initial=0.0, op0=ALU.mult, op1=ALU.add)),
                reads=["apre", "apre0", "upre", "xn_dead"], writes=["hpre"])
            em.op("vector", (lambda h, m=m: h.tensor_copy(
                out=Hout[:, m * NST:(m + 1) * NST],
                in_=hpre[:].rearrange("p (n w) -> p n w", n=NST)[:, :, Wn - 1])),
                reads=["hpre"], writes=[("Hout", m)])

    if fused:
        em.dma("scalar", io["hout_st"], Hout[:].rearrange("p (m n) -> p m n", n=NST),
               reads=[("Hout", m) for m in range(MC)], writes=["hout_st"],
               allow_slow_non_contiguous=True)

    return io


def np_phase_a(x_in, W_in, W_xproj, W_dt, b_dt, conv_w, conv_b, norm_w, cfg):
    bf = ml_dtypes.bfloat16

    def b16(v):
        return v.astype(bf).astype(np.float32)
    T, D, E, R, NST = cfg.T, cfg.D, cfg.E, cfg.R, cfg.NST
    ssum = (x_in * x_in).sum(axis=1)
    rstd = np.exp(-0.5 * np.log(ssum / D + 1e-5)).astype(np.float32)
    xn = b16(x_in * rstd[:, None])
    WinT = b16(W_in.T * norm_w[:, None])
    xz_fm = (WinT.T.astype(np.float32) @ xn.T.astype(np.float32)).astype(np.float32)
    xc = xz_fm[:E]
    res = xz_fm[E:][:, 128:]
    acc = np.zeros((E, T), np.float32)
    for k in range(4):
        acc += xc[:, 125 + k:125 + k + T] * conv_w[:, k:k + 1]
    acc_b = acc + conv_b[:, None]
    xs = (acc_b / (1 + np.exp(-acc_b))).astype(np.float32)
    xs_bf = b16(xs)
    g = b16(res / (1 + np.exp(-res)))
    WxT = b16(W_xproj.T)
    dbc = (WxT.T.astype(np.float32) @ xs_bf).astype(np.float32)
    dt_bf = b16(dbc[:R])
    B = dbc[R:R + NST].astype(np.float32)
    C = dbc[R + NST:R + 2 * NST].astype(np.float32)
    WdtT = b16(W_dt.T)
    dl_raw = (WdtT.T.astype(np.float32) @ dt_bf).astype(np.float32)
    z = dl_raw + b_dt[:, None]
    delta = np.log1p(np.exp(z)).astype(np.float32)
    w = b16(delta * xs_bf)
    return dict(delta=delta, w=w, xs=xs_bf, g=g, B=B, C=C,
                Brep=np.broadcast_to(b16(B)[None], (128, NST, T)).copy(),
                Crep=np.broadcast_to(b16(C)[None], (128, NST, T)).copy())


class CfgB:
    def __init__(self, L=16384, Dsl=256, NST=16, Fc=2048, G=2):
        self.L, self.Dsl, self.NST, self.Fc, self.G = L, Dsl, NST, Fc, G
        self.DT = Dsl // 128
        self.NC = L // Fc
        self.NG = NST // G


def emit_phase_b(nc, em, EN, cfg, io=None):
    L, Dsl, NST, Fc, G = cfg.L, cfg.Dsl, cfg.NST, cfg.Fc, cfg.G
    DT, NC, NG = cfg.DT, cfg.NC, cfg.NG

    if io is None:
        io = {}
        io["delta_in"] = nc.dram_tensor("delta_in", [Dsl, L], F32, kind="ExternalInput").ap()
        io["wv_in"] = nc.dram_tensor("wv_in", [Dsl, L], BF16, kind="ExternalInput").ap()
        io["Brep_in"] = nc.dram_tensor("Brep_in", [128, NST, L], BF16, kind="ExternalInput").ap()
        io["Crep_in"] = nc.dram_tensor("Crep_in", [128, NST, L], BF16, kind="ExternalInput").ap()
        io["A_log_in"] = nc.dram_tensor("A_log_in", [Dsl, NST], F32, kind="ExternalInput").ap()
        io["y_out"] = nc.dram_tensor("y_out", [Dsl, L], BF16, kind="ExternalOutput").ap()

    A_sb = [EN(nc.sbuf_tensor(f"A_sb{d}", [128, NST], F32)) for d in range(DT)]
    carry = [EN(nc.sbuf_tensor(f"carry{d}", [128, NST], F32)) for d in range(DT)]
    dc = [EN(nc.sbuf_tensor(f"dc{i}", [128, Fc], F32)) for i in range(2)]
    wc = [EN(nc.sbuf_tensor(f"wc{i}", [128, Fc], BF16)) for i in range(2)]
    Bg = [EN(nc.sbuf_tensor(f"Bg{i}", [128, G * Fc], BF16)) for i in range(2)]
    Cg = [EN(nc.sbuf_tensor(f"Cg{i}", [128, G * Fc], BF16)) for i in range(2)]
    a_pl = [EN(nc.sbuf_tensor(f"a_pl{i}", [128, Fc], F32)) for i in range(3)]
    u_pl = [EN(nc.sbuf_tensor(f"u_pl{i}", [128, Fc], BF16)) for i in range(3)]
    h_pl = [EN(nc.sbuf_tensor(f"h_pl{i}", [128, Fc], BF16)) for i in range(3)]
    z_pl = [EN(nc.sbuf_tensor(f"z_pl{i}", [128, Fc], BF16)) for i in range(2)]
    yac = [EN(nc.sbuf_tensor(f"yac{i}", [128, Fc], BF16)) for i in range(2)]

    for d in range(DT):
        em.dma("sync", A_sb[d][:], io["A_log_in"][d * 128:(d + 1) * 128, :], writes=[("Asb", d)])
        em.op("scalar", (lambda h, d=d: h.activation(out=A_sb[d][:], in_=A_sb[d][:], func=AF.Exp)),
              reads=[("Asb", d)], writes=[("Asb", d)])
        em.op("vector", (lambda h, d=d: h.tensor_scalar_mul(A_sb[d][:], A_sb[d][:], -1.0)),
              reads=[("Asb", d)], writes=[("Asb", d)])
        em.op("vector", (lambda h, d=d: h.memset(carry[d][:], 0.0)), writes=[("carry", d, n) for n in range(NST)])

    for d in range(DT):
        for c in range(NC):
            ci = c % 2
            c0 = c * Fc
            em.dma("sync", dc[ci][:], io["delta_in"][d * 128:(d + 1) * 128, c0:c0 + Fc],
                   writes=[("dc", ci)])
            em.dma("sync", wc[ci][:], io["wv_in"][d * 128:(d + 1) * 128, c0:c0 + Fc],
                   writes=[("wc", ci)])
            for g in range(NG):
                gi = (c * NG + g) % 2
                n0 = g * G
                em.dma("sync", Bg[gi][:], io["Brep_in"][:, n0:n0 + G, c0:c0 + Fc],
                       writes=[("Bg", gi)])
                em.dma("sync", Cg[gi][:], io["Crep_in"][:, n0:n0 + G, c0:c0 + Fc],
                       writes=[("Cg", gi)])
                for j in range(G):
                    n = n0 + j
                    ai = n % 3
                    em.op("scalar", (lambda h, d=d, ci=ci, n=n, ai=ai: h.activation(
                        out=a_pl[ai][:], in_=dc[ci][:], func=AF.Exp,
                        scale=A_sb[d][:, n:n + 1])),
                        reads=[("dc", ci), ("Asb", d)], writes=[("a", ai)])
                    em.op("vector", (lambda h, ci=ci, gi=gi, j=j, ai=ai: h.tensor_tensor(
                        out=u_pl[ai][:], in0=Bg[gi][:, j * Fc:(j + 1) * Fc], in1=wc[ci][:],
                        op=ALU.mult)),
                        reads=[("Bg", gi), ("wc", ci)], writes=[("u", ai)])
                    em.op("vector", (lambda h, d=d, n=n, ai=ai: h.tensor_tensor_scan(
                        out=h_pl[ai][:], data0=a_pl[ai][:], data1=u_pl[ai][:],
                        initial=carry[d][:, n:n + 1], op0=ALU.mult, op1=ALU.add)),
                        reads=[("a", ai), ("u", ai), ("carry", d, n)], writes=[("h", ai)])
                    em.op("vector", (lambda h, d=d, n=n, ai=ai: h.tensor_copy(
                        out=carry[d][:, n:n + 1], in_=h_pl[ai][:, Fc - 1:Fc])),
                        reads=[("h", ai)], writes=[("carry", d, n)])
                    if n == 0:
                        em.op("vector", (lambda h, gi=gi, j=j, ai=ai, ci=ci: h.tensor_tensor(
                            out=yac[ci][:], in0=h_pl[ai][:], in1=Cg[gi][:, j * Fc:(j + 1) * Fc],
                            op=ALU.mult)),
                            reads=[("h", ai), ("Cg", gi)], writes=[("yac", ci)])
                    else:
                        zi = n % 2
                        em.op("vector", (lambda h, gi=gi, j=j, ai=ai, zi=zi: h.tensor_tensor(
                            out=z_pl[zi][:], in0=h_pl[ai][:], in1=Cg[gi][:, j * Fc:(j + 1) * Fc],
                            op=ALU.mult)),
                            reads=[("h", ai), ("Cg", gi)], writes=[("z", zi)])
                        em.op("vector", (lambda h, zi=zi, ci=ci: h.tensor_tensor(
                            out=yac[ci][:], in0=yac[ci][:], in1=z_pl[zi][:], op=ALU.add)),
                            reads=[("z", zi), ("yac", ci)], writes=[("yac", ci)])
            em.dma("scalar", io["y_out"][d * 128:(d + 1) * 128, c0:c0 + Fc], yac[ci][:],
                   reads=[("yac", ci)])
    return io


def np_phase_b(delta_in, wv_in, Brep_in, Crep_in, A_log_in, cfg):
    bf = ml_dtypes.bfloat16

    def b16(v):
        return v.astype(bf).astype(np.float32)
    L, Dsl, NST, Fc = cfg.L, cfg.Dsl, cfg.NST, cfg.Fc
    A = -np.exp(A_log_in.astype(np.float32))
    B = Brep_in[0].astype(np.float32)
    C = Crep_in[0].astype(np.float32)
    w = wv_in.astype(np.float32)
    y = np.zeros((Dsl, L), np.float32)
    h = np.zeros((Dsl, NST), np.float32)
    for t in range(L):
        a = np.exp(A * delta_in[:, t:t + 1])
        u = b16(w[:, t:t + 1] * B[:, t][None, :])
        h = a * h + u
        hb = b16(h)
        z = b16(hb * C[:, t][None, :])
        acc = z[:, 0]
        for n in range(1, NST):
            acc = b16(acc + z[:, n])
        y[:, t] = acc
        if (t + 1) % Fc == 0:
            h = b16(h)
    return b16(y)


class CfgC:
    def __init__(self, T=2048, D=1024, E=2048):
        self.T, self.D, self.E = T, D, E
        self.MC = E // 128
        self.KD = D // 128
        self.NTT = T // 128
        self.MAGIC = 12582912.0


def emit_phase_c(nc, em, EN, cfg, io=None, fused=False, pspool=None):
    T, D, E = cfg.T, cfg.D, cfg.E
    MC, KD, NTT = cfg.MC, cfg.KD, cfg.NTT
    MAGIC = cfg.MAGIC

    if io is None:
        io = {}
        io["y_in"] = nc.dram_tensor("y_in", [E, T], BF16, kind="ExternalInput").ap()
        io["xs_in"] = nc.dram_tensor("xs_in", [E, T], BF16, kind="ExternalInput").ap()
        io["g_in"] = nc.dram_tensor("g_in", [E, T], BF16, kind="ExternalInput").ap()
        io["x2_in"] = nc.dram_tensor("x2_in", [T, D], F32, kind="ExternalInput").ap()
        io["D_in"] = nc.dram_tensor("D_in", [E], F32, kind="ExternalInput").ap()
        io["W_out"] = nc.dram_tensor("W_out", [D, E], F32, kind="ExternalInput").ap()
        io["rhs"] = nc.dram_tensor("rhs", [D, D], F32, kind="ExternalInput").ap()
        io["out"] = nc.dram_tensor("out", [T, D], F32, kind="ExternalOutput").ap()
    x2_src = io["x2_in"]
    identc = nc.inline_tensor(np.eye(128, dtype=np.float32), "ident128c").ap()

    ident_sb = EN(nc.sbuf_tensor("identc_sb", [128, 128], F32))
    onesc_sb = EN(nc.sbuf_tensor("onesc_sb", [1, 128], F32))
    if not fused:
        D_col = EN(nc.sbuf_tensor("D_col", [128, MC], F32))
        ym = [EN(nc.sbuf_tensor(f"ym{i}", [128, T], BF16)) for i in range(2)]
        xsm = [EN(nc.sbuf_tensor(f"xsm{i}", [128, T], BF16)) for i in range(2)]
        gm = [EN(nc.sbuf_tensor(f"gm{i}", [128, T], BF16)) for i in range(2)]
        t1 = EN(nc.sbuf_tensor("t1", [128, T], F32))
    NBUF = 4 if fused else 2
    if fused:
        y2mt = [EN(nc.sbuf_tensor(f"y2mt{i}", [128, MC * 128], BF16)) for i in range(NBUF)]
    else:
        y2 = [EN(nc.sbuf_tensor(f"y2_{k}", [128, T], BF16)) for k in range(MC)]
    Wo_nat = [EN(nc.sbuf_tensor(f"Wo_nat{i}", [128, E], F32)) for i in range(2)]
    WoT = [EN(nc.sbuf_tensor(f"WoT{k}", [128, D], BF16)) for k in range(MC)]
    rh_nat = [EN(nc.sbuf_tensor(f"rh_nat{i}", [128, 128], F32)) for i in range(2)]
    rt_buf = EN(nc.sbuf_tensor("rt_buf", [128, D], F32))
    qr_e = EN(nc.sbuf_tensor("qr_e", [128, D], F32))
    q_rhs = [EN(nc.sbuf_tensor(f"q_rhs{k}", [128, D], BF16)) for k in range(KD)]
    rsc = EN(nc.sbuf_tensor("rsc", [128, KD], F32))
    rsc_T = EN(nc.sbuf_tensor("rsc_T", [KD, 128], F32))
    rs_row = EN(nc.sbuf_tensor("rs_row", [1, D], F32))
    rs_rep = EN(nc.sbuf_tensor("rs_rep", [128, D], F32))
    x2t = [EN(nc.sbuf_tensor(f"x2t{i}", [128, D], F32)) for i in range(NBUF)]
    res1b = [EN(nc.sbuf_tensor(f"res1_{i}", [128, D], F32)) for i in range(NBUF)]
    out1b = [EN(nc.sbuf_tensor(f"out1t{i}", [128, D], F32)) for i in range(NBUF)]
    qvb = [EN(nc.sbuf_tensor(f"qv{i}", [128, D], F32)) for i in range(NBUF)]
    qbfb = [EN(nc.sbuf_tensor(f"qbf{i}", [128, D], F32)) for i in range(NBUF)]
    qTtb = [EN(nc.sbuf_tensor(f"qTt{i}", [128, KD * 128], BF16)) for i in range(NBUF)]
    colsb = [EN(nc.sbuf_tensor(f"cols{i}", [128, 16], F32)) for i in range(NBUF)]
    outt = [EN(nc.sbuf_tensor(f"outt{i}", [128, D], F32)) for i in range(NBUF)]

    if pspool is None:
        psc = [EN(nc.psum_tensor(f"psc{i}", [128, 512], F32)) for i in range(4)]
        psck = "psc"
    else:
        psc = pspool["ps"]
        psck = "ps"
    NPS = len(psc)
    psi = [0]

    def next_ps():
        k = psi[0] % NPS
        psi[0] += 1
        return psc[k], f"{psck}{k}"

    magic_c = nc.inline_tensor(np.full((1, 1), MAGIC, np.float32), "magic_c").ap()
    nmagic_c = nc.inline_tensor(np.full((1, 1), -MAGIC, np.float32), "nmagic_c").ap()
    nc.const_aps.register(F32, MAGIC, magic_c)
    nc.const_aps.register(F32, -MAGIC, nmagic_c)
    em.dma("sync", ident_sb[:], identc, writes=["ident_sb"])
    if not fused:
        em.dma("sync", D_col[:], io["D_in"].rearrange("(m p) -> p m", p=128),
               writes=["D_col"], allow_slow_non_contiguous=True)
    em.op("vector", lambda h: h.memset(onesc_sb[:], 1.0), writes=["ones"])

    if fused:
        pass
    else:
        for m in range(MC):
            b = m % 2
            em.dma("sync", ym[b][:], io["y_in"][m * 128:(m + 1) * 128, :], writes=[("ym", b)])
            em.dma("sync", xsm[b][:], io["xs_in"][m * 128:(m + 1) * 128, :], writes=[("xsm", b)])
            em.dma("sync", gm[b][:], io["g_in"][m * 128:(m + 1) * 128, :], writes=[("gm", b)])
            em.op("vector", (lambda h, b=b, m=m: h.scalar_tensor_tensor(
                out=t1[:], in0=xsm[b][:], scalar=D_col[:, m:m + 1], in1=ym[b][:],
                op0=ALU.mult, op1=ALU.add)),
                reads=[("xsm", b), ("ym", b), "D_col"], writes=["t1"])
            em.op("vector", (lambda h, b=b, m=m: h.tensor_tensor(
                out=y2[m][:], in0=t1[:], in1=gm[b][:], op=ALU.mult)),
                reads=["t1", ("gm", b)], writes=[("y2", m)])

    for j in range(KD):
        b = j % 2
        em.dma("sync", Wo_nat[b][:], io["W_out"][j * 128:(j + 1) * 128, :], writes=[("Wo", b)])
        for g0 in range(0, MC, 4):
            gn = min(4, MC - g0)
            pst, psk = next_ps()

            def emit_tp(h, b=b, g0=g0, gn=gn, pst=pst):
                last = None
                for jj in range(gn):
                    k = g0 + jj
                    last = h.transpose(pst[:, jj * 128:(jj + 1) * 128],
                                       Wo_nat[b][:, k * 128:(k + 1) * 128], ident_sb[:])
                return last
            em.op("tensor", emit_tp, reads=[("Wo", b), "ident_sb"], writes=[psk])

            def emit_ev(h, j=j, g0=g0, gn=gn, pst=pst):
                last = None
                for jj in range(gn):
                    k = g0 + jj
                    last = h.tensor_copy(out=WoT[k][:, j * 128:(j + 1) * 128],
                                         in_=pst[:, jj * 128:(jj + 1) * 128])
                return last
            em.op("vector", emit_ev, reads=[psk], writes=[("WoT", j, g0)])
    wot_all = [("WoT", j, g0) for j in range(KD) for g0 in range(0, MC, 4)]

    for e in range(KD):
        for k in range(KD):
            rb = k % 2
            em.dma("sync", rh_nat[rb][:], io["rhs"][k * 128:(k + 1) * 128, e * 128:(e + 1) * 128],
                   writes=[("rh_nat", rb)], allow_slow_non_contiguous=True)
            pst, psk = next_ps()
            em.op("tensor", (lambda h, rb=rb, pst=pst: h.transpose(
                pst[:, 0:128], rh_nat[rb][:], ident_sb[:])),
                reads=[("rh_nat", rb), "ident_sb"], writes=[psk])
            em.op("vector", (lambda h, k=k, pst=pst: h.tensor_copy(
                out=rt_buf[:, k * 128:(k + 1) * 128], in_=pst[:, 0:128])),
                reads=[psk], writes=[("rt", k)])
        rtk = [("rt", k) for k in range(KD)]
        em.op("vector", (lambda h: h.tensor_reduce(
            out=cols[:, 0:1], in_=rt_buf[:], axis=mybir.AxisListType.X, op=ALU.max,
            apply_absolute_value=True)),
            reads=rtk, writes=[("cols", 0)])
        em.op("vector", (lambda h: h.tensor_scalar(
            out=cols[:, 1:2], in0=cols[:, 0:1], scalar1=1e-12, scalar2=1.0 / 127.0,
            op0=ALU.max, op1=ALU.mult)),
            reads=[("cols", 0)], writes=[("cols", 1)])
        em.op("vector", (lambda h: h.reciprocal(cols[:, 2:3], cols[:, 1:2])),
              reads=[("cols", 1)], writes=[("cols", 2)])
        em.op("vector", (lambda h, e=e: h.tensor_copy(out=rsc[:, e:e + 1], in_=cols[:, 1:2])),
              reads=[("cols", 1)], writes=[("rsc", e)])
        em.op("vector", (lambda h: h.tensor_scalar(
            out=rt_buf[:], in0=rt_buf[:], scalar1=cols[:, 2:3], scalar2=MAGIC,
            op0=ALU.mult, op1=ALU.add)),
            reads=rtk + [("cols", 2)], writes=rtk)
        em.op("vector", (lambda h: h.tensor_scalar(
            out=qr_e[:], in0=rt_buf[:], scalar1=MAGIC, scalar2=None, op0=ALU.subtract)),
            reads=rtk, writes=["qr_e"])
        for k in range(KD):
            pst, psk = next_ps()
            em.op("tensor", (lambda h, k=k, pst=pst: h.transpose(
                pst[:, 0:128], qr_e[:, k * 128:(k + 1) * 128], ident_sb[:])),
                reads=["qr_e", "ident_sb"], writes=[psk])
            em.op("vector", (lambda h, e=e, k=k, pst=pst: h.tensor_copy(
                out=q_rhs[k][:, e * 128:(e + 1) * 128], in_=pst[:, 0:128])),
                reads=[psk], writes=[("q_rhs", k, e)])
    qrhs_all = [("q_rhs", k, e) for k in range(KD) for e in range(KD)]
    rsc_all = [("rsc", e) for e in range(KD)]

    pst, psk = next_ps()
    em.op("tensor", (lambda h, pst=pst: h.transpose(pst[0:KD, 0:128], rsc[:, 0:KD], ident_sb[:])),
          reads=rsc_all + ["ident_sb"], writes=[psk])
    em.op("vector", (lambda h, pst=pst: h.tensor_copy(out=rsc_T[0:KD, :], in_=pst[0:KD, 0:128])),
          reads=[psk], writes=["rsc_T"])
    em.dma("sync", rs_row[0:1, :], rsc_T[0:KD, :], reads=["rsc_T"], writes=["rs_row"],
           allow_slow_non_contiguous=True)
    for i in range(0, D, 512):
        cw = min(512, D - i)
        pst, psk = next_ps()
        em.op("tensor", (lambda h, i=i, cw=cw, pst=pst: h.matmul(
            pst[:, 0:cw], onesc_sb[0:1, :], rs_row[0:1, i:i + cw], start=True, stop=True)),
            reads=["rs_row", "ones"], writes=[psk])
        em.op("vector", (lambda h, i=i, cw=cw, pst=pst: h.tensor_copy(
            out=rs_rep[:, i:i + cw], in_=pst[:, 0:cw])),
            reads=[psk], writes=[("rs_rep", i)])
    rsrep_all = [("rs_rep", i) for i in range(0, D, 512)]

    y2_all = [("y2", m) for m in range(MC)] if not fused else []
    for mt in range(NTT):
        xb = mt % NBUF
        ob = xb
        res1 = res1b[xb]
        out1t = out1b[xb]
        qv = qvb[xb]
        qbf = qbfb[xb]
        qTt = qTtb[xb]
        cols = colsb[xb]
        f2 = res1
        em.dma("sync", x2t[xb][:], x2_src[mt * 128:(mt + 1) * 128, :], writes=[("x2t", xb)])
        if fused:
            em.dma("sync", y2mt[xb][:].rearrange("p (k c) -> p k c", c=128),
                   io["y2_st"].rearrange("(k p) t -> p k t", p=128)[:, :, mt * 128:(mt + 1) * 128],
                   reads=[("y2st", m) for m in range(MC)], writes=[("y2mt", xb)],
                   allow_slow_non_contiguous=True)
        NTD = (D + 511) // 512
        base = 0 if (mt % 2 == 0) else 2
        alt = 2 - base + 2
        alt = (base + 2) % 4
        if fused:
            for nt in range(NTD):
                c0 = nt * 512
                cw = min(512, D - c0)
                pst, psk = psc[base + nt], f"{psck}{base + nt}"

                def emit_mm(h, xb=xb, c0=c0, cw=cw, pst=pst):
                    last = None
                    for k in range(MC):
                        last = h.matmul(pst[:, 0:cw], y2mt[xb][:, k * 128:(k + 1) * 128],
                                        WoT[k][:, c0:c0 + cw],
                                        start=(k == 0), stop=(k == MC - 1))
                    return last
                em.op("tensor", emit_mm, reads=[("y2mt", xb)] + wot_all, writes=[psk])
                em.op("vector", (lambda h, c0=c0, cw=cw, xb=xb, pst=pst: h.scalar_tensor_tensor(
                    out=res1[:, c0:c0 + cw], in0=x2t[xb][:, c0:c0 + cw],
                    scalar=2.0, in1=pst[:, 0:cw], op0=ALU.mult, op1=ALU.add)),
                    reads=[psk, ("x2t", xb)], writes=[("res1", xb, nt)])
        else:
            for nt in range(NTD):
                c0 = nt * 512
                cw = min(512, D - c0)
                pst, psk = psc[base + nt], f"{psck}{base + nt}"

                def emit_mm(h, mt=mt, c0=c0, cw=cw, pst=pst):
                    last = None
                    for k in range(MC):
                        last = h.matmul(pst[:, 0:cw], y2[k][:, mt * 128:(mt + 1) * 128],
                                        WoT[k][:, c0:c0 + cw],
                                        start=(k == 0), stop=(k == MC - 1))
                    return last
                em.op("tensor", emit_mm, reads=y2_all + wot_all, writes=[psk])
                em.op("vector", (lambda h, c0=c0, cw=cw, xb=xb, pst=pst: h.scalar_tensor_tensor(
                    out=res1[:, c0:c0 + cw], in0=x2t[xb][:, c0:c0 + cw],
                    scalar=2.0, in1=pst[:, 0:cw], op0=ALU.mult, op1=ALU.add)),
                    reads=[psk, ("x2t", xb)], writes=[("res1", xb, nt)])
        r1k = [("res1", xb, nt) for nt in range(NTD)]
        em.op("vector", (lambda h: h.tensor_reduce(
            out=cols[:, 0:1], in_=res1[:], axis=mybir.AxisListType.X, op=ALU.add)),
            reads=r1k, writes=[("cols", xb, 0)])
        em.op("scalar", (lambda h: h.activation(
            out=qv[:], in_=res1[:], func=AF.Square, accum_out=cols[:, 1:2])),
            reads=r1k, writes=[("qv", xb), ("cols", xb, 1)])
        em.op("vector", (lambda h: h.tensor_scalar(
            out=cols[:, 2:3], in0=cols[:, 0:1], scalar1=1.0 / D, scalar2=None, op0=ALU.mult)),
            reads=[("cols", xb, 0)], writes=[("cols", xb, 2)])
        em.op("vector", (lambda h: h.tensor_tensor(
            out=cols[:, 3:4], in0=cols[:, 2:3], in1=cols[:, 2:3], op=ALU.mult)),
            reads=[("cols", xb, 2)], writes=[("cols", xb, 3)])
        em.op("vector", (lambda h: h.tensor_scalar(
            out=cols[:, 4:5], in0=cols[:, 1:2], scalar1=1.0 / D, scalar2=1e-5,
            op0=ALU.mult, op1=ALU.add)),
            reads=[("cols", xb, 1)], writes=[("cols", xb, 4)])
        em.op("vector", (lambda h: h.tensor_tensor(
            out=cols[:, 5:6], in0=cols[:, 4:5], in1=cols[:, 3:4], op=ALU.subtract)),
            reads=[("cols", xb, 3), ("cols", xb, 4)], writes=[("cols", xb, 5)])
        em.op("scalar", (lambda h: h.activation(
            out=cols[:, 6:7], in_=cols[:, 5:6], func=AF.Ln)),
            reads=[("cols", xb, 5)], writes=[("cols", xb, 6)])
        em.op("scalar", (lambda h: h.activation(
            out=cols[:, 7:8], in_=cols[:, 6:7], func=AF.Exp, scale=-0.5)),
            reads=[("cols", xb, 6)], writes=[("cols", xb, 7)])
        em.op("vector", (lambda h: h.tensor_scalar(
            out=cols[:, 8:9], in0=cols[:, 2:3], scalar1=-1.0, scalar2=None, op0=ALU.mult)),
            reads=[("cols", xb, 2)], writes=[("cols", xb, 8)])
        em.op("vector", (lambda h: h.tensor_scalar(
            out=out1t[:], in0=res1[:], scalar1=cols[:, 8:9], scalar2=cols[:, 7:8],
            op0=ALU.add, op1=ALU.mult)),
            reads=r1k + [("cols", xb, 7), ("cols", xb, 8)], writes=[("out1t", xb)])
        em.op("vector", (lambda h: h.tensor_reduce(
            out=cols[:, 9:10], in_=out1t[:], axis=mybir.AxisListType.X, op=ALU.max,
            apply_absolute_value=True)),
            reads=[("out1t", xb)], writes=[("cols", xb, 9)])
        em.op("vector", (lambda h: h.tensor_scalar(
            out=cols[:, 10:11], in0=cols[:, 9:10], scalar1=1e-12, scalar2=1.0 / 127.0,
            op0=ALU.max, op1=ALU.mult)),
            reads=[("cols", xb, 9)], writes=[("cols", xb, 10)])
        em.op("vector", (lambda h: h.reciprocal(cols[:, 11:12], cols[:, 10:11])),
              reads=[("cols", xb, 10)], writes=[("cols", xb, 11)])
        em.op("vector", (lambda h: h.tensor_scalar(
            out=qv[:], in0=out1t[:], scalar1=cols[:, 11:12], scalar2=MAGIC,
            op0=ALU.mult, op1=ALU.add)),
            reads=[("out1t", xb), ("cols", xb, 11)], writes=[("qv", xb)])
        em.op("vector", (lambda h: h.tensor_scalar(
            out=qbf[:], in0=qv[:], scalar1=MAGIC, scalar2=None, op0=ALU.subtract)),
            reads=[("qv", xb)], writes=[("qbf", xb)])
        for g0 in range(0, KD, 4):
            gn = min(4, KD - g0)
            gi = g0 // 4
            pst, psk = psc[alt + gi], f"{psck}{alt + gi}"

            def emit_tp(h, g0=g0, gn=gn, pst=pst):
                last = None
                for jj in range(gn):
                    k = g0 + jj
                    last = h.transpose(pst[:, jj * 128:(jj + 1) * 128],
                                       qbf[:, k * 128:(k + 1) * 128], ident_sb[:])
                return last
            em.op("tensor", emit_tp, reads=[("qbf", xb), "ident_sb"], writes=[psk])

            def emit_ev(h, g0=g0, gn=gn, pst=pst):
                last = None
                for jj in range(gn):
                    k = g0 + jj
                    last = h.activation(out=qTt[:, k * 128:(k + 1) * 128],
                                        in_=pst[:, jj * 128:(jj + 1) * 128], func=AF.Copy)
                return last
            em.op("scalar", emit_ev, reads=[psk], writes=[("qTt", xb, g0)])
        qt_all = [("qTt", xb, g0) for g0 in range(0, KD, 4)]
        for nt in range(NTD):
            c0 = nt * 512
            cw = min(512, D - c0)
            pst, psk = psc[alt + nt], f"{psck}{alt + nt}"

            def emit_mm(h, c0=c0, cw=cw, pst=pst):
                last = None
                for k in range(KD):
                    last = h.matmul(pst[:, 0:cw], qTt[:, k * 128:(k + 1) * 128],
                                    q_rhs[k][:, c0:c0 + cw],
                                    start=(k == 0), stop=(k == KD - 1))
                return last
            em.op("tensor", emit_mm, reads=qt_all + qrhs_all, writes=[psk])
            em.op("vector", (lambda h, c0=c0, cw=cw, pst=pst: h.scalar_tensor_tensor(
                out=f2[:, c0:c0 + cw], in0=pst[:, 0:cw],
                scalar=cols[:, 10:11], in1=rs_rep[:, c0:c0 + cw],
                op0=ALU.mult, op1=ALU.mult)),
                reads=[psk, ("cols", xb, 10)] + rsrep_all, writes=[("f2", xb, nt), ("res1", xb, nt)])
        f2k = [("f2", xb, nt) for nt in range(NTD)]
        em.op("vector", (lambda h: h.tensor_tensor(
            out=f2[:], in0=f2[:], in1=out1t[:], op=ALU.add)),
            reads=f2k + [("out1t", xb)], writes=[("f2s", xb)] + f2k)
        em.op("vector", (lambda h: h.tensor_reduce(
            out=cols[:, 0:1], in_=f2[:], axis=mybir.AxisListType.X, op=ALU.add)),
            reads=[("f2s", xb)], writes=[("cols", xb, 0)])
        em.op("scalar", (lambda h: h.activation(
            out=qv[:], in_=f2[:], func=AF.Square, accum_out=cols[:, 1:2])),
            reads=[("f2s", xb)], writes=[("qv", xb), ("cols", xb, 1)])
        em.op("vector", (lambda h: h.tensor_scalar(
            out=cols[:, 2:3], in0=cols[:, 0:1], scalar1=1.0 / D, scalar2=None, op0=ALU.mult)),
            reads=[("cols", xb, 0)], writes=[("cols", xb, 2)])
        em.op("vector", (lambda h: h.tensor_tensor(
            out=cols[:, 3:4], in0=cols[:, 2:3], in1=cols[:, 2:3], op=ALU.mult)),
            reads=[("cols", xb, 2)], writes=[("cols", xb, 3)])
        em.op("vector", (lambda h: h.tensor_scalar(
            out=cols[:, 4:5], in0=cols[:, 1:2], scalar1=1.0 / D, scalar2=1e-5,
            op0=ALU.mult, op1=ALU.add)),
            reads=[("cols", xb, 1)], writes=[("cols", xb, 4)])
        em.op("vector", (lambda h: h.tensor_tensor(
            out=cols[:, 5:6], in0=cols[:, 4:5], in1=cols[:, 3:4], op=ALU.subtract)),
            reads=[("cols", xb, 3), ("cols", xb, 4)], writes=[("cols", xb, 5)])
        em.op("scalar", (lambda h: h.activation(
            out=cols[:, 6:7], in_=cols[:, 5:6], func=AF.Ln)),
            reads=[("cols", xb, 5)], writes=[("cols", xb, 6)])
        em.op("scalar", (lambda h: h.activation(
            out=cols[:, 7:8], in_=cols[:, 6:7], func=AF.Exp, scale=-0.5)),
            reads=[("cols", xb, 6)], writes=[("cols", xb, 7)])
        em.op("vector", (lambda h: h.tensor_scalar(
            out=cols[:, 8:9], in0=cols[:, 2:3], scalar1=-1.0, scalar2=None, op0=ALU.mult)),
            reads=[("cols", xb, 2)], writes=[("cols", xb, 8)])
        em.op("vector", (lambda h, ob=ob: h.tensor_scalar(
            out=outt[ob][:], in0=f2[:], scalar1=cols[:, 8:9], scalar2=cols[:, 7:8],
            op0=ALU.add, op1=ALU.mult)),
            reads=[("f2s", xb), ("cols", xb, 7), ("cols", xb, 8)], writes=[("outt", ob)])
        em.dma("scalar", io["out"][mt * 128:(mt + 1) * 128, :], outt[ob][:],
               reads=[("outt", ob)])
    return io


def np_phase_c(y_in, xs_in, g_in, x2_in, D_in, W_out, rhs, cfg):
    bf = ml_dtypes.bfloat16

    def b16(v):
        return v.astype(bf).astype(np.float32)
    T, D, E = cfg.T, cfg.D, cfg.E
    y = y_in.astype(np.float32)
    xs = xs_in.astype(np.float32)
    g = g_in.astype(np.float32)
    t1 = xs * D_in[:, None] + y
    y2 = b16(t1 * g)
    WoT = b16(W_out.T)
    mam = (y2.T @ WoT.astype(np.float32)).astype(np.float32)
    res1 = 2.0 * x2_in + mam
    m = res1.mean(axis=1, keepdims=True)
    v = (res1 * res1).mean(axis=1, keepdims=True) - m * m
    rstd = np.exp(-0.5 * np.log(v + 1e-5)).astype(np.float32)
    out1 = ((res1 - m) * rstd).astype(np.float32)
    ls = np.maximum(np.abs(out1).max(axis=1, keepdims=True), 1e-12) / 127.0
    ql = np.round(out1 * (1.0 / ls)).astype(np.float32)
    rs = np.maximum(np.abs(rhs).max(axis=0, keepdims=True), 1e-12) / 127.0
    qr = np.round(rhs * (1.0 / rs)).astype(np.float32)
    acc = ql @ qr
    f2 = acc * ls * rs + out1
    m2 = f2.mean(axis=1, keepdims=True)
    v2 = (f2 * f2).mean(axis=1, keepdims=True) - m2 * m2
    rstd2 = np.exp(-0.5 * np.log(v2 + 1e-5)).astype(np.float32)
    return ((f2 - m2) * rstd2).astype(np.float32)


def cfg2_nt4(T):
    return [(i * 512, min(512, T - i * 512)) for i in range((T + 511) // 512)]


class CfgS:
    def __init__(self, T=2048, E=2048, R=64, NST=16, W=160, NH=4):
        self.T, self.E, self.R, self.NST, self.W = T, E, R, NST, W
        self.MC = E // 128
        self.NH = NH
        self.NG = NST // NH


def emit_scan_fused(nc, em, EN, cfg, io, bc_sb, pspool, ncore=8):
    T, E, NST, W = cfg.T, cfg.E, cfg.NST, cfg.W
    MC, NH, NG = cfg.MC, cfg.NH, cfg.NG

    ones1 = EN(nc.sbuf_tensor("ones1_s", [1, 128], F32))
    ones1b = EN(nc.sbuf_tensor("ones1b_s", [1, 128], BF16))
    bcB = EN(nc.sbuf_tensor("bcB_row", [1, max(cfg.NG * cfg.T, cfg.NST * cfg.W)], BF16))
    bcC = EN(nc.sbuf_tensor("bcC_row", [1, cfg.NG * cfg.T], BF16))
    A_sb = pspool["A_sb"]
    D_col = EN(nc.sbuf_tensor("D_col_s", [128, MC], F32))
    selv_sb = EN(nc.sbuf_tensor("selv_sb", [128, 8], F32))
    Hin = EN(nc.sbuf_tensor("Hin", [128, NST * MC], F32))
    Hall_sb = EN(nc.sbuf_tensor("Hall_sb", [128, ncore * MC * NST], F32))
    Brep = EN(nc.sbuf_tensor("Brep", [128, NG * T], BF16))
    Crep = EN(nc.sbuf_tensor("Crep", [128, NG * T], BF16))
    dcm = [EN(nc.sbuf_tensor(f"dcm{i}", [128, T], F32)) for i in range(2)]
    wcm = [EN(nc.sbuf_tensor(f"wcm{i}", [128, T], BF16)) for i in range(2)]
    xscm = [EN(nc.sbuf_tensor(f"xscm{i}", [128, T], BF16)) for i in range(2)]
    gcm = [EN(nc.sbuf_tensor(f"gcm{i}", [128, T], BF16)) for i in range(2)]
    a_big = EN(nc.sbuf_tensor("a_big", [128, 3 * T], F32))
    u_big = EN(nc.sbuf_tensor("u_big", [128, 3 * T], BF16))
    h_big = EN(nc.sbuf_tensor("h_big", [128, 3 * T], BF16))
    a_pl = [a_big[:, i * T:(i + 1) * T] for i in range(3)]
    u_pl = [u_big[:, i * T:(i + 1) * T] for i in range(3)]
    h_pl = [h_big[:, i * T:(i + 1) * T] for i in range(3)]
    z_pl = [EN(nc.sbuf_tensor(f"z_pl_s{i}", [128, T], BF16)) for i in range(2)]
    yacb = [EN(nc.sbuf_tensor(f"yac_s{i}", [128, T], BF16)) for i in range(2)]
    yh0 = [EN(nc.sbuf_tensor(f"yh0_{i}", [128, T], BF16)) for i in range(2)]
    y2m = [EN(nc.sbuf_tensor(f"y2m_{i}", [128, T], BF16)) for i in range(2)]
    t1s = EN(nc.sbuf_tensor("t1s", [128, T], F32))

    pss = pspool["ps"]
    NPS = len(pss)
    psi = [0]

    def next_ps():
        k = psi[0] % NPS
        psi[0] += 1
        return pss[k], f"ps{k}"

    em.op("vector", lambda h: h.memset(ones1[:], 1.0), writes=["ones1s_f"])
    em.op("vector", lambda h: h.memset(ones1b[:], 1.0), writes=["ones1s"])
    em.op("gpsimd", (lambda h: h.collective_compute(
        "AllGather", ALU.bypass, replica_groups=[list(range(ncore))],
        ins=[io["hout_st"].rearrange("p m n -> (p m n)")],
        outs=[io["hall_st"].rearrange("s p m n -> (s p m n)")])),
        reads=["hout_st"], writes=["hall_st"])
    em.dma("sync", D_col[:], io["D_in"].rearrange("(m p) -> p m", p=128),
           writes=["Dcol"], allow_slow_non_contiguous=True)
    em.dma("sync", selv_sb[:], io["selv"], writes=["selv"])

    def load_rows(buf, bufkey, n0, nplanes, col0, width):
        em.dma("sync", buf[0:1, 0:nplanes * width],
               io["bc_hbm"][n0:n0 + nplanes, col0:col0 + width],
               reads=["bc_hbm"], writes=[bufkey], allow_slow_non_contiguous=True)

    def bcast_build(dst, rows_key, srcbuf, srckey, nplanes, width, dst_w):
        for j in range(nplanes):
            for cc in range(0, width, 512):
                cw = min(512, width - cc)
                pst, psk = next_ps()
                em.op("tensor", (lambda h, j=j, cc=cc, cw=cw, pst=pst, srcbuf=srcbuf, width=width:
                                 h.matmul(pst[:, 0:cw], ones1b[0:1, :],
                                          srcbuf[0:1, j * width + cc:j * width + cc + cw],
                                          start=True, stop=True)),
                      reads=[srckey, "ones1s"], writes=[psk])
                em.op("scalar", (lambda h, j=j, cc=cc, cw=cw, pst=pst:
                                 h.activation(out=dst[:, j * dst_w + cc:j * dst_w + cc + cw],
                                              in_=pst[:, 0:cw], func=AF.Copy)),
                      reads=[psk], writes=[(rows_key, j)])

    S1 = ["xn_dead", "apre", "apre0", "upre", "hpre"]
    for half in range(NH):
        n0 = half * NG
        load_rows(bcB, "bcB", n0, NG, 0, T)
        load_rows(bcC, "bcC", NST + n0, NG, 0, T)
        bcast_build(Brep, "BrhJ", bcB, "bcB", NG, T, T)
        bcast_build(Crep, "CrhJ", bcC, "bcC", NG, T, T)
        if half == 0:
            em.dma("sync", Hall_sb[:].rearrange("p (s m n) -> p s m n", s=ncore, n=NST),
                   io["hall_st"].rearrange("s p m n -> p s m n"),
                   reads=["hall_st"], writes=["Hall_sb"], allow_slow_non_contiguous=True)
            for si in range(ncore):
                if si == 0:
                    em.op("vector", (lambda h, si=si: h.tensor_scalar(
                        out=Hin[:], in0=Hall_sb[:, si * MC * NST:(si + 1) * MC * NST],
                        scalar1=selv_sb[:, si:si + 1], scalar2=None, op0=ALU.mult)),
                        reads=["Hall_sb", "selv"], writes=["Hin"])
                else:
                    em.op("vector", (lambda h, si=si: h.scalar_tensor_tensor(
                        out=Hin[:], in0=Hall_sb[:, si * MC * NST:(si + 1) * MC * NST],
                        scalar=selv_sb[:, si:si + 1], in1=Hin[:],
                        op0=ALU.mult, op1=ALU.add)),
                        reads=["Hall_sb", "selv", "Hin"], writes=["Hin"])

        for m in range(MC):
            mb = m % 2
            yac = yacb[mb]
            yk = ("yacs", mb)
            em.dma("sync", dcm[mb][:], io["delta_st"][m * 128:(m + 1) * 128, :],
                   reads=[("dhbm", m)], writes=[("dcm", mb)])
            em.dma("sync", wcm[mb][:], io["w_st"][m * 128:(m + 1) * 128, :],
                   reads=[("whbm", m)], writes=[("wcm", mb)])
            if half == NH - 1:
                em.dma("sync", xscm[mb][:], io["xs_st"][m * 128:(m + 1) * 128, :],
                       reads=[("xshbm", m)], writes=[("xscm", mb)])
                em.dma("sync", gcm[mb][:], io["g_st"][m * 128:(m + 1) * 128, :],
                       reads=[("ghbm", m, it) for it in range(len(cfg2_nt4(T)))],
                       writes=[("gcm", mb)])
            for j in range(NG):
                n = n0 + j
                ai = n % 3
                em.op("scalar", (lambda h, m=m, mb=mb, n=n, ai=ai: h.activation(
                    out=a_pl[ai][:], in_=dcm[mb][:], func=AF.Exp,
                    scale=A_sb[:, m * NST + n:m * NST + n + 1])),
                    reads=[("dcm", mb), "Asb"], writes=[("a", ai)])
                em.op("vector", (lambda h, mb=mb, j=j, ai=ai: h.tensor_tensor(
                    out=u_pl[ai][:], in0=Brep[:, j * T:(j + 1) * T], in1=wcm[mb][:],
                    op=ALU.mult)),
                    reads=[("BrhJ", j), ("wcm", mb)],
                    writes=[("u", ai)])
                em.op("vector", (lambda h, m=m, n=n, ai=ai: h.tensor_tensor_scan(
                    out=h_pl[ai][:], data0=a_pl[ai][:], data1=u_pl[ai][:],
                    initial=Hin[:, m * NST + n:m * NST + n + 1],
                    op0=ALU.mult, op1=ALU.add)),
                    reads=[("a", ai), ("u", ai), "Hin"], writes=[("h", ai)])
                zi = n % 2
                if j == 0:
                    em.op("vector", (lambda h, j=j, ai=ai, yac=yac: h.tensor_tensor(
                        out=yac[:], in0=h_pl[ai][:], in1=Crep[:, j * T:(j + 1) * T],
                        op=ALU.mult)),
                        reads=[("h", ai), ("CrhJ", j)], writes=[yk])
                else:
                    em.op("vector", (lambda h, j=j, ai=ai, zi=zi: h.tensor_tensor(
                        out=z_pl[zi][:], in0=h_pl[ai][:], in1=Crep[:, j * T:(j + 1) * T],
                        op=ALU.mult)),
                        reads=[("h", ai), ("CrhJ", j)], writes=[("z", zi)])
                    em.dma("gpsimd", yac[:], z_pl[zi][:], accum_op=ALU.add,
                           reads=[("z", zi)] + S1, writes=[yk])
            if half == 0:
                em.dma("scalar", io["yh_st"][m * 128:(m + 1) * 128, :], yac[:],
                       reads=[yk] + S1, writes=[("yhst", m)])
            elif half < NH - 1:
                em.dma("gpsimd", io["yh_st"][m * 128:(m + 1) * 128, :], yac[:],
                       accum_op=ALU.add,
                       reads=[yk, ("yhst", m)] + S1, writes=[("yhst", m)])
            else:
                em.dma("gpsimd", yac[:], io["yh_st"][m * 128:(m + 1) * 128, :],
                       accum_op=ALU.add,
                       reads=[("yhst", m)] + S1, writes=[yk])
                em.op("vector", (lambda h, m=m, mb=mb, yac=yac: h.scalar_tensor_tensor(
                    out=t1s[:], in0=xscm[mb][:], scalar=D_col[:, m:m + 1], in1=yac[:],
                    op0=ALU.mult, op1=ALU.add)),
                    reads=[("xscm", mb), yk, "Dcol"], writes=["t1s"])
                y2b = m % 2
                em.op("vector", (lambda h, mb=mb, y2b=y2b: h.tensor_tensor(
                    out=y2m[y2b][:], in0=t1s[:], in1=gcm[mb][:], op=ALU.mult)),
                    reads=["t1s", ("gcm", mb)], writes=[("y2m", y2b)])
                em.dma("scalar", io["y2_st"][m * 128:(m + 1) * 128, :], y2m[y2b][:],
                       reads=[("y2m", y2b)] + S1, writes=[("y2st", m)])
    return io


def build_fused(T=2048, D=1024, E=2048, R=64, NST=16, W=160, ncore=8):
    cfa = CfgA(T=T, D=D, E=E, R=R, NST=NST)
    cfs = CfgS(T=T, E=E, R=R, NST=NST, W=W)
    cfc = CfgC(T=T, D=D, E=E)
    nc = bass.Bass("TRN2", target_bir_lowering=False)
    MC = E // 128

    io = {}
    io["x_in"] = nc.dram_tensor("x_in", [T + 128, D], F32, kind="ExternalInput").ap()
    io["W_in"] = nc.dram_tensor("W_in", [2 * E, D], F32, kind="ExternalInput").ap()
    io["W_xproj"] = nc.dram_tensor("W_xproj", [96, E], F32, kind="ExternalInput").ap()
    io["W_dt"] = nc.dram_tensor("W_dt", [E, R], F32, kind="ExternalInput").ap()
    io["b_dt"] = nc.dram_tensor("b_dt", [E], F32, kind="ExternalInput").ap()
    io["conv_w"] = nc.dram_tensor("conv_w", [E, 4], F32, kind="ExternalInput").ap()
    io["conv_b"] = nc.dram_tensor("conv_b", [E], F32, kind="ExternalInput").ap()
    io["norm_w"] = nc.dram_tensor("norm_w", [D], F32, kind="ExternalInput").ap()
    io["A_log_in"] = nc.dram_tensor("A_log_in", [E, NST], F32, kind="ExternalInput").ap()
    io["D_in"] = nc.dram_tensor("D_in", [E], F32, kind="ExternalInput").ap()
    io["W_out"] = nc.dram_tensor("W_out", [D, E], F32, kind="ExternalInput").ap()
    io["rhs"] = nc.dram_tensor("rhs", [D, D], F32, kind="ExternalInput").ap()
    io["selv"] = nc.dram_tensor("selv", [128, 8], F32, kind="ExternalInput").ap()
    io["out"] = nc.dram_tensor("out", [T, D], F32, kind="ExternalOutput").ap()
    io["delta_out"] = nc.dram_tensor("delta_st", [E, T], F32, kind="Internal").ap()
    io["w_out"] = nc.dram_tensor("w_st", [E, T], BF16, kind="Internal").ap()
    io["xs_out"] = nc.dram_tensor("xs_st", [E, T], BF16, kind="Internal").ap()
    io["g_out"] = nc.dram_tensor("g_st", [E, T], BF16, kind="Internal").ap()
    io["delta_st"] = io["delta_out"]
    io["w_st"] = io["w_out"]
    io["xs_st"] = io["xs_out"]
    io["g_st"] = io["g_out"]
    io["y2_st"] = nc.dram_tensor("y2_st", [E, T], BF16, kind="Internal").ap()
    io["yh_st"] = nc.dram_tensor("yh_st", [E, T], BF16, kind="Internal").ap()
    io["bc_hbm"] = nc.dram_tensor("bc_hbm", [2 * NST, T], BF16, kind="Internal").ap()
    io["hout_st"] = nc.dram_tensor("hout_st", [128, MC, NST], F32, kind="Internal").ap()
    io["hall_st"] = nc.dram_tensor("hall_st", [ncore, 128, MC, NST], F32, kind="Internal").ap()

    with contextlib.ExitStack() as octx:
        em = Emitter(nc, octx)
        EN0 = octx.enter_context
        pspool = {
            "ps": [EN0(nc.psum_tensor(f"ps{i}", [128, 512], F32)) for i in range(4)],
            "ps_big": EN0(nc.psum_tensor("ps_big", [128, 4 * 512], F32)),
            "A_sb": EN0(nc.sbuf_tensor("A_sb", [128, NST * MC], F32)),
            "Hout": EN0(nc.sbuf_tensor("Hout", [128, MC * NST], F32)),
        }
        pspool["W"] = W
        pspool["ncore"] = ncore
        bc_sb = None
        with contextlib.ExitStack() as c1:
            emit_phase_a(nc, em, c1.enter_context, cfa, io=io, fused=True,
                         bc_sb=bc_sb, pspool=pspool)
            em.barrier()
        with contextlib.ExitStack() as c2:
            emit_scan_fused(nc, em, c2.enter_context, cfs, io, bc_sb, pspool, ncore=ncore)
            em.barrier()
        with contextlib.ExitStack() as c3:
            io["x2_in"] = io["x_in"][128:, :]
            io["y_in"] = io["y2_st"]; io["xs_in"] = io["xs_st"]; io["g_in"] = io["g_st"]
            emit_phase_c(nc, em, c3.enter_context, cfc, io=io, fused=True, pspool=pspool)
        for e in ("sync", "tensor", "vector", "scalar", "gpsimd"):
            em.wait_all(e)
        block = octx.enter_context(nc.Block())
        em.build(block)
    return nc


def make_in_maps(ins0, T, ncore=8):
    x = np.ascontiguousarray(np.asarray(ins0["L0_attn_x"])[0], dtype=np.float32)
    L, D = x.shape
    xpad = np.concatenate([np.zeros((128, D), np.float32), x], axis=0)
    common = dict(W_in=np.asarray(ins0["L0_attn_W_in"]), W_xproj=np.asarray(ins0["L0_attn_W_xproj"]),
                  W_dt=np.asarray(ins0["L0_attn_W_dt"]), b_dt=np.asarray(ins0["L0_attn_b_dt"]),
                  conv_w=np.asarray(ins0["L0_attn_conv_w"]), conv_b=np.asarray(ins0["L0_attn_conv_b"]),
                  norm_w=np.asarray(ins0["L0_attn_norm_w"]),
                  A_log_in=np.asarray(ins0["L0_attn_A_log"]), D_in=np.asarray(ins0["L0_attn_D"]),
                  W_out=np.asarray(ins0["L0_attn_W_out"]), rhs=np.asarray(ins0["L0_ffn_rhs"]))
    maps = []
    for i in range(ncore):
        sel = np.zeros((128, 8), np.float32)
        if i > 0:
            sel[:, i - 1] = 1.0
        maps.append(dict(
            x_in=np.ascontiguousarray(xpad[i * T:(i + 1) * T + 128]),
            selv=sel,
            **common))
    return maps


def kernel(L0_attn_x, L0_attn_W_in, L0_attn_conv_w, L0_attn_conv_b, L0_attn_W_xproj,
           L0_attn_W_dt, L0_attn_b_dt, L0_attn_A_log, L0_attn_D, L0_attn_W_out,
           L0_attn_norm_w, L0_ffn_rhs):
    from concourse.bass_utils import run_bass_kernel_spmd
    NCORE = 8
    x = np.asarray(L0_attn_x)
    L = x.shape[1]
    T = L // NCORE
    ins0 = dict(L0_attn_x=x, L0_attn_W_in=L0_attn_W_in, L0_attn_conv_w=L0_attn_conv_w,
                L0_attn_conv_b=L0_attn_conv_b, L0_attn_W_xproj=L0_attn_W_xproj,
                L0_attn_W_dt=L0_attn_W_dt, L0_attn_b_dt=L0_attn_b_dt,
                L0_attn_A_log=L0_attn_A_log, L0_attn_D=L0_attn_D,
                L0_attn_W_out=L0_attn_W_out, L0_attn_norm_w=L0_attn_norm_w,
                L0_ffn_rhs=L0_ffn_rhs)
    nc = build_fused(T=T, ncore=NCORE)
    maps = make_in_maps(ins0, T, NCORE)
    res = run_bass_kernel_spmd(nc, maps, core_ids=list(range(NCORE)))
    out = np.concatenate([res.results[i]["out"] for i in range(NCORE)], axis=0)
    return out[None]



# version 10 (final)
# speedup vs baseline: 1.1318x; 1.0220x over previous
import contextlib
import numpy as np
import concourse.bass as bass
import concourse.mybir as mybir

F32 = mybir.dt.float32
BF16 = mybir.dt.bfloat16
AF = mybir.ActivationFunctionType
ALU = mybir.AluOpType




DMA_POOL = 10


class _Eng:
    def __init__(self, name, sem, dsems):
        self.name = name
        self.sem = sem
        self.dsems = dsems
        self.n_ops = 0
        self.n_dma = 0
        self.marks = {}


class Emitter:
    def __init__(self, nc, ctx, engines=("sync", "tensor", "vector", "scalar", "gpsimd")):
        self.nc = nc
        self.engs = {}
        for name in engines:
            sem = ctx.enter_context(nc.semaphore(f"op_{name}"))
            dsems = ([ctx.enter_context(nc.semaphore(f"dma_{name}{i}")) for i in range(DMA_POOL)]
                     if name in ("sync", "scalar", "gpsimd") else [])
            self.engs[name] = _Eng(name, sem, dsems)
        self.last_w = {}
        self.readers = {}
        self._bodies = {n: [] for n in engines}
        self._final = {n: [] for n in engines}

    def op(self, eng, emit, reads=(), writes=(), dma=False):
        e = self.engs[eng]
        deps = []
        for k in reads:
            lw = self.last_w.get(k)
            if lw is not None:
                deps.append(lw)
        for k in writes:
            lw = self.last_w.get(k)
            if lw is not None:
                deps.append(lw)
            deps.extend(self.readers.get(k, ()))
        if dma:
            slot = e.n_dma % DMA_POOL
            rank = e.n_dma // DMA_POOL
            sem = e.dsems[slot]
            rec = (eng, sem, 16 * (rank + 1))
            throttle = (sem, 16 * rank) if rank >= 1 else None
            e.n_dma += 1
        else:
            sem = e.sem
            rec = (eng, sem, e.n_ops + 1)
            throttle = None
            e.n_ops += 1
        for k in reads:
            self.readers.setdefault(k, []).append(rec)
        for k in writes:
            self.last_w[k] = rec
            self.readers[k] = []
        self._bodies[eng].append((emit, deps, dma, sem, throttle))
        return rec

    def dma(self, eng, out, in_, reads=(), writes=(), **kw):
        def emit(h):
            return h.dma_start(out=out, in_=in_, **kw)
        return self.op(eng, emit, reads=reads, writes=writes, dma=True)

    def build(self, block):
        for name in self.engs:
            body = self._bodies[name]
            fins = self._final[name]
            if not body and not fins:
                continue

            def make(body=body, fins=fins, name=name):
                e = self.engs[name]

                def run(h):
                    for emit, deps, dma, sem, throttle in body:
                        for (peng, psem, pval) in deps:
                            mk = e.marks.get(id(psem), -1)
                            if pval > mk:
                                h.wait_ge(psem, pval)
                                e.marks[id(psem)] = pval
                        if throttle is not None:
                            tsem, tval = throttle
                            mk = e.marks.get(id(tsem), -1)
                            if tval > mk:
                                h.wait_ge(tsem, tval)
                                e.marks[id(tsem)] = tval
                        inst = emit(h)
                        inst.then_inc(sem, 16 if dma else 1)
                    for fin in fins:
                        fin(h)
                return run
            getattr(block, name)(make())

    def final(self, eng, fn):
        self._final[eng].append(fn)

    def wait_all(self, eng):
        targets = []
        for name, e in self.engs.items():
            if e.n_ops:
                targets.append((e.sem, e.n_ops))
            for i, ds in enumerate(e.dsems):
                cnt = (e.n_dma - i + DMA_POOL - 1) // DMA_POOL
                if cnt > 0:
                    targets.append((ds, 16 * cnt))

        def fin(h):
            for sem, val in targets:
                h.wait_ge(sem, val)
        self._final[eng].append(fin)

    def barrier(self):
        targets = []
        for name, e in self.engs.items():
            if e.n_ops:
                targets.append((e.sem, e.n_ops))
            for i, ds in enumerate(e.dsems):
                cnt = (e.n_dma - i + DMA_POOL - 1) // DMA_POOL
                if cnt > 0:
                    targets.append((ds, 16 * cnt))
        for name in self.engs:
            def emit(h, targets=targets):
                for sem, val in targets:
                    h.wait_ge(sem, val)
                return h.nop()
            self.op(name, emit)


class CfgA:
    def __init__(self, T=2048, D=1024, E=2048, R=64, NST=16):
        self.T, self.D, self.E, self.R, self.NST = T, D, E, R, NST
        self.TP = T + 128
        self.NT = self.TP // 128
        self.KD = D // 128
        self.ME = 2 * E // 128
        self.MC = E // 128
        self.NT5 = [(i * 512, min(512, self.TP - i * 512)) for i in range((self.TP + 511) // 512)]
        self.NT4 = [(i * 512, min(512, T - i * 512)) for i in range((T + 511) // 512)]


def emit_phase_a(nc, em, EN, cfg, io=None, fused=False, bc_sb=None, pspool=None):
    T, D, E, R, NST = cfg.T, cfg.D, cfg.E, cfg.R, cfg.NST
    TP, NT, KD, ME, MC = cfg.TP, cfg.NT, cfg.KD, cfg.ME, cfg.MC

    if io is None:
        io = {}
        io["x_in"] = nc.dram_tensor("x_in", [TP, D], F32, kind="ExternalInput").ap()
        io["W_in"] = nc.dram_tensor("W_in", [2 * E, D], F32, kind="ExternalInput").ap()
        io["W_xproj"] = nc.dram_tensor("W_xproj", [96, E], F32, kind="ExternalInput").ap()
        io["W_dt"] = nc.dram_tensor("W_dt", [E, R], F32, kind="ExternalInput").ap()
        io["b_dt"] = nc.dram_tensor("b_dt", [E], F32, kind="ExternalInput").ap()
        io["conv_w"] = nc.dram_tensor("conv_w", [E, 4], F32, kind="ExternalInput").ap()
        io["conv_b"] = nc.dram_tensor("conv_b", [E], F32, kind="ExternalInput").ap()
        io["norm_w"] = nc.dram_tensor("norm_w", [D], F32, kind="ExternalInput").ap()
        io["delta_out"] = nc.dram_tensor("delta_out", [E, T], F32, kind="ExternalOutput").ap()
        io["w_out"] = nc.dram_tensor("w_out", [E, T], BF16, kind="ExternalOutput").ap()
        io["xs_out"] = nc.dram_tensor("xs_out", [E, T], BF16, kind="ExternalOutput").ap()
        io["g_out"] = nc.dram_tensor("g_out", [E, T], BF16, kind="ExternalOutput").ap()
        io["B_out"] = nc.dram_tensor("B_out", [NST, T], F32, kind="ExternalOutput").ap()
        io["C_out"] = nc.dram_tensor("C_out", [NST, T], F32, kind="ExternalOutput").ap()
        io["Brep_out"] = nc.dram_tensor("Brep_out", [128, NST, T], BF16, kind="ExternalOutput").ap()
        io["Crep_out"] = nc.dram_tensor("Crep_out", [128, NST, T], BF16, kind="ExternalOutput").ap()
    ident = nc.inline_tensor(np.eye(128, dtype=np.float32), "ident128").ap()

    ident_sb = EN(nc.sbuf_tensor("ident_sb", [128, 128], F32))
    ones_sb = EN(nc.sbuf_tensor("ones_sb", [1, 128], F32))
    Wnat = [EN(nc.sbuf_tensor(f"Wnat{i}", [128, D], F32)) for i in range(2)]
    WinT = [EN(nc.sbuf_tensor(f"WinT{i}", [128, D], BF16)) for i in range(2)]
    WxN = EN(nc.sbuf_tensor("WxN", [96, E], F32))
    WxT = EN(nc.sbuf_tensor("WxT", [128, 96 * MC], BF16))
    WdtN = EN(nc.sbuf_tensor("WdtN", [128, R * MC], F32))
    WdtT = EN(nc.sbuf_tensor("WdtT", [R, E], BF16))
    nw_col = EN(nc.sbuf_tensor("nw_col", [128, KD], F32))
    bdt_col = EN(nc.sbuf_tensor("bdt_col", [128, MC], F32))
    cw_col = EN(nc.sbuf_tensor("cw_col", [128, 4 * MC], F32))
    cb_col = EN(nc.sbuf_tensor("cb_col", [128, MC], F32))

    x_tm = [EN(nc.sbuf_tensor(f"x_tm{i}", [128, D], F32)) for i in range(2)]
    ssum = EN(nc.sbuf_tensor("ssum", [128, NT], F32))
    rstd_cols = EN(nc.sbuf_tensor("rstd_cols", [128, NT], F32))
    rstd_T = EN(nc.sbuf_tensor("rstd_T", [NT, 128], F32))
    rstd_row = EN(nc.sbuf_tensor("rstd_row", [1, TP], F32))
    rstd_rep = EN(nc.sbuf_tensor("rstd_rep", [128, TP], F32))
    xn_big = EN(nc.sbuf_tensor("xn_big", [128, KD * TP], BF16))
    xn_bf = [xn_big[:, k * TP:(k + 1) * TP] for k in range(KD)]

    xc_strip = EN(nc.sbuf_tensor("xc_strip", [128, TP], F32))
    acc_str = EN(nc.sbuf_tensor("acc_str", [128, T], F32))
    xs_m = [EN(nc.sbuf_tensor(f"xs_m{i}", [128, T], BF16)) for i in range(2)]
    xs_rl = [EN(nc.sbuf_tensor(f"xs_rl{i}", [128, T], BF16)) for i in range(2)]
    g_stage = [EN(nc.sbuf_tensor(f"g_stage{i}", [128, 512], BF16)) for i in range(2)]

    dbc_sb = EN(nc.sbuf_tensor("dbc_sb", [96, T], F32))
    dt_bf = EN(nc.sbuf_tensor("dt_bf", [R, T], BF16))
    bcr = EN(nc.sbuf_tensor("bcr", [1, NST * 512], F32))
    brep_stage = [EN(nc.sbuf_tensor(f"brep_stage{i}", [128, 512], BF16)) for i in range(4)]
    e1_sb = acc_str
    delta_sb = [EN(nc.sbuf_tensor(f"delta_sb{i}", [128, T], F32)) for i in range(2)]
    wm_sb = [EN(nc.sbuf_tensor(f"wm_sb{i}", [128, T], BF16)) for i in range(2)]

    if pspool is None:
        ps = [EN(nc.psum_tensor(f"ps{i}", [128, 512], F32)) for i in range(4)]
        ps_big = EN(nc.psum_tensor("ps_big", [128, 4 * 512], F32))
    else:
        ps = pspool["ps"]
        ps_big = pspool["ps_big"]
    NPS = len(ps)
    psi = [0]

    def next_ps():
        k = psi[0] % NPS
        psi[0] += 1
        return ps[k], f"ps{k}"

    em.dma("sync", ident_sb[:], ident, writes=["ident_sb"])
    em.dma("sync", nw_col[:], io["norm_w"].rearrange("(k p) -> p k", p=128), writes=["nw_col"], allow_slow_non_contiguous=True)
    em.dma("sync", bdt_col[:], io["b_dt"].rearrange("(m p) -> p m", p=128), writes=["bdt_col"], allow_slow_non_contiguous=True)
    em.dma("sync", cb_col[:], io["conv_b"].rearrange("(m p) -> p m", p=128), writes=["cb_col"], allow_slow_non_contiguous=True)
    em.dma("sync", cw_col[:].rearrange("p (m k) -> p m k", k=4),
           io["conv_w"].rearrange("(m p) k -> p m k", p=128), writes=["cw_col"], allow_slow_non_contiguous=True)
    em.dma("sync", WxN[:], io["W_xproj"], writes=["WxN"])
    em.dma("sync", WdtN[:].rearrange("p (m r) -> p m r", r=R),
           io["W_dt"].rearrange("(m p) r -> p m r", p=128), writes=["WdtN"], allow_slow_non_contiguous=True)
    em.op("vector", lambda h: h.memset(ones_sb[:], 1.0), writes=["ones_sb"])
    if fused:
        A_sb = pspool["A_sb"]
        em.dma("sync", A_sb[:].rearrange("p (m n) -> p m n", n=NST),
               io["A_log_in"].rearrange("(m p) n -> p m n", p=128),
               writes=["Asb"], allow_slow_non_contiguous=True)
        em.op("scalar", lambda h: h.activation(out=A_sb[:], in_=A_sb[:], func=AF.Exp),
              reads=["Asb"], writes=["Asb"])
        em.op("vector", lambda h: h.tensor_scalar_mul(A_sb[:], A_sb[:], -1.0),
              reads=["Asb"], writes=["Asb"])

    for g0 in range(0, MC, 4):
        gn = min(4, MC - g0)
        pst, psk = next_ps()

        def emit_tp(h, g0=g0, gn=gn, pst=pst):
            last = None
            for j in range(gn):
                k = g0 + j
                last = h.transpose(pst[:, j * 128:j * 128 + 96],
                                   WxN[0:96, k * 128:(k + 1) * 128], ident_sb[0:96, 0:96])
            return last
        em.op("tensor", emit_tp, reads=["WxN", "ident_sb"], writes=[psk])

        def emit_ev(h, g0=g0, gn=gn, pst=pst):
            last = None
            for j in range(gn):
                k = g0 + j
                last = h.tensor_copy(out=WxT[:, k * 96:(k + 1) * 96],
                                     in_=pst[:, j * 128:j * 128 + 96])
            return last
        em.op("vector", emit_ev, reads=[psk], writes=[("WxT", g0)])
    wxk = [("WxT", g0) for g0 in range(0, MC, 4)]
    for g0 in range(0, MC, 4):
        gn = min(4, MC - g0)
        pst, psk = next_ps()

        def emit_tp(h, g0=g0, gn=gn, pst=pst):
            last = None
            for j in range(gn):
                mm_ = g0 + j
                last = h.transpose(pst[0:R, j * 128:(j + 1) * 128],
                                   WdtN[:, mm_ * R:(mm_ + 1) * R], ident_sb[:])
            return last
        em.op("tensor", emit_tp, reads=["WdtN", "ident_sb"], writes=[psk])

        def emit_ev(h, g0=g0, gn=gn, pst=pst):
            last = None
            for j in range(gn):
                mm_ = g0 + j
                last = h.tensor_copy(out=WdtT[0:R, mm_ * 128:(mm_ + 1) * 128],
                                     in_=pst[0:R, j * 128:(j + 1) * 128])
            return last
        em.op("vector", emit_ev, reads=[psk], writes=["WdtT"])

    for j in range(NT):
        xb = j % 2
        em.dma("sync", x_tm[xb][:], io["x_in"][j * 128:(j + 1) * 128, :], writes=[("x_tm", xb)])
        em.op("scalar", (lambda h, xb=xb, j=j: h.activation(
            out=x_tm[xb][:], in_=x_tm[xb][:], func=AF.Square, accum_out=ssum[:, j:j + 1])),
            reads=[("x_tm", xb)], writes=[("x_tm", xb), ("ssum", j)])
    em.op("vector", lambda h: h.tensor_scalar(
        out=rstd_cols[:, 0:NT], in0=ssum[:, 0:NT], scalar1=1.0 / D, scalar2=1e-5,
        op0=ALU.mult, op1=ALU.add),
        reads=[("ssum", j) for j in range(NT)], writes=["rstd_cols"])
    em.op("scalar", lambda h: h.activation(out=rstd_cols[:, 0:NT], in_=rstd_cols[:, 0:NT],
                                           func=AF.Ln), reads=["rstd_cols"], writes=["rstd_cols"])
    em.op("scalar", lambda h: h.activation(out=rstd_cols[:, 0:NT], in_=rstd_cols[:, 0:NT],
                                           func=AF.Exp, scale=-0.5),
          reads=["rstd_cols"], writes=["rstd_cols"])
    pst, psk = next_ps()
    em.op("tensor", (lambda h, pst=pst: h.transpose(pst[0:NT, 0:128], rstd_cols[:, 0:NT], ident_sb[:])),
          reads=["rstd_cols", "ident_sb"], writes=[psk])
    em.op("vector", (lambda h, pst=pst: h.tensor_copy(out=rstd_T[0:NT, :], in_=pst[0:NT, 0:128])),
          reads=[psk], writes=["rstd_T"])
    em.dma("sync", rstd_row[0:1, :], rstd_T[0:NT, :],
           reads=["rstd_T"], writes=["rstd_row"], allow_slow_non_contiguous=True)
    for i, (c0, cw) in enumerate(cfg.NT5):
        pst, psk = next_ps()
        em.op("tensor", (lambda h, c0=c0, cw=cw, pst=pst: h.matmul(
            pst[:, 0:cw], ones_sb[0:1, :], rstd_row[0:1, c0:c0 + cw], start=True, stop=True)),
            reads=["rstd_row", "ones_sb"], writes=[psk])
        em.op("vector", (lambda h, c0=c0, cw=cw, pst=pst: h.tensor_copy(
            out=rstd_rep[:, c0:c0 + cw], in_=pst[:, 0:cw])),
            reads=[psk], writes=[("rstd_rep", i)])
    rstd_all = [("rstd_rep", i) for i in range(len(cfg.NT5))]

    for j in range(NT):
        xb = j % 2
        em.dma("sync", x_tm[xb][:], io["x_in"][j * 128:(j + 1) * 128, :], writes=[("x_tm", xb)])
        for g0 in range(0, KD, 4):
            gn = min(4, KD - g0)
            pst, psk = next_ps()

            def emit_tp(h, xb=xb, g0=g0, gn=gn, pst=pst):
                last = None
                for jj in range(gn):
                    k = g0 + jj
                    last = h.transpose(pst[:, jj * 128:(jj + 1) * 128],
                                       x_tm[xb][:, k * 128:(k + 1) * 128], ident_sb[:])
                return last
            em.op("tensor", emit_tp, reads=[("x_tm", xb), "ident_sb"], writes=[psk])

            def emit_ev(h, j=j, g0=g0, gn=gn, pst=pst):
                last = None
                for jj in range(gn):
                    k = g0 + jj
                    last = h.tensor_tensor(out=xn_bf[k][:, j * 128:(j + 1) * 128],
                                           in0=pst[:, jj * 128:(jj + 1) * 128],
                                           in1=rstd_rep[:, j * 128:(j + 1) * 128],
                                           op=ALU.mult)
                return last
            em.op("vector", emit_ev, reads=[psk] + rstd_all, writes=[("xn_bf", j, g0)])
    xn_all = [("xn_bf", j, g0) for j in range(NT) for g0 in range(0, KD, 4)]

    dbc_pending = {}
    for m in range(ME):
        wb = m % 2
        isconv = m < MC
        if m - 2 in dbc_pending:
            dbc_pending.pop(m - 2)()
        em.dma("sync", Wnat[wb][:], io["W_in"][m * 128:(m + 1) * 128, :], writes=[("Wnat", wb)])
        for g0 in range(0, KD, 4):
            gn = min(4, KD - g0)
            pst, psk = next_ps()

            def emit_tp(h, wb=wb, g0=g0, gn=gn, pst=pst):
                last = None
                for jj in range(gn):
                    k = g0 + jj
                    last = h.transpose(pst[:, jj * 128:(jj + 1) * 128],
                                       Wnat[wb][:, k * 128:(k + 1) * 128], ident_sb[:])
                return last
            em.op("tensor", emit_tp, reads=[("Wnat", wb), "ident_sb"], writes=[psk])

            def emit_ev(h, wb=wb, g0=g0, gn=gn, pst=pst):
                last = None
                for jj in range(gn):
                    k = g0 + jj
                    last = h.tensor_scalar(out=WinT[wb][:, k * 128:(k + 1) * 128],
                                           in0=pst[:, jj * 128:(jj + 1) * 128],
                                           scalar1=nw_col[:, k:k + 1], scalar2=None,
                                           op0=ALU.mult)
                return last
            em.op("vector", emit_ev, reads=[psk, "nw_col"], writes=[("WinT", wb, g0)])
        wkeys = [("WinT", wb, g0) for g0 in range(0, KD, 4)]

        for it, (c0, cw) in enumerate(cfg.NT5):
            pst, psk = next_ps()

            def emit_mm(h, wb=wb, c0=c0, cw=cw, pst=pst):
                last = None
                for k in range(KD):
                    last = h.matmul(pst[:, 0:cw], WinT[wb][:, k * 128:(k + 1) * 128],
                                    xn_bf[k][:, c0:c0 + cw],
                                    start=(k == 0), stop=(k == KD - 1))
                return last
            em.op("tensor", emit_mm, reads=wkeys + xn_all, writes=[psk])
            if isconv:
                em.op("vector", (lambda h, c0=c0, cw=cw, pst=pst: h.tensor_copy(
                    out=xc_strip[:, c0:c0 + cw], in_=pst[:, 0:cw])),
                    reads=[psk], writes=[("xc", it)])
            else:
                slot = it % 2
                o0 = max(0, c0 - 128)
                s0 = max(0, 128 - c0)
                em.op("scalar", (lambda h, cw=cw, pst=pst, slot=slot: h.activation(
                    out=g_stage[slot][:, 0:cw], in_=pst[:, 0:cw], func=AF.Silu)),
                    reads=[psk], writes=[("g_stage", slot)])
                if cw - s0 > 0:
                    em.dma("scalar",
                           io["g_out"][(m - MC) * 128:(m - MC + 1) * 128, o0:o0 + cw - s0],
                           g_stage[slot][:, s0:cw],
                           reads=[("g_stage", slot)], writes=[("ghbm", m - MC, it)])
        if isconv:
            xck = [("xc", it) for it in range(len(cfg.NT5))]
            em.op("vector", (lambda h, m=m: h.tensor_scalar(
                out=acc_str[:], in0=xc_strip[:, 125:125 + T], scalar1=cw_col[:, 4 * m:4 * m + 1],
                scalar2=None, op0=ALU.mult)),
                reads=xck + ["cw_col"], writes=["acc"])
            for kk in range(1, 4):
                em.op("vector", (lambda h, m=m, kk=kk: h.scalar_tensor_tensor(
                    out=acc_str[:], in0=xc_strip[:, 125 + kk:125 + kk + T],
                    scalar=cw_col[:, 4 * m + kk:4 * m + kk + 1], in1=acc_str[:],
                    op0=ALU.mult, op1=ALU.add)),
                    reads=xck + ["cw_col", "acc"], writes=["acc"])
            xb2 = m % 2
            em.op("scalar", (lambda h, m=m, xb2=xb2: h.activation(
                out=xs_m[xb2][:], in_=acc_str[:], func=AF.Silu, bias=cb_col[:, m:m + 1])),
                reads=["acc", "cb_col"], writes=[("xs_m", xb2)])
            em.dma("scalar", io["xs_out"][m * 128:(m + 1) * 128, :], xs_m[xb2][:],
                   reads=[("xs_m", xb2)], writes=[("xshbm", m)])
            def emit_dbc(h, m=m, xb2=xb2):
                last = None
                for it, (c0, cw) in enumerate(cfg.NT4):
                    last = h.matmul(ps_big[0:96, it * 512:it * 512 + cw],
                                    WxT[:, m * 96:(m + 1) * 96],
                                    xs_m[xb2][:, c0:c0 + cw],
                                    start=(m == 0), stop=(m == MC - 1),
                                    skip_group_check=True)
                return last
            dbc_pending[m] = (lambda emit_dbc=emit_dbc, xb2=xb2: em.op(
                "tensor", emit_dbc, reads=[("xs_m", xb2)] + wxk, writes=["ps_big"]))
    for mm_ in sorted(dbc_pending):
        dbc_pending[mm_]()

    em.op("vector", (lambda h: h.tensor_copy(out=dbc_sb[0:96, :],
                                             in_=ps_big[0:96, 0:T])),
          reads=["ps_big"], writes=["dbc"])
    em.op("vector", lambda h: h.tensor_copy(out=dt_bf[0:R, :], in_=dbc_sb[0:R, :]),
          reads=["dbc"], writes=["dt_bf"])
    if fused:
        bc_bf = EN(nc.sbuf_tensor("bc_bf", [2 * NST, T], BF16))
        em.op("vector", (lambda h: h.tensor_copy(out=bc_bf[:, :],
                                                 in_=dbc_sb[R:R + 2 * NST, :])),
              reads=["dbc"], writes=["bc_bf"])
        em.dma("sync", io["bc_hbm"], bc_bf[:, :], reads=["bc_bf"], writes=["bc_hbm"],
               allow_slow_non_contiguous=True)
        Wn = pspool["W"]
        FW = NST * Wn
        ones1b_a = EN(nc.sbuf_tensor("ones1b_a", [1, 128], BF16))
        if 6 * FW <= KD * TP:
            apre = xn_big[:, 0:2 * FW].bitcast(F32)
            upre = xn_big[:, 2 * FW:3 * FW]
            hpre = xn_big[:, 3 * FW:4 * FW]
            BrepW = xn_big[:, 4 * FW:5 * FW]
            bcWrow = xn_big[0:1, 5 * FW:6 * FW]
        else:
            apre = EN(nc.sbuf_tensor("apre_a", [128, FW], F32))[:]
            upre = EN(nc.sbuf_tensor("upre_a", [128, FW], BF16))[:]
            hpre = EN(nc.sbuf_tensor("hpre_a", [128, FW], BF16))[:]
            BrepW = EN(nc.sbuf_tensor("BrepW_a", [128, FW], BF16))[:]
            bcWrow = EN(nc.sbuf_tensor("bcWrow_a", [1, FW], BF16))[:]
        Hout = pspool["Hout"]
        em.op("vector", lambda h: h.memset(ones1b_a[:], 1.0),
              writes=["ones1b_a"]
              + [("xn_bf", j, g0) for j in range(NT) for g0 in range(0, KD, 4)]
              + ["xn_dead"])
        em.dma("sync", bcWrow[0:1, :], bc_bf[0:NST, T - Wn:T],
               reads=["bc_bf", "xn_dead"], writes=["bcWrow"], allow_slow_non_contiguous=True)
        for n in range(NST):
            pst, psk = next_ps()
            em.op("tensor", (lambda h, n=n, pst=pst: h.matmul(
                pst[:, 0:Wn], ones1b_a[0:1, :], bcWrow[0:1, n * Wn:(n + 1) * Wn],
                start=True, stop=True)),
                reads=["bcWrow", "ones1b_a"], writes=[psk])
            em.op("vector", (lambda h, n=n, pst=pst: h.tensor_copy(
                out=BrepW[:, n * Wn:(n + 1) * Wn], in_=pst[:, 0:Wn])),
                reads=[psk, "xn_dead"], writes=[("BrepW", n)])
    else:
        em.dma("scalar", io["B_out"], dbc_sb[R:R + NST, :], reads=["dbc"])
        em.dma("scalar", io["C_out"], dbc_sb[R + NST:R + 2 * NST, :], reads=["dbc"])

    for i, (c0, cw) in enumerate([] if fused else cfg.NT4):
        for half in range(2):
            dst = io["Brep_out"] if half == 0 else io["Crep_out"]
            em.dma("sync", bcr[0:1, 0:NST * cw],
                   dbc_sb[R + half * NST:R + (half + 1) * NST, c0:c0 + cw],
                   reads=["dbc"], writes=["bcr"], allow_slow_non_contiguous=True)
            for n in range(NST):
                pst, psk = next_ps()
                sb = brep_stage[n % 4]
                em.op("tensor", (lambda h, n=n, cw=cw, pst=pst: h.matmul(
                    pst[:, 0:cw], ones_sb[0:1, :], bcr[0:1, n * cw:(n + 1) * cw],
                    start=True, stop=True)),
                    reads=["bcr", "ones_sb"], writes=[psk])
                em.op("vector", (lambda h, cw=cw, pst=pst, sb=sb: h.tensor_copy(
                    out=sb[:, 0:cw], in_=pst[:, 0:cw])),
                    reads=[psk], writes=[("brs", n % 4)])
                em.dma("scalar", dst[:, n, c0:c0 + cw], sb[:, 0:cw], reads=[("brs", n % 4)])

    for m in range(MC):
        xb2 = m % 2
        em.dma("sync", xs_rl[xb2][:], io["xs_out"][m * 128:(m + 1) * 128, :],
               reads=[("xshbm", m)], writes=[("xs_rl", xb2)])

        def emit_mm(h, m=m):
            last = None
            for it, (c0, cw) in enumerate(cfg.NT4):
                last = h.matmul(ps_big[:, c0:c0 + cw], WdtT[0:R, m * 128:(m + 1) * 128],
                                dt_bf[0:R, c0:c0 + cw], start=True, stop=True)
            return last
        em.op("tensor", emit_mm, reads=["dt_bf", "WdtT"], writes=["ps_big"])
        em.op("scalar", (lambda h, m=m: h.activation(
            out=e1_sb[:], in_=ps_big[:, 0:T], func=AF.Exp, bias=bdt_col[:, m:m + 1])),
            reads=["ps_big", "bdt_col", "acc"], writes=["e1", "acc"])
        db = m % 2
        em.op("scalar", (lambda h, db=db: h.activation(
            out=delta_sb[db][:], in_=e1_sb[:], func=AF.Ln, bias=1.0)),
            reads=["e1"], writes=[("delta", db)])
        em.op("vector", (lambda h, db=db, xb2=xb2: h.tensor_tensor(
            out=wm_sb[db][:], in0=delta_sb[db][:], in1=xs_rl[xb2][:], op=ALU.mult)),
            reads=[("delta", db), ("xs_rl", xb2)], writes=[("wm", db)])
        em.dma("scalar", io["delta_out"][m * 128:(m + 1) * 128, :], delta_sb[db][:],
               reads=[("delta", db)], writes=[("dhbm", m)])
        em.dma("scalar", io["w_out"][m * 128:(m + 1) * 128, :], wm_sb[db][:],
               reads=[("wm", db)], writes=[("whbm", m)])
        if fused:
            em.op("vector", (lambda h, db=db, m=m: h.tensor_tensor(
                out=apre[:].rearrange("p (n w) -> p n w", n=NST),
                in0=delta_sb[db][:, T - Wn:T].unsqueeze(1).broadcast_to([128, NST, Wn]),
                in1=A_sb[:, m * NST:(m + 1) * NST].unsqueeze(2).broadcast_to([128, NST, Wn]),
                op=ALU.mult)),
                reads=[("delta", db), "Asb", "xn_dead"], writes=["apre"])
            em.op("scalar", (lambda h: h.activation(
                out=apre[:], in_=apre[:], func=AF.Exp)),
                reads=["apre"], writes=["apre"])
            em.op("vector", (lambda h: h.memset(
                apre[:].rearrange("p (n w) -> p n w", n=NST)[:, :, 0:1], 0.0)),
                reads=["apre"], writes=["apre0"])
            em.op("vector", (lambda h, db=db: h.tensor_tensor(
                out=upre[:].rearrange("p (n w) -> p n w", n=NST),
                in0=BrepW[:].rearrange("p (n w) -> p n w", n=NST),
                in1=wm_sb[db][:, T - Wn:T].unsqueeze(1).broadcast_to([128, NST, Wn]),
                op=ALU.mult)),
                reads=[("BrepW", n) for n in range(NST)] + [("wm", db), "xn_dead"],
                writes=["upre"])
            em.op("vector", (lambda h: h.tensor_tensor_scan(
                out=hpre[:], data0=apre[:], data1=upre[:],
                initial=0.0, op0=ALU.mult, op1=ALU.add)),
                reads=["apre", "apre0", "upre", "xn_dead"], writes=["hpre"])
            em.op("vector", (lambda h, m=m: h.tensor_copy(
                out=Hout[:, m * NST:(m + 1) * NST],
                in_=hpre[:].rearrange("p (n w) -> p n w", n=NST)[:, :, Wn - 1])),
                reads=["hpre"], writes=[("Hout", m)])

    if fused:
        em.dma("scalar", io["hout_st"], Hout[:].rearrange("p (m n) -> p m n", n=NST),
               reads=[("Hout", m) for m in range(MC)], writes=["hout_st"],
               allow_slow_non_contiguous=True)

    return io


def np_phase_a(x_in, W_in, W_xproj, W_dt, b_dt, conv_w, conv_b, norm_w, cfg):
    bf = ml_dtypes.bfloat16

    def b16(v):
        return v.astype(bf).astype(np.float32)
    T, D, E, R, NST = cfg.T, cfg.D, cfg.E, cfg.R, cfg.NST
    ssum = (x_in * x_in).sum(axis=1)
    rstd = np.exp(-0.5 * np.log(ssum / D + 1e-5)).astype(np.float32)
    xn = b16(x_in * rstd[:, None])
    WinT = b16(W_in.T * norm_w[:, None])
    xz_fm = (WinT.T.astype(np.float32) @ xn.T.astype(np.float32)).astype(np.float32)
    xc = xz_fm[:E]
    res = xz_fm[E:][:, 128:]
    acc = np.zeros((E, T), np.float32)
    for k in range(4):
        acc += xc[:, 125 + k:125 + k + T] * conv_w[:, k:k + 1]
    acc_b = acc + conv_b[:, None]
    xs = (acc_b / (1 + np.exp(-acc_b))).astype(np.float32)
    xs_bf = b16(xs)
    g = b16(res / (1 + np.exp(-res)))
    WxT = b16(W_xproj.T)
    dbc = (WxT.T.astype(np.float32) @ xs_bf).astype(np.float32)
    dt_bf = b16(dbc[:R])
    B = dbc[R:R + NST].astype(np.float32)
    C = dbc[R + NST:R + 2 * NST].astype(np.float32)
    WdtT = b16(W_dt.T)
    dl_raw = (WdtT.T.astype(np.float32) @ dt_bf).astype(np.float32)
    z = dl_raw + b_dt[:, None]
    delta = np.log1p(np.exp(z)).astype(np.float32)
    w = b16(delta * xs_bf)
    return dict(delta=delta, w=w, xs=xs_bf, g=g, B=B, C=C,
                Brep=np.broadcast_to(b16(B)[None], (128, NST, T)).copy(),
                Crep=np.broadcast_to(b16(C)[None], (128, NST, T)).copy())


class CfgB:
    def __init__(self, L=16384, Dsl=256, NST=16, Fc=2048, G=2):
        self.L, self.Dsl, self.NST, self.Fc, self.G = L, Dsl, NST, Fc, G
        self.DT = Dsl // 128
        self.NC = L // Fc
        self.NG = NST // G


def emit_phase_b(nc, em, EN, cfg, io=None):
    L, Dsl, NST, Fc, G = cfg.L, cfg.Dsl, cfg.NST, cfg.Fc, cfg.G
    DT, NC, NG = cfg.DT, cfg.NC, cfg.NG

    if io is None:
        io = {}
        io["delta_in"] = nc.dram_tensor("delta_in", [Dsl, L], F32, kind="ExternalInput").ap()
        io["wv_in"] = nc.dram_tensor("wv_in", [Dsl, L], BF16, kind="ExternalInput").ap()
        io["Brep_in"] = nc.dram_tensor("Brep_in", [128, NST, L], BF16, kind="ExternalInput").ap()
        io["Crep_in"] = nc.dram_tensor("Crep_in", [128, NST, L], BF16, kind="ExternalInput").ap()
        io["A_log_in"] = nc.dram_tensor("A_log_in", [Dsl, NST], F32, kind="ExternalInput").ap()
        io["y_out"] = nc.dram_tensor("y_out", [Dsl, L], BF16, kind="ExternalOutput").ap()

    A_sb = [EN(nc.sbuf_tensor(f"A_sb{d}", [128, NST], F32)) for d in range(DT)]
    carry = [EN(nc.sbuf_tensor(f"carry{d}", [128, NST], F32)) for d in range(DT)]
    dc = [EN(nc.sbuf_tensor(f"dc{i}", [128, Fc], F32)) for i in range(2)]
    wc = [EN(nc.sbuf_tensor(f"wc{i}", [128, Fc], BF16)) for i in range(2)]
    Bg = [EN(nc.sbuf_tensor(f"Bg{i}", [128, G * Fc], BF16)) for i in range(2)]
    Cg = [EN(nc.sbuf_tensor(f"Cg{i}", [128, G * Fc], BF16)) for i in range(2)]
    a_pl = [EN(nc.sbuf_tensor(f"a_pl{i}", [128, Fc], F32)) for i in range(3)]
    u_pl = [EN(nc.sbuf_tensor(f"u_pl{i}", [128, Fc], BF16)) for i in range(3)]
    h_pl = [EN(nc.sbuf_tensor(f"h_pl{i}", [128, Fc], BF16)) for i in range(3)]
    z_pl = [EN(nc.sbuf_tensor(f"z_pl{i}", [128, Fc], BF16)) for i in range(2)]
    yac = [EN(nc.sbuf_tensor(f"yac{i}", [128, Fc], BF16)) for i in range(2)]

    for d in range(DT):
        em.dma("sync", A_sb[d][:], io["A_log_in"][d * 128:(d + 1) * 128, :], writes=[("Asb", d)])
        em.op("scalar", (lambda h, d=d: h.activation(out=A_sb[d][:], in_=A_sb[d][:], func=AF.Exp)),
              reads=[("Asb", d)], writes=[("Asb", d)])
        em.op("vector", (lambda h, d=d: h.tensor_scalar_mul(A_sb[d][:], A_sb[d][:], -1.0)),
              reads=[("Asb", d)], writes=[("Asb", d)])
        em.op("vector", (lambda h, d=d: h.memset(carry[d][:], 0.0)), writes=[("carry", d, n) for n in range(NST)])

    for d in range(DT):
        for c in range(NC):
            ci = c % 2
            c0 = c * Fc
            em.dma("sync", dc[ci][:], io["delta_in"][d * 128:(d + 1) * 128, c0:c0 + Fc],
                   writes=[("dc", ci)])
            em.dma("sync", wc[ci][:], io["wv_in"][d * 128:(d + 1) * 128, c0:c0 + Fc],
                   writes=[("wc", ci)])
            for g in range(NG):
                gi = (c * NG + g) % 2
                n0 = g * G
                em.dma("sync", Bg[gi][:], io["Brep_in"][:, n0:n0 + G, c0:c0 + Fc],
                       writes=[("Bg", gi)])
                em.dma("sync", Cg[gi][:], io["Crep_in"][:, n0:n0 + G, c0:c0 + Fc],
                       writes=[("Cg", gi)])
                for j in range(G):
                    n = n0 + j
                    ai = n % 3
                    em.op("scalar", (lambda h, d=d, ci=ci, n=n, ai=ai: h.activation(
                        out=a_pl[ai][:], in_=dc[ci][:], func=AF.Exp,
                        scale=A_sb[d][:, n:n + 1])),
                        reads=[("dc", ci), ("Asb", d)], writes=[("a", ai)])
                    em.op("vector", (lambda h, ci=ci, gi=gi, j=j, ai=ai: h.tensor_tensor(
                        out=u_pl[ai][:], in0=Bg[gi][:, j * Fc:(j + 1) * Fc], in1=wc[ci][:],
                        op=ALU.mult)),
                        reads=[("Bg", gi), ("wc", ci)], writes=[("u", ai)])
                    em.op("vector", (lambda h, d=d, n=n, ai=ai: h.tensor_tensor_scan(
                        out=h_pl[ai][:], data0=a_pl[ai][:], data1=u_pl[ai][:],
                        initial=carry[d][:, n:n + 1], op0=ALU.mult, op1=ALU.add)),
                        reads=[("a", ai), ("u", ai), ("carry", d, n)], writes=[("h", ai)])
                    em.op("vector", (lambda h, d=d, n=n, ai=ai: h.tensor_copy(
                        out=carry[d][:, n:n + 1], in_=h_pl[ai][:, Fc - 1:Fc])),
                        reads=[("h", ai)], writes=[("carry", d, n)])
                    if n == 0:
                        em.op("vector", (lambda h, gi=gi, j=j, ai=ai, ci=ci: h.tensor_tensor(
                            out=yac[ci][:], in0=h_pl[ai][:], in1=Cg[gi][:, j * Fc:(j + 1) * Fc],
                            op=ALU.mult)),
                            reads=[("h", ai), ("Cg", gi)], writes=[("yac", ci)])
                    else:
                        zi = n % 2
                        em.op("vector", (lambda h, gi=gi, j=j, ai=ai, zi=zi: h.tensor_tensor(
                            out=z_pl[zi][:], in0=h_pl[ai][:], in1=Cg[gi][:, j * Fc:(j + 1) * Fc],
                            op=ALU.mult)),
                            reads=[("h", ai), ("Cg", gi)], writes=[("z", zi)])
                        em.op("vector", (lambda h, zi=zi, ci=ci: h.tensor_tensor(
                            out=yac[ci][:], in0=yac[ci][:], in1=z_pl[zi][:], op=ALU.add)),
                            reads=[("z", zi), ("yac", ci)], writes=[("yac", ci)])
            em.dma("scalar", io["y_out"][d * 128:(d + 1) * 128, c0:c0 + Fc], yac[ci][:],
                   reads=[("yac", ci)])
    return io


def np_phase_b(delta_in, wv_in, Brep_in, Crep_in, A_log_in, cfg):
    bf = ml_dtypes.bfloat16

    def b16(v):
        return v.astype(bf).astype(np.float32)
    L, Dsl, NST, Fc = cfg.L, cfg.Dsl, cfg.NST, cfg.Fc
    A = -np.exp(A_log_in.astype(np.float32))
    B = Brep_in[0].astype(np.float32)
    C = Crep_in[0].astype(np.float32)
    w = wv_in.astype(np.float32)
    y = np.zeros((Dsl, L), np.float32)
    h = np.zeros((Dsl, NST), np.float32)
    for t in range(L):
        a = np.exp(A * delta_in[:, t:t + 1])
        u = b16(w[:, t:t + 1] * B[:, t][None, :])
        h = a * h + u
        hb = b16(h)
        z = b16(hb * C[:, t][None, :])
        acc = z[:, 0]
        for n in range(1, NST):
            acc = b16(acc + z[:, n])
        y[:, t] = acc
        if (t + 1) % Fc == 0:
            h = b16(h)
    return b16(y)


class CfgC:
    def __init__(self, T=2048, D=1024, E=2048):
        self.T, self.D, self.E = T, D, E
        self.MC = E // 128
        self.KD = D // 128
        self.NTT = T // 128
        self.MAGIC = 12582912.0


def emit_phase_c(nc, em, EN, cfg, io=None, fused=False, pspool=None):
    T, D, E = cfg.T, cfg.D, cfg.E
    MC, KD, NTT = cfg.MC, cfg.KD, cfg.NTT
    MAGIC = cfg.MAGIC

    if io is None:
        io = {}
        io["y_in"] = nc.dram_tensor("y_in", [E, T], BF16, kind="ExternalInput").ap()
        io["xs_in"] = nc.dram_tensor("xs_in", [E, T], BF16, kind="ExternalInput").ap()
        io["g_in"] = nc.dram_tensor("g_in", [E, T], BF16, kind="ExternalInput").ap()
        io["x2_in"] = nc.dram_tensor("x2_in", [T, D], F32, kind="ExternalInput").ap()
        io["D_in"] = nc.dram_tensor("D_in", [E], F32, kind="ExternalInput").ap()
        io["W_out"] = nc.dram_tensor("W_out", [D, E], F32, kind="ExternalInput").ap()
        io["rhs"] = nc.dram_tensor("rhs", [D, D], F32, kind="ExternalInput").ap()
        io["out"] = nc.dram_tensor("out", [T, D], F32, kind="ExternalOutput").ap()
    x2_src = io["x2_in"]
    identc = nc.inline_tensor(np.eye(128, dtype=np.float32), "ident128c").ap()

    ident_sb = EN(nc.sbuf_tensor("identc_sb", [128, 128], F32))
    onesc_sb = EN(nc.sbuf_tensor("onesc_sb", [1, 128], F32))
    if not fused:
        D_col = EN(nc.sbuf_tensor("D_col", [128, MC], F32))
        ym = [EN(nc.sbuf_tensor(f"ym{i}", [128, T], BF16)) for i in range(2)]
        xsm = [EN(nc.sbuf_tensor(f"xsm{i}", [128, T], BF16)) for i in range(2)]
        gm = [EN(nc.sbuf_tensor(f"gm{i}", [128, T], BF16)) for i in range(2)]
        t1 = EN(nc.sbuf_tensor("t1", [128, T], F32))
    NBUF = 4 if fused else 2
    if fused:
        y2mt = [EN(nc.sbuf_tensor(f"y2mt{i}", [128, MC * 128], BF16)) for i in range(NBUF)]
    else:
        y2 = [EN(nc.sbuf_tensor(f"y2_{k}", [128, T], BF16)) for k in range(MC)]
    Wo_nat = [EN(nc.sbuf_tensor(f"Wo_nat{i}", [128, E], F32)) for i in range(2)]
    WoT = [EN(nc.sbuf_tensor(f"WoT{k}", [128, D], BF16)) for k in range(MC)]
    rh_nat = [EN(nc.sbuf_tensor(f"rh_nat{i}", [128, 128], F32)) for i in range(2)]
    rt_buf = EN(nc.sbuf_tensor("rt_buf", [128, D], F32))
    qr_e = EN(nc.sbuf_tensor("qr_e", [128, D], F32))
    q_rhs = [EN(nc.sbuf_tensor(f"q_rhs{k}", [128, D], BF16)) for k in range(KD)]
    rsc = EN(nc.sbuf_tensor("rsc", [128, KD], F32))
    rsc_T = EN(nc.sbuf_tensor("rsc_T", [KD, 128], F32))
    rs_row = EN(nc.sbuf_tensor("rs_row", [1, D], F32))
    rs_rep = EN(nc.sbuf_tensor("rs_rep", [128, D], F32))
    x2t = [EN(nc.sbuf_tensor(f"x2t{i}", [128, D], F32)) for i in range(NBUF)]
    res1b = [EN(nc.sbuf_tensor(f"res1_{i}", [128, D], F32)) for i in range(NBUF)]
    out1b = [EN(nc.sbuf_tensor(f"out1t{i}", [128, D], F32)) for i in range(NBUF)]
    qvb = [EN(nc.sbuf_tensor(f"qv{i}", [128, D], F32)) for i in range(NBUF)]
    qbfb = [EN(nc.sbuf_tensor(f"qbf{i}", [128, D], F32)) for i in range(NBUF)]
    qTtb = [EN(nc.sbuf_tensor(f"qTt{i}", [128, KD * 128], BF16)) for i in range(NBUF)]
    colsb = [EN(nc.sbuf_tensor(f"cols{i}", [128, 16], F32)) for i in range(NBUF)]
    outt = [EN(nc.sbuf_tensor(f"outt{i}", [128, D], F32)) for i in range(NBUF)]

    if pspool is None:
        psc = [EN(nc.psum_tensor(f"psc{i}", [128, 512], F32)) for i in range(4)]
        psck = "psc"
    else:
        psc = pspool["ps"]
        psck = "ps"
    NPS = len(psc)
    psi = [0]

    def next_ps():
        k = psi[0] % NPS
        psi[0] += 1
        return psc[k], f"{psck}{k}"

    magic_c = nc.inline_tensor(np.full((1, 1), MAGIC, np.float32), "magic_c").ap()
    nmagic_c = nc.inline_tensor(np.full((1, 1), -MAGIC, np.float32), "nmagic_c").ap()
    nc.const_aps.register(F32, MAGIC, magic_c)
    nc.const_aps.register(F32, -MAGIC, nmagic_c)
    em.dma("sync", ident_sb[:], identc, writes=["ident_sb"])
    if not fused:
        em.dma("sync", D_col[:], io["D_in"].rearrange("(m p) -> p m", p=128),
               writes=["D_col"], allow_slow_non_contiguous=True)
    em.op("vector", lambda h: h.memset(onesc_sb[:], 1.0), writes=["ones"])

    if fused:
        pass
    else:
        for m in range(MC):
            b = m % 2
            em.dma("sync", ym[b][:], io["y_in"][m * 128:(m + 1) * 128, :], writes=[("ym", b)])
            em.dma("sync", xsm[b][:], io["xs_in"][m * 128:(m + 1) * 128, :], writes=[("xsm", b)])
            em.dma("sync", gm[b][:], io["g_in"][m * 128:(m + 1) * 128, :], writes=[("gm", b)])
            em.op("vector", (lambda h, b=b, m=m: h.scalar_tensor_tensor(
                out=t1[:], in0=xsm[b][:], scalar=D_col[:, m:m + 1], in1=ym[b][:],
                op0=ALU.mult, op1=ALU.add)),
                reads=[("xsm", b), ("ym", b), "D_col"], writes=["t1"])
            em.op("vector", (lambda h, b=b, m=m: h.tensor_tensor(
                out=y2[m][:], in0=t1[:], in1=gm[b][:], op=ALU.mult)),
                reads=["t1", ("gm", b)], writes=[("y2", m)])

    for j in range(KD):
        b = j % 2
        em.dma("sync", Wo_nat[b][:], io["W_out"][j * 128:(j + 1) * 128, :], writes=[("Wo", b)])
        for g0 in range(0, MC, 4):
            gn = min(4, MC - g0)
            pst, psk = next_ps()

            def emit_tp(h, b=b, g0=g0, gn=gn, pst=pst):
                last = None
                for jj in range(gn):
                    k = g0 + jj
                    last = h.transpose(pst[:, jj * 128:(jj + 1) * 128],
                                       Wo_nat[b][:, k * 128:(k + 1) * 128], ident_sb[:])
                return last
            em.op("tensor", emit_tp, reads=[("Wo", b), "ident_sb"], writes=[psk])

            def emit_ev(h, j=j, g0=g0, gn=gn, pst=pst):
                last = None
                for jj in range(gn):
                    k = g0 + jj
                    last = h.tensor_copy(out=WoT[k][:, j * 128:(j + 1) * 128],
                                         in_=pst[:, jj * 128:(jj + 1) * 128])
                return last
            em.op("vector", emit_ev, reads=[psk], writes=[("WoT", j, g0)])
    wot_all = [("WoT", j, g0) for j in range(KD) for g0 in range(0, MC, 4)]

    for e in range(KD):
        for k in range(KD):
            rb = k % 2
            em.dma("sync", rh_nat[rb][:], io["rhs"][k * 128:(k + 1) * 128, e * 128:(e + 1) * 128],
                   writes=[("rh_nat", rb)], allow_slow_non_contiguous=True)
            pst, psk = next_ps()
            em.op("tensor", (lambda h, rb=rb, pst=pst: h.transpose(
                pst[:, 0:128], rh_nat[rb][:], ident_sb[:])),
                reads=[("rh_nat", rb), "ident_sb"], writes=[psk])
            em.op("vector", (lambda h, k=k, pst=pst: h.tensor_copy(
                out=rt_buf[:, k * 128:(k + 1) * 128], in_=pst[:, 0:128])),
                reads=[psk], writes=[("rt", k)])
        rtk = [("rt", k) for k in range(KD)]
        em.op("vector", (lambda h: h.tensor_reduce(
            out=cols[:, 0:1], in_=rt_buf[:], axis=mybir.AxisListType.X, op=ALU.max,
            apply_absolute_value=True)),
            reads=rtk, writes=[("cols", 0)])
        em.op("vector", (lambda h: h.tensor_scalar(
            out=cols[:, 1:2], in0=cols[:, 0:1], scalar1=1e-12, scalar2=1.0 / 127.0,
            op0=ALU.max, op1=ALU.mult)),
            reads=[("cols", 0)], writes=[("cols", 1)])
        em.op("vector", (lambda h: h.reciprocal(cols[:, 2:3], cols[:, 1:2])),
              reads=[("cols", 1)], writes=[("cols", 2)])
        em.op("vector", (lambda h, e=e: h.tensor_copy(out=rsc[:, e:e + 1], in_=cols[:, 1:2])),
              reads=[("cols", 1)], writes=[("rsc", e)])
        em.op("vector", (lambda h: h.tensor_scalar(
            out=rt_buf[:], in0=rt_buf[:], scalar1=cols[:, 2:3], scalar2=MAGIC,
            op0=ALU.mult, op1=ALU.add)),
            reads=rtk + [("cols", 2)], writes=rtk)
        em.op("vector", (lambda h: h.tensor_scalar(
            out=qr_e[:], in0=rt_buf[:], scalar1=MAGIC, scalar2=None, op0=ALU.subtract)),
            reads=rtk, writes=["qr_e"])
        for k in range(KD):
            pst, psk = next_ps()
            em.op("tensor", (lambda h, k=k, pst=pst: h.transpose(
                pst[:, 0:128], qr_e[:, k * 128:(k + 1) * 128], ident_sb[:])),
                reads=["qr_e", "ident_sb"], writes=[psk])
            em.op("vector", (lambda h, e=e, k=k, pst=pst: h.tensor_copy(
                out=q_rhs[k][:, e * 128:(e + 1) * 128], in_=pst[:, 0:128])),
                reads=[psk], writes=[("q_rhs", k, e)])
    qrhs_all = [("q_rhs", k, e) for k in range(KD) for e in range(KD)]
    rsc_all = [("rsc", e) for e in range(KD)]

    pst, psk = next_ps()
    em.op("tensor", (lambda h, pst=pst: h.transpose(pst[0:KD, 0:128], rsc[:, 0:KD], ident_sb[:])),
          reads=rsc_all + ["ident_sb"], writes=[psk])
    em.op("vector", (lambda h, pst=pst: h.tensor_copy(out=rsc_T[0:KD, :], in_=pst[0:KD, 0:128])),
          reads=[psk], writes=["rsc_T"])
    em.dma("sync", rs_row[0:1, :], rsc_T[0:KD, :], reads=["rsc_T"], writes=["rs_row"],
           allow_slow_non_contiguous=True)
    for i in range(0, D, 512):
        cw = min(512, D - i)
        pst, psk = next_ps()
        em.op("tensor", (lambda h, i=i, cw=cw, pst=pst: h.matmul(
            pst[:, 0:cw], onesc_sb[0:1, :], rs_row[0:1, i:i + cw], start=True, stop=True)),
            reads=["rs_row", "ones"], writes=[psk])
        em.op("vector", (lambda h, i=i, cw=cw, pst=pst: h.tensor_copy(
            out=rs_rep[:, i:i + cw], in_=pst[:, 0:cw])),
            reads=[psk], writes=[("rs_rep", i)])
    rsrep_all = [("rs_rep", i) for i in range(0, D, 512)]

    y2_all = [("y2", m) for m in range(MC)] if not fused else []
    for mt in range(NTT):
        xb = mt % NBUF
        ob = xb
        res1 = res1b[xb]
        out1t = out1b[xb]
        qv = qvb[xb]
        qbf = qbfb[xb]
        qTt = qTtb[xb]
        cols = colsb[xb]
        f2 = res1
        em.dma("sync", x2t[xb][:], x2_src[mt * 128:(mt + 1) * 128, :], writes=[("x2t", xb)])
        if fused:
            em.dma("sync", y2mt[xb][:].rearrange("p (k c) -> p k c", c=128),
                   io["y2_st"].rearrange("(k p) t -> p k t", p=128)[:, :, mt * 128:(mt + 1) * 128],
                   reads=[("y2st", m) for m in range(MC)], writes=[("y2mt", xb)],
                   allow_slow_non_contiguous=True)
        NTD = (D + 511) // 512
        base = 0 if (mt % 2 == 0) else 2
        alt = 2 - base + 2
        alt = (base + 2) % 4
        if fused:
            for nt in range(NTD):
                c0 = nt * 512
                cw = min(512, D - c0)
                pst, psk = psc[base + nt], f"{psck}{base + nt}"

                def emit_mm(h, xb=xb, c0=c0, cw=cw, pst=pst):
                    last = None
                    for k in range(MC):
                        last = h.matmul(pst[:, 0:cw], y2mt[xb][:, k * 128:(k + 1) * 128],
                                        WoT[k][:, c0:c0 + cw],
                                        start=(k == 0), stop=(k == MC - 1))
                    return last
                em.op("tensor", emit_mm, reads=[("y2mt", xb)] + wot_all, writes=[psk])
                em.op("vector", (lambda h, c0=c0, cw=cw, xb=xb, pst=pst: h.scalar_tensor_tensor(
                    out=res1[:, c0:c0 + cw], in0=x2t[xb][:, c0:c0 + cw],
                    scalar=2.0, in1=pst[:, 0:cw], op0=ALU.mult, op1=ALU.add)),
                    reads=[psk, ("x2t", xb)], writes=[("res1", xb, nt)])
        else:
            for nt in range(NTD):
                c0 = nt * 512
                cw = min(512, D - c0)
                pst, psk = psc[base + nt], f"{psck}{base + nt}"

                def emit_mm(h, mt=mt, c0=c0, cw=cw, pst=pst):
                    last = None
                    for k in range(MC):
                        last = h.matmul(pst[:, 0:cw], y2[k][:, mt * 128:(mt + 1) * 128],
                                        WoT[k][:, c0:c0 + cw],
                                        start=(k == 0), stop=(k == MC - 1))
                    return last
                em.op("tensor", emit_mm, reads=y2_all + wot_all, writes=[psk])
                em.op("vector", (lambda h, c0=c0, cw=cw, xb=xb, pst=pst: h.scalar_tensor_tensor(
                    out=res1[:, c0:c0 + cw], in0=x2t[xb][:, c0:c0 + cw],
                    scalar=2.0, in1=pst[:, 0:cw], op0=ALU.mult, op1=ALU.add)),
                    reads=[psk, ("x2t", xb)], writes=[("res1", xb, nt)])
        r1k = [("res1", xb, nt) for nt in range(NTD)]
        em.op("vector", (lambda h: h.tensor_reduce(
            out=cols[:, 0:1], in_=res1[:], axis=mybir.AxisListType.X, op=ALU.add)),
            reads=r1k, writes=[("cols", xb, 0)])
        em.op("scalar", (lambda h: h.activation(
            out=qv[:], in_=res1[:], func=AF.Square, accum_out=cols[:, 1:2])),
            reads=r1k, writes=[("qv", xb), ("cols", xb, 1)])
        em.op("vector", (lambda h: h.tensor_scalar(
            out=cols[:, 2:3], in0=cols[:, 0:1], scalar1=1.0 / D, scalar2=None, op0=ALU.mult)),
            reads=[("cols", xb, 0)], writes=[("cols", xb, 2)])
        em.op("vector", (lambda h: h.tensor_tensor(
            out=cols[:, 3:4], in0=cols[:, 2:3], in1=cols[:, 2:3], op=ALU.mult)),
            reads=[("cols", xb, 2)], writes=[("cols", xb, 3)])
        em.op("vector", (lambda h: h.tensor_scalar(
            out=cols[:, 4:5], in0=cols[:, 1:2], scalar1=1.0 / D, scalar2=1e-5,
            op0=ALU.mult, op1=ALU.add)),
            reads=[("cols", xb, 1)], writes=[("cols", xb, 4)])
        em.op("vector", (lambda h: h.tensor_tensor(
            out=cols[:, 5:6], in0=cols[:, 4:5], in1=cols[:, 3:4], op=ALU.subtract)),
            reads=[("cols", xb, 3), ("cols", xb, 4)], writes=[("cols", xb, 5)])
        em.op("scalar", (lambda h: h.activation(
            out=cols[:, 6:7], in_=cols[:, 5:6], func=AF.Ln)),
            reads=[("cols", xb, 5)], writes=[("cols", xb, 6)])
        em.op("scalar", (lambda h: h.activation(
            out=cols[:, 7:8], in_=cols[:, 6:7], func=AF.Exp, scale=-0.5)),
            reads=[("cols", xb, 6)], writes=[("cols", xb, 7)])
        em.op("vector", (lambda h: h.tensor_scalar(
            out=cols[:, 8:9], in0=cols[:, 2:3], scalar1=-1.0, scalar2=None, op0=ALU.mult)),
            reads=[("cols", xb, 2)], writes=[("cols", xb, 8)])
        em.op("vector", (lambda h: h.tensor_scalar(
            out=out1t[:], in0=res1[:], scalar1=cols[:, 8:9], scalar2=cols[:, 7:8],
            op0=ALU.add, op1=ALU.mult)),
            reads=r1k + [("cols", xb, 7), ("cols", xb, 8)], writes=[("out1t", xb)])
        em.op("vector", (lambda h: h.tensor_reduce(
            out=cols[:, 9:10], in_=out1t[:], axis=mybir.AxisListType.X, op=ALU.max,
            apply_absolute_value=True)),
            reads=[("out1t", xb)], writes=[("cols", xb, 9)])
        em.op("vector", (lambda h: h.tensor_scalar(
            out=cols[:, 10:11], in0=cols[:, 9:10], scalar1=1e-12, scalar2=1.0 / 127.0,
            op0=ALU.max, op1=ALU.mult)),
            reads=[("cols", xb, 9)], writes=[("cols", xb, 10)])
        em.op("vector", (lambda h: h.reciprocal(cols[:, 11:12], cols[:, 10:11])),
              reads=[("cols", xb, 10)], writes=[("cols", xb, 11)])
        em.op("vector", (lambda h: h.tensor_scalar(
            out=qv[:], in0=out1t[:], scalar1=cols[:, 11:12], scalar2=MAGIC,
            op0=ALU.mult, op1=ALU.add)),
            reads=[("out1t", xb), ("cols", xb, 11)], writes=[("qv", xb)])
        em.op("vector", (lambda h: h.tensor_scalar(
            out=qbf[:], in0=qv[:], scalar1=MAGIC, scalar2=None, op0=ALU.subtract)),
            reads=[("qv", xb)], writes=[("qbf", xb)])
        for g0 in range(0, KD, 4):
            gn = min(4, KD - g0)
            gi = g0 // 4
            pst, psk = psc[alt + gi], f"{psck}{alt + gi}"

            def emit_tp(h, g0=g0, gn=gn, pst=pst):
                last = None
                for jj in range(gn):
                    k = g0 + jj
                    last = h.transpose(pst[:, jj * 128:(jj + 1) * 128],
                                       qbf[:, k * 128:(k + 1) * 128], ident_sb[:])
                return last
            em.op("tensor", emit_tp, reads=[("qbf", xb), "ident_sb"], writes=[psk])

            def emit_ev(h, g0=g0, gn=gn, pst=pst):
                last = None
                for jj in range(gn):
                    k = g0 + jj
                    last = h.activation(out=qTt[:, k * 128:(k + 1) * 128],
                                        in_=pst[:, jj * 128:(jj + 1) * 128], func=AF.Copy)
                return last
            em.op("scalar", emit_ev, reads=[psk], writes=[("qTt", xb, g0)])
        qt_all = [("qTt", xb, g0) for g0 in range(0, KD, 4)]
        for nt in range(NTD):
            c0 = nt * 512
            cw = min(512, D - c0)
            pst, psk = psc[alt + nt], f"{psck}{alt + nt}"

            def emit_mm(h, c0=c0, cw=cw, pst=pst):
                last = None
                for k in range(KD):
                    last = h.matmul(pst[:, 0:cw], qTt[:, k * 128:(k + 1) * 128],
                                    q_rhs[k][:, c0:c0 + cw],
                                    start=(k == 0), stop=(k == KD - 1))
                return last
            em.op("tensor", emit_mm, reads=qt_all + qrhs_all, writes=[psk])
            em.op("vector", (lambda h, c0=c0, cw=cw, pst=pst: h.scalar_tensor_tensor(
                out=f2[:, c0:c0 + cw], in0=pst[:, 0:cw],
                scalar=cols[:, 10:11], in1=rs_rep[:, c0:c0 + cw],
                op0=ALU.mult, op1=ALU.mult)),
                reads=[psk, ("cols", xb, 10)] + rsrep_all, writes=[("f2", xb, nt), ("res1", xb, nt)])
        f2k = [("f2", xb, nt) for nt in range(NTD)]
        em.op("vector", (lambda h: h.tensor_tensor(
            out=f2[:], in0=f2[:], in1=out1t[:], op=ALU.add)),
            reads=f2k + [("out1t", xb)], writes=[("f2s", xb)] + f2k)
        em.op("vector", (lambda h: h.tensor_reduce(
            out=cols[:, 0:1], in_=f2[:], axis=mybir.AxisListType.X, op=ALU.add)),
            reads=[("f2s", xb)], writes=[("cols", xb, 0)])
        em.op("scalar", (lambda h: h.activation(
            out=qv[:], in_=f2[:], func=AF.Square, accum_out=cols[:, 1:2])),
            reads=[("f2s", xb)], writes=[("qv", xb), ("cols", xb, 1)])
        em.op("vector", (lambda h: h.tensor_scalar(
            out=cols[:, 2:3], in0=cols[:, 0:1], scalar1=1.0 / D, scalar2=None, op0=ALU.mult)),
            reads=[("cols", xb, 0)], writes=[("cols", xb, 2)])
        em.op("vector", (lambda h: h.tensor_tensor(
            out=cols[:, 3:4], in0=cols[:, 2:3], in1=cols[:, 2:3], op=ALU.mult)),
            reads=[("cols", xb, 2)], writes=[("cols", xb, 3)])
        em.op("vector", (lambda h: h.tensor_scalar(
            out=cols[:, 4:5], in0=cols[:, 1:2], scalar1=1.0 / D, scalar2=1e-5,
            op0=ALU.mult, op1=ALU.add)),
            reads=[("cols", xb, 1)], writes=[("cols", xb, 4)])
        em.op("vector", (lambda h: h.tensor_tensor(
            out=cols[:, 5:6], in0=cols[:, 4:5], in1=cols[:, 3:4], op=ALU.subtract)),
            reads=[("cols", xb, 3), ("cols", xb, 4)], writes=[("cols", xb, 5)])
        em.op("scalar", (lambda h: h.activation(
            out=cols[:, 6:7], in_=cols[:, 5:6], func=AF.Ln)),
            reads=[("cols", xb, 5)], writes=[("cols", xb, 6)])
        em.op("scalar", (lambda h: h.activation(
            out=cols[:, 7:8], in_=cols[:, 6:7], func=AF.Exp, scale=-0.5)),
            reads=[("cols", xb, 6)], writes=[("cols", xb, 7)])
        em.op("vector", (lambda h: h.tensor_scalar(
            out=cols[:, 8:9], in0=cols[:, 2:3], scalar1=-1.0, scalar2=None, op0=ALU.mult)),
            reads=[("cols", xb, 2)], writes=[("cols", xb, 8)])
        em.op("vector", (lambda h, ob=ob: h.tensor_scalar(
            out=outt[ob][:], in0=f2[:], scalar1=cols[:, 8:9], scalar2=cols[:, 7:8],
            op0=ALU.add, op1=ALU.mult)),
            reads=[("f2s", xb), ("cols", xb, 7), ("cols", xb, 8)], writes=[("outt", ob)])
        em.dma("scalar", io["out"][mt * 128:(mt + 1) * 128, :], outt[ob][:],
               reads=[("outt", ob)])
    return io


def np_phase_c(y_in, xs_in, g_in, x2_in, D_in, W_out, rhs, cfg):
    bf = ml_dtypes.bfloat16

    def b16(v):
        return v.astype(bf).astype(np.float32)
    T, D, E = cfg.T, cfg.D, cfg.E
    y = y_in.astype(np.float32)
    xs = xs_in.astype(np.float32)
    g = g_in.astype(np.float32)
    t1 = xs * D_in[:, None] + y
    y2 = b16(t1 * g)
    WoT = b16(W_out.T)
    mam = (y2.T @ WoT.astype(np.float32)).astype(np.float32)
    res1 = 2.0 * x2_in + mam
    m = res1.mean(axis=1, keepdims=True)
    v = (res1 * res1).mean(axis=1, keepdims=True) - m * m
    rstd = np.exp(-0.5 * np.log(v + 1e-5)).astype(np.float32)
    out1 = ((res1 - m) * rstd).astype(np.float32)
    ls = np.maximum(np.abs(out1).max(axis=1, keepdims=True), 1e-12) / 127.0
    ql = np.round(out1 * (1.0 / ls)).astype(np.float32)
    rs = np.maximum(np.abs(rhs).max(axis=0, keepdims=True), 1e-12) / 127.0
    qr = np.round(rhs * (1.0 / rs)).astype(np.float32)
    acc = ql @ qr
    f2 = acc * ls * rs + out1
    m2 = f2.mean(axis=1, keepdims=True)
    v2 = (f2 * f2).mean(axis=1, keepdims=True) - m2 * m2
    rstd2 = np.exp(-0.5 * np.log(v2 + 1e-5)).astype(np.float32)
    return ((f2 - m2) * rstd2).astype(np.float32)


def cfg2_nt4(T):
    return [(i * 512, min(512, T - i * 512)) for i in range((T + 511) // 512)]


class CfgS:
    def __init__(self, T=2048, E=2048, R=64, NST=16, W=160, NH=4):
        self.T, self.E, self.R, self.NST, self.W = T, E, R, NST, W
        self.MC = E // 128
        self.NH = NH
        self.NG = NST // NH


def emit_scan_fused(nc, em, EN, cfg, io, bc_sb, pspool, ncore=8):
    T, E, NST, W = cfg.T, cfg.E, cfg.NST, cfg.W
    MC, NH, NG = cfg.MC, cfg.NH, cfg.NG

    ones1 = EN(nc.sbuf_tensor("ones1_s", [1, 128], F32))
    ones1b = EN(nc.sbuf_tensor("ones1b_s", [1, 128], BF16))
    bcB = EN(nc.sbuf_tensor("bcB_row", [1, max(cfg.NG * cfg.T, cfg.NST * cfg.W)], BF16))
    bcC = EN(nc.sbuf_tensor("bcC_row", [1, cfg.NG * cfg.T], BF16))
    A_sb = pspool["A_sb"]
    D_col = EN(nc.sbuf_tensor("D_col_s", [128, MC], F32))
    selv_sb = EN(nc.sbuf_tensor("selv_sb", [128, 8], F32))
    Hin = EN(nc.sbuf_tensor("Hin", [128, NST * MC], F32))
    Hall_sb = EN(nc.sbuf_tensor("Hall_sb", [128, ncore * MC * NST], F32))
    Brep = EN(nc.sbuf_tensor("Brep", [128, NG * T], BF16))
    Crep = EN(nc.sbuf_tensor("Crep", [128, NG * T], BF16))
    dcm = [EN(nc.sbuf_tensor(f"dcm{i}", [128, T], F32)) for i in range(2)]
    wcm = [EN(nc.sbuf_tensor(f"wcm{i}", [128, T], BF16)) for i in range(2)]
    xscm = [EN(nc.sbuf_tensor(f"xscm{i}", [128, T], BF16)) for i in range(2)]
    gcm = [EN(nc.sbuf_tensor(f"gcm{i}", [128, T], BF16)) for i in range(2)]
    a_big = EN(nc.sbuf_tensor("a_big", [128, 3 * T], F32))
    u_big = EN(nc.sbuf_tensor("u_big", [128, 3 * T], BF16))
    h_big = EN(nc.sbuf_tensor("h_big", [128, 3 * T], BF16))
    a_pl = [a_big[:, i * T:(i + 1) * T] for i in range(3)]
    u_pl = [u_big[:, i * T:(i + 1) * T] for i in range(3)]
    h_pl = [h_big[:, i * T:(i + 1) * T] for i in range(3)]
    z_pl = [EN(nc.sbuf_tensor(f"z_pl_s{i}", [128, T], BF16)) for i in range(2)]
    yacb = [EN(nc.sbuf_tensor(f"yac_s{i}", [128, T], BF16)) for i in range(2)]
    yh0 = [EN(nc.sbuf_tensor(f"yh0_{i}", [128, T], BF16)) for i in range(2)]
    y2m = [EN(nc.sbuf_tensor(f"y2m_{i}", [128, T], BF16)) for i in range(2)]
    t1s = EN(nc.sbuf_tensor("t1s", [128, T], F32))

    pss = pspool["ps"]
    NPS = len(pss)
    psi = [0]

    def next_ps():
        k = psi[0] % NPS
        psi[0] += 1
        return pss[k], f"ps{k}"

    em.op("vector", lambda h: h.memset(ones1[:], 1.0), writes=["ones1s_f"])
    em.op("vector", lambda h: h.memset(ones1b[:], 1.0), writes=["ones1s"])
    em.op("gpsimd", (lambda h: h.collective_compute(
        "AllGather", ALU.bypass, replica_groups=[list(range(ncore))],
        ins=[io["hout_st"].rearrange("p m n -> (p m n)")],
        outs=[io["hall_st"].rearrange("s p m n -> (s p m n)")])),
        reads=["hout_st"], writes=["hall_st"])
    em.dma("sync", D_col[:], io["D_in"].rearrange("(m p) -> p m", p=128),
           writes=["Dcol"], allow_slow_non_contiguous=True)
    em.dma("sync", selv_sb[:], io["selv"], writes=["selv"])

    def load_rows(buf, bufkey, n0, nplanes, col0, width):
        em.dma("sync", buf[0:1, 0:nplanes * width],
               io["bc_hbm"][n0:n0 + nplanes, col0:col0 + width],
               reads=["bc_hbm"], writes=[bufkey], allow_slow_non_contiguous=True)

    def bcast_build(dst, rows_key, srcbuf, srckey, nplanes, width, dst_w):
        for j in range(nplanes):
            for cc in range(0, width, 512):
                cw = min(512, width - cc)
                pst, psk = next_ps()
                em.op("tensor", (lambda h, j=j, cc=cc, cw=cw, pst=pst, srcbuf=srcbuf, width=width:
                                 h.matmul(pst[:, 0:cw], ones1b[0:1, :],
                                          srcbuf[0:1, j * width + cc:j * width + cc + cw],
                                          start=True, stop=True)),
                      reads=[srckey, "ones1s"], writes=[psk])
                em.op("scalar", (lambda h, j=j, cc=cc, cw=cw, pst=pst:
                                 h.activation(out=dst[:, j * dst_w + cc:j * dst_w + cc + cw],
                                              in_=pst[:, 0:cw], func=AF.Copy)),
                      reads=[psk], writes=[(rows_key, j)])

    S1 = ["xn_dead", "apre", "apre0", "upre", "hpre"]
    for half in range(NH):
        n0 = half * NG
        load_rows(bcB, "bcB", n0, NG, 0, T)
        load_rows(bcC, "bcC", NST + n0, NG, 0, T)
        bcast_build(Brep, "BrhJ", bcB, "bcB", NG, T, T)
        bcast_build(Crep, "CrhJ", bcC, "bcC", NG, T, T)
        if half == 0:
            em.dma("sync", Hall_sb[:].rearrange("p (s m n) -> p s m n", s=ncore, n=NST),
                   io["hall_st"].rearrange("s p m n -> p s m n"),
                   reads=["hall_st"], writes=["Hall_sb"], allow_slow_non_contiguous=True)
            for si in range(ncore):
                if si == 0:
                    em.op("vector", (lambda h, si=si: h.tensor_scalar(
                        out=Hin[:], in0=Hall_sb[:, si * MC * NST:(si + 1) * MC * NST],
                        scalar1=selv_sb[:, si:si + 1], scalar2=None, op0=ALU.mult)),
                        reads=["Hall_sb", "selv"], writes=["Hin"])
                else:
                    em.op("vector", (lambda h, si=si: h.scalar_tensor_tensor(
                        out=Hin[:], in0=Hall_sb[:, si * MC * NST:(si + 1) * MC * NST],
                        scalar=selv_sb[:, si:si + 1], in1=Hin[:],
                        op0=ALU.mult, op1=ALU.add)),
                        reads=["Hall_sb", "selv", "Hin"], writes=["Hin"])

        for m in range(MC):
            mb = m % 2
            yac = yacb[mb]
            yk = ("yacs", mb)
            em.dma("sync", dcm[mb][:], io["delta_st"][m * 128:(m + 1) * 128, :],
                   reads=[("dhbm", m)], writes=[("dcm", mb)])
            em.dma("sync", wcm[mb][:], io["w_st"][m * 128:(m + 1) * 128, :],
                   reads=[("whbm", m)], writes=[("wcm", mb)])
            if half == NH - 1:
                em.dma("sync", xscm[mb][:], io["xs_st"][m * 128:(m + 1) * 128, :],
                       reads=[("xshbm", m)], writes=[("xscm", mb)])
                em.dma("sync", gcm[mb][:], io["g_st"][m * 128:(m + 1) * 128, :],
                       reads=[("ghbm", m, it) for it in range(len(cfg2_nt4(T)))],
                       writes=[("gcm", mb)])
            for j in range(NG):
                n = n0 + j
                ai = n % 3
                em.op("scalar", (lambda h, m=m, mb=mb, n=n, ai=ai: h.activation(
                    out=a_pl[ai][:], in_=dcm[mb][:], func=AF.Exp,
                    scale=A_sb[:, m * NST + n:m * NST + n + 1])),
                    reads=[("dcm", mb), "Asb"], writes=[("a", ai)])
                em.op("vector", (lambda h, mb=mb, j=j, ai=ai: h.tensor_tensor(
                    out=u_pl[ai][:], in0=Brep[:, j * T:(j + 1) * T], in1=wcm[mb][:],
                    op=ALU.mult)),
                    reads=[("BrhJ", j), ("wcm", mb)],
                    writes=[("u", ai)])
                em.op("vector", (lambda h, m=m, n=n, ai=ai: h.tensor_tensor_scan(
                    out=h_pl[ai][:], data0=a_pl[ai][:], data1=u_pl[ai][:],
                    initial=Hin[:, m * NST + n:m * NST + n + 1],
                    op0=ALU.mult, op1=ALU.add)),
                    reads=[("a", ai), ("u", ai), "Hin"], writes=[("h", ai)])
                zi = n % 2
                if j == 0:
                    em.op("vector", (lambda h, j=j, ai=ai, yac=yac: h.tensor_tensor(
                        out=yac[:], in0=h_pl[ai][:], in1=Crep[:, j * T:(j + 1) * T],
                        op=ALU.mult)),
                        reads=[("h", ai), ("CrhJ", j)], writes=[yk])
                else:
                    em.op("vector", (lambda h, j=j, ai=ai, zi=zi: h.tensor_tensor(
                        out=z_pl[zi][:], in0=h_pl[ai][:], in1=Crep[:, j * T:(j + 1) * T],
                        op=ALU.mult)),
                        reads=[("h", ai), ("CrhJ", j)], writes=[("z", zi)])
                    em.dma("gpsimd", yac[:], z_pl[zi][:], accum_op=ALU.add,
                           reads=[("z", zi)] + S1, writes=[yk])
            if half == 0:
                em.dma("scalar", io["yh_st"][m * 128:(m + 1) * 128, :], yac[:],
                       reads=[yk] + S1, writes=[("yhst", m)])
            elif half < NH - 1:
                em.dma("gpsimd", io["yh_st"][m * 128:(m + 1) * 128, :], yac[:],
                       accum_op=ALU.add,
                       reads=[yk, ("yhst", m)] + S1, writes=[("yhst", m)])
            else:
                em.dma("gpsimd", yac[:], io["yh_st"][m * 128:(m + 1) * 128, :],
                       accum_op=ALU.add,
                       reads=[("yhst", m)] + S1, writes=[yk])
                em.op("vector", (lambda h, m=m, mb=mb, yac=yac: h.scalar_tensor_tensor(
                    out=t1s[:], in0=xscm[mb][:], scalar=D_col[:, m:m + 1], in1=yac[:],
                    op0=ALU.mult, op1=ALU.add)),
                    reads=[("xscm", mb), yk, "Dcol"], writes=["t1s"])
                y2b = m % 2
                em.op("vector", (lambda h, mb=mb, y2b=y2b: h.tensor_tensor(
                    out=y2m[y2b][:], in0=t1s[:], in1=gcm[mb][:], op=ALU.mult)),
                    reads=["t1s", ("gcm", mb)], writes=[("y2m", y2b)])
                em.dma("scalar", io["y2_st"][m * 128:(m + 1) * 128, :], y2m[y2b][:],
                       reads=[("y2m", y2b)] + S1, writes=[("y2st", m)])
    return io


def build_fused(T=2048, D=1024, E=2048, R=64, NST=16, W=160, ncore=8):
    cfa = CfgA(T=T, D=D, E=E, R=R, NST=NST)
    cfs = CfgS(T=T, E=E, R=R, NST=NST, W=W)
    cfc = CfgC(T=T, D=D, E=E)
    nc = bass.Bass("TRN2", target_bir_lowering=False)
    MC = E // 128

    io = {}
    io["x_in"] = nc.dram_tensor("x_in", [T + 128, D], F32, kind="ExternalInput").ap()
    io["W_in"] = nc.dram_tensor("W_in", [2 * E, D], F32, kind="ExternalInput").ap()
    io["W_xproj"] = nc.dram_tensor("W_xproj", [96, E], F32, kind="ExternalInput").ap()
    io["W_dt"] = nc.dram_tensor("W_dt", [E, R], F32, kind="ExternalInput").ap()
    io["b_dt"] = nc.dram_tensor("b_dt", [E], F32, kind="ExternalInput").ap()
    io["conv_w"] = nc.dram_tensor("conv_w", [E, 4], F32, kind="ExternalInput").ap()
    io["conv_b"] = nc.dram_tensor("conv_b", [E], F32, kind="ExternalInput").ap()
    io["norm_w"] = nc.dram_tensor("norm_w", [D], F32, kind="ExternalInput").ap()
    io["A_log_in"] = nc.dram_tensor("A_log_in", [E, NST], F32, kind="ExternalInput").ap()
    io["D_in"] = nc.dram_tensor("D_in", [E], F32, kind="ExternalInput").ap()
    io["W_out"] = nc.dram_tensor("W_out", [D, E], F32, kind="ExternalInput").ap()
    io["rhs"] = nc.dram_tensor("rhs", [D, D], F32, kind="ExternalInput").ap()
    io["selv"] = nc.dram_tensor("selv", [128, 8], F32, kind="ExternalInput").ap()
    io["out"] = nc.dram_tensor("out", [T, D], F32, kind="ExternalOutput").ap()
    io["delta_out"] = nc.dram_tensor("delta_st", [E, T], F32, kind="Internal").ap()
    io["w_out"] = nc.dram_tensor("w_st", [E, T], BF16, kind="Internal").ap()
    io["xs_out"] = nc.dram_tensor("xs_st", [E, T], BF16, kind="Internal").ap()
    io["g_out"] = nc.dram_tensor("g_st", [E, T], BF16, kind="Internal").ap()
    io["delta_st"] = io["delta_out"]
    io["w_st"] = io["w_out"]
    io["xs_st"] = io["xs_out"]
    io["g_st"] = io["g_out"]
    io["y2_st"] = nc.dram_tensor("y2_st", [E, T], BF16, kind="Internal").ap()
    io["yh_st"] = nc.dram_tensor("yh_st", [E, T], BF16, kind="Internal").ap()
    io["bc_hbm"] = nc.dram_tensor("bc_hbm", [2 * NST, T], BF16, kind="Internal").ap()
    io["hout_st"] = nc.dram_tensor("hout_st", [128, MC, NST], F32, kind="Internal").ap()
    io["hall_st"] = nc.dram_tensor("hall_st", [ncore, 128, MC, NST], F32, kind="Internal").ap()

    with contextlib.ExitStack() as octx:
        em = Emitter(nc, octx)
        EN0 = octx.enter_context
        pspool = {
            "ps": [EN0(nc.psum_tensor(f"ps{i}", [128, 512], F32)) for i in range(4)],
            "ps_big": EN0(nc.psum_tensor("ps_big", [128, 4 * 512], F32)),
            "A_sb": EN0(nc.sbuf_tensor("A_sb", [128, NST * MC], F32)),
            "Hout": EN0(nc.sbuf_tensor("Hout", [128, MC * NST], F32)),
        }
        pspool["W"] = W
        pspool["ncore"] = ncore
        bc_sb = None
        with contextlib.ExitStack() as c1:
            emit_phase_a(nc, em, c1.enter_context, cfa, io=io, fused=True,
                         bc_sb=bc_sb, pspool=pspool)
            em.barrier()
        with contextlib.ExitStack() as c2:
            emit_scan_fused(nc, em, c2.enter_context, cfs, io, bc_sb, pspool, ncore=ncore)
            em.barrier()
        with contextlib.ExitStack() as c3:
            io["x2_in"] = io["x_in"][128:, :]
            io["y_in"] = io["y2_st"]; io["xs_in"] = io["xs_st"]; io["g_in"] = io["g_st"]
            emit_phase_c(nc, em, c3.enter_context, cfc, io=io, fused=True, pspool=pspool)
        for e in ("sync", "tensor", "vector", "scalar", "gpsimd"):
            em.wait_all(e)
        block = octx.enter_context(nc.Block())
        em.build(block)
    return nc


def make_in_maps(ins0, T, ncore=8):
    x = np.ascontiguousarray(np.asarray(ins0["L0_attn_x"])[0], dtype=np.float32)
    L, D = x.shape
    xpad = np.concatenate([np.zeros((128, D), np.float32), x], axis=0)
    common = dict(W_in=np.asarray(ins0["L0_attn_W_in"]), W_xproj=np.asarray(ins0["L0_attn_W_xproj"]),
                  W_dt=np.asarray(ins0["L0_attn_W_dt"]), b_dt=np.asarray(ins0["L0_attn_b_dt"]),
                  conv_w=np.asarray(ins0["L0_attn_conv_w"]), conv_b=np.asarray(ins0["L0_attn_conv_b"]),
                  norm_w=np.asarray(ins0["L0_attn_norm_w"]),
                  A_log_in=np.asarray(ins0["L0_attn_A_log"]), D_in=np.asarray(ins0["L0_attn_D"]),
                  W_out=np.asarray(ins0["L0_attn_W_out"]), rhs=np.asarray(ins0["L0_ffn_rhs"]))
    maps = []
    for i in range(ncore):
        sel = np.zeros((128, 8), np.float32)
        if i > 0:
            sel[:, i - 1] = 1.0
        maps.append(dict(
            x_in=np.ascontiguousarray(xpad[i * T:(i + 1) * T + 128]),
            selv=sel,
            **common))
    return maps


def kernel(L0_attn_x, L0_attn_W_in, L0_attn_conv_w, L0_attn_conv_b, L0_attn_W_xproj,
           L0_attn_W_dt, L0_attn_b_dt, L0_attn_A_log, L0_attn_D, L0_attn_W_out,
           L0_attn_norm_w, L0_ffn_rhs):
    from concourse.bass_utils import run_bass_kernel_spmd
    NCORE = 8
    x = np.asarray(L0_attn_x)
    L = x.shape[1]
    T = L // NCORE
    ins0 = dict(L0_attn_x=x, L0_attn_W_in=L0_attn_W_in, L0_attn_conv_w=L0_attn_conv_w,
                L0_attn_conv_b=L0_attn_conv_b, L0_attn_W_xproj=L0_attn_W_xproj,
                L0_attn_W_dt=L0_attn_W_dt, L0_attn_b_dt=L0_attn_b_dt,
                L0_attn_A_log=L0_attn_A_log, L0_attn_D=L0_attn_D,
                L0_attn_W_out=L0_attn_W_out, L0_attn_norm_w=L0_attn_norm_w,
                L0_ffn_rhs=L0_ffn_rhs)
    nc = build_fused(T=T, ncore=NCORE)
    maps = make_in_maps(ins0, T, NCORE)
    res = run_bass_kernel_spmd(nc, maps, core_ids=list(range(NCORE)))
    out = np.concatenate([res.results[i]["out"] for i in range(NCORE)], axis=0)
    return out[None]

